# version 17
# baseline (speedup 1.0000x reference)
import numpy as np
import ml_dtypes

import concourse.bass as bass
import concourse.mybir as mybir
from concourse.bass_utils import run_bass_kernel_spmd
from concourse.aot_env import allow_ici_all_to_all

F32 = mybir.dt.float32
BF16 = mybir.dt.bfloat16
I32 = mybir.dt.int32
AF = mybir.ActivationFunctionType
ALU = mybir.AluOpType
AX = mybir.AxisListType
BF16NP = ml_dtypes.bfloat16

NCORES = 8
D = 1024
SEQ = 2048
BPC = 2
NT = SEQ // 128

ENGINES = ("sync", "scalar", "vector", "tensor", "gpsimd")


class Tok:
    __slots__ = ("sem", "val")

    def __init__(self, sem, val):
        self.sem = sem
        self.val = val


class Buf:
    __slots__ = ("ap", "wtok", "rtoks", "name")

    def __init__(self, ap, name=""):
        self.ap = ap
        self.wtok = None
        self.rtoks = []
        self.name = name


class Prog:
    def __init__(self, nc):
        self.nc = nc
        self.q = {e: [] for e in ENGINES}
        self.cnt = {}
        self.lastw = {e: {} for e in ENGINES}
        self.semnames = []
        for e in ENGINES:
            for pfx in ("c_", "d_"):
                self.cnt[pfx + e] = 0
                self.semnames.append(pfx + e)
        self.cnt["cc"] = 0
        self.semnames.append("cc")
        NSLOT = 28
        self.nslot = NSLOT
        for i in range(NSLOT):
            self.cnt[f"dq{i}"] = 0
            self.semnames.append(f"dq{i}")
        self.slot_pool = {"sync": list(range(0, 14)), "scalar": list(range(14, 20)),
                          "gpsimd": list(range(20, 26)), "vector": list(range(26, 28)),
                          "tensor": list(range(26, 28))}
        self.slot_rr = {e: 0 for e in ENGINES}
        self.sems = None
        self.ckpts = {e: [] for e in ENGINES}

    def _ckpt_for(self, q, val):
        for total, cval in self.ckpts[q]:
            if total >= val:
                return Tok("c_" + q, cval)
        total = self.cnt["d_" + q]
        assert total >= val
        self.cnt["c_" + q] += 1
        cval = self.cnt["c_" + q]

        def closure(eng, _tot=total, _q=q, _cv=cval):
            eng.wait_ge(self.sems["d_" + _q], _tot)
            eng.nop().then_inc(self.sems["c_" + _q], 1)
            eng.wait_ge(self.sems["c_" + _q], _cv)

        self.q[q].append(closure)
        self.ckpts[q].append((total, cval))
        return Tok("c_" + q, cval)

    def _collect(self, e, sem, tokval, reads, writes):
        deps = []
        for r in reads:
            if isinstance(r, Buf):
                if r.wtok is not None:
                    deps.append(r.wtok)
            elif r is not None:
                deps.append(r)
        for w in writes:
            if w.wtok is not None:
                deps.append(w.wtok)
            deps.extend(w.rtoks)
        deps2 = []
        for t in deps:
            if t.sem.startswith("d_"):
                q = t.sem[2:]
                if e == q:
                    deps2.append(Tok(t.sem, self.cnt["d_" + q]))
                else:
                    deps2.append(self._ckpt_for(q, t.val))
            else:
                deps2.append(t)
        deps = deps2
        lw = self.lastw[e]
        need = {}
        for t in deps:
            if t.sem == sem and t.val >= tokval:
                continue
            if t.val > max(lw.get(t.sem, 0), need.get(t.sem, 0)):
                need[t.sem] = t.val
        for s, v in need.items():
            lw[s] = v
        return need

    def _finish(self, tok, reads, writes):
        for r in reads:
            if isinstance(r, Buf):
                r.rtoks.append(tok)
        for w in writes:
            w.wtok = tok
            w.rtoks = []

    def op(self, e, fn, reads=(), writes=(), dma=False, cc=False):
        writes = [w for w in writes if w is not None]
        sem = "cc" if cc else (("d_" if dma else "c_") + e)
        amt = 16 if dma else 1
        self.cnt[sem] += amt
        tok = Tok(sem, self.cnt[sem])
        need = self._collect(e, sem, tok.val, reads, writes)

        selfwait = (e == "gpsimd" and not dma and not cc)

        def closure(eng, _fn=fn, _need=need, _sem=sem, _amt=amt, _sw=selfwait,
                    _val=tok.val):
            for s, v in _need.items():
                eng.wait_ge(self.sems[s], v)
            _fn(eng).then_inc(self.sems[_sem], _amt)
            if _sw:
                eng.wait_ge(self.sems[_sem], _val)

        self.q[e].append(closure)
        self._finish(tok, reads, writes)
        return tok

    def mm_group(self, calls, reads=(), writes=()):
        writes = [w for w in writes if w is not None]
        sem = "c_tensor"
        self.cnt[sem] += 1
        tok = Tok(sem, self.cnt[sem])
        need = self._collect("tensor", sem, tok.val, reads, writes)

        def closure(eng, _calls=list(calls), _need=need):
            for s, v in _need.items():
                eng.wait_ge(self.sems[s], v)
            inst = None
            for fn in _calls:
                inst = fn(eng)
            inst.then_inc(self.sems["c_tensor"], 1)

        self.q["tensor"].append(closure)
        self._finish(tok, reads, writes)
        return tok

    def dma(self, e, out, in_, reads=(), writes=()):
        return self.dma_group(e, [(out, in_)], reads=reads, writes=writes)

    def dma_group(self, e, pairs, reads=(), writes=()):
        writes = [w for w in writes if w is not None]
        pool = self.slot_pool[e]
        slot = pool[self.slot_rr[e] % len(pool)]
        self.slot_rr[e] += 1
        sem = f"dq{slot}"
        prev_end = self.cnt[sem]
        self.cnt[sem] += 16 * len(pairs)
        tok = Tok(sem, self.cnt[sem])
        need = self._collect(e, sem, tok.val, reads, writes)
        if prev_end > self.lastw[e].get(sem, 0):
            need = dict(need)
            need[sem] = max(need.get(sem, 0), prev_end)
            self.lastw[e][sem] = need[sem]

        def closure(eng, _pairs=list(pairs), _need=need, _sem=sem):
            for sn, v in _need.items():
                eng.wait_ge(self.sems[sn], v)
            for o, i in _pairs:
                eng.dma_start(out=o, in_=i).then_inc(self.sems[_sem], 16)

        self.q[e].append(closure)
        self._finish(tok, reads, writes)
        return tok

    def dma_multi(self, e, pairs, writebuf, reads=()):
        return self.dma_group(e, pairs, reads=reads, writes=(writebuf,))

    def barrier(self):
        for q in ENGINES:
            if self.cnt["d_" + q] > 0:
                self._ckpt_for(q, self.cnt["d_" + q])
        toks = [Tok(s, self.cnt[s]) for s in self.semnames
                if self.cnt[s] > 0 and not s.startswith("d_")]
        for e in ENGINES:
            lw = self.lastw[e]
            need = {}
            for t in toks:
                if t.sem == "c_" + e:
                    continue
                if t.val > lw.get(t.sem, 0):
                    need[t.sem] = t.val
            for s, v in need.items():
                lw[s] = v
            if need:
                def closure(eng, _need=need):
                    for s, v in _need.items():
                        eng.wait_ge(self.sems[s], v)
                self.q[e].append(closure)

    def tok(self, e):
        return Tok("c_" + e, self.cnt["c_" + e])

    def emit(self, stack):
        self.sems = {s: stack.enter_context(self.nc.semaphore(s)) for s in self.semnames}
        stack.enter_context(allow_ici_all_to_all())
        block = stack.enter_context(self.nc.Block())
        for e in ENGINES:
            items = self.q[e]
            if items:
                getattr(block, e)(lambda eng, _items=items: [f(eng) for f in _items])


def mk_sb(stack, nc, name, shape, dtype=F32):
    return Buf(stack.enter_context(nc.sbuf_tensor(name, shape, dtype))[:], name)


def mk_ps(stack, nc, name, shape, dtype=F32):
    return Buf(stack.enter_context(nc.psum_tensor(name, shape, dtype))[:], name)



class Consts:

    def __init__(self, p, stack, capd):
        nc = p.nc
        self.ident = mk_sb(stack, nc, "c_ident", [128, 128], BF16)
        self.onescol = mk_sb(stack, nc, "c_onescol", [128, 1], BF16)
        self.onesrow = mk_sb(stack, nc, "c_onesrow", [1, 128], BF16)
        self.onesrow33 = mk_sb(stack, nc, "c_onesrow33", [33, 128], BF16)
        self.sel2 = mk_sb(stack, nc, "c_sel2", [2, 128], BF16)
        self.cols = mk_sb(stack, nc, "c_cols", [128, 8])
        p.dma("sync", self.ident.ap, capd["ident"], writes=(self.ident,))
        p.dma("sync", self.onescol.ap, capd["onescol"], writes=(self.onescol,))
        p.dma("sync", self.onesrow.ap, capd["onesrow"], writes=(self.onesrow,))
        p.dma("sync", self.onesrow33.ap[32:33, :], capd["onesrow"], writes=(self.onesrow33,))
        p.op("vector", lambda vv: vv.memset(self.onesrow33.ap[0:32, :], 0.0), writes=(self.onesrow33,))
        p.dma("sync", self.sel2.ap, capd["sel2"], writes=(self.sel2,))
        vals = [1e-5, -0.5, -1.0, 1e-6, 1e-24, -100.0, 0.5, 1.0]
        for i, v in enumerate(vals):
            p.op("vector", lambda vv, _i=i, _v=v: vv.memset(self.cols.ap[:, _i:_i + 1], _v),
                 writes=(self.cols,))

    def col(self, i, n=128):
        return self.cols.ap[0:n, i:i + 1]


def ln_tile(p, C, xbuf, xap, mr, sq_scr, ncol=D, eps=1e-5):
    mrap = mr.ap
    p.op("vector", lambda v, _x=xap: v.tensor_reduce(mrap[:, 2:3], _x, AX.X, ALU.add),
         reads=(xbuf,), writes=(mr,))
    p.op("scalar", lambda s, _x=xap: s.activation(sq_scr.ap[:, 0:ncol], _x, AF.Square,
                                                  accum_out=mrap[:, 3:4]),
         reads=(xbuf,), writes=(sq_scr, mr))
    inv = 1.0 / ncol
    p.op("vector", lambda v: v.tensor_scalar(mrap[:, 1:2], mrap[:, 2:3], inv, None, ALU.mult),
         reads=(mr,), writes=(mr,))
    p.op("vector", lambda v: v.tensor_scalar(mrap[:, 3:4], mrap[:, 3:4], inv, None, ALU.mult),
         reads=(mr,), writes=(mr,))
    p.op("vector", lambda v: v.tensor_tensor(mrap[:, 2:3], mrap[:, 1:2], mrap[:, 1:2], ALU.mult),
         reads=(mr,), writes=(mr,))
    p.op("vector", lambda v: v.tensor_tensor(mrap[:, 3:4], mrap[:, 3:4], mrap[:, 2:3], ALU.subtract),
         reads=(mr,), writes=(mr,))
    epscol = C.col(0) if eps == 1e-5 else None
    assert epscol is not None
    p.op("scalar", lambda s: s.activation(mrap[:, 3:4], mrap[:, 3:4], AF.Ln, bias=epscol),
         reads=(mr, C.cols), writes=(mr,))
    p.op("scalar", lambda s: s.activation(mrap[:, 0:1], mrap[:, 3:4], AF.Exp, scale=C.col(1)),
         reads=(mr, C.cols), writes=(mr,))


def _make_strided_mid(d0, stride, cnt):
    return _ap_mid(d0, stride, cnt)


def _ap_mid(d0, stride, cnt):
    import bass_rust
    ap = d0.unsqueeze(1)
    lst = [list(x) for x in ap.ap]
    lst[1] = [stride, cnt]
    new = bass_rust.AP(tensor=ap.tensor, offset=ap.offset, ap=lst)
    return new


def transpose4(p, C, src_buf, src_ap_fn, ps_tr, dst_buf, dst_ap_fn, n=8):
    src_aps = [src_ap_fn(i) for i in range(n)]
    dst_aps = [dst_ap_fn(i) for i in range(n)]
    for g in range((n + 3) // 4):
        cnt = min(4, n - g * 4)
        for u in range(cnt):
            dt_ = g * 4 + u
            p.op("tensor", lambda tn, _s=src_aps[dt_], _u=u: tn.transpose(
                ps_tr.ap[:, _u * 128:(_u + 1) * 128], _s, C.ident.ap),
                reads=(src_buf, C.ident), writes=(ps_tr,))
        offs = [d.offset for d in dst_aps[g * 4:g * 4 + cnt]]
        uniform = (cnt > 1 and len(set(offs[i + 1] - offs[i] for i in range(cnt - 1))) == 1)
        if uniform:
            stride = offs[1] - offs[0]
            dv = _make_strided_mid(dst_aps[g * 4], stride, cnt)
            p.op("vector", lambda v, _d=dv, _c=cnt: v.tensor_copy(
                _d, ps_tr.ap[:, 0:_c * 128].rearrange("p (c f) -> p c f", f=128)),
                reads=(ps_tr,), writes=(dst_buf,))
        else:
            for u in range(cnt):
                dt_ = g * 4 + u
                p.op("vector", lambda v, _d=dst_aps[dt_], _u=u: v.tensor_copy(
                    _d, ps_tr.ap[:, _u * 128:(_u + 1) * 128]),
                    reads=(ps_tr,), writes=(dst_buf,))



def build_constants():
    c = {}
    c["ident"] = np.eye(128, dtype=BF16NP)
    c["onescol"] = np.ones((128, 1), dtype=BF16NP)
    c["onesrow"] = np.ones((1, 128), dtype=BF16NP)
    sel2 = np.zeros((2, 128), dtype=BF16NP)
    sel2[0, 0:64] = 1
    sel2[1, 64:128] = 1
    c["sel2"] = sel2
    ROT = 32
    inv_freq = 1.0 / 10000.0 ** (np.arange(0, ROT, 2, dtype=np.float64) / ROT)
    freqs = np.arange(SEQ, dtype=np.float64)[:, None] * inv_freq[None, :]
    c["rope_cos"] = np.cos(freqs).astype(np.float32)
    c["rope_sin"] = np.sin(freqs).astype(np.float32)
    return c


HEADPERM = []
for _m in range(4):
    HEADPERM += [4 * _m, 4 * _m + 2, 4 * _m + 1, 4 * _m + 3]



B0_PHASES = 4
B0_ATTN_NORM = True


def emit_block0(p, stack, C, x_in, x_out, W):
    nc = p.nc
    sb = lambda n, s, d=F32: mk_sb(stack, nc, n, s, d)
    ps = lambda n, s, d=F32: mk_ps(stack, nc, n, s, d)
    HEADS, KVH = 16, 8

    ropec = sb("b0_ropec", [128, NT * 16])
    ropes = sb("b0_ropes", [128, NT * 16])
    p.dma("sync", ropec.ap.rearrange("p (t f) -> p t f", f=16),
          W["rope_cos"].rearrange("(t p) f -> p t f", p=128), writes=(ropec,))
    p.dma("sync", ropes.ap.rearrange("p (t f) -> p t f", f=16),
          W["rope_sin"].rearrange("(t p) f -> p t f", p=128), writes=(ropes,))

    wq = sb("b0_wq", [128, 8 * D], BF16)
    wk = sb("b0_wk", [128, 8 * 512], BF16)
    wv = sb("b0_wv", [128, 8 * 512], BF16)
    wo = sb("b0_wo", [128, 8 * D], BF16)
    bo = sb("b0_bo", [1, D], BF16)
    wstage = sb("b0_wstage", [128, 512])

    def load_weight(dst, dram, cols, fold):
        for t in range(8):
            for c0 in range(0, cols, 512):
                w = min(512, cols - c0)
                src = dram[t * 128:(t + 1) * 128, c0:c0 + w]
                p.dma("sync", wstage.ap[:, 0:w], src, writes=(wstage,))
                p.op("vector", lambda v, _t=t, _c0=c0, _w=w: v.tensor_scalar(
                    dst.ap[:, _t * cols + _c0:_t * cols + _c0 + _w], wstage.ap[:, 0:_w],
                    fold, None, ALU.mult),
                    reads=(wstage,), writes=(dst,))

    load_weight(wq, W["Wq"], D, 0.125)
    load_weight(wk, W["Wk"], 512, 1.0)
    load_weight(wv, W["Wv"], 512, 1.0)
    load_weight(wo, W["Wo"], D, 1.0)
    for c0 in (0, 512):
        p.dma("sync", wstage.ap[0:1, 0:512], W["bo"][:, c0:c0 + 512], writes=(wstage,))
        p.op("vector", lambda v, _c0=c0: v.tensor_scalar(bo.ap[:, _c0:_c0 + 512],
                                                         wstage.ap[0:1, 0:512], 1.0, None, ALU.mult),
             reads=(wstage,), writes=(bo,))

    xtP = [sb("b0_xt0", [128, D]), sb("b0_xt1", [128, D])]
    sq_scr = sb("b0_sq", [128, D], BF16)
    mrP = [sb("b0_mr0", [128, 4]), sb("b0_mr1", [128, 4])]
    lnbfP = [sb("b0_lnbf0", [128, D], BF16), sb("b0_lnbf1", [128, D], BF16)]
    lnxT = sb("b0_lnxT", [128, 8 * SEQ], BF16)
    qnatP = [sb("b0_qnat0", [128, D]), sb("b0_qnat1", [128, D])]
    knatP = [sb("b0_knat0", [128, 512]), sb("b0_knat1", [128, 512])]
    qbfP = [sb("b0_qbf0", [128, D], BF16), sb("b0_qbf1", [128, D], BF16)]
    kbfP = [sb("b0_kbf0", [128, 512], BF16), sb("b0_kbf1", [128, 512], BF16)]
    qT = sb("b0_qT", [128, 8 * SEQ], BF16)
    kT = sb("b0_kT", [128, 4 * SEQ], BF16)
    vnat = sb("b0_vnat", [128, NT * 512], BF16)
    attnT = lnxT
    expab = [sb("b0_expab0", [128, 1024], BF16), sb("b0_expab1", [128, 1024], BF16)]
    acc_s = sb("b0_accs", [128, 1024])
    lnrec = sb("b0_lnrec", [33, 512])
    recipb = sb("b0_recipb", [33, 512], BF16)
    rsb = sb("b0_rsb", [128, 512])
    onatP = [sb("b0_onat0", [128, D]), sb("b0_onat1", [128, D])]

    P0 = ps("b0_P0", [128, 1024])
    P1 = ps("b0_P1", [128, 1024])
    ps_mm = [P0, P0]
    ps_tr = ps("b0_pstr", [128, 512], BF16)
    ps_l = [P0, P1]
    ps_av = ps("b0_psav", [128, 512])
    ps_s = ps("b0_pss", [33, 512])
    ps_r = ps("b0_psr", [128, 512])
    p.op("vector", lambda v: v.memset(ps_s.ap, 1.0), writes=(ps_s,))

    for b in range(BPC):
        xin_b = x_in[b]
        xout_b = x_out[b]
        for t in range(NT):
            xt, mr, lnbf = xtP[t % 2], mrP[t % 2], lnbfP[t % 2]
            p.dma("sync", xt.ap, xin_b[t * 128:(t + 1) * 128, :], writes=(xt,))
            ln_tile(p, C, xt, xt.ap, mr, sq_scr)
            p.op("vector", lambda v, _x=xt, _m=mr, _l=lnbf: v.tensor_scalar(
                _l.ap, _x.ap, _m.ap[:, 1:2], _m.ap[:, 0:1], ALU.subtract, ALU.mult),
                 reads=(xt, mr), writes=(lnbf,))
            transpose4(p, C, lnbf, lambda dt_, _l=lnbf: _l.ap[:, dt_ * 128:(dt_ + 1) * 128], ps_tr,
                       lnxT, lambda dt_, _t=t: lnxT.ap[:, dt_ * SEQ + _t * 128:dt_ * SEQ + (_t + 1) * 128], 8)

        def _qkv_proj(t):
            knat = knatP[t % 2]
            qnat = qnatP[t % 2]
            lh = [lnxT.ap[:, kt * SEQ + t * 128:kt * SEQ + (t + 1) * 128] for kt in range(8)]
            p.mm_group([(lambda tn, _l=lh[kt], _r=wk.ap[:, kt * 512:(kt + 1) * 512],
                         _s=(kt == 0), _e=(kt == 7):
                         tn.matmul(P0.ap[:, 0:512], _l, _r, start=_s, stop=_e)) for kt in range(8)],
                       reads=(lnxT, wk), writes=(P0,))
            p.op("vector", lambda v, _k=knat: v.tensor_copy(_k.ap, P0.ap[:, 0:512]),
                 reads=(P0,), writes=(knat,))
            p.mm_group([(lambda tn, _l=lh[kt], _r=wv.ap[:, kt * 512:(kt + 1) * 512],
                         _s=(kt == 0), _e=(kt == 7):
                         tn.matmul(P0.ap[:, 512:1024], _l, _r, start=_s, stop=_e)) for kt in range(8)],
                       reads=(lnxT, wv), writes=(P0,))
            p.op("vector", lambda v, _t=t: v.tensor_copy(vnat.ap[:, _t * 512:(_t + 1) * 512],
                                                         P0.ap[:, 512:1024]),
                 reads=(P0,), writes=(vnat,))
            for ch in range(2):
                dstp = P1.ap[:, ch * 512:(ch + 1) * 512]
                p.mm_group([(lambda tn, _l=lh[kt], _d=dstp,
                             _r=wq.ap[:, kt * D + ch * 512:kt * D + (ch + 1) * 512],
                             _s=(kt == 0), _e=(kt == 7):
                             tn.matmul(_d, _l, _r, start=_s, stop=_e)) for kt in range(8)],
                           reads=(lnxT, wq), writes=(P1,))
                p.op("vector", lambda v, _ch=ch, _d=dstp, _q=qnat: v.tensor_copy(
                    _q.ap[:, _ch * 512:(_ch + 1) * 512], _d),
                     reads=(P1,), writes=(qnat,))

        def _qkv_post(t):
            knat, kbf = knatP[t % 2], kbfP[t % 2]
            qnat, qbf = qnatP[t % 2], qbfP[t % 2]
            rope_apply(p, C, knat, kbf, ropec, ropes, t, KVH)
            transpose4(p, C, kbf, lambda dt_, _k=kbf: _k.ap[:, dt_ * 128:(dt_ + 1) * 128], ps_tr,
                       kT, lambda dt_, _t=t: kT.ap[:, dt_ * SEQ + _t * 128:dt_ * SEQ + (_t + 1) * 128], 4)
            rope_apply(p, C, qnat, qbf, ropec, ropes, t, HEADS)
            transpose4(p, C, qbf, lambda dt_, _q=qbf: _q.ap[:, dt_ * 128:(dt_ + 1) * 128], ps_tr,
                       qT, lambda dt_, _t=t: qT.ap[:, dt_ * SEQ + _t * 128:dt_ * SEQ + (_t + 1) * 128], 8)

        for t in range(NT if B0_PHASES >= 2 else 0):
            _qkv_proj(t)
            if t > 0:
                _qkv_post(t - 1)
        if B0_PHASES >= 2:
            _qkv_post(NT - 1)

        for m in range(4 if B0_PHASES >= 3 else 0):
            for jj in range(2):
                tq = 2 * m + jj
                for ch in range(4):
                    for kt in range(NT):
                        pl = ps_l[kt % 2]
                        exb = expab[kt % 2]
                        la = kT.ap[0:64, m * SEQ + kt * 128:m * SEQ + (kt + 1) * 128]
                        lb = kT.ap[64:128, m * SEQ + kt * 128:m * SEQ + (kt + 1) * 128]
                        ra = qT.ap[0:64, tq * SEQ + ch * 512:tq * SEQ + (ch + 1) * 512]
                        rb = qT.ap[64:128, tq * SEQ + ch * 512:tq * SEQ + (ch + 1) * 512]
                        p.mm_group(
                            [lambda tn, _la=la, _ra=ra, _pl=pl: tn.matmul(
                                _pl.ap[:, 0:512], _la, _ra, start=True, stop=True,
                                tile_position=(0, 0)),
                             lambda tn, _lb=lb, _rb=rb, _pl=pl: tn.matmul(
                                _pl.ap[:, 512:1024], _lb, _rb, start=True, stop=True,
                                tile_position=(64, 0))],
                            reads=(kT, qT), writes=(pl,))
                        p.op("scalar", lambda s, _pl=pl, _e=exb: s.activation(_e.ap, _pl.ap, AF.Exp),
                             reads=(pl,), writes=(exb,))
                        st_, sp_ = (kt == 0), (kt == NT - 1)
                        va = vnat.ap[:, kt * 512 + (2 * m) * 64:kt * 512 + (2 * m + 1) * 64]
                        vb = vnat.ap[:, kt * 512 + (2 * m + 1) * 64:kt * 512 + (2 * m + 2) * 64]
                        p.mm_group(
                            [lambda tn, _va=va, _s=st_, _e=sp_, _x=exb: tn.matmul(
                                ps_av.ap[0:64, :], _va, _x.ap[:, 0:512], start=_s, stop=_e,
                                tile_position=(0, 0)),
                             lambda tn, _vb=vb, _s=st_, _e=sp_, _x=exb: tn.matmul(
                                ps_av.ap[64:128, :], _vb, _x.ap[:, 512:1024], start=_s, stop=_e,
                                skip_group_check=True, tile_position=(0, 64))],
                            reads=(vnat, exb), writes=(ps_av,))
                        p.mm_group(
                            [lambda tn, _x=exb, _s=st_, _e=sp_: tn.matmul(
                                ps_s.ap[0:1, :], C.onescol.ap, _x.ap[:, 0:512], start=_s, stop=_e,
                                tile_position=(0, 0)),
                             lambda tn, _x=exb, _s=st_, _e=sp_: tn.matmul(
                                ps_s.ap[32:33, :], C.onescol.ap, _x.ap[:, 512:1024], start=_s,
                                stop=_e, skip_group_check=True, tile_position=(0, 32))],
                            reads=(exb, C.onescol), writes=(ps_s,))
                    if not B0_ATTN_NORM:
                        continue
                    p.op("scalar", lambda s: s.activation(lnrec.ap, ps_s.ap, AF.Ln),
                         reads=(ps_s,), writes=(lnrec,))
                    p.op("scalar", lambda s: s.activation(recipb.ap, lnrec.ap, AF.Exp,
                                                          scale=C.col(2, 33)),
                         reads=(lnrec, C.cols), writes=(recipb,))
                    p.op("tensor", lambda tn: tn.matmul(P0.ap[:, 0:512], C.onesrow.ap,
                                                        recipb.ap[0:1, :], start=True, stop=True),
                         reads=(recipb, C.onesrow), writes=(P0,))
                    p.op("tensor", lambda tn: tn.matmul(P0.ap[:, 512:1024], C.onesrow33.ap[32:33, :],
                                                        recipb.ap[32:33, :], start=True, stop=True,
                                                        skip_group_check=True, tile_position=(32, 0)),
                         reads=(recipb, C.onesrow33), writes=(P0,))
                    p.op("vector", lambda v: v.tensor_copy(rsb.ap[0:64, :], P0.ap[0:64, 0:512]),
                         reads=(P0,), writes=(rsb,))
                    p.op("vector", lambda v: v.tensor_copy(rsb.ap[64:128, :], P0.ap[64:128, 512:1024]),
                         reads=(P0,), writes=(rsb,))
                    dst = attnT.ap[:, tq * SEQ + ch * 512:tq * SEQ + (ch + 1) * 512]
                    p.op("vector", lambda v, _d=dst: v.tensor_tensor(_d, ps_av.ap, rsb.ap, ALU.mult),
                         reads=(ps_av, rsb), writes=(attnT,))

        if B0_PHASES < 4:
            for t in range(NT):
                p.dma("sync", xt.ap, xin_b[t * 128:(t + 1) * 128, :], writes=(xt,))
                p.dma("gpsimd", x_out[b][t * 128:(t + 1) * 128, :], xt.ap, reads=(xt,))
            continue
        for t in range(NT):
            onat, mr, xt = onatP[t % 2], mrP[t % 2], xtP[t % 2]
            for ch in range(2):
                dstp = P1.ap[:, ch * 512:(ch + 1) * 512]
                calls = []
                for tau in range(8):
                    lhs = attnT.ap[:, tau * SEQ + t * 128:tau * SEQ + (t + 1) * 128]
                    rhs = wo.ap[:, tau * D + ch * 512:tau * D + (ch + 1) * 512]
                    calls.append(lambda tn, _l=lhs, _r=rhs, _d=dstp, _s=(tau == 0):
                                 tn.matmul(_d, _l, _r, start=_s, stop=False))
                calls.append(lambda tn, _d=dstp, _r=bo.ap[:, ch * 512:(ch + 1) * 512]:
                             tn.matmul(_d, C.onesrow.ap, _r, start=False, stop=True))
                p.mm_group(calls, reads=(attnT, wo, bo, C.onesrow), writes=(P1,))
            for ch in range(2):
                p.op("vector", lambda v, _ch=ch, _o=onat: v.tensor_copy(
                    _o.ap[:, _ch * 512:(_ch + 1) * 512], P1.ap[:, _ch * 512:(_ch + 1) * 512]),
                     reads=(P1,), writes=(onat,))
            ln_tile(p, C, onat, onat.ap, mr, sq_scr)
            p.dma("sync", xt.ap, xin_b[t * 128:(t + 1) * 128, :], writes=(xt,))
            p.op("vector", lambda v, _o=onat, _m=mr: v.tensor_scalar(
                _o.ap, _o.ap, _m.ap[:, 1:2], _m.ap[:, 0:1], ALU.subtract, ALU.mult),
                 reads=(mr,), writes=(onat,))
            p.op("vector", lambda v, _o=onat, _x=xt: v.tensor_tensor(_o.ap, _o.ap, _x.ap, ALU.add),
                 reads=(xt,), writes=(onat,))
            p.dma("gpsimd", xout_b[t * 128:(t + 1) * 128, :], onat.ap, reads=(onat,))


def rope_apply(p, C, nat, bfout, ropec, ropes, t, nheads):
    natv = nat.ap.rearrange("p (h d) -> p h d", d=64)
    bfv = bfout.ap[:, 0:nheads * 64].rearrange("p (h d) -> p h d", d=64)
    cosap = ropec.ap[:, t * 16:(t + 1) * 16].unsqueeze(1).to_broadcast([128, nheads, 16])
    sinap = ropes.ap[:, t * 16:(t + 1) * 16].unsqueeze(1).to_broadcast([128, nheads, 16])
    rotv = natv[:, :, 0:32].rearrange("p h (i two) -> p h i two", two=2)
    bfrot = bfv[:, :, 0:32].rearrange("p h (i two) -> p h i two", two=2)
    ev, ov = rotv[:, :, :, 0], rotv[:, :, :, 1]
    be, bo_ = bfrot[:, :, :, 0], bfrot[:, :, :, 1]
    p.op("vector", lambda v: v.tensor_tensor(be, ev, cosap, ALU.mult),
         reads=(nat, ropec), writes=(bfout,))
    p.op("vector", lambda v: v.tensor_tensor(bo_, ov, sinap, ALU.mult),
         reads=(nat, ropes), writes=(bfout,))
    p.op("vector", lambda v: v.tensor_tensor(be, be, bo_, ALU.subtract),
         reads=(bfout,), writes=(bfout,))
    p.op("vector", lambda v: v.tensor_tensor(bo_, ov, cosap, ALU.mult),
         reads=(nat, ropec), writes=(bfout,))
    p.op("vector", lambda v: v.tensor_tensor(ev, ev, sinap, ALU.mult),
         reads=(ropes,), writes=(nat,))
    p.op("vector", lambda v: v.tensor_tensor(bo_, bo_, ev, ALU.add),
         reads=(nat,), writes=(bfout,))
    p.op("vector", lambda v: v.tensor_copy(bfv[:, :, 32:64], natv[:, :, 32:64]),
         reads=(nat,), writes=(bfout,))



def prep_block0_host(inputs):
    Wq = np.ascontiguousarray(inputs["L0_attn_Wq"])
    Wo = np.ascontiguousarray(inputs["L0_attn_Wo"])
    Wq_p = np.empty_like(Wq)
    Wo_p = np.empty_like(Wo)
    for g, h in enumerate(HEADPERM):
        Wq_p[:, g * 64:(g + 1) * 64] = Wq[:, h * 64:(h + 1) * 64]
        Wo_p[g * 64:(g + 1) * 64, :] = Wo[h * 64:(h + 1) * 64, :]
    return {
        "Wq": Wq_p, "Wk": np.ascontiguousarray(inputs["L0_attn_Wk"]),
        "Wv": np.ascontiguousarray(inputs["L0_attn_Wv"]),
        "Wo": Wo_p, "bo": inputs["L0_attn_bo"].reshape(1, D),
    }


def build_block0_nc():
    from contextlib import ExitStack
    nc = bass.Bass("TRN2", target_bir_lowering=False)
    x_in = nc.dram_tensor("x_in", [BPC, SEQ, D], F32, kind="ExternalInput").ap()
    x_out = nc.dram_tensor("x_out", [BPC, SEQ, D], F32, kind="ExternalOutput").ap()
    wd = {
        "Wq": nc.dram_tensor("Wq", [D, D], F32, kind="ExternalInput").ap(),
        "Wk": nc.dram_tensor("Wk", [D, 512], F32, kind="ExternalInput").ap(),
        "Wv": nc.dram_tensor("Wv", [D, 512], F32, kind="ExternalInput").ap(),
        "Wo": nc.dram_tensor("Wo", [D, D], F32, kind="ExternalInput").ap(),
        "bo": nc.dram_tensor("bo", [1, D], F32, kind="ExternalInput").ap(),
        "rope_cos": nc.dram_tensor("rope_cos", [SEQ, 16], F32, kind="ExternalInput").ap(),
        "rope_sin": nc.dram_tensor("rope_sin", [SEQ, 16], F32, kind="ExternalInput").ap(),
    }
    capd = {}
    cnp = build_constants()
    for k in ("ident", "onescol", "onesrow", "sel2"):
        capd[k] = nc.dram_tensor("cin_" + k, list(cnp[k].shape), BF16, kind="ExternalInput").ap()
    stack = ExitStack()
    p = Prog(nc)
    C = Consts(p, stack, capd)
    emit_block0(p, stack, C, x_in, x_out, wd)
    p.barrier()
    p.emit(stack)
    stack.close()
    return nc, cnp


def run_block0(inputs, x_full):
    nc, cnp = build_block0_nc()
    wh = prep_block0_host(inputs)
    in_maps = []
    for c in range(NCORES):
        m = {"x_in": np.ascontiguousarray(x_full[c * BPC:(c + 1) * BPC])}
        m.update(wh)
        m["rope_cos"] = cnp["rope_cos"]
        m["rope_sin"] = cnp["rope_sin"]
        for k in ("ident", "onescol", "onesrow", "sel2"):
            m["cin_" + k] = cnp[k]
        in_maps.append(m)
    res = run_bass_kernel_spmd(nc, in_maps, core_ids=list(range(NCORES)))
    return np.concatenate([r["x_out"] for r in res.results], axis=0)







def emit_block1(p, stack, C, x_in, x_out, W, dram):
    nc = p.nc
    sb = lambda n, s, d=F32: mk_sb(stack, nc, n, s, d)
    ps = lambda n, s, d=F32: mk_ps(stack, nc, n, s, d)
    ROWS = BPC * SEQ
    NIT = 24
    NRT = ROWS // 128
    xr = x_in.rearrange("b s d -> (b s) d")
    xro = x_out.rearrange("b s d -> (b s) d")

    P0 = ps("b1_P0", [128, 1024])
    P1 = ps("b1_P1", [128, 1024])
    ps_tr = ps("b1_pstr", [128, 512], BF16)
    ps_sc = ps("b1_pssc", [128, 8])

    xt = sb("b1_xt", [128, D])
    sq_scr = sb("b1_sq", [128, D], BF16)
    mr = sb("b1_mr", [128, 4])
    lnbf = sb("b1_lnbf", [128, D], BF16)
    lnT = sb("b1_lnT", [128, 8 * ROWS], BF16)
    amx = sb("b1_amx", [128, 8])
    arow = sb("b1_arow", [1, 128])
    asc = sb("b1_asc", [1, 16])
    ascb = sb("b1_ascb", [1, 16], BF16)
    scc = sb("b1_scc", [128, 12])
    wtile = sb("b1_wtile", [128, 512])
    dqs = sb("b1_dqs", [128, 32])
    dqbf = sb("b1_dqbf", [128, 1024], BF16)
    lhsb = [sb(f"b1_lhs{i}", [128, 8 * 128], BF16) for i in range(2)]
    evacP = [sb("b1_evac", [128, 4096]), sb("b1_evacB", [128, 4096])]
    evac2 = sb("b1_evac2", [128, 4096])
    evac2P = [evac2, evac2]
    evac = evacP[0]
    scrF = sb("b1_scrF", [128, 4096])
    scrG = sb("b1_scrG", [128, 4096])
    scrI = Buf(stack.enter_context(nc.sbuf_tensor("b1_scrI", [128, 4096], I32))[:], "b1_scrI")
    obf = sb("b1_obf", [128, 4096], BF16)
    _osl0 = sb("b1_osl0", [128, 24 * 256], BF16)
    osl = [_osl0, _osl0]
    onat = sb("b1_onat", [128, D])
    dqd = lnT

    dq_loc = dram["dq_loc"]; ag_ug = dram["ag_ug"]
    dq_dn_loc = dram["dq_dn_loc"]; ag_dn = dram["ag_dn"]
    raw_up = dram["raw_up"]; raw_gate = dram["raw_gate"]
    t_sp = dram["t_sp"]; u_sp = dram["u_sp"]; p_sp = dram["p_sp"]; o_sp = dram["o_sp"]
    ccs = [dram[f"cc{i}_in"] for i in range(4)]
    cco = [dram[f"cc{i}_out"] for i in range(4)]

    def amax_finalize(col_i, dst_col):
        p.dma("scalar", arow.ap, amx.ap[:, col_i:col_i + 1], reads=(amx,), writes=(arow,))
        p.op("vector", lambda v, _d=dst_col: v.tensor_reduce(asc.ap[0:1, _d:_d + 1], arow.ap,
                                                             AX.X, ALU.max),
             reads=(arow,), writes=(asc,))

    def allreduce_max(rnd, ncells):
        p.dma("gpsimd", ccs[rnd].ap, asc.ap[0:1, 0:16], reads=(asc,), writes=(ccs[rnd],))
        p.op("gpsimd", lambda g, _r=rnd: g.collective_compute(
            "AllReduce", ALU.max, replica_groups=[list(range(NCORES))],
            ins=[ccs[_r].ap], outs=[cco[_r].ap]),
            reads=(ccs[rnd],), writes=(cco[rnd],), cc=True)
        p.dma("sync", asc.ap[0:1, 0:16], cco[rnd].ap, reads=(cco[rnd],), writes=(asc,))

    def make_scales(src_col, rs_col, s_col, extra=1.0):
        p.op("vector", lambda v: v.tensor_scalar(asc.ap[0:1, 14:15], asc.ap[0:1, src_col:src_col + 1],
                                                 1.0 / 32767.0, 1e-12, ALU.mult, ALU.max),
             reads=(asc,), writes=(asc,))
        p.op("vector", lambda v: v.reciprocal(asc.ap[0:1, 15:16], asc.ap[0:1, 14:15]),
             reads=(asc,), writes=(asc,))
        p.op("vector", lambda v: v.tensor_copy(ascb.ap[0:1, 0:1], asc.ap[0:1, 15:16]),
             reads=(asc,), writes=(ascb,))
        p.mm_group([lambda tn: tn.matmul(ps_sc.ap[:, 0:1], C.onesrow.ap, ascb.ap[0:1, 0:1],
                                         start=True, stop=True)],
                   reads=(ascb, C.onesrow), writes=(ps_sc,))
        p.op("vector", lambda v, _rc=rs_col: v.tensor_copy(scc.ap[:, _rc:_rc + 1], ps_sc.ap[:, 0:1]),
             reads=(ps_sc,), writes=(scc,))
        p.op("vector", lambda v, _rc=rs_col, _sc=s_col: v.reciprocal(scc.ap[:, _sc:_sc + 1],
                                                                     scc.ap[:, _rc:_rc + 1]),
             reads=(scc,), writes=(scc,))
        if extra != 1.0:
            p.op("vector", lambda v, _sc=s_col, _x=extra: v.tensor_scalar(
                scc.ap[:, _sc:_sc + 1], scc.ap[:, _sc:_sc + 1], _x, None, ALU.mult),
                reads=(scc,), writes=(scc,))


    def _deq_tile(ncols):
        nb = ncols // 32
        wv = wtile.ap[:, 0:ncols].rearrange("p (b k) -> p b k", k=32)
        p.op("vector", lambda v: v.tensor_reduce(dqs.ap[:, 0:nb], wv, AX.X, ALU.max,
                                                 apply_absolute_value=True),
             reads=(wtile,), writes=(dqs,))
        p.op("vector", lambda v: v.tensor_scalar(dqs.ap[:, 0:nb], dqs.ap[:, 0:nb],
                                                 1.0 / 7.0, 1e-12, ALU.mult, ALU.max),
             reads=(dqs,), writes=(dqs,))
        p.op("vector", lambda v: v.reciprocal(dqs.ap[:, 16:16 + nb], dqs.ap[:, 0:nb]),
             reads=(dqs,), writes=(dqs,))
        rsb_ = dqs.ap[:, 16:16 + nb].unsqueeze(2).to_broadcast([128, nb, 32])
        sb_ = dqs.ap[:, 0:nb].unsqueeze(2).to_broadcast([128, nb, 32])
        iv = scrI.ap[:, 0:ncols]
        p.op("vector", lambda v: v.tensor_tensor(iv.rearrange("p (b k) -> p b k", k=32), wv, rsb_,
                                                 ALU.mult),
             reads=(wtile, dqs), writes=(scrI,))
        p.op("vector", lambda v: v.tensor_tensor(dqbf.ap[:, 0:ncols].rearrange("p (b k) -> p b k", k=32),
                                                 iv.rearrange("p (b k) -> p b k", k=32), sb_, ALU.mult),
             reads=(scrI, dqs), writes=(dqbf,))

    for wname, locrow in (("w_up_sl", 0), ("w_gate_sl", 1)):
        for it in range(NIT):
            src = W[wname][it * 128:(it + 1) * 128, :]
            p.dma("sync", wtile.ap[:, 0:128], src, writes=(wtile,))
            _deq_tile(128)
            p.op("tensor", lambda tn: tn.transpose(ps_tr.ap[:, 0:128], dqbf.ap[:, 0:128], C.ident.ap),
                 reads=(dqbf, C.ident), writes=(ps_tr,))
            p.op("vector", lambda v: v.tensor_copy(dqbf.ap[:, 512:640], ps_tr.ap[:, 0:128]),
                 reads=(ps_tr,), writes=(dqbf,))
            p.dma("gpsimd", dq_loc.ap[locrow * 128:(locrow + 1) * 128, it * 128:(it + 1) * 128],
                  dqbf.ap[:, 512:640], reads=(dqbf,), writes=(dq_loc,))
    for dt in range(8):
        src = W["w_down_sl"][dt * 128:(dt + 1) * 128, :]
        p.dma("sync", wtile.ap[:, 0:384], src, writes=(wtile,))
        _deq_tile(384)
        for u in range(3):
            p.op("tensor", lambda tn, _u=u: tn.transpose(ps_tr.ap[:, _u * 128:(_u + 1) * 128],
                                                         dqbf.ap[:, _u * 128:(_u + 1) * 128], C.ident.ap),
                 reads=(dqbf, C.ident), writes=(ps_tr,))
        for u in range(3):
            p.op("vector", lambda v, _u=u: v.tensor_copy(dqbf.ap[:, 512 + _u * 128:512 + (_u + 1) * 128],
                                                         ps_tr.ap[:, _u * 128:(_u + 1) * 128]),
                 reads=(ps_tr,), writes=(dqbf,))
        for u in range(3):
            p.dma("gpsimd", dq_dn_loc.ap[u * 128:(u + 1) * 128, dt * 128:(dt + 1) * 128],
                  dqbf.ap[:, 512 + u * 128:512 + (u + 1) * 128], reads=(dqbf,), writes=(dq_dn_loc,))
    p.op("gpsimd", lambda g: g.collective_compute(
        "AllGather", ALU.bypass, replica_groups=[list(range(NCORES))],
        ins=[dq_loc.ap], outs=[ag_ug.ap]),
        reads=(dq_loc,), writes=(ag_ug,), cc=True)
    p.op("gpsimd", lambda g: g.collective_compute(
        "AllGather", ALU.bypass, replica_groups=[list(range(NCORES))],
        ins=[dq_dn_loc.ap], outs=[ag_dn.ap]),
        reads=(dq_dn_loc,), writes=(ag_dn,), cc=True)

    p.op("vector", lambda v: v.memset(amx.ap, 0.0), writes=(amx,))
    p.op("vector", lambda v: v.memset(asc.ap, 0.0), writes=(asc,))
    for r in range(NRT):
        p.dma("sync", xt.ap, xr[r * 128:(r + 1) * 128, :], writes=(xt,))
        ln_tile(p, C, xt, xt.ap, mr, sq_scr)
        p.op("vector", lambda v: v.tensor_scalar(lnbf.ap, xt.ap, mr.ap[:, 1:2], mr.ap[:, 0:1],
                                                 ALU.subtract, ALU.mult),
             reads=(xt, mr), writes=(lnbf,))
        p.op("vector", lambda v: v.tensor_reduce(amx.ap[:, 1:2], lnbf.ap, AX.X, ALU.max,
                                                 apply_absolute_value=True),
             reads=(lnbf,), writes=(amx,))
        p.op("vector", lambda v: v.tensor_tensor(amx.ap[:, 0:1], amx.ap[:, 0:1], amx.ap[:, 1:2],
                                                 ALU.max),
             reads=(amx,), writes=(amx,))
        transpose4(p, C, lnbf, lambda dt_: lnbf.ap[:, dt_ * 128:(dt_ + 1) * 128], ps_tr,
                   lnT, lambda dt_, _r=r: lnT.ap[:, dt_ * ROWS + _r * 128:dt_ * ROWS + (_r + 1) * 128], 8)
    amax_finalize(0, 0)
    allreduce_max(0, 1)
    make_scales(0, 0, 1)

    for dt in range(8):
        seg = lnT.ap[:, dt * ROWS:(dt + 1) * ROWS]
        p.op("vector", lambda v, _s=seg: v.tensor_scalar(scrI.ap, _s, scc.ap[:, 0:1], None, ALU.mult),
             reads=(lnT, scc), writes=(scrI,))
        p.op("vector", lambda v, _s=seg: v.tensor_scalar(_s, scrI.ap, scc.ap[:, 1:2], None, ALU.mult),
             reads=(scrI, scc), writes=(lnT,))

    def _phaseB_it(wi, raw, acol, it, lb, evac):
        p.dma_multi("sync",
                    [(lb.ap[:, dt * 128:(dt + 1) * 128],
                      ag_ug.ap[256 * dt + wi * 128:256 * dt + wi * 128 + 128,
                               it * 128:(it + 1) * 128]) for dt in range(8)],
                    lb, reads=(ag_ug,))
        for rc in range(8):
            slot = rc % 4
            pb = (P0, P0, P1, P1)[slot]
            half = (0, 512, 0, 512)[slot]
            dst = pb.ap[:, half:half + 512]
            p.mm_group([(lambda tn, _l=lb.ap[:, dt2 * 128:(dt2 + 1) * 128],
                         _r=lnT.ap[:, dt2 * ROWS + rc * 512:dt2 * ROWS + (rc + 1) * 512],
                         _d=dst, _s=(dt2 == 0), _e=(dt2 == 7):
                         tn.matmul(_d, _l, _r, start=_s, stop=_e)) for dt2 in range(8)],
                       reads=(lb, lnT), writes=(pb,))
            p.op("vector", lambda v, _d=dst, _rc=rc, _e=evac: v.tensor_copy(
                _e.ap[:, _rc * 512:(_rc + 1) * 512], _d),
                reads=(pb,), writes=(evac,))
        p.op("vector", lambda v, _e=evac: v.tensor_reduce(amx.ap[:, 3:4], _e.ap, AX.X, ALU.max,
                                                          apply_absolute_value=True),
             reads=(evac,), writes=(amx,))
        p.op("vector", lambda v, _a=acol: v.tensor_tensor(amx.ap[:, _a:_a + 1],
                                                          amx.ap[:, _a:_a + 1],
                                                          amx.ap[:, 3:4], ALU.max),
             reads=(amx,), writes=(amx,))
        p.dma("gpsimd", raw.ap[it * 128:(it + 1) * 128, :], evac.ap, reads=(evac,), writes=(raw,))

    for wi, (raw, acol) in enumerate(((raw_up, 1), (raw_gate, 2))):
        for it in range(NIT):
            _phaseB_it(wi, raw, acol, it, lhsb[it % 2], evacP[it % 2])
    amax_finalize(1, 0)
    amax_finalize(2, 1)
    allreduce_max(1, 2)
    make_scales(0, 2, 3)
    make_scales(1, 4, 5)

    def _phaseC_it(it, evac, evac2):
        p.dma("sync", evac.ap, raw_gate.ap[it * 128:(it + 1) * 128, :], writes=(evac,))
        p.dma("sync", evac2.ap, raw_up.ap[it * 128:(it + 1) * 128, :], writes=(evac2,))
        p.op("vector", lambda v, _e=evac: v.tensor_scalar(scrI.ap, _e.ap, scc.ap[:, 4:5], None,
                                                          ALU.mult),
             reads=(evac, scc), writes=(scrI,))
        p.op("vector", lambda v: v.tensor_scalar(scrF.ap, scrI.ap, scc.ap[:, 5:6], None, ALU.mult),
             reads=(scrI, scc), writes=(scrF,))
        p.op("scalar", lambda s: s.activation(scrG.ap, scrF.ap, AF.Sigmoid),
             reads=(scrF,), writes=(scrG,))
        p.op("vector", lambda v: v.tensor_scalar(scrI.ap, scrG.ap, 65536.0, None, ALU.mult),
             reads=(scrG,), writes=(scrI,))
        p.op("vector", lambda v: v.tensor_scalar(scrG.ap, scrI.ap, 1.0, 65535.0, ALU.mult, ALU.min),
             reads=(scrI,), writes=(scrG,))
        p.op("vector", lambda v: v.tensor_tensor(scrF.ap, scrF.ap, scrG.ap, ALU.mult),
             reads=(scrG,), writes=(scrF,))
        p.op("vector", lambda v: v.tensor_reduce(amx.ap[:, 3:4], scrF.ap, AX.X, ALU.max,
                                                 apply_absolute_value=True),
             reads=(scrF,), writes=(amx,))
        p.op("vector", lambda v: v.tensor_tensor(amx.ap[:, 4:5], amx.ap[:, 4:5], amx.ap[:, 3:4],
                                                 ALU.max),
             reads=(amx,), writes=(amx,))
        p.dma("gpsimd", t_sp.ap[it * 128:(it + 1) * 128, :], scrF.ap, reads=(scrF,), writes=(t_sp,))
        p.op("vector", lambda v, _e=evac2: v.tensor_scalar(scrI.ap, _e.ap, scc.ap[:, 2:3], None,
                                                           ALU.mult),
             reads=(evac2, scc), writes=(scrI,))
        p.op("vector", lambda v, _e=evac2: v.tensor_scalar(_e.ap, scrI.ap, 1.0, None, ALU.mult),
             reads=(scrI,), writes=(evac2,))
        p.dma("gpsimd", u_sp.ap[it * 128:(it + 1) * 128, :], evac2.ap, reads=(evac2,), writes=(u_sp,))

    for it in range(NIT):
        _phaseC_it(it, evacP[it % 2], evac2P[it % 2])

    amax_finalize(4, 0)
    allreduce_max(2, 1)
    make_scales(0, 6, 7, extra=2.0 ** -16)

    def _phaseD_it(it, evac, evac2):
        p.dma("sync", evac.ap, t_sp.ap[it * 128:(it + 1) * 128, :], writes=(evac,))
        p.dma("sync", evac2.ap, u_sp.ap[it * 128:(it + 1) * 128, :], writes=(evac2,))
        p.op("vector", lambda v, _e=evac: v.tensor_scalar(scrI.ap, _e.ap, scc.ap[:, 6:7], None,
                                                          ALU.mult),
             reads=(evac, scc), writes=(scrI,))
        p.op("vector", lambda v, _e=evac2: v.scalar_tensor_tensor(scrF.ap, scrI.ap, 1.0, _e.ap,
                                                                  ALU.mult, ALU.mult),
             reads=(scrI, evac2), writes=(scrF,))
        p.op("vector", lambda v: v.tensor_reduce(amx.ap[:, 3:4], scrF.ap, AX.X, ALU.max,
                                                 apply_absolute_value=True),
             reads=(scrF,), writes=(amx,))
        p.op("vector", lambda v: v.tensor_tensor(amx.ap[:, 5:6], amx.ap[:, 5:6], amx.ap[:, 3:4],
                                                 ALU.max),
             reads=(amx,), writes=(amx,))
        p.dma("gpsimd", p_sp.ap[it * 128:(it + 1) * 128, :], scrF.ap, reads=(scrF,), writes=(p_sp,))

    for it in range(NIT):
        _phaseD_it(it, evacP[it % 2], evac2P[it % 2])

    amax_finalize(5, 0)
    allreduce_max(3, 1)
    p.op("vector", lambda v: v.tensor_tensor(asc.ap[0:1, 2:3], scc.ap[0:1, 7:8], scc.ap[0:1, 3:4],
                                             ALU.mult),
         reads=(scc,), writes=(asc,))
    p.op("vector", lambda v: v.tensor_tensor(asc.ap[0:1, 0:1], asc.ap[0:1, 0:1], asc.ap[0:1, 2:3],
                                             ALU.mult),
         reads=(asc,), writes=(asc,))
    make_scales(0, 8, 9)
    p.op("vector", lambda v: v.tensor_scalar(scc.ap[:, 10:11], scc.ap[:, 7:8], 1.0, None, ALU.mult),
         reads=(scc,), writes=(scc,))
    p.op("vector", lambda v: v.tensor_tensor(scc.ap[:, 10:11], scc.ap[:, 10:11], scc.ap[:, 3:4],
                                             ALU.mult),
         reads=(scc,), writes=(scc,))
    p.op("vector", lambda v: v.tensor_tensor(scc.ap[:, 8:9], scc.ap[:, 8:9], scc.ap[:, 10:11],
                                             ALU.mult),
         reads=(scc,), writes=(scc,))

    def _phaseE1_it(it, evac):
        p.dma("sync", evac.ap, p_sp.ap[it * 128:(it + 1) * 128, :], writes=(evac,))
        p.op("vector", lambda v, _e=evac: v.tensor_scalar(scrI.ap, _e.ap, scc.ap[:, 8:9], None,
                                                          ALU.mult),
             reads=(evac, scc), writes=(scrI,))
        p.op("vector", lambda v: v.tensor_copy(obf.ap, scrI.ap), reads=(scrI,), writes=(obf,))
        p.dma("gpsimd", o_sp.ap[it * 128:(it + 1) * 128, :], obf.ap, reads=(obf,), writes=(o_sp,))

    for it in range(NIT):
        _phaseE1_it(it, evacP[it % 2])

    p.dma_multi("sync",
                [(dqd.ap[:, it * 1024:(it + 1) * 1024], ag_dn.ap[it * 128:(it + 1) * 128, :])
                 for it in range(NIT)],
                dqd, reads=(ag_dn,))
    for rc in range(16):
        ob = osl[rc % 2]
        p.dma_multi("sync",
                    [(ob.ap[:, it * 256:(it + 1) * 256],
                      o_sp.ap[it * 128:(it + 1) * 128, rc * 256:(rc + 1) * 256])
                     for it in range(NIT)],
                    ob, reads=(o_sp,))
        for rt in range(2):
            for ch in range(2):
                slot = rt * 2 + ch
                pb = (P0, P0, P1, P1)[slot]
                half = (0, 512, 0, 512)[slot]
                dst = pb.ap[:, half:half + 512]
                p.mm_group([(lambda tn, _l=ob.ap[:, it2 * 256 + rt * 128:it2 * 256 + (rt + 1) * 128],
                             _r=dqd.ap[:, it2 * 1024 + ch * 512:it2 * 1024 + (ch + 1) * 512],
                             _d=dst, _s=(it2 == 0), _e=(it2 == 23):
                             tn.matmul(_d, _l, _r, start=_s, stop=_e)) for it2 in range(NIT)],
                           reads=(ob, dqd), writes=(pb,))
        for rt in range(2):
            r = rc * 2 + rt
            for ch in range(2):
                slot = rt * 2 + ch
                pb = (P0, P0, P1, P1)[slot]
                half = (0, 512, 0, 512)[slot]
                p.op("vector", lambda v, _pb=pb, _h=half, _c=ch: v.tensor_scalar(
                    onat.ap[:, _c * 512:(_c + 1) * 512], _pb.ap[:, _h:_h + 512],
                    scc.ap[:, 9:10], None, ALU.mult),
                    reads=(pb, scc), writes=(onat,))
            ln_tile(p, C, onat, onat.ap, mr, sq_scr)
            p.dma("sync", xt.ap, xr[r * 128:(r + 1) * 128, :], writes=(xt,))
            p.op("vector", lambda v: v.tensor_scalar(onat.ap, onat.ap, mr.ap[:, 1:2], mr.ap[:, 0:1],
                                                     ALU.subtract, ALU.mult),
                 reads=(mr,), writes=(onat,))
            p.op("vector", lambda v: v.tensor_tensor(onat.ap, onat.ap, xt.ap, ALU.add),
                 reads=(xt,), writes=(onat,))
            p.dma("gpsimd", xro[r * 128:(r + 1) * 128, :], onat.ap, reads=(onat,))


def make_block1_drams(nc):
    mk = lambda n, s, d=F32: Buf(nc.dram_tensor(n, s, d).ap(), n)
    dram = {
        "dq_loc": mk("b1_dq_loc", [256, 3072], BF16),
        "ag_ug": mk("b1_ag_ug", [2048, 3072], BF16),
        "dq_dn_loc": mk("b1_dq_dn_loc", [384, 1024], BF16),
        "ag_dn": mk("b1_ag_dn", [3072, 1024], BF16),
        "raw_up": mk("b1_raw_up", [3072, 4096]),
        "raw_gate": mk("b1_raw_gate", [3072, 4096]),
        "t_sp": mk("b1_t_sp", [3072, 4096]),
        "u_sp": mk("b1_u_sp", [3072, 4096]),
        "p_sp": mk("b1_p_sp", [3072, 4096]),
        "o_sp": mk("b1_o_sp", [3072, 4096], BF16),
    }
    for i in range(4):
        dram[f"cc{i}_in"] = mk(f"b1_cc{i}_in", [1, 16])
        dram[f"cc{i}_out"] = mk(f"b1_cc{i}_out", [1, 16])
    return dram


def build_block1_nc():
    from contextlib import ExitStack
    nc = bass.Bass("TRN2", target_bir_lowering=False)
    x_in = nc.dram_tensor("x_in", [BPC, SEQ, D], F32, kind="ExternalInput").ap()
    x_out = nc.dram_tensor("x_out", [BPC, SEQ, D], F32, kind="ExternalOutput").ap()
    wd = {
        "w_up_sl": nc.dram_tensor("w_up_sl", [3072, 128], F32, kind="ExternalInput").ap(),
        "w_gate_sl": nc.dram_tensor("w_gate_sl", [3072, 128], F32, kind="ExternalInput").ap(),
        "w_down_sl": nc.dram_tensor("w_down_sl", [1024, 384], F32, kind="ExternalInput").ap(),
    }
    capd = {}
    cnp = build_constants()
    for k in ("ident", "onescol", "onesrow", "sel2"):
        capd[k] = nc.dram_tensor("cin_" + k, list(cnp[k].shape), BF16, kind="ExternalInput").ap()
    stack = ExitStack()
    p = Prog(nc)
    C = Consts(p, stack, capd)
    dram = make_block1_drams(nc)
    emit_block1(p, stack, C, x_in, x_out, wd, dram)
    p.barrier()
    p.emit(stack)
    stack.close()
    return nc, cnp


def block1_inmaps(inputs, x_full, cnp):
    wu = np.ascontiguousarray(inputs["L0_ffn_w_up"])
    wg = np.ascontiguousarray(inputs["L0_ffn_w_gate"])
    wdn = np.ascontiguousarray(inputs["L0_ffn_w_down"])
    in_maps = []
    for c in range(NCORES):
        m = {"x_in": np.ascontiguousarray(x_full[c * BPC:(c + 1) * BPC]),
             "w_up_sl": np.ascontiguousarray(wu[:, c * 128:(c + 1) * 128]),
             "w_gate_sl": np.ascontiguousarray(wg[:, c * 128:(c + 1) * 128]),
             "w_down_sl": np.ascontiguousarray(wdn[:, c * 384:(c + 1) * 384])}
        for k in ("ident", "onescol", "onesrow", "sel2"):
            m["cin_" + k] = cnp[k]
        in_maps.append(m)
    return in_maps


def run_block1(inputs, x_full):
    nc, cnp = build_block1_nc()
    in_maps = block1_inmaps(inputs, x_full, cnp)
    res = run_bass_kernel_spmd(nc, in_maps, core_ids=list(range(NCORES)))
    return np.concatenate([r["x_out"].reshape(BPC, SEQ, D) for r in res.results], axis=0)


class _Rec:
    def __init__(self):
        self.ops = []

    def op(self, e, fn, reads=(), writes=(), dma=False, cc=False):
        self.ops.append(("op", e, fn, reads, writes))

    def dma(self, e, out, in_, reads=(), writes=()):
        self.ops.append(("dma", e, out, in_, reads, writes))

    def mm_group(self, calls, reads=(), writes=()):
        self.ops.append(("mm", calls, reads, writes))


def emit_block2(p, stack, C, x_in, x_out, W, dram):
    nc = p.nc
    sb = lambda n, s, d=F32: mk_sb(stack, nc, n, s, d)
    ps = lambda n, s, d=F32: mk_ps(stack, nc, n, s, d)
    T, BL = 16, 256
    xr = x_in.rearrange("b s d -> (b s) d")
    xro = x_out.rearrange("b s d -> (b s) d")

    P0 = ps("b2_P0", [128, 1024])
    P1 = ps("b2_P1", [128, 1024])
    UP = ps("b2_UP", [128, 1024])
    ps_tr = ps("b2_pstr", [128, 512], BF16)

    xt = sb("b2_xt", [128, D])
    sq_scr = sb("b2_sq", [128, D], BF16)
    mr = sb("b2_mr", [128, 4])
    lnbf = sb("b2_lnbf", [128, D], BF16)
    xTs = sb("b2_xTs", [128, 8 * 256], BF16)
    wkT = sb("b2_wkT", [128, 8 * D], BF16)
    wvT = sb("b2_wvT", [128, 8 * D], BF16)
    wqT = sb("b2_wqT", [128, 8 * D], BF16)
    wstage = sb("b2_wstage", [128, 512])
    S = [sb(f"b2_S{i}", [128, 8192], BF16) for i in range(2)]
    scr = sb("b2_scr", [128, 8192], BF16)
    kt = [sb(f"b2_kt{i}", [128, 1024]) for i in range(2)]
    vt = [sb(f"b2_vt{i}", [128, 1024]) for i in range(2)]
    qt = [sb(f"b2_qt{i}", [128, 1024], BF16) for i in range(2)]
    khat = sb("b2_khat", [128, 1024], BF16)
    kss = sb("b2_kss", [128, 128])
    ret = sb("b2_ret", [128, 1024])
    sqv = sb("b2_sqv", [128, 1024])
    sgv = sb("b2_sgv", [128, 1024])
    onat = sb("b2_onat", [128, D])
    scr2 = sb("b2_scr2", [128, 8192], BF16)
    khat2 = sb("b2_khat2", [128, 1024], BF16)
    kss2 = sb("b2_kss2", [128, 128])
    ret2 = sb("b2_ret2", [128, 1024])
    sqv2 = sb("b2_sqv2", [128, 1024])
    sgv2 = sb("b2_sgv2", [128, 1024])
    mr2 = sb("b2_mr2", [128, 4])
    scrL = [scr, scr2]; khatL = [khat, khat2]; kssL = [kss, kss2]
    retL = [ret, ret2]; sqvL = [sqv, sqv2]; sgvL = [sgv, sgv2]; mrL = [mr, mr2]

    a1_in = dram["a1_in"]; a1_out = dram["a1_out"]
    a2_in = dram["a2_in"]; a2_out = dram["a2_out"]

    def load_weight(dst, dname):
        for t8 in range(8):
            for c0 in (0, 512):
                src = W[dname][t8 * 128:(t8 + 1) * 128, c0:c0 + 512]
                p.dma("sync", wstage.ap[:, 0:512], src, writes=(wstage,))
                p.op("vector", lambda v, _t=t8, _c=c0: v.tensor_copy(
                    dst.ap[:, _t * D + _c:_t * D + _c + 512], wstage.ap[:, 0:512]),
                    reads=(wstage,), writes=(dst,))
    load_weight(wkT, "wkT")
    load_weight(wvT, "wvT")
    load_weight(wqT, "wqT")

    for b in range(BPC):
        for r in range(NT):
            row = b * SEQ + r * 128
            p.dma("sync", xt.ap, xr[row:row + 128, :], writes=(xt,))
            ln_tile(p, C, xt, xt.ap, mr, sq_scr)
            p.op("vector", lambda v: v.tensor_scalar(lnbf.ap, xt.ap, mr.ap[:, 1:2], mr.ap[:, 0:1],
                                                     ALU.subtract, ALU.mult),
                 reads=(xt, mr), writes=(lnbf,))
            dstrow = 512 * (r // 2) + 256 * b + 128 * (r % 2)
            p.dma("gpsimd", a1_in.ap[dstrow:dstrow + 128, :], lnbf.ap,
                  reads=(lnbf,), writes=(a1_in,))
    with allow_ici_all_to_all():
        p.op("gpsimd", lambda g: g.collective_compute(
            "AllToAll", ALU.bypass, replica_groups=[list(range(NCORES))],
            ins=[a1_in.ap], outs=[a1_out.ap]),
            reads=(a1_in,), writes=(a1_out,), cc=True)

    p.op("vector", lambda v: v.memset(S[0].ap, 0.0), writes=(S[0],))
    p.op("gpsimd", lambda g: g.memset(S[1].ap, 0.0), writes=(S[1],))

    def v4(buf):
        return buf.ap.rearrange("p (n i j) -> p n i j", i=8, j=8)

    def v3(buf):
        return buf.ap.rearrange("p (n j) -> p n j", j=8)

    for t in range(T):
        for bt in range(2):
            srow = 512 * (t // 2) + 256 * (t % 2) + 128 * bt
            p.dma("sync", lnbf.ap, a1_out.ap[srow:srow + 128, :], reads=(a1_out,), writes=(lnbf,))
            transpose4(p, C, lnbf, lambda dt_: lnbf.ap[:, dt_ * 128:(dt_ + 1) * 128], ps_tr,
                       xTs, lambda dt_: xTs.ap[:, dt_ * 256 + bt * 128:dt_ * 256 + bt * 128 + 128], 8)
        for bt in range(2):
            for dst, wsrc in ((kt[bt], wkT), (vt[bt], wvT), (qt[bt], wqT)):
                for ch in range(2):
                    pb = P0 if ch == 0 else P1
                    dd = pb.ap[:, 0:512]
                    p.mm_group([(lambda tn, _l=xTs.ap[:, d8 * 256 + bt * 128:d8 * 256 + bt * 128 + 128],
                                 _r=wsrc.ap[:, d8 * D + ch * 512:d8 * D + (ch + 1) * 512],
                                 _d=dd, _s=(d8 == 0), _e=(d8 == 7):
                                 tn.matmul(_d, _l, _r, start=_s, stop=_e)) for d8 in range(8)],
                               reads=(xTs, wsrc), writes=(pb,))
                    p.op("vector", lambda v, _d=dd, _c=ch, _o=dst: v.tensor_copy(
                        _o.ap[:, _c * 512:(_c + 1) * 512], _d),
                        reads=(pb,), writes=(dst,))
        recs = []
        for bt in range(2):
            q = _Rec()
            Sb = S[bt]
            _scr = scrL[bt]; _khat = khatL[bt]; _kss = kssL[bt]
            _ret = retL[bt]; _sqv = sqvL[bt]; _sgv = sgvL[bt]; _mr = mrL[bt]
            q.op("scalar", lambda s, _k=kt[bt], _o=_scr: s.activation(
                _o.ap[:, 0:1024], _k.ap, AF.Square),
                reads=(kt[bt],), writes=(_scr,))
            q.op("vector", lambda v, _i=_scr, _o=_kss: v.tensor_reduce(
                _o.ap, _i.ap[:, 0:1024].rearrange("p (n j) -> p n j", j=8), AX.X, ALU.add),
                reads=(_scr,), writes=(_kss,))
            q.op("scalar", lambda s, _o=_kss: s.activation(_o.ap, _o.ap, AF.Ln, bias=C.col(4)),
                 reads=(_kss, C.cols), writes=(_kss,))
            q.op("scalar", lambda s, _o=_kss: s.activation(_o.ap, _o.ap, AF.Exp, scale=C.col(6)),
                 reads=(_kss, C.cols), writes=(_kss,))
            q.op("scalar", lambda s, _o=_kss: s.activation(_o.ap, _o.ap, AF.Ln, bias=C.col(3)),
                 reads=(_kss, C.cols), writes=(_kss,))
            q.op("scalar", lambda s, _o=_kss: s.activation(_o.ap, _o.ap, AF.Exp, scale=C.col(2)),
                 reads=(_kss, C.cols), writes=(_kss,))
            q.op("vector", lambda v, _k=kt[bt], _o=_khat, _s=_kss: v.tensor_tensor(
                v3(_o), v3(_k), _s.ap.unsqueeze(2).to_broadcast([128, 128, 8]), ALU.mult),
                reads=(kt[bt], _kss), writes=(_khat,))
            khb = _khat.ap.rearrange("p (n j) -> p n j", j=8).unsqueeze(2).to_broadcast([128, 128, 8, 8])
            q.op("vector", lambda v, _S=Sb, _k=khb, _o=_scr: v.tensor_tensor(
                v4(_o), v4(_S), _k, ALU.mult),
                reads=(Sb, _khat), writes=(_scr,))
            q.op("vector", lambda v, _i=_scr, _o=_ret: v.tensor_reduce(
                _o.ap.rearrange("p (n i) -> p n i", i=8), v4(_i), AX.X, ALU.add),
                reads=(_scr,), writes=(_ret,))
            q.op("vector", lambda v, _v=vt[bt], _o=_ret: v.tensor_tensor(
                _o.ap, _v.ap, _o.ap, ALU.subtract),
                reads=(vt[bt],), writes=(_ret,))
            NS = 85
            q.op("vector", lambda v,
                 _o=_scr.ap[:, 0:NS * 64].rearrange("p (n i j) -> p n i j", i=8, j=8),
                 _d=_ret.ap[:, 0:NS * 8].rearrange("p (n i) -> p n i", i=8).unsqueeze(3).to_broadcast([128, NS, 8, 8]),
                 _k=_khat.ap[:, 0:NS * 8].rearrange("p (n j) -> p n j", j=8).unsqueeze(2).to_broadcast([128, NS, 8, 8]):
                 v.tensor_tensor(_o, _d, _k, ALU.mult),
                 reads=(_ret, _khat), writes=(_scr,))
            q.op("gpsimd", lambda g,
                 _o=_scr.ap[:, NS * 64:8192].rearrange("p (n i j) -> p n i j", i=8, j=8),
                 _d=_ret.ap[:, NS * 8:1024].rearrange("p (n i) -> p n i", i=8).unsqueeze(3).to_broadcast([128, 128 - NS, 8, 8]),
                 _k=_khat.ap[:, NS * 8:1024].rearrange("p (n j) -> p n j", j=8).unsqueeze(2).to_broadcast([128, 128 - NS, 8, 8]):
                 g.tensor_tensor(_o, _d, _k, ALU.mult),
                 reads=(_ret, _khat), writes=(_scr,))
            SA = 4288
            q.op("vector", lambda v, _S=Sb.ap, _P=_scr.ap: v.tensor_tensor(
                _S[:, 0:SA], _S[:, 0:SA], _P[:, 0:SA], ALU.add),
                reads=(_scr,), writes=(Sb,))
            q.op("gpsimd", lambda g, _S=Sb.ap, _P=_scr.ap: g.tensor_tensor(
                _S[:, SA:8192], _S[:, SA:8192], _P[:, SA:8192], ALU.add),
                reads=(_scr,), writes=(Sb,))
            q.op("scalar", lambda s, _S=Sb.ap: s.activation(_S, _S, AF.Tanh),
                 reads=(), writes=(Sb,))
            qbb = qt[bt].ap.rearrange("p (n j) -> p n j", j=8).unsqueeze(2).to_broadcast([128, 128, 8, 8])
            q.op("vector", lambda v, _S=Sb, _q=qbb, _o=_scr: v.tensor_tensor(
                v4(_o), v4(_S), _q, ALU.mult),
                reads=(Sb, qt[bt]), writes=(_scr,))
            q.op("vector", lambda v, _i=_scr, _o=_sqv: v.tensor_reduce(
                _o.ap.rearrange("p (n i) -> p n i", i=8), v4(_i), AX.X, ALU.add),
                reads=(_scr,), writes=(_sqv,))
            q.op("scalar", lambda s, _i=_sqv, _o=_sgv: s.activation(_o.ap, _i.ap, AF.Exp, scale=C.col(2)),
                 reads=(_sqv, C.cols), writes=(_sgv,))
            q.op("scalar", lambda s, _o=_sgv: s.activation(_o.ap, _o.ap, AF.Ln, bias=C.col(7)),
                 reads=(_sgv, C.cols), writes=(_sgv,))
            q.op("scalar", lambda s, _o=_sgv: s.activation(_o.ap, _o.ap, AF.Exp, scale=C.col(2)),
                 reads=(_sgv, C.cols), writes=(_sgv,))
            q.op("vector", lambda v, _a=_sgv, _b=_sqv: v.tensor_tensor(
                _a.ap, _a.ap, _b.ap, ALU.mult),
                reads=(_sqv,), writes=(_sgv,))
            q.op("vector", lambda v, _a=_sgv, _b=_sqv: v.tensor_tensor(
                _b.ap, _a.ap, _b.ap, ALU.mult),
                reads=(_sgv,), writes=(_sqv,))
            ln_tile(q, C, _sqv, _sqv.ap, _mr, _scr)
            q.op("vector", lambda v, _o=_sqv, _m=_mr: v.tensor_scalar(
                _o.ap, _o.ap, _m.ap[:, 1:2], _m.ap[:, 0:1], ALU.subtract, ALU.mult),
                reads=(_mr,), writes=(_sqv,))
            drow = 512 * (t // 2) + 256 * (t % 2) + 128 * bt
            q.dma("gpsimd", a2_in.ap[drow:drow + 128, :], _sqv.ap, reads=(_sqv,), writes=(a2_in,))
            recs.append(q.ops)
        for i in range(max(len(r_) for r_ in recs)):
            for r_ in recs:
                if i < len(r_):
                    it = r_[i]
                    if it[0] == "op":
                        p.op(it[1], it[2], reads=it[3], writes=it[4])
                    elif it[0] == "mm":
                        p.mm_group(it[1], reads=it[2], writes=it[3])
                    else:
                        p.dma(it[1], it[2], it[3], reads=it[4], writes=it[5])
    with allow_ici_all_to_all():
        p.op("gpsimd", lambda g: g.collective_compute(
            "AllToAll", ALU.bypass, replica_groups=[list(range(NCORES))],
            ins=[a2_in.ap], outs=[a2_out.ap]),
            reads=(a2_in,), writes=(a2_out,), cc=True)

    for b in range(BPC):
        for r in range(NT):
            row = b * SEQ + r * 128
            srow = 512 * (r // 2) + 256 * b + 128 * (r % 2)
            p.dma("sync", xt.ap, xr[row:row + 128, :], writes=(xt,))
            p.dma("sync", onat.ap, a2_out.ap[srow:srow + 128, :], reads=(a2_out,), writes=(onat,))
            p.op("vector", lambda v: v.tensor_tensor(onat.ap, onat.ap, xt.ap, ALU.add),
                 reads=(xt,), writes=(onat,))
            p.dma("gpsimd", xro[row:row + 128, :], onat.ap, reads=(onat,))


def make_block2_drams(nc):
    mk = lambda n, s, d=F32: Buf(nc.dram_tensor(n, s, d).ap(), n)
    return {
        "a1_in": mk("b2_a1_in", [8 * 512, 1024], BF16),
        "a1_out": mk("b2_a1_out", [8 * 512, 1024], BF16),
        "a2_in": mk("b2_a2_in", [8 * 512, 1024]),
        "a2_out": mk("b2_a2_out", [8 * 512, 1024]),
    }


def build_block2_nc():
    from contextlib import ExitStack
    nc = bass.Bass("TRN2", target_bir_lowering=False)
    x_in = nc.dram_tensor("x_in", [BPC, SEQ, D], F32, kind="ExternalInput").ap()
    x_out = nc.dram_tensor("x_out", [BPC, SEQ, D], F32, kind="ExternalOutput").ap()
    wd = {
        "wkT": nc.dram_tensor("wkT", [D, D], F32, kind="ExternalInput").ap(),
        "wvT": nc.dram_tensor("wvT", [D, D], F32, kind="ExternalInput").ap(),
        "wqT": nc.dram_tensor("wqT", [D, D], F32, kind="ExternalInput").ap(),
    }
    capd = {}
    cnp = build_constants()
    for k in ("ident", "onescol", "onesrow", "sel2"):
        capd[k] = nc.dram_tensor("cin_" + k, list(cnp[k].shape), BF16, kind="ExternalInput").ap()
    stack = ExitStack()
    p = Prog(nc)
    C = Consts(p, stack, capd)
    dram = make_block2_drams(nc)
    emit_block2(p, stack, C, x_in, x_out, wd, dram)
    p.barrier()
    p.emit(stack)
    stack.close()
    return nc, cnp


def block2_inmaps(inputs, x_full, cnp):
    wk = np.ascontiguousarray(inputs["L1_attn_W_k"].T)
    wv = np.ascontiguousarray(inputs["L1_attn_W_v"].T)
    wq = np.ascontiguousarray(inputs["L1_attn_W_q"].T)
    in_maps = []
    for c in range(NCORES):
        m = {"x_in": np.ascontiguousarray(x_full[c * BPC:(c + 1) * BPC]),
             "wkT": wk, "wvT": wv, "wqT": wq}
        for k in ("ident", "onescol", "onesrow", "sel2"):
            m["cin_" + k] = cnp[k]
        in_maps.append(m)
    return in_maps


def run_block2(inputs, x_full):
    nc, cnp = build_block2_nc()
    in_maps = block2_inmaps(inputs, x_full, cnp)
    res = run_bass_kernel_spmd(nc, in_maps, core_ids=list(range(NCORES)))
    return np.concatenate([r["x_out"].reshape(BPC, SEQ, D) for r in res.results], axis=0)


B3_HEADS, B3_DH, B3_WIN, B3_INNER, B3_HID = 8, 64, 512, 512, 2730
B3_LAYERS = 2
B3_STOP_C2 = False
B3_SKIP_W1PREP = False
B3_SKIP_ATTN = False


def emit_block3(p, stack, C, x_in, x_out, W, dram):
    nc = p.nc
    sb = lambda n, s, d=F32: mk_sb(stack, nc, n, s, d)
    ps = lambda n, s, d=F32: mk_ps(stack, nc, n, s, d)
    xr = x_in.rearrange("b s d -> (b s) d")
    xro = x_out.rearrange("b s d -> (b s) d")
    TCOL = 2049

    PA = ps("b3_PA", [128, 1024])
    ps_tr = ps("b3_pstr", [128, 512], BF16)
    ps_l = ps("b3_psl", [128, 1024])
    ps_av = ps("b3_psav", [128, 256])
    ps_s = ps("b3_pss", [33, 256])
    ps_r = ps_l

    AR = Buf(stack.enter_context(nc.sbuf_tensor("b3_AR", [128, 65544], BF16))[:], "b3_AR")
    def carve(off, n):
        return Buf(AR.ap[:, off:off + n], f"b3_ar_{off}"), off + n
    off = 0
    yT, off = carve(off, 8 * TCOL)
    qT, off = carve(off, 4 * SEQ)
    kT, off = carve(off, 4 * SEQ)
    vnat, off = carve(off, 16 * 512)
    attnT, off = carve(off, 4 * SEQ)
    wqkv, off = carve(off, 8 * 1536)
    wo_b, off = carve(off, 4 * 1024)
    assert off <= 65544, off
    off2 = 0
    h2T, off2 = carve(off2, 8 * TCOL)
    h3seg, off2 = carve(off2, 22 * 1024)
    w2_b, off2 = carve(off2, 22 * 1024)
    assert off2 <= 65544, off2

    xt = sb("b3_xt", [128, D])
    sq_scr = sb("b3_sq", [128, D], BF16)
    mr = sb("b3_mr", [128, 4])
    lnbf = sb("b3_lnbf", [128, D], BF16)
    qnP = [sb("b3_qn0", [128, 512]), sb("b3_qn1", [128, 512])]
    scrA = sb("b3_scrA", [128, 1024])
    scrB = sb("b3_scrB", [128, 1024])
    _qbf0 = sb("b3_qbf0", [128, 512], BF16)
    qbfP = [_qbf0, _qbf0]
    nrm = sb("b3_nrm", [128, 8])
    expb2_ = [sb("b3_expb0", [128, 512], BF16), sb("b3_expb1", [128, 512], BF16)]
    acc_s3 = sb("b3_accs", [128, 512], BF16)
    lnrec = sb("b3_lnrec", [33, 256])
    recipb = sb("b3_recipb", [33, 256], BF16)
    rsb = sb("b3_rsb", [128, 256])
    onat = sb("b3_onat", [128, D])
    yt2 = onat
    tabs = [sb(f"b3_tab{i}", [128, 16 * 64], BF16) for i in range(4)]
    masks = sb("b3_masks", [128, 4 * 256], BF16)
    qsb = sb("b3_qsb", [128, 128], BF16)
    statb = sb("b3_statb", [128, 64])
    stsh = sb("b3_stsh", [128, 64])
    mcols = sb("b3_mcols", [128, 16])
    rcols = sb("b3_rcols", [128, 16])
    msh = sb("b3_msh", [128, 32])
    shm = sb("b3_shm", [128, 512])

    lgc = sb("b3_lgc", [128, 8])
    lbb = sb("b3_lbb", [128, 8], BF16)
    wstage = sb("b3_wstage", [128, 512])
    wstage2 = sb("b3_wstage2", [128, 512])
    rawb = sb("b3_rawb", [128, 512], BF16)
    rawb2 = sb("b3_rawb2", [128, 512], BF16)
    lw1 = [sb(f"b3_lw1{i}", [128, 8 * 128], BF16) for i in range(4)]
    c2row = sb("b3_c2row", [1, 5504], BF16)
    b1aL = [sb(f"b3_b1a{i}", [128, 22]) for i in range(2)]
    b1gL = [sb(f"b3_b1g{i}", [128, 22]) for i in range(2)]
    b2b = sb("b3_b2b", [1, 1024], BF16)
    csh = sb("b3_csh", [128, 2])
    cell = sb("b3_cell", [1, 8])
    cellb = sb("b3_cellb", [1, 4], BF16)

    y_a = dram["y_a"]; y_b = dram["y_b"]; y_mid = dram["y_mid"]
    w1bfL = [dram["w1bf0"], dram["w1bf1"]]

    for i in range(4):
        p.dma("sync", tabs[i].ap.rearrange("p (t f) -> p t f", f=64),
              W["xpos"][i].rearrange("(t p) f -> p t f", p=128), writes=(tabs[i],))
    p.dma("sync", masks.ap, W["masks"], writes=(masks,))

    NLAYER = B3_LAYERS

    def shifted_lhs(tile, dt, c0, n):
        base = dt * TCOL + (1 + c0 if dt < 4 else c0)
        return tile.ap[:, base:base + n]

    p.op("vector", lambda v: v.memset(shm.ap, 0.0), writes=(shm,))
    p.dma("gpsimd", shm.ap[0:127, 1:128], C.ident.ap[0:127, 0:127], reads=(C.ident,), writes=(shm,))
    p.dma("gpsimd", shm.ap[1:128, 128:255], C.ident.ap[1:128, 1:128], reads=(C.ident,), writes=(shm,))
    p.dma("gpsimd", shm.ap[127:128, 256:257], C.ident.ap[0:1, 0:1], reads=(C.ident,), writes=(shm,))
    p.dma("gpsimd", shm.ap[0:1, 511:512], C.ident.ap[0:1, 0:1], reads=(C.ident,), writes=(shm,))
    for b in range(BPC):
        for r in range(NT):
            row = b * SEQ + r * 128
            p.dma("sync", xt.ap, xr[row:row + 128, :], writes=(xt,))
            ln_tile(p, C, xt, xt.ap, mr, sq_scr)
            p.op("vector", lambda v: v.tensor_scalar(yt2.ap, xt.ap, mr.ap[:, 1:2], mr.ap[:, 0:1],
                                                     ALU.subtract, ALU.mult),
                 reads=(xt, mr), writes=(yt2,))
            p.dma("gpsimd", y_a.ap[r * 128:(r + 1) * 128, :], yt2.ap, reads=(yt2,), writes=(y_a,))
        for l in range(NLAYER):
            yin = y_a if l == 0 else y_b
            ynext = y_b if l == 0 else y_a
            for dt in range(8):
                for cc_ in range(3):
                    src = W["wqkv"][l][dt * 128:(dt + 1) * 128, cc_ * 512:(cc_ + 1) * 512]
                    p.dma("sync", wstage.ap, src, writes=(wstage,))
                    p.op("vector", lambda v, _dt=dt, _c=cc_: v.tensor_copy(
                        wqkv.ap[:, _dt * 1536 + _c * 512:_dt * 1536 + (_c + 1) * 512], wstage.ap),
                        reads=(wstage,), writes=(wqkv,))
            for dt in range(4):
                for cc_ in range(2):
                    src = W["wo"][l][dt * 128:(dt + 1) * 128, cc_ * 512:(cc_ + 1) * 512]
                    p.dma("sync", wstage.ap, src, writes=(wstage,))
                    p.op("vector", lambda v, _dt=dt, _c=cc_: v.tensor_copy(
                        wo_b.ap[:, _dt * 1024 + _c * 512:_dt * 1024 + (_c + 1) * 512], wstage.ap),
                        reads=(wstage,), writes=(wo_b,))
            p.dma("sync", cell.ap[0:1, 0:8], W["qscale"][l][0:1, 0:8], writes=(cell,))
            p.dma("sync", scrA.ap[0:1, 0:64], W["qscale"][l], writes=(scrA,))
            p.dma("sync", scrA.ap[0:1, 64:128], W["kscale"][l], writes=(scrA,))
            p.op("vector", lambda v: v.tensor_scalar(scrA.ap[0:1, 0:64], scrA.ap[0:1, 0:64],
                                                     8.0, None, ALU.mult),
                 reads=(), writes=(scrA,))
            p.op("vector", lambda v: v.tensor_tensor(scrA.ap[0:1, 128:192], scrA.ap[0:1, 0:64],
                                                     scrA.ap[0:1, 64:128], ALU.mult),
                 reads=(), writes=(scrA,))
            p.op("vector", lambda v: v.tensor_reduce(cell.ap[0:1, 0:1], scrA.ap[0:1, 128:192],
                                                     AX.X, ALU.max, apply_absolute_value=True),
                 reads=(scrA,), writes=(cell,))
            p.op("vector", lambda v: v.tensor_scalar(cell.ap[0:1, 1:2], cell.ap[0:1, 0:1],
                                                     -1.0, None, ALU.mult),
                 reads=(cell,), writes=(cell,))
            p.op("vector", lambda v: v.tensor_copy(cellb.ap[0:1, 0:1], cell.ap[0:1, 1:2]),
                 reads=(cell,), writes=(cellb,))
            p.mm_group([lambda tn: tn.matmul(ps_r.ap[:, 0:1], C.onesrow.ap, cellb.ap[0:1, 0:1],
                                             start=True, stop=True)],
                       reads=(cellb, C.onesrow), writes=(ps_r,))
            p.op("vector", lambda v: v.tensor_copy(csh.ap[:, 0:1], ps_r.ap[:, 0:1]),
                 reads=(ps_r,), writes=(csh,))
            p.op("vector", lambda v: v.tensor_copy(cellb.ap[0:1, 0:4], cell.ap[0:1, 0:4]),
                 reads=(cell,), writes=(cellb,))
            p.op("vector", lambda v: v.tensor_copy(sq_scr.ap[0:1, 0:128], scrA.ap[0:1, 0:128]),
                 reads=(scrA,), writes=(sq_scr,))
            p.mm_group([lambda tn: tn.matmul(ps_r.ap[:, 0:128], C.onesrow.ap, sq_scr.ap[0:1, 0:128],
                                             start=True, stop=True)],
                       reads=(sq_scr, C.onesrow), writes=(ps_r,))
            p.op("vector", lambda v: v.tensor_copy(qsb.ap, ps_r.ap[:, 0:128]),
                 reads=(ps_r,), writes=(qsb,))
            for dt in range(8 if (not B3_SKIP_W1PREP and b == 0) else 0):
                p.dma("sync", lgc.ap[:, dt:dt + 1],
                      W["lng"][l][dt * 128:(dt + 1) * 128].unsqueeze(1), writes=(lgc,))
                p.dma("sync", scrB.ap[0:128, dt:dt + 1],
                      W["lnb"][l][dt * 128:(dt + 1) * 128].unsqueeze(1), writes=(scrB,))
            if b == 0:
                p.op("vector", lambda v: v.tensor_copy(lbb.ap, scrB.ap[:, 0:8]),
                     reads=(scrB,), writes=(lbb,))
            NCW = ([512] * 10 + [340]) if (not B3_SKIP_W1PREP and b == 0) else []
            for ci, cw in enumerate(NCW):
                c0 = ci * 512
                for dt in range(8):
                    ws_ = wstage if dt % 2 == 0 else wstage2
                    rb_ = rawb if dt % 2 == 0 else rawb2
                    src = W["w1"][l][dt * 128:(dt + 1) * 128, c0:c0 + cw]
                    p.dma("sync", ws_.ap[:, 0:cw], src, writes=(ws_,))
                    p.op("vector", lambda v, _cw=cw, _w=ws_, _b=rb_: v.tensor_copy(
                        _b.ap[:, 0:_cw], _w.ap[:, 0:_cw]),
                         reads=(ws_,), writes=(rb_,))
                    p.mm_group(
                        [lambda tn, _dt=dt, _cw=cw, _b=rb_: tn.matmul(
                            ps_s.ap[0:1, 0:min(_cw, 256)], lbb.ap[:, _dt:_dt + 1],
                            _b.ap[:, 0:min(_cw, 256)], start=(_dt == 0), stop=(_dt == 7)),
                         lambda tn, _dt=dt, _cw=cw, _b=rb_: tn.matmul(
                            ps_s.ap[32:33, 0:max(_cw - 256, 1)], lbb.ap[:, _dt:_dt + 1],
                            _b.ap[:, 256:256 + max(_cw - 256, 1)], start=(_dt == 0),
                            stop=(_dt == 7), skip_group_check=True, tile_position=(0, 32))],
                        reads=(rb_, lbb), writes=(ps_s,))
                    p.op("vector", lambda v, _dt=dt, _cw=cw, _w=ws_, _b=rb_: v.tensor_scalar(
                        _b.ap[:, 0:_cw], _w.ap[:, 0:_cw], lgc.ap[:, _dt:_dt + 1],
                        None, ALU.mult),
                        reads=(ws_, lgc), writes=(rb_,))
                    p.dma("gpsimd", w1bfL[l].ap[dt * 128:(dt + 1) * 128, c0:c0 + cw],
                          rb_.ap[:, 0:cw], reads=(rb_,), writes=(w1bfL[l],))
                p.op("vector", lambda v, _c0=c0, _cw=cw: v.tensor_copy(
                    c2row.ap[0:1, _c0:_c0 + min(_cw, 256)], ps_s.ap[0:1, 0:min(_cw, 256)]),
                    reads=(ps_s,), writes=(c2row,))
                p.op("vector", lambda v, _c0=c0, _cw=cw: v.tensor_copy(
                    c2row.ap[0:1, _c0 + 256:_c0 + 256 + max(_cw - 256, 1)],
                    ps_s.ap[32:33, 0:max(_cw - 256, 1)]),
                    reads=(ps_s,), writes=(c2row,))
            for c0 in (range(0, 5460, 1024) if (not B3_SKIP_W1PREP and b == 0) else ()):
                cw = min(1024, 5460 - c0)
                p.dma("sync", scrB.ap[0:1, 0:cw], W["b1"][l][0:1, c0:c0 + cw], writes=(scrB,))
                p.op("vector", lambda v, _c0=c0, _cw=cw: v.tensor_tensor(
                    c2row.ap[0:1, _c0:_c0 + _cw], c2row.ap[0:1, _c0:_c0 + _cw],
                    scrB.ap[0:1, 0:_cw], ALU.add),
                    reads=(scrB,), writes=(c2row,))
            for t22 in (range(22) if (not B3_SKIP_W1PREP and b == 0) else ()):
                cnt = 128 if t22 < 21 else 42
                p.dma("gpsimd", b1aL[l].ap[0:cnt, t22:t22 + 1],
                      c2row.ap[0:1, t22 * 128:t22 * 128 + cnt], reads=(c2row,), writes=(b1aL[l],))
                p.dma("gpsimd", b1gL[l].ap[0:cnt, t22:t22 + 1],
                      c2row.ap[0:1, 2730 + t22 * 128:2730 + t22 * 128 + cnt],
                      reads=(c2row,), writes=(b1gL[l],))

            for dt in range(8):
                p.op("vector", lambda v, _dt=dt: v.memset(yT.ap[:, _dt * TCOL:_dt * TCOL + 1], 0.0),
                     writes=(yT,))
            for r in range(NT):
                p.dma("sync", lnbf.ap, yin.ap[r * 128:(r + 1) * 128, :], reads=(yin,), writes=(lnbf,))
                transpose4(p, C, lnbf, lambda dt_: lnbf.ap[:, dt_ * 128:(dt_ + 1) * 128], ps_tr,
                           yT, lambda dt_, _r=r: yT.ap[:, dt_ * TCOL + 1 + _r * 128:
                                                       dt_ * TCOL + 1 + (_r + 1) * 128], 8)
            for r in range(NT):
                qn, qbf = qnP[r % 2], qbfP[r % 2]
                c0 = r * 128
                for cc_, dstb in ((0, None), (1, None), (2, None)):
                    dstp = PA.ap[:, cc_ * 256:cc_ * 256 + 512] if False else PA.ap[:, 0:512]
                for idx, (lo, evac_to) in enumerate(((0, qn), (512, qn), (1024, None))):
                    pass
                p.mm_group([(lambda tn, _l=shifted_lhs(yT, d8, c0, 128),
                             _r=wqkv.ap[:, d8 * 1536 + 0:d8 * 1536 + 512],
                             _s=(d8 == 0), _e=(d8 == 7):
                             tn.matmul(PA.ap[:, 0:512], _l, _r, start=_s, stop=_e))
                            for d8 in range(8)],
                           reads=(yT, wqkv), writes=(PA,))
                p.op("vector", lambda v, _q=qn: v.tensor_copy(_q.ap, PA.ap[:, 0:512]),
                     reads=(PA,), writes=(qn,))
                rope_b3(p, C, qn, qbf, tabs[0], tabs[1], qsb, 0, r, nrm, sq_scr)
                transpose4(p, C, qbf, lambda u: qbf.ap[:, u * 128:(u + 1) * 128], ps_tr,
                           qT, lambda u, _r=r: qT.ap[:, u * SEQ + _r * 128:u * SEQ + (_r + 1) * 128], 4)
                p.mm_group([(lambda tn, _l=shifted_lhs(yT, d8, c0, 128),
                             _r=wqkv.ap[:, d8 * 1536 + 512:d8 * 1536 + 1024],
                             _s=(d8 == 0), _e=(d8 == 7):
                             tn.matmul(PA.ap[:, 512:1024], _l, _r, start=_s, stop=_e))
                            for d8 in range(8)],
                           reads=(yT, wqkv), writes=(PA,))
                p.op("vector", lambda v, _q=qn: v.tensor_copy(_q.ap, PA.ap[:, 512:1024]),
                     reads=(PA,), writes=(qn,))
                rope_b3(p, C, qn, qbf, tabs[2], tabs[3], qsb, 1, r, nrm, sq_scr)
                transpose4(p, C, qbf, lambda u: qbf.ap[:, u * 128:(u + 1) * 128], ps_tr,
                           kT, lambda u, _r=r: kT.ap[:, u * SEQ + _r * 128:u * SEQ + (_r + 1) * 128], 4)
                p.mm_group([(lambda tn, _l=shifted_lhs(yT, d8, c0, 128),
                             _r=wqkv.ap[:, d8 * 1536 + 1024:d8 * 1536 + 1536],
                             _s=(d8 == 0), _e=(d8 == 7):
                             tn.matmul(PA.ap[:, 0:512], _l, _r, start=_s, stop=_e))
                            for d8 in range(8)],
                           reads=(yT, wqkv), writes=(PA,))
                p.op("vector", lambda v, _r=r: v.tensor_copy(vnat.ap[:, _r * 512:(_r + 1) * 512],
                                                             PA.ap[:, 0:512]),
                     reads=(PA,), writes=(vnat,))
            p.op("vector", lambda v: v.memset(ps_s.ap, 1.0), writes=(ps_s,))
            for u in range(4 if not B3_SKIP_ATTN else 0):
                for ch in range(8):
                    kts = [kt_ for kt_ in range(2 * ch - 4, 2 * ch + 2) if kt_ >= 0]
                    first = True
                    for kt_ in kts:
                        expb = expb2_[kt_ % 2]
                        rel = kt_ - (2 * ch - 4)
                        la = kT.ap[0:64, u * SEQ + kt_ * 128:u * SEQ + (kt_ + 1) * 128]
                        lb_ = kT.ap[64:128, u * SEQ + kt_ * 128:u * SEQ + (kt_ + 1) * 128]
                        ra = qT.ap[0:64, u * SEQ + ch * 256:u * SEQ + (ch + 1) * 256]
                        rb = qT.ap[64:128, u * SEQ + ch * 256:u * SEQ + (ch + 1) * 256]
                        p.mm_group(
                            [lambda tn, _la=la, _ra=ra: tn.matmul(
                                ps_l.ap[:, 0:256], _la, _ra, start=True, stop=True,
                                tile_position=(0, 0)),
                             lambda tn, _lb=lb_, _rb=rb: tn.matmul(
                                ps_l.ap[:, 512:768], _lb, _rb, start=True, stop=True,
                                tile_position=(64, 0))],
                            reads=(kT, qT), writes=(ps_l,))
                        p.op("scalar", lambda s, _x=expb: s.activation(_x.ap[:, 0:256],
                                                              ps_l.ap[:, 0:256],
                                                              AF.Exp, bias=csh.ap[:, 0:1]),
                             reads=(ps_l, csh), writes=(expb,))
                        p.op("scalar", lambda s, _x=expb: s.activation(_x.ap[:, 256:512],
                                                              ps_l.ap[:, 512:768],
                                                              AF.Exp, bias=csh.ap[:, 0:1]),
                             reads=(ps_l, csh), writes=(expb,))
                        mi = {0: 0, 1: 1, 4: 2, 5: 3}.get(rel)
                        if mi is not None:
                            p.op("vector", lambda v, _m=mi, _x=expb: v.tensor_tensor(
                                _x.ap.rearrange("p (h n) -> p h n", h=2),
                                _x.ap.rearrange("p (h n) -> p h n", h=2),
                                masks.ap[:, _m * 256:(_m + 1) * 256].unsqueeze(1)
                                .to_broadcast([128, 2, 256]), ALU.mult),
                                reads=(masks,), writes=(expb,))
                        st_, sp_ = first, (kt_ == kts[-1])
                        va = vnat.ap[:, kt_ * 512 + (2 * u) * 64:kt_ * 512 + (2 * u + 1) * 64]
                        vb = vnat.ap[:, kt_ * 512 + (2 * u + 1) * 64:kt_ * 512 + (2 * u + 2) * 64]
                        p.mm_group(
                            [lambda tn, _va=va, _s=st_, _e=sp_, _x=expb: tn.matmul(
                                ps_av.ap[0:64, :], _va, _x.ap[:, 0:256], start=_s, stop=_e,
                                tile_position=(0, 0)),
                             lambda tn, _vb=vb, _s=st_, _e=sp_, _x=expb: tn.matmul(
                                ps_av.ap[64:128, :], _vb, _x.ap[:, 256:512], start=_s, stop=_e,
                                skip_group_check=True, tile_position=(0, 64))],
                            reads=(vnat, expb), writes=(ps_av,))
                        p.mm_group(
                            [lambda tn, _x=expb, _s=st_, _e=sp_: tn.matmul(
                                ps_s.ap[0:1, :], C.onescol.ap, _x.ap[:, 0:256], start=_s, stop=_e,
                                tile_position=(0, 0)),
                             lambda tn, _x=expb, _s=st_, _e=sp_: tn.matmul(
                                ps_s.ap[32:33, :], C.onescol.ap, _x.ap[:, 256:512], start=_s,
                                stop=_e, skip_group_check=True, tile_position=(0, 32))],
                            reads=(expb, C.onescol), writes=(ps_s,))
                        first = False
                    p.op("scalar", lambda s: s.activation(lnrec.ap, ps_s.ap, AF.Ln),
                         reads=(ps_s,), writes=(lnrec,))
                    p.op("scalar", lambda s: s.activation(recipb.ap, lnrec.ap, AF.Exp,
                                                          scale=C.col(2, 33)),
                         reads=(lnrec, C.cols), writes=(recipb,))
                    p.op("tensor", lambda tn: tn.matmul(ps_l.ap[:, 0:256], C.onesrow.ap,
                                                        recipb.ap[0:1, :], start=True, stop=True),
                         reads=(recipb, C.onesrow), writes=(ps_l,))
                    p.op("tensor", lambda tn: tn.matmul(ps_l.ap[:, 512:768], C.onesrow33.ap[32:33, :],
                                                        recipb.ap[32:33, :], start=True, stop=True,
                                                        skip_group_check=True, tile_position=(32, 0)),
                         reads=(recipb, C.onesrow33), writes=(ps_l,))
                    p.op("vector", lambda v: v.tensor_copy(rsb.ap[0:64, :], ps_l.ap[0:64, 0:256]),
                         reads=(ps_l,), writes=(rsb,))
                    p.op("vector", lambda v: v.tensor_copy(rsb.ap[64:128, :], ps_l.ap[64:128, 512:768]),
                         reads=(ps_l,), writes=(rsb,))
                    dst = attnT.ap[:, u * SEQ + ch * 256:u * SEQ + (ch + 1) * 256]
                    p.op("vector", lambda v, _d=dst: v.tensor_tensor(_d, ps_av.ap, rsb.ap, ALU.mult),
                         reads=(ps_av, rsb), writes=(attnT,))
            for r in range(NT):
                for cc_ in range(2):
                    dstp = PA.ap[:, cc_ * 512:(cc_ + 1) * 512]
                    p.mm_group([(lambda tn, _l=attnT.ap[:, tau * SEQ + r * 128:tau * SEQ + (r + 1) * 128],
                                 _r=wo_b.ap[:, tau * 1024 + cc_ * 512:tau * 1024 + (cc_ + 1) * 512],
                                 _d=dstp, _s=(tau == 0), _e=(tau == 3):
                                 tn.matmul(_d, _l, _r, start=_s, stop=_e)) for tau in range(4)],
                               reads=(attnT, wo_b), writes=(PA,))
                p.dma("gpsimd", xt.ap, yin.ap[r * 128:(r + 1) * 128, :], reads=(yin,), writes=(xt,))
                for cc_ in range(2):
                    p.op("vector", lambda v, _c=cc_: v.tensor_tensor(
                        onat.ap[:, _c * 512:(_c + 1) * 512], PA.ap[:, _c * 512:(_c + 1) * 512],
                        xt.ap[:, _c * 512:(_c + 1) * 512], ALU.add),
                        reads=(PA, xt), writes=(onat,))
                p.dma("gpsimd", y_mid.ap[r * 128:(r + 1) * 128, :], onat.ap,
                      reads=(onat,), writes=(y_mid,))
                p.op("vector", lambda v, _r=r: v.tensor_reduce(statb.ap[:, 4 * _r:4 * _r + 1],
                                                               onat.ap[:, 0:512], AX.X, ALU.add),
                     reads=(onat,), writes=(statb,))
                p.op("vector", lambda v, _r=r: v.tensor_reduce(statb.ap[:, 4 * _r + 1:4 * _r + 2],
                                                               onat.ap[:, 512:1024], AX.X, ALU.add),
                     reads=(onat,), writes=(statb,))
                p.op("scalar", lambda s, _r=r: s.activation(sq_scr.ap[:, 0:512], onat.ap[:, 0:512],
                                                            AF.Square,
                                                            accum_out=statb.ap[:, 4 * _r + 2:4 * _r + 3]),
                     reads=(onat,), writes=(sq_scr, statb))
                p.op("scalar", lambda s, _r=r: s.activation(sq_scr.ap[:, 512:1024], onat.ap[:, 512:1024],
                                                            AF.Square,
                                                            accum_out=statb.ap[:, 4 * _r + 3:4 * _r + 4]),
                     reads=(onat,), writes=(sq_scr, statb))
            if B3_STOP_C2:
                continue
            p.mm_group([
                lambda tn: tn.matmul(ps_l.ap[:, 0:64], shm.ap[:, 0:128], statb.ap,
                                     start=True, stop=True),
                lambda tn: tn.matmul(ps_l.ap[:, 100:160], shm.ap[:, 256:384], statb.ap[:, 0:60],
                                     start=True, stop=True),
            ], reads=(statb, shm), writes=(ps_l,))
            p.op("vector", lambda v: v.tensor_copy(stsh.ap, ps_l.ap[:, 0:64]),
                 reads=(ps_l,), writes=(stsh,))
            p.op("vector", lambda v: v.tensor_tensor(stsh.ap[:, 4:64], stsh.ap[:, 4:64],
                                                     ps_l.ap[:, 100:160], ALU.add),
                 reads=(ps_l,), writes=(stsh,))
            inv = 1.0 / D
            for r in range(NT):
                p.op("vector", lambda v, _r=r: v.tensor_tensor(
                    mr.ap[:, 2:3], statb.ap[:, 4 * _r:4 * _r + 1], stsh.ap[:, 4 * _r + 1:4 * _r + 2],
                    ALU.add), reads=(statb, stsh), writes=(mr,))
                p.op("vector", lambda v, _r=r: v.tensor_scalar(mcols.ap[:, _r:_r + 1],
                                                               mr.ap[:, 2:3], inv, None, ALU.mult),
                     reads=(mr,), writes=(mcols,))
                p.op("vector", lambda v, _r=r: v.tensor_tensor(
                    mr.ap[:, 3:4], statb.ap[:, 4 * _r + 2:4 * _r + 3],
                    stsh.ap[:, 4 * _r + 3:4 * _r + 4], ALU.add),
                    reads=(statb, stsh), writes=(mr,))
                p.op("vector", lambda v: v.tensor_scalar(mr.ap[:, 3:4], mr.ap[:, 3:4], inv,
                                                         None, ALU.mult),
                     reads=(mr,), writes=(mr,))
                p.op("vector", lambda v, _r=r: v.tensor_tensor(mr.ap[:, 2:3], mcols.ap[:, _r:_r + 1],
                                                               mcols.ap[:, _r:_r + 1], ALU.mult),
                     reads=(mcols,), writes=(mr,))
                p.op("vector", lambda v: v.tensor_tensor(mr.ap[:, 3:4], mr.ap[:, 3:4], mr.ap[:, 2:3],
                                                         ALU.subtract),
                     reads=(mr,), writes=(mr,))
                p.op("scalar", lambda s: s.activation(mr.ap[:, 3:4], mr.ap[:, 3:4], AF.Ln,
                                                      bias=C.col(0)),
                     reads=(mr, C.cols), writes=(mr,))
                p.op("scalar", lambda s, _r=r: s.activation(rcols.ap[:, _r:_r + 1], mr.ap[:, 3:4],
                                                            AF.Exp, scale=C.col(1)),
                     reads=(mr, C.cols), writes=(rcols,))
            p.mm_group([
                lambda tn: tn.matmul(ps_l.ap[:, 160:176], shm.ap[:, 128:256], mcols.ap,
                                     start=True, stop=True),
                lambda tn: tn.matmul(ps_l.ap[:, 176:191], shm.ap[:, 384:512], mcols.ap[:, 1:16],
                                     start=True, stop=True),
                lambda tn: tn.matmul(ps_l.ap[:, 192:208], shm.ap[:, 128:256], rcols.ap,
                                     start=True, stop=True),
                lambda tn: tn.matmul(ps_l.ap[:, 208:223], shm.ap[:, 384:512], rcols.ap[:, 1:16],
                                     start=True, stop=True),
            ], reads=(mcols, rcols, shm), writes=(ps_l,))
            mshv = msh.ap.rearrange("p (r c) -> p r c", c=2)
            p.op("vector", lambda v: v.tensor_copy(mshv[:, :, 0:1],
                                                   ps_l.ap[:, 160:176].unsqueeze(2)),
                 reads=(ps_l,), writes=(msh,))
            p.op("vector", lambda v: v.tensor_copy(mshv[:, :, 1:2],
                                                   ps_l.ap[:, 192:208].unsqueeze(2)),
                 reads=(ps_l,), writes=(msh,))
            p.op("vector", lambda v: v.tensor_tensor(
                mshv[:, 0:15, 0:1], mshv[:, 0:15, 0:1],
                ps_l.ap[:, 176:191].unsqueeze(2), ALU.add),
                reads=(ps_l,), writes=(msh,))
            p.op("vector", lambda v: v.tensor_tensor(
                mshv[:, 0:15, 1:2], mshv[:, 0:15, 1:2],
                ps_l.ap[:, 208:223].unsqueeze(2), ALU.add),
                reads=(ps_l,), writes=(msh,))

            p.barrier()
            for t22 in range(22):
                cnt = 128 if t22 < 21 else 42
                for cc_ in range(2):
                    src = W["w2"][l][t22 * 128:t22 * 128 + cnt, cc_ * 512:(cc_ + 1) * 512]
                    p.dma("sync", wstage.ap[0:cnt, :], src, writes=(wstage,))
                    p.op("vector", lambda v, _t=t22, _c=cc_, _n=cnt: v.tensor_copy(
                        w2_b.ap[0:_n, _t * 1024 + _c * 512:_t * 1024 + (_c + 1) * 512],
                        wstage.ap[0:_n, :]),
                        reads=(wstage,), writes=(w2_b,))
            p.dma("sync", scrB.ap[0:1, 0:1024], W["b2"][l], writes=(scrB,))
            p.op("vector", lambda v: v.tensor_copy(b2b.ap, scrB.ap[0:1, 0:1024]),
                 reads=(scrB,), writes=(b2b,))
            for dt in range(8):
                p.op("vector", lambda v, _dt=dt: v.memset(h2T.ap[:, _dt * TCOL:_dt * TCOL + 1], 0.0),
                     writes=(h2T,))
            p.op("vector", lambda v: v.tensor_tensor(cell.ap[0:1, 2:3], mcols.ap[0:1, 0:1],
                                                     rcols.ap[0:1, 0:1], ALU.mult),
                 reads=(mcols, rcols), writes=(cell,))
            p.op("vector", lambda v: v.tensor_scalar(cell.ap[0:1, 2:3], cell.ap[0:1, 2:3],
                                                     -1.0, None, ALU.mult),
                 reads=(), writes=(cell,))
            p.op("vector", lambda v: v.tensor_copy(cellb.ap[0:1, 1:2], cell.ap[0:1, 2:3]),
                 reads=(cell,), writes=(cellb,))
            p.mm_group([lambda tn: tn.matmul(ps_r.ap[:, 0:1], C.onesrow.ap, cellb.ap[0:1, 1:2],
                                             start=True, stop=True)],
                       reads=(cellb, C.onesrow), writes=(ps_r,))
            for dt in range(4, 8):
                p.op("vector", lambda v, _dt=dt: v.tensor_copy(
                    h2T.ap[:, _dt * TCOL:_dt * TCOL + 1], ps_r.ap[:, 0:1]),
                    reads=(ps_r,), writes=(h2T,))
            for r in range(NT):
                p.dma("gpsimd", xt.ap, y_mid.ap[r * 128:(r + 1) * 128, :], reads=(y_mid,), writes=(xt,))
                p.op("vector", lambda v, _r=r: v.tensor_scalar(
                    lnbf.ap[:, 0:512], xt.ap[:, 0:512], mcols.ap[:, _r:_r + 1],
                    rcols.ap[:, _r:_r + 1], ALU.subtract, ALU.mult),
                    reads=(xt, mcols, rcols), writes=(lnbf,))
                p.op("vector", lambda v, _r=r: v.tensor_scalar(
                    lnbf.ap[:, 512:1024], xt.ap[:, 512:1024], msh.ap[:, 2 * _r:2 * _r + 1],
                    msh.ap[:, 2 * _r + 1:2 * _r + 2], ALU.subtract, ALU.mult),
                    reads=(xt, msh), writes=(lnbf,))
                transpose4(p, C, lnbf, lambda dt_: lnbf.ap[:, dt_ * 128:(dt_ + 1) * 128], ps_tr,
                           h2T, lambda dt_, _r=r: h2T.ap[:, dt_ * TCOL + 1 + _r * 128:
                                                         dt_ * TCOL + 1 + (_r + 1) * 128], 8)
            for ch in range(2):
                n0 = ch * 1024
                for t22 in range(22):
                    cnt = 128 if t22 < 21 else 42
                    la_ = lw1[(t22 % 2) * 2]
                    lg_ = lw1[(t22 % 2) * 2 + 1]
                    p.dma_multi("sync",
                                [(la_.ap[:, d8 * 128:d8 * 128 + cnt],
                                  w1bfL[l].ap[d8 * 128:(d8 + 1) * 128, t22 * 128:t22 * 128 + cnt])
                                 for d8 in range(8)], la_, reads=(w1bfL[l],))
                    p.dma_multi("sync",
                                [(lg_.ap[:, d8 * 128:d8 * 128 + cnt],
                                  w1bfL[l].ap[d8 * 128:(d8 + 1) * 128,
                                          2730 + t22 * 128:2730 + t22 * 128 + cnt])
                                 for d8 in range(8)], lg_, reads=(w1bfL[l],))
                    for half in range(2):
                        p.mm_group([(lambda tn, _l=la_.ap[:, d8 * 128:d8 * 128 + cnt],
                                     _r=shifted_lhs(h2T, d8, n0 + half * 512, 512),
                                     _d=PA.ap[0:cnt, 0:512] if half == 0 else PA.ap[0:cnt, 512:1024],
                                     _s=(d8 == 0), _e=(d8 == 7):
                                     tn.matmul(_d, _l, _r, start=_s, stop=_e))
                                    for d8 in range(8)],
                                   reads=(la_, h2T), writes=(PA,))
                        p.op("vector", lambda v, _t=t22, _n=cnt, _h=half, _b1=b1aL[l]:
                            v.tensor_scalar(
                            scrA.ap[0:_n, _h * 512:(_h + 1) * 512],
                            PA.ap[0:_n, _h * 512:(_h + 1) * 512],
                            _b1.ap[0:_n, _t:_t + 1], None, ALU.add),
                            reads=(PA, b1aL[l]), writes=(scrA,))
                    for half in range(2):
                        p.mm_group([(lambda tn, _l=lg_.ap[:, d8 * 128:d8 * 128 + cnt],
                                     _r=shifted_lhs(h2T, d8, n0 + half * 512, 512),
                                     _d=ps_l.ap[0:cnt, 0:512] if half == 0 else ps_l.ap[0:cnt, 512:1024],
                                     _s=(d8 == 0), _e=(d8 == 7):
                                     tn.matmul(_d, _l, _r, start=_s, stop=_e))
                                    for d8 in range(8)],
                                   reads=(lg_, h2T), writes=(ps_l,))
                        p.op("vector", lambda v, _t=t22, _n=cnt, _h=half, _b1=b1gL[l]:
                            v.tensor_scalar(
                            scrB.ap[0:_n, _h * 512:(_h + 1) * 512],
                            ps_l.ap[0:_n, _h * 512:(_h + 1) * 512],
                            _b1.ap[0:_n, _t:_t + 1], None, ALU.add),
                            reads=(ps_l, b1gL[l]), writes=(scrB,))
                    p.op("scalar", lambda s, _n=cnt: s.activation(scrB.ap[0:_n, :], scrB.ap[0:_n, :],
                                                                  AF.Gelu),
                         reads=(), writes=(scrB,))
                    p.op("vector", lambda v, _t=t22, _n=cnt: v.tensor_tensor(
                        h3seg.ap[0:_n, _t * 1024:(_t + 1) * 1024], scrA.ap[0:_n, :],
                        scrB.ap[0:_n, :], ALU.mult),
                        reads=(scrA, scrB), writes=(h3seg,))
                for nt in range(8):
                    r = ch * 8 + nt
                    wps = PA if nt % 2 == 0 else ps_l
                    p.dma("gpsimd", xt.ap, y_mid.ap[r * 128:(r + 1) * 128, :],
                          reads=(y_mid,), writes=(xt,))
                    for och in range(2):
                        dstp = wps.ap[:, och * 512:(och + 1) * 512]
                        calls = [(lambda tn, _l=h3seg.ap[0:(128 if t2 < 21 else 42),
                                                         t2 * 1024 + nt * 128:t2 * 1024 + (nt + 1) * 128],
                                  _r=w2_b.ap[0:(128 if t2 < 21 else 42),
                                             t2 * 1024 + och * 512:t2 * 1024 + (och + 1) * 512],
                                  _d=dstp, _s=(t2 == 0):
                                  tn.matmul(_d, _l, _r, start=_s, stop=False)) for t2 in range(22)]
                        calls.append(lambda tn, _d=dstp, _o=och: tn.matmul(
                            _d, C.onesrow.ap, b2b.ap[0:1, _o * 512:(_o + 1) * 512],
                            start=False, stop=True))
                        p.mm_group(calls, reads=(h3seg, w2_b, b2b, C.onesrow), writes=(wps,))
                    for och in range(2):
                        p.op("vector", lambda v, _o=och, _w=wps: v.tensor_tensor(
                            onat.ap[:, _o * 512:(_o + 1) * 512], _w.ap[:, _o * 512:(_o + 1) * 512],
                            xt.ap[:, _o * 512:(_o + 1) * 512], ALU.add),
                            reads=(wps, xt), writes=(onat,))
                    p.dma("gpsimd", ynext.ap[r * 128:(r + 1) * 128, :], onat.ap,
                          reads=(onat,), writes=(ynext,))
            p.barrier()
        for r in range(NT):
            row = b * SEQ + r * 128
            p.dma("gpsimd", yt2.ap, y_a.ap[r * 128:(r + 1) * 128, :], reads=(y_a,), writes=(yt2,))
            ln_tile(p, C, yt2, yt2.ap, mr, sq_scr)
            p.dma("sync", xt.ap, xr[row:row + 128, :], writes=(xt,))
            p.op("vector", lambda v: v.tensor_scalar(yt2.ap, yt2.ap, mr.ap[:, 1:2], mr.ap[:, 0:1],
                                                     ALU.subtract, ALU.mult),
                 reads=(mr,), writes=(yt2,))
            p.op("vector", lambda v: v.tensor_tensor(yt2.ap, yt2.ap, xt.ap, ALU.add),
                 reads=(xt,), writes=(yt2,))
            p.dma("gpsimd", xro[row:row + 128, :], yt2.ap, reads=(yt2,))


def rope_b3(p, C, qn, qbf, tabc, tabs_, qsb, which, r, nrm, sq_scr):
    hv = qn.ap.rearrange("p (h d) -> p h d", d=64)
    p.op("scalar", lambda s: s.activation(sq_scr.ap[:, 0:512], qn.ap, AF.Square),
         reads=(qn,), writes=(sq_scr,))
    p.op("vector", lambda v: v.tensor_reduce(nrm.ap, sq_scr.ap[:, 0:512]
                                             .rearrange("p (h d) -> p h d", d=64), AX.X, ALU.add),
         reads=(sq_scr,), writes=(nrm,))
    p.op("vector", lambda v: v.tensor_scalar(nrm.ap, nrm.ap, 1.0, 1e-24, ALU.mult, ALU.max),
         reads=(), writes=(nrm,))
    p.op("scalar", lambda s: s.activation(nrm.ap, nrm.ap, AF.Ln),
         reads=(), writes=(nrm,))
    p.op("scalar", lambda s: s.activation(nrm.ap, nrm.ap, AF.Exp, scale=C.col(1)),
         reads=(C.cols,), writes=(nrm,))
    p.op("vector", lambda v: v.tensor_tensor(hv, hv, nrm.ap.unsqueeze(2).to_broadcast([128, 8, 64]),
                                             ALU.mult),
         reads=(nrm,), writes=(qn,))
    sc = qsb.ap[:, which * 64:(which + 1) * 64].unsqueeze(1).to_broadcast([128, 8, 64])
    p.op("vector", lambda v, _s=sc: v.tensor_tensor(hv, hv, _s, ALU.mult),
         reads=(qsb,), writes=(qn,))
    cs = tabc.ap[:, r * 64:(r + 1) * 64]
    sn = tabs_.ap[:, r * 64:(r + 1) * 64]
    bv = qbf.ap.rearrange("p (h d) -> p h d", d=64)
    x1, x2 = hv[:, :, 0:32], hv[:, :, 32:64]
    c1 = cs[:, 0:32].unsqueeze(1).to_broadcast([128, 8, 32])
    c2 = cs[:, 32:64].unsqueeze(1).to_broadcast([128, 8, 32])
    s1 = sn[:, 0:32].unsqueeze(1).to_broadcast([128, 8, 32])
    s2 = sn[:, 32:64].unsqueeze(1).to_broadcast([128, 8, 32])
    b1_, b2_ = bv[:, :, 0:32], bv[:, :, 32:64]
    p.op("vector", lambda v, _a=b1_, _x=x2, _s=s1: v.tensor_tensor(_a, _x, _s, ALU.mult),
         reads=(qn, tabs_), writes=(qbf,))
    p.op("vector", lambda v, _a=b2_, _x=x1, _s=s2: v.tensor_tensor(_a, _x, _s, ALU.mult),
         reads=(qn, tabs_), writes=(qbf,))
    p.op("vector", lambda v, _x=x1, _c=c1: v.tensor_tensor(_x, _x, _c, ALU.mult),
         reads=(tabc,), writes=(qn,))
    p.op("vector", lambda v, _x=x2, _c=c2: v.tensor_tensor(_x, _x, _c, ALU.mult),
         reads=(tabc,), writes=(qn,))
    p.op("vector", lambda v, _a=b1_, _x=x1: v.tensor_tensor(_a, _x, _a, ALU.subtract),
         reads=(qn,), writes=(qbf,))
    p.op("vector", lambda v, _a=b2_, _x=x2: v.tensor_tensor(_a, _x, _a, ALU.add),
         reads=(qn,), writes=(qbf,))


def build_b3_host():
    DH, WIN = 64, 512
    half = np.arange(0, DH, 2, dtype=np.float64)
    inv_freq = 1.0 / 10000.0 ** (half / DH)
    t = np.arange(SEQ, dtype=np.float64)
    freqs = t[:, None] * inv_freq[None, :]
    base = (half + 0.4 * DH) / (1.4 * DH)
    power = (t - SEQ // 2) / (WIN // 2)
    scale = base[None, :] ** power[:, None]
    cosf, sinf = np.cos(freqs), np.sin(freqs)
    qc = np.concatenate([cosf * scale, cosf * scale], axis=1).astype(BF16NP)
    qs = np.concatenate([sinf * scale, sinf * scale], axis=1).astype(BF16NP)
    kc = np.concatenate([cosf / scale, cosf / scale], axis=1).astype(BF16NP)
    ks = np.concatenate([sinf / scale, sinf / scale], axis=1).astype(BF16NP)
    kk, qq = np.meshgrid(np.arange(128), np.arange(128), indexing="ij")
    tri = (kk <= qq).astype(BF16NP)
    anti = (kk >= qq).astype(BF16NP)
    ones = np.ones((128, 128), BF16NP)
    zeros = np.zeros((128, 128), BF16NP)
    m0 = np.concatenate([anti, zeros], axis=1)
    m1 = np.concatenate([ones, anti], axis=1)
    m4 = np.concatenate([tri, ones], axis=1)
    m5 = np.concatenate([zeros, tri], axis=1)
    masks = np.concatenate([m0, m1, m4, m5], axis=1)
    return (qc, qs, kc, ks), masks


def make_block3_drams(nc):
    mk = lambda n, s, d=F32: Buf(nc.dram_tensor(n, s, d).ap(), n)
    return {
        "y_a": mk("b3_y_a", [SEQ, D], BF16),
        "y_b": mk("b3_y_b", [SEQ, D], BF16),
        "y_mid": mk("b3_y_mid", [SEQ, D], BF16),
        "w1bf0": mk("b3_w1bf0", [1024, 5460], BF16),
        "w1bf1": mk("b3_w1bf1", [1024, 5460], BF16),
    }


def build_block3_nc():
    from contextlib import ExitStack
    nc = bass.Bass("TRN2", target_bir_lowering=False)
    x_in = nc.dram_tensor("x_in", [BPC, SEQ, D], F32, kind="ExternalInput").ap()
    x_out = nc.dram_tensor("x_out", [BPC, SEQ, D], F32, kind="ExternalOutput").ap()
    wd = {
        "wqkv": [nc.dram_tensor(f"wqkv{l}", [D, 1536], F32, kind="ExternalInput").ap() for l in range(2)],
        "wo": [nc.dram_tensor(f"wo{l}", [512, D], F32, kind="ExternalInput").ap() for l in range(2)],
        "w1": [nc.dram_tensor(f"w1_{l}", [D, 5460], F32, kind="ExternalInput").ap() for l in range(2)],
        "w2": [nc.dram_tensor(f"w2_{l}", [B3_HID, D], F32, kind="ExternalInput").ap() for l in range(2)],
        "b1": [nc.dram_tensor(f"b1_{l}", [1, 5460], F32, kind="ExternalInput").ap() for l in range(2)],
        "b2": [nc.dram_tensor(f"b2_{l}", [1, D], F32, kind="ExternalInput").ap() for l in range(2)],
        "lng": [nc.dram_tensor(f"lng{l}", [D], F32, kind="ExternalInput").ap() for l in range(2)],
        "lnb": [nc.dram_tensor(f"lnb{l}", [D], F32, kind="ExternalInput").ap() for l in range(2)],
        "qscale": [nc.dram_tensor(f"qsc{l}", [1, 64], F32, kind="ExternalInput").ap() for l in range(2)],
        "kscale": [nc.dram_tensor(f"ksc{l}", [1, 64], F32, kind="ExternalInput").ap() for l in range(2)],
        "xpos": [nc.dram_tensor(f"xpos{i}", [SEQ, 64], BF16, kind="ExternalInput").ap() for i in range(4)],
        "masks": nc.dram_tensor("b3masks", [128, 1024], BF16, kind="ExternalInput").ap(),
    }
    capd = {}
    cnp = build_constants()
    for k in ("ident", "onescol", "onesrow", "sel2"):
        capd[k] = nc.dram_tensor("cin_" + k, list(cnp[k].shape), BF16, kind="ExternalInput").ap()
    stack = ExitStack()
    p = Prog(nc)
    C = Consts(p, stack, capd)
    dram = make_block3_drams(nc)
    emit_block3(p, stack, C, x_in, x_out, wd, dram)
    p.barrier()
    p.emit(stack)
    stack.close()
    return nc, cnp


def block3_inmaps(inputs, x_full, cnp):
    xpos, masks = build_b3_host()
    base = {}
    for l in range(2):
        base[f"wqkv{l}"] = np.ascontiguousarray(inputs["L1_ffn_wqkv"][l])
        base[f"wo{l}"] = np.ascontiguousarray(inputs["L1_ffn_wo"][l])
        base[f"w1_{l}"] = np.ascontiguousarray(inputs["L1_ffn_w1"][l])
        base[f"w2_{l}"] = np.ascontiguousarray(inputs["L1_ffn_w2"][l])
        base[f"b1_{l}"] = np.ascontiguousarray(inputs["L1_ffn_b1"][l]).reshape(1, 5460)
        base[f"b2_{l}"] = np.ascontiguousarray(inputs["L1_ffn_b2"][l]).reshape(1, D)
        base[f"lng{l}"] = np.ascontiguousarray(inputs["L1_ffn_ln_g"][l])
        base[f"lnb{l}"] = np.ascontiguousarray(inputs["L1_ffn_ln_b"][l])
        base[f"qsc{l}"] = np.ascontiguousarray(inputs["L1_ffn_q_scale"][l]).reshape(1, 64)
        base[f"ksc{l}"] = np.ascontiguousarray(inputs["L1_ffn_k_scale"][l]).reshape(1, 64)
    for i in range(4):
        base[f"xpos{i}"] = xpos[i]
    base["b3masks"] = masks
    for k in ("ident", "onescol", "onesrow", "sel2"):
        base["cin_" + k] = cnp[k]
    in_maps = []
    for c in range(NCORES):
        m = dict(base)
        m["x_in"] = np.ascontiguousarray(x_full[c * BPC:(c + 1) * BPC])
        in_maps.append(m)
    return in_maps


def run_block3(inputs, x_full):
    nc, cnp = build_block3_nc()
    in_maps = block3_inmaps(inputs, x_full, cnp)
    res = run_bass_kernel_spmd(nc, in_maps, core_ids=list(range(NCORES)))
    return np.concatenate([r["x_out"].reshape(BPC, SEQ, D) for r in res.results], axis=0)


def build_fused_nc():
    from contextlib import ExitStack
    nc = bass.Bass("TRN2", target_bir_lowering=False)
    x0 = nc.dram_tensor("x_in", [BPC, SEQ, D], F32, kind="ExternalInput").ap()
    x_out = nc.dram_tensor("x_out", [BPC, SEQ, D], F32, kind="ExternalOutput").ap()
    x1 = nc.dram_tensor("fx1", [BPC, SEQ, D], F32).ap()
    x2 = nc.dram_tensor("fx2", [BPC, SEQ, D], F32).ap()
    x3 = nc.dram_tensor("fx3", [BPC, SEQ, D], F32).ap()
    w0 = {
        "Wq": nc.dram_tensor("Wq", [D, D], F32, kind="ExternalInput").ap(),
        "Wk": nc.dram_tensor("Wk", [D, 512], F32, kind="ExternalInput").ap(),
        "Wv": nc.dram_tensor("Wv", [D, 512], F32, kind="ExternalInput").ap(),
        "Wo": nc.dram_tensor("Wo", [D, D], F32, kind="ExternalInput").ap(),
        "bo": nc.dram_tensor("bo", [1, D], F32, kind="ExternalInput").ap(),
        "rope_cos": nc.dram_tensor("rope_cos", [SEQ, 16], F32, kind="ExternalInput").ap(),
        "rope_sin": nc.dram_tensor("rope_sin", [SEQ, 16], F32, kind="ExternalInput").ap(),
    }
    w1 = {
        "w_up_sl": nc.dram_tensor("w_up_sl", [3072, 128], F32, kind="ExternalInput").ap(),
        "w_gate_sl": nc.dram_tensor("w_gate_sl", [3072, 128], F32, kind="ExternalInput").ap(),
        "w_down_sl": nc.dram_tensor("w_down_sl", [1024, 384], F32, kind="ExternalInput").ap(),
    }
    w2 = {
        "wkT": nc.dram_tensor("wkT", [D, D], F32, kind="ExternalInput").ap(),
        "wvT": nc.dram_tensor("wvT", [D, D], F32, kind="ExternalInput").ap(),
        "wqT": nc.dram_tensor("wqT", [D, D], F32, kind="ExternalInput").ap(),
    }
    w3 = {
        "wqkv": [nc.dram_tensor(f"wqkv{l}", [D, 1536], F32, kind="ExternalInput").ap() for l in range(2)],
        "wo": [nc.dram_tensor(f"wo{l}", [512, D], F32, kind="ExternalInput").ap() for l in range(2)],
        "w1": [nc.dram_tensor(f"w1_{l}", [D, 5460], F32, kind="ExternalInput").ap() for l in range(2)],
        "w2": [nc.dram_tensor(f"w2_{l}", [B3_HID, D], F32, kind="ExternalInput").ap() for l in range(2)],
        "b1": [nc.dram_tensor(f"b1_{l}", [1, 5460], F32, kind="ExternalInput").ap() for l in range(2)],
        "b2": [nc.dram_tensor(f"b2_{l}", [1, D], F32, kind="ExternalInput").ap() for l in range(2)],
        "lng": [nc.dram_tensor(f"lng{l}", [D], F32, kind="ExternalInput").ap() for l in range(2)],
        "lnb": [nc.dram_tensor(f"lnb{l}", [D], F32, kind="ExternalInput").ap() for l in range(2)],
        "qscale": [nc.dram_tensor(f"qsc{l}", [1, 64], F32, kind="ExternalInput").ap() for l in range(2)],
        "kscale": [nc.dram_tensor(f"ksc{l}", [1, 64], F32, kind="ExternalInput").ap() for l in range(2)],
        "xpos": [nc.dram_tensor(f"xpos{i}", [SEQ, 64], BF16, kind="ExternalInput").ap() for i in range(4)],
        "masks": nc.dram_tensor("b3masks", [128, 1024], BF16, kind="ExternalInput").ap(),
    }
    cnp = build_constants()
    capd = {k: nc.dram_tensor("cin_" + k, list(cnp[k].shape), BF16, kind="ExternalInput").ap()
            for k in ("ident", "onescol", "onesrow", "sel2")}
    stack = ExitStack()
    p = Prog(nc)
    C = Consts(p, stack, capd)
    d1 = make_block1_drams(nc)
    d2 = make_block2_drams(nc)
    d3 = make_block3_drams(nc)
    sub = ExitStack()
    emit_block0(p, sub, C, x0, x1, w0)
    p.barrier()
    sub.close()
    sub = ExitStack()
    emit_block1(p, sub, C, x1, x2, w1, d1)
    p.barrier()
    sub.close()
    sub = ExitStack()
    emit_block2(p, sub, C, x2, x3, w2, d2)
    p.barrier()
    sub.close()
    sub = ExitStack()
    emit_block3(p, sub, C, x3, x_out, w3, d3)
    p.barrier()
    sub.close()
    p.emit(stack)
    stack.close()
    return nc, cnp


def fused_inmaps(inputs, cnp):
    x = np.ascontiguousarray(inputs["L0_attn_x"], dtype=np.float32)
    m0 = prep_block0_host(inputs)
    wu = np.ascontiguousarray(inputs["L0_ffn_w_up"])
    wg = np.ascontiguousarray(inputs["L0_ffn_w_gate"])
    wdn = np.ascontiguousarray(inputs["L0_ffn_w_down"])
    wk = np.ascontiguousarray(inputs["L1_attn_W_k"].T)
    wv = np.ascontiguousarray(inputs["L1_attn_W_v"].T)
    wq = np.ascontiguousarray(inputs["L1_attn_W_q"].T)
    xpos, masks = build_b3_host()
    base = dict(m0)
    base.update({"wkT": wk, "wvT": wv, "wqT": wq, "b3masks": masks,
                 "rope_cos": cnp["rope_cos"], "rope_sin": cnp["rope_sin"]})
    for i in range(4):
        base[f"xpos{i}"] = xpos[i]
    for l in range(2):
        base[f"wqkv{l}"] = np.ascontiguousarray(inputs["L1_ffn_wqkv"][l])
        base[f"wo{l}"] = np.ascontiguousarray(inputs["L1_ffn_wo"][l])
        base[f"w1_{l}"] = np.ascontiguousarray(inputs["L1_ffn_w1"][l])
        base[f"w2_{l}"] = np.ascontiguousarray(inputs["L1_ffn_w2"][l])
        base[f"b1_{l}"] = np.ascontiguousarray(inputs["L1_ffn_b1"][l]).reshape(1, 5460)
        base[f"b2_{l}"] = np.ascontiguousarray(inputs["L1_ffn_b2"][l]).reshape(1, D)
        base[f"lng{l}"] = np.ascontiguousarray(inputs["L1_ffn_ln_g"][l])
        base[f"lnb{l}"] = np.ascontiguousarray(inputs["L1_ffn_ln_b"][l])
        base[f"qsc{l}"] = np.ascontiguousarray(inputs["L1_ffn_q_scale"][l]).reshape(1, 64)
        base[f"ksc{l}"] = np.ascontiguousarray(inputs["L1_ffn_k_scale"][l]).reshape(1, 64)
    for k in ("ident", "onescol", "onesrow", "sel2"):
        base["cin_" + k] = cnp[k]
    in_maps = []
    for c in range(NCORES):
        m = dict(base)
        m["x_in"] = np.ascontiguousarray(x[c * BPC:(c + 1) * BPC])
        m["w_up_sl"] = np.ascontiguousarray(wu[:, c * 128:(c + 1) * 128])
        m["w_gate_sl"] = np.ascontiguousarray(wg[:, c * 128:(c + 1) * 128])
        m["w_down_sl"] = np.ascontiguousarray(wdn[:, c * 384:(c + 1) * 384])
        in_maps.append(m)
    return in_maps


FUSED = True


def kernel(**inputs):
    if not FUSED:
        x = np.ascontiguousarray(inputs["L0_attn_x"], dtype=np.float32)
        x = run_block0(inputs, x)
        x = run_block1(inputs, x)
        x = run_block2(inputs, x)
        x = run_block3(inputs, x)
        return x
    nc, cnp = build_fused_nc()
    in_maps = fused_inmaps(inputs, cnp)
    res = run_bass_kernel_spmd(nc, in_maps, core_ids=list(range(NCORES)))
    return np.concatenate([r["x_out"].reshape(BPC, SEQ, D) for r in res.results], axis=0)


# version 18
# speedup vs baseline: 1.0135x; 1.0135x over previous
import numpy as np
import ml_dtypes

import concourse.bass as bass
import concourse.mybir as mybir
from concourse.bass_utils import run_bass_kernel_spmd
from concourse.aot_env import allow_ici_all_to_all

F32 = mybir.dt.float32
BF16 = mybir.dt.bfloat16
I32 = mybir.dt.int32
AF = mybir.ActivationFunctionType
ALU = mybir.AluOpType
AX = mybir.AxisListType
BF16NP = ml_dtypes.bfloat16

NCORES = 8
D = 1024
SEQ = 2048
BPC = 2
NT = SEQ // 128

ENGINES = ("sync", "scalar", "vector", "tensor", "gpsimd")


class Tok:
    __slots__ = ("sem", "val")

    def __init__(self, sem, val):
        self.sem = sem
        self.val = val


class Buf:
    __slots__ = ("ap", "wtok", "rtoks", "name")

    def __init__(self, ap, name=""):
        self.ap = ap
        self.wtok = None
        self.rtoks = []
        self.name = name


class Prog:
    def __init__(self, nc):
        self.nc = nc
        self.q = {e: [] for e in ENGINES}
        self.cnt = {}
        self.lastw = {e: {} for e in ENGINES}
        self.semnames = []
        for e in ENGINES:
            for pfx in ("c_", "d_"):
                self.cnt[pfx + e] = 0
                self.semnames.append(pfx + e)
        self.cnt["cc"] = 0
        self.semnames.append("cc")
        NSLOT = 28
        self.nslot = NSLOT
        for i in range(NSLOT):
            self.cnt[f"dq{i}"] = 0
            self.semnames.append(f"dq{i}")
        self.slot_pool = {"sync": list(range(0, 14)), "scalar": list(range(14, 20)),
                          "gpsimd": list(range(20, 26)), "vector": list(range(26, 28)),
                          "tensor": list(range(26, 28))}
        self.slot_rr = {e: 0 for e in ENGINES}
        self.sems = None
        self.ckpts = {e: [] for e in ENGINES}

    def _ckpt_for(self, q, val):
        for total, cval in self.ckpts[q]:
            if total >= val:
                return Tok("c_" + q, cval)
        total = self.cnt["d_" + q]
        assert total >= val
        self.cnt["c_" + q] += 1
        cval = self.cnt["c_" + q]

        def closure(eng, _tot=total, _q=q, _cv=cval):
            eng.wait_ge(self.sems["d_" + _q], _tot)
            eng.nop().then_inc(self.sems["c_" + _q], 1)
            eng.wait_ge(self.sems["c_" + _q], _cv)

        self.q[q].append(closure)
        self.ckpts[q].append((total, cval))
        return Tok("c_" + q, cval)

    def _collect(self, e, sem, tokval, reads, writes):
        deps = []
        for r in reads:
            if isinstance(r, Buf):
                if r.wtok is not None:
                    deps.append(r.wtok)
            elif r is not None:
                deps.append(r)
        for w in writes:
            if w.wtok is not None:
                deps.append(w.wtok)
            deps.extend(w.rtoks)
        deps2 = []
        for t in deps:
            if t.sem.startswith("d_"):
                q = t.sem[2:]
                if e == q:
                    deps2.append(Tok(t.sem, self.cnt["d_" + q]))
                else:
                    deps2.append(self._ckpt_for(q, t.val))
            else:
                deps2.append(t)
        deps = deps2
        lw = self.lastw[e]
        need = {}
        for t in deps:
            if t.sem == sem and t.val >= tokval:
                continue
            if t.val > max(lw.get(t.sem, 0), need.get(t.sem, 0)):
                need[t.sem] = t.val
        for s, v in need.items():
            lw[s] = v
        return need

    def _finish(self, tok, reads, writes):
        for r in reads:
            if isinstance(r, Buf):
                r.rtoks.append(tok)
        for w in writes:
            w.wtok = tok
            w.rtoks = []

    def op(self, e, fn, reads=(), writes=(), dma=False, cc=False):
        writes = [w for w in writes if w is not None]
        sem = "cc" if cc else (("d_" if dma else "c_") + e)
        amt = 16 if dma else 1
        self.cnt[sem] += amt
        tok = Tok(sem, self.cnt[sem])
        need = self._collect(e, sem, tok.val, reads, writes)

        selfwait = (e == "gpsimd" and not dma and not cc)

        def closure(eng, _fn=fn, _need=need, _sem=sem, _amt=amt, _sw=selfwait,
                    _val=tok.val):
            for s, v in _need.items():
                eng.wait_ge(self.sems[s], v)
            _fn(eng).then_inc(self.sems[_sem], _amt)
            if _sw:
                eng.wait_ge(self.sems[_sem], _val)

        self.q[e].append(closure)
        self._finish(tok, reads, writes)
        return tok

    def mm_group(self, calls, reads=(), writes=()):
        writes = [w for w in writes if w is not None]
        sem = "c_tensor"
        self.cnt[sem] += 1
        tok = Tok(sem, self.cnt[sem])
        need = self._collect("tensor", sem, tok.val, reads, writes)

        def closure(eng, _calls=list(calls), _need=need):
            for s, v in _need.items():
                eng.wait_ge(self.sems[s], v)
            inst = None
            for fn in _calls:
                inst = fn(eng)
            inst.then_inc(self.sems["c_tensor"], 1)

        self.q["tensor"].append(closure)
        self._finish(tok, reads, writes)
        return tok

    def dma(self, e, out, in_, reads=(), writes=()):
        return self.dma_group(e, [(out, in_)], reads=reads, writes=writes)

    def dma_group(self, e, pairs, reads=(), writes=()):
        writes = [w for w in writes if w is not None]
        pool = self.slot_pool[e]
        slot = pool[self.slot_rr[e] % len(pool)]
        self.slot_rr[e] += 1
        sem = f"dq{slot}"
        prev_end = self.cnt[sem]
        self.cnt[sem] += 16 * len(pairs)
        tok = Tok(sem, self.cnt[sem])
        need = self._collect(e, sem, tok.val, reads, writes)
        if prev_end > self.lastw[e].get(sem, 0):
            need = dict(need)
            need[sem] = max(need.get(sem, 0), prev_end)
            self.lastw[e][sem] = need[sem]

        def closure(eng, _pairs=list(pairs), _need=need, _sem=sem):
            for sn, v in _need.items():
                eng.wait_ge(self.sems[sn], v)
            for o, i in _pairs:
                eng.dma_start(out=o, in_=i).then_inc(self.sems[_sem], 16)

        self.q[e].append(closure)
        self._finish(tok, reads, writes)
        return tok

    def dma_multi(self, e, pairs, writebuf, reads=()):
        return self.dma_group(e, pairs, reads=reads, writes=(writebuf,))

    def barrier(self):
        for q in ENGINES:
            if self.cnt["d_" + q] > 0:
                self._ckpt_for(q, self.cnt["d_" + q])
        toks = [Tok(s, self.cnt[s]) for s in self.semnames
                if self.cnt[s] > 0 and not s.startswith("d_")]
        for e in ENGINES:
            lw = self.lastw[e]
            need = {}
            for t in toks:
                if t.sem == "c_" + e:
                    continue
                if t.val > lw.get(t.sem, 0):
                    need[t.sem] = t.val
            for s, v in need.items():
                lw[s] = v
            if need:
                def closure(eng, _need=need):
                    for s, v in _need.items():
                        eng.wait_ge(self.sems[s], v)
                self.q[e].append(closure)

    def tok(self, e):
        return Tok("c_" + e, self.cnt["c_" + e])

    def emit(self, stack):
        self.sems = {s: stack.enter_context(self.nc.semaphore(s)) for s in self.semnames}
        stack.enter_context(allow_ici_all_to_all())
        block = stack.enter_context(self.nc.Block())
        for e in ENGINES:
            items = self.q[e]
            if items:
                getattr(block, e)(lambda eng, _items=items: [f(eng) for f in _items])


def mk_sb(stack, nc, name, shape, dtype=F32):
    return Buf(stack.enter_context(nc.sbuf_tensor(name, shape, dtype))[:], name)


def mk_ps(stack, nc, name, shape, dtype=F32):
    return Buf(stack.enter_context(nc.psum_tensor(name, shape, dtype))[:], name)



class Consts:

    def __init__(self, p, stack, capd):
        nc = p.nc
        self.ident = mk_sb(stack, nc, "c_ident", [128, 128], BF16)
        self.onescol = mk_sb(stack, nc, "c_onescol", [128, 1], BF16)
        self.onesrow = mk_sb(stack, nc, "c_onesrow", [1, 128], BF16)
        self.onesrow33 = mk_sb(stack, nc, "c_onesrow33", [33, 128], BF16)
        self.sel2 = mk_sb(stack, nc, "c_sel2", [2, 128], BF16)
        self.cols = mk_sb(stack, nc, "c_cols", [128, 8])
        p.dma("sync", self.ident.ap, capd["ident"], writes=(self.ident,))
        p.dma("sync", self.onescol.ap, capd["onescol"], writes=(self.onescol,))
        p.dma("sync", self.onesrow.ap, capd["onesrow"], writes=(self.onesrow,))
        p.dma("sync", self.onesrow33.ap[32:33, :], capd["onesrow"], writes=(self.onesrow33,))
        p.op("vector", lambda vv: vv.memset(self.onesrow33.ap[0:32, :], 0.0), writes=(self.onesrow33,))
        p.dma("sync", self.sel2.ap, capd["sel2"], writes=(self.sel2,))
        vals = [1e-5, -0.5, -1.0, 1e-6, 1e-24, -100.0, 0.5, 1.0]
        for i, v in enumerate(vals):
            p.op("vector", lambda vv, _i=i, _v=v: vv.memset(self.cols.ap[:, _i:_i + 1], _v),
                 writes=(self.cols,))

    def col(self, i, n=128):
        return self.cols.ap[0:n, i:i + 1]


def ln_tile(p, C, xbuf, xap, mr, sq_scr, ncol=D, eps=1e-5):
    mrap = mr.ap
    p.op("vector", lambda v, _x=xap: v.tensor_reduce(mrap[:, 2:3], _x, AX.X, ALU.add),
         reads=(xbuf,), writes=(mr,))
    p.op("scalar", lambda s, _x=xap: s.activation(sq_scr.ap[:, 0:ncol], _x, AF.Square,
                                                  accum_out=mrap[:, 3:4]),
         reads=(xbuf,), writes=(sq_scr, mr))
    inv = 1.0 / ncol
    p.op("vector", lambda v: v.tensor_scalar(mrap[:, 1:2], mrap[:, 2:3], inv, None, ALU.mult),
         reads=(mr,), writes=(mr,))
    p.op("vector", lambda v: v.tensor_scalar(mrap[:, 3:4], mrap[:, 3:4], inv, None, ALU.mult),
         reads=(mr,), writes=(mr,))
    p.op("vector", lambda v: v.tensor_tensor(mrap[:, 2:3], mrap[:, 1:2], mrap[:, 1:2], ALU.mult),
         reads=(mr,), writes=(mr,))
    p.op("vector", lambda v: v.tensor_tensor(mrap[:, 3:4], mrap[:, 3:4], mrap[:, 2:3], ALU.subtract),
         reads=(mr,), writes=(mr,))
    epscol = C.col(0) if eps == 1e-5 else None
    assert epscol is not None
    p.op("scalar", lambda s: s.activation(mrap[:, 3:4], mrap[:, 3:4], AF.Ln, bias=epscol),
         reads=(mr, C.cols), writes=(mr,))
    p.op("scalar", lambda s: s.activation(mrap[:, 0:1], mrap[:, 3:4], AF.Exp, scale=C.col(1)),
         reads=(mr, C.cols), writes=(mr,))


def _make_strided_mid(d0, stride, cnt):
    return _ap_mid(d0, stride, cnt)


def _ap_mid(d0, stride, cnt):
    import bass_rust
    ap = d0.unsqueeze(1)
    lst = [list(x) for x in ap.ap]
    lst[1] = [stride, cnt]
    new = bass_rust.AP(tensor=ap.tensor, offset=ap.offset, ap=lst)
    return new


def transpose4(p, C, src_buf, src_ap_fn, ps_tr, dst_buf, dst_ap_fn, n=8):
    src_aps = [src_ap_fn(i) for i in range(n)]
    dst_aps = [dst_ap_fn(i) for i in range(n)]
    for g in range((n + 3) // 4):
        cnt = min(4, n - g * 4)
        for u in range(cnt):
            dt_ = g * 4 + u
            p.op("tensor", lambda tn, _s=src_aps[dt_], _u=u: tn.transpose(
                ps_tr.ap[:, _u * 128:(_u + 1) * 128], _s, C.ident.ap),
                reads=(src_buf, C.ident), writes=(ps_tr,))
        offs = [d.offset for d in dst_aps[g * 4:g * 4 + cnt]]
        uniform = (cnt > 1 and len(set(offs[i + 1] - offs[i] for i in range(cnt - 1))) == 1)
        if uniform:
            stride = offs[1] - offs[0]
            dv = _make_strided_mid(dst_aps[g * 4], stride, cnt)
            p.op("vector", lambda v, _d=dv, _c=cnt: v.tensor_copy(
                _d, ps_tr.ap[:, 0:_c * 128].rearrange("p (c f) -> p c f", f=128)),
                reads=(ps_tr,), writes=(dst_buf,))
        else:
            for u in range(cnt):
                dt_ = g * 4 + u
                p.op("vector", lambda v, _d=dst_aps[dt_], _u=u: v.tensor_copy(
                    _d, ps_tr.ap[:, _u * 128:(_u + 1) * 128]),
                    reads=(ps_tr,), writes=(dst_buf,))



def build_constants():
    c = {}
    c["ident"] = np.eye(128, dtype=BF16NP)
    c["onescol"] = np.ones((128, 1), dtype=BF16NP)
    c["onesrow"] = np.ones((1, 128), dtype=BF16NP)
    sel2 = np.zeros((2, 128), dtype=BF16NP)
    sel2[0, 0:64] = 1
    sel2[1, 64:128] = 1
    c["sel2"] = sel2
    ROT = 32
    inv_freq = 1.0 / 10000.0 ** (np.arange(0, ROT, 2, dtype=np.float64) / ROT)
    freqs = np.arange(SEQ, dtype=np.float64)[:, None] * inv_freq[None, :]
    c["rope_cos"] = np.cos(freqs).astype(np.float32)
    c["rope_sin"] = np.sin(freqs).astype(np.float32)
    return c


HEADPERM = []
for _m in range(4):
    HEADPERM += [4 * _m, 4 * _m + 2, 4 * _m + 1, 4 * _m + 3]



B0_PHASES = 4
B0_ATTN_NORM = True


def emit_block0(p, stack, C, x_in, x_out, W):
    nc = p.nc
    sb = lambda n, s, d=F32: mk_sb(stack, nc, n, s, d)
    ps = lambda n, s, d=F32: mk_ps(stack, nc, n, s, d)
    HEADS, KVH = 16, 8

    ropec = sb("b0_ropec", [128, NT * 16])
    ropes = sb("b0_ropes", [128, NT * 16])
    p.dma("sync", ropec.ap.rearrange("p (t f) -> p t f", f=16),
          W["rope_cos"].rearrange("(t p) f -> p t f", p=128), writes=(ropec,))
    p.dma("sync", ropes.ap.rearrange("p (t f) -> p t f", f=16),
          W["rope_sin"].rearrange("(t p) f -> p t f", p=128), writes=(ropes,))

    wq = sb("b0_wq", [128, 8 * D], BF16)
    wk = sb("b0_wk", [128, 8 * 512], BF16)
    wv = sb("b0_wv", [128, 8 * 512], BF16)
    wo = sb("b0_wo", [128, 8 * D], BF16)
    bo = sb("b0_bo", [1, D], BF16)
    wstage = sb("b0_wstage", [128, 512])

    def load_weight(dst, dram, cols, fold):
        for t in range(8):
            for c0 in range(0, cols, 512):
                w = min(512, cols - c0)
                src = dram[t * 128:(t + 1) * 128, c0:c0 + w]
                p.dma("sync", wstage.ap[:, 0:w], src, writes=(wstage,))
                p.op("vector", lambda v, _t=t, _c0=c0, _w=w: v.tensor_scalar(
                    dst.ap[:, _t * cols + _c0:_t * cols + _c0 + _w], wstage.ap[:, 0:_w],
                    fold, None, ALU.mult),
                    reads=(wstage,), writes=(dst,))

    load_weight(wq, W["Wq"], D, 0.125)
    load_weight(wk, W["Wk"], 512, 1.0)
    load_weight(wv, W["Wv"], 512, 1.0)
    load_weight(wo, W["Wo"], D, 1.0)
    for c0 in (0, 512):
        p.dma("sync", wstage.ap[0:1, 0:512], W["bo"][:, c0:c0 + 512], writes=(wstage,))
        p.op("vector", lambda v, _c0=c0: v.tensor_scalar(bo.ap[:, _c0:_c0 + 512],
                                                         wstage.ap[0:1, 0:512], 1.0, None, ALU.mult),
             reads=(wstage,), writes=(bo,))

    xtP = [sb("b0_xt0", [128, D]), sb("b0_xt1", [128, D])]
    sq_scr = sb("b0_sq", [128, D], BF16)
    mrP = [sb("b0_mr0", [128, 4]), sb("b0_mr1", [128, 4])]
    lnbfP = [sb("b0_lnbf0", [128, D], BF16), sb("b0_lnbf1", [128, D], BF16)]
    lnxT = sb("b0_lnxT", [128, 8 * SEQ], BF16)
    qnatP = [sb("b0_qnat0", [128, D]), sb("b0_qnat1", [128, D])]
    knatP = [sb("b0_knat0", [128, 512]), sb("b0_knat1", [128, 512])]
    qbfP = [sb("b0_qbf0", [128, D], BF16), sb("b0_qbf1", [128, D], BF16)]
    kbfP = [sb("b0_kbf0", [128, 512], BF16), sb("b0_kbf1", [128, 512], BF16)]
    qT = sb("b0_qT", [128, 8 * SEQ], BF16)
    kT = sb("b0_kT", [128, 4 * SEQ], BF16)
    vnat = sb("b0_vnat", [128, NT * 512], BF16)
    attnT = lnxT
    expab = [sb("b0_expab0", [128, 1024], BF16), sb("b0_expab1", [128, 1024], BF16)]
    acc_s = sb("b0_accs", [128, 1024])
    acc_p = sb("b0_accp", [128, 1024], BF16)
    lnrec = sb("b0_lnrec", [33, 512])
    recipb = sb("b0_recipb", [33, 512], BF16)
    rsb = sb("b0_rsb", [128, 512])
    onatP = [sb("b0_onat0", [128, D]), sb("b0_onat1", [128, D])]

    P0 = ps("b0_P0", [128, 1024])
    P1 = ps("b0_P1", [128, 1024])
    ps_mm = [P0, P0]
    ps_tr = ps("b0_pstr", [128, 512], BF16)
    ps_l = [P0, P1]
    ps_av = ps("b0_psav", [128, 512])
    ps_s = ps("b0_pss", [33, 512])
    ps_r = ps("b0_psr", [128, 512])
    p.op("vector", lambda v: v.memset(ps_s.ap, 1.0), writes=(ps_s,))

    for b in range(BPC):
        xin_b = x_in[b]
        xout_b = x_out[b]
        for t in range(NT):
            xt, mr, lnbf = xtP[t % 2], mrP[t % 2], lnbfP[t % 2]
            p.dma("sync", xt.ap, xin_b[t * 128:(t + 1) * 128, :], writes=(xt,))
            ln_tile(p, C, xt, xt.ap, mr, sq_scr)
            p.op("vector", lambda v, _x=xt, _m=mr, _l=lnbf: v.tensor_scalar(
                _l.ap, _x.ap, _m.ap[:, 1:2], _m.ap[:, 0:1], ALU.subtract, ALU.mult),
                 reads=(xt, mr), writes=(lnbf,))
            transpose4(p, C, lnbf, lambda dt_, _l=lnbf: _l.ap[:, dt_ * 128:(dt_ + 1) * 128], ps_tr,
                       lnxT, lambda dt_, _t=t: lnxT.ap[:, dt_ * SEQ + _t * 128:dt_ * SEQ + (_t + 1) * 128], 8)

        def _qkv_proj(t):
            knat = knatP[t % 2]
            qnat = qnatP[t % 2]
            lh = [lnxT.ap[:, kt * SEQ + t * 128:kt * SEQ + (t + 1) * 128] for kt in range(8)]
            p.mm_group([(lambda tn, _l=lh[kt], _r=wk.ap[:, kt * 512:(kt + 1) * 512],
                         _s=(kt == 0), _e=(kt == 7):
                         tn.matmul(P0.ap[:, 0:512], _l, _r, start=_s, stop=_e)) for kt in range(8)],
                       reads=(lnxT, wk), writes=(P0,))
            p.op("vector", lambda v, _k=knat: v.tensor_copy(_k.ap, P0.ap[:, 0:512]),
                 reads=(P0,), writes=(knat,))
            p.mm_group([(lambda tn, _l=lh[kt], _r=wv.ap[:, kt * 512:(kt + 1) * 512],
                         _s=(kt == 0), _e=(kt == 7):
                         tn.matmul(P0.ap[:, 512:1024], _l, _r, start=_s, stop=_e)) for kt in range(8)],
                       reads=(lnxT, wv), writes=(P0,))
            p.op("vector", lambda v, _t=t: v.tensor_copy(vnat.ap[:, _t * 512:(_t + 1) * 512],
                                                         P0.ap[:, 512:1024]),
                 reads=(P0,), writes=(vnat,))
            for ch in range(2):
                dstp = P1.ap[:, ch * 512:(ch + 1) * 512]
                p.mm_group([(lambda tn, _l=lh[kt], _d=dstp,
                             _r=wq.ap[:, kt * D + ch * 512:kt * D + (ch + 1) * 512],
                             _s=(kt == 0), _e=(kt == 7):
                             tn.matmul(_d, _l, _r, start=_s, stop=_e)) for kt in range(8)],
                           reads=(lnxT, wq), writes=(P1,))
                p.op("vector", lambda v, _ch=ch, _d=dstp, _q=qnat: v.tensor_copy(
                    _q.ap[:, _ch * 512:(_ch + 1) * 512], _d),
                     reads=(P1,), writes=(qnat,))

        def _qkv_post(t):
            knat, kbf = knatP[t % 2], kbfP[t % 2]
            qnat, qbf = qnatP[t % 2], qbfP[t % 2]
            rope_apply(p, C, knat, kbf, ropec, ropes, t, KVH)
            transpose4(p, C, kbf, lambda dt_, _k=kbf: _k.ap[:, dt_ * 128:(dt_ + 1) * 128], ps_tr,
                       kT, lambda dt_, _t=t: kT.ap[:, dt_ * SEQ + _t * 128:dt_ * SEQ + (_t + 1) * 128], 4)
            rope_apply(p, C, qnat, qbf, ropec, ropes, t, HEADS)
            transpose4(p, C, qbf, lambda dt_, _q=qbf: _q.ap[:, dt_ * 128:(dt_ + 1) * 128], ps_tr,
                       qT, lambda dt_, _t=t: qT.ap[:, dt_ * SEQ + _t * 128:dt_ * SEQ + (_t + 1) * 128], 8)

        for t in range(NT if B0_PHASES >= 2 else 0):
            _qkv_proj(t)
            if t > 0:
                _qkv_post(t - 1)
        if B0_PHASES >= 2:
            _qkv_post(NT - 1)

        for m in range(4 if B0_PHASES >= 3 else 0):
            for jj in range(2):
                tq = 2 * m + jj
                for ch in range(4):
                    for kt in range(NT):
                        pl = ps_l[kt % 2]
                        exb = expab[kt % 2]
                        la = kT.ap[0:64, m * SEQ + kt * 128:m * SEQ + (kt + 1) * 128]
                        lb = kT.ap[64:128, m * SEQ + kt * 128:m * SEQ + (kt + 1) * 128]
                        ra = qT.ap[0:64, tq * SEQ + ch * 512:tq * SEQ + (ch + 1) * 512]
                        rb = qT.ap[64:128, tq * SEQ + ch * 512:tq * SEQ + (ch + 1) * 512]
                        p.mm_group(
                            [lambda tn, _la=la, _ra=ra, _pl=pl: tn.matmul(
                                _pl.ap[:, 0:512], _la, _ra, start=True, stop=True,
                                tile_position=(0, 0)),
                             lambda tn, _lb=lb, _rb=rb, _pl=pl: tn.matmul(
                                _pl.ap[:, 512:1024], _lb, _rb, start=True, stop=True,
                                tile_position=(64, 0))],
                            reads=(kT, qT), writes=(pl,))
                        p.op("scalar", lambda s, _pl=pl, _e=exb: s.activation(_e.ap, _pl.ap, AF.Exp),
                             reads=(pl,), writes=(exb,))
                        st_, sp_ = (kt == 0), (kt == NT - 1)
                        va = vnat.ap[:, kt * 512 + (2 * m) * 64:kt * 512 + (2 * m + 1) * 64]
                        vb = vnat.ap[:, kt * 512 + (2 * m + 1) * 64:kt * 512 + (2 * m + 2) * 64]
                        p.mm_group(
                            [lambda tn, _va=va, _s=st_, _e=sp_, _x=exb: tn.matmul(
                                ps_av.ap[0:64, :], _va, _x.ap[:, 0:512], start=_s, stop=_e,
                                tile_position=(0, 0)),
                             lambda tn, _vb=vb, _s=st_, _e=sp_, _x=exb: tn.matmul(
                                ps_av.ap[64:128, :], _vb, _x.ap[:, 512:1024], start=_s, stop=_e,
                                skip_group_check=True, tile_position=(0, 64))],
                            reads=(vnat, exb), writes=(ps_av,))
                        eng_, acc_ = ("vector", acc_s) if kt % 2 == 0 else ("gpsimd", acc_p)
                        if kt < 2:
                            p.op(eng_, lambda e, _x=exb, _a=acc_: e.tensor_copy(_a.ap, _x.ap),
                                 reads=(exb,), writes=(acc_,))
                        else:
                            p.op(eng_, lambda e, _x=exb, _a=acc_: e.tensor_tensor(
                                _a.ap, _a.ap, _x.ap, ALU.add),
                                reads=(exb,), writes=(acc_,))
                    if not B0_ATTN_NORM:
                        continue
                    p.op("vector", lambda v: v.tensor_tensor(sq_scr.ap, acc_s.ap, acc_p.ap,
                                                             ALU.add),
                         reads=(acc_s, acc_p), writes=(sq_scr,))
                    p.mm_group(
                        [lambda tn: tn.matmul(ps_s.ap[0:1, :], C.onescol.ap, sq_scr.ap[:, 0:512],
                                              start=True, stop=True, tile_position=(0, 0)),
                         lambda tn: tn.matmul(ps_s.ap[32:33, :], C.onescol.ap, sq_scr.ap[:, 512:1024],
                                              start=True, stop=True, skip_group_check=True,
                                              tile_position=(0, 32))],
                        reads=(sq_scr, C.onescol), writes=(ps_s,))
                    p.op("scalar", lambda s: s.activation(lnrec.ap, ps_s.ap, AF.Ln),
                         reads=(ps_s,), writes=(lnrec,))
                    p.op("scalar", lambda s: s.activation(recipb.ap, lnrec.ap, AF.Exp,
                                                          scale=C.col(2, 33)),
                         reads=(lnrec, C.cols), writes=(recipb,))
                    p.op("tensor", lambda tn: tn.matmul(P0.ap[:, 0:512], C.onesrow.ap,
                                                        recipb.ap[0:1, :], start=True, stop=True),
                         reads=(recipb, C.onesrow), writes=(P0,))
                    p.op("tensor", lambda tn: tn.matmul(P0.ap[:, 512:1024], C.onesrow33.ap[32:33, :],
                                                        recipb.ap[32:33, :], start=True, stop=True,
                                                        skip_group_check=True, tile_position=(32, 0)),
                         reads=(recipb, C.onesrow33), writes=(P0,))
                    p.op("vector", lambda v: v.tensor_copy(rsb.ap[0:64, :], P0.ap[0:64, 0:512]),
                         reads=(P0,), writes=(rsb,))
                    p.op("vector", lambda v: v.tensor_copy(rsb.ap[64:128, :], P0.ap[64:128, 512:1024]),
                         reads=(P0,), writes=(rsb,))
                    dst = attnT.ap[:, tq * SEQ + ch * 512:tq * SEQ + (ch + 1) * 512]
                    p.op("vector", lambda v, _d=dst: v.tensor_tensor(_d, ps_av.ap, rsb.ap, ALU.mult),
                         reads=(ps_av, rsb), writes=(attnT,))

        if B0_PHASES < 4:
            for t in range(NT):
                p.dma("sync", xt.ap, xin_b[t * 128:(t + 1) * 128, :], writes=(xt,))
                p.dma("gpsimd", x_out[b][t * 128:(t + 1) * 128, :], xt.ap, reads=(xt,))
            continue
        for t in range(NT):
            onat, mr, xt = onatP[t % 2], mrP[t % 2], xtP[t % 2]
            for ch in range(2):
                dstp = P1.ap[:, ch * 512:(ch + 1) * 512]
                calls = []
                for tau in range(8):
                    lhs = attnT.ap[:, tau * SEQ + t * 128:tau * SEQ + (t + 1) * 128]
                    rhs = wo.ap[:, tau * D + ch * 512:tau * D + (ch + 1) * 512]
                    calls.append(lambda tn, _l=lhs, _r=rhs, _d=dstp, _s=(tau == 0):
                                 tn.matmul(_d, _l, _r, start=_s, stop=False))
                calls.append(lambda tn, _d=dstp, _r=bo.ap[:, ch * 512:(ch + 1) * 512]:
                             tn.matmul(_d, C.onesrow.ap, _r, start=False, stop=True))
                p.mm_group(calls, reads=(attnT, wo, bo, C.onesrow), writes=(P1,))
            for ch in range(2):
                p.op("vector", lambda v, _ch=ch, _o=onat: v.tensor_copy(
                    _o.ap[:, _ch * 512:(_ch + 1) * 512], P1.ap[:, _ch * 512:(_ch + 1) * 512]),
                     reads=(P1,), writes=(onat,))
            ln_tile(p, C, onat, onat.ap, mr, sq_scr)
            p.dma("sync", xt.ap, xin_b[t * 128:(t + 1) * 128, :], writes=(xt,))
            p.op("vector", lambda v, _o=onat, _m=mr: v.tensor_scalar(
                _o.ap, _o.ap, _m.ap[:, 1:2], _m.ap[:, 0:1], ALU.subtract, ALU.mult),
                 reads=(mr,), writes=(onat,))
            p.op("vector", lambda v, _o=onat, _x=xt: v.tensor_tensor(_o.ap, _o.ap, _x.ap, ALU.add),
                 reads=(xt,), writes=(onat,))
            p.dma("gpsimd", xout_b[t * 128:(t + 1) * 128, :], onat.ap, reads=(onat,))


def rope_apply(p, C, nat, bfout, ropec, ropes, t, nheads):
    natv = nat.ap.rearrange("p (h d) -> p h d", d=64)
    bfv = bfout.ap[:, 0:nheads * 64].rearrange("p (h d) -> p h d", d=64)
    cosap = ropec.ap[:, t * 16:(t + 1) * 16].unsqueeze(1).to_broadcast([128, nheads, 16])
    sinap = ropes.ap[:, t * 16:(t + 1) * 16].unsqueeze(1).to_broadcast([128, nheads, 16])
    rotv = natv[:, :, 0:32].rearrange("p h (i two) -> p h i two", two=2)
    bfrot = bfv[:, :, 0:32].rearrange("p h (i two) -> p h i two", two=2)
    ev, ov = rotv[:, :, :, 0], rotv[:, :, :, 1]
    be, bo_ = bfrot[:, :, :, 0], bfrot[:, :, :, 1]
    p.op("vector", lambda v: v.tensor_tensor(be, ev, cosap, ALU.mult),
         reads=(nat, ropec), writes=(bfout,))
    p.op("vector", lambda v: v.tensor_tensor(bo_, ov, sinap, ALU.mult),
         reads=(nat, ropes), writes=(bfout,))
    p.op("vector", lambda v: v.tensor_tensor(be, be, bo_, ALU.subtract),
         reads=(bfout,), writes=(bfout,))
    p.op("vector", lambda v: v.tensor_tensor(bo_, ov, cosap, ALU.mult),
         reads=(nat, ropec), writes=(bfout,))
    p.op("vector", lambda v: v.tensor_tensor(ev, ev, sinap, ALU.mult),
         reads=(ropes,), writes=(nat,))
    p.op("vector", lambda v: v.tensor_tensor(bo_, bo_, ev, ALU.add),
         reads=(nat,), writes=(bfout,))
    p.op("vector", lambda v: v.tensor_copy(bfv[:, :, 32:64], natv[:, :, 32:64]),
         reads=(nat,), writes=(bfout,))



def prep_block0_host(inputs):
    Wq = np.ascontiguousarray(inputs["L0_attn_Wq"])
    Wo = np.ascontiguousarray(inputs["L0_attn_Wo"])
    Wq_p = np.empty_like(Wq)
    Wo_p = np.empty_like(Wo)
    for g, h in enumerate(HEADPERM):
        Wq_p[:, g * 64:(g + 1) * 64] = Wq[:, h * 64:(h + 1) * 64]
        Wo_p[g * 64:(g + 1) * 64, :] = Wo[h * 64:(h + 1) * 64, :]
    return {
        "Wq": Wq_p, "Wk": np.ascontiguousarray(inputs["L0_attn_Wk"]),
        "Wv": np.ascontiguousarray(inputs["L0_attn_Wv"]),
        "Wo": Wo_p, "bo": inputs["L0_attn_bo"].reshape(1, D),
    }


def build_block0_nc():
    from contextlib import ExitStack
    nc = bass.Bass("TRN2", target_bir_lowering=False)
    x_in = nc.dram_tensor("x_in", [BPC, SEQ, D], F32, kind="ExternalInput").ap()
    x_out = nc.dram_tensor("x_out", [BPC, SEQ, D], F32, kind="ExternalOutput").ap()
    wd = {
        "Wq": nc.dram_tensor("Wq", [D, D], F32, kind="ExternalInput").ap(),
        "Wk": nc.dram_tensor("Wk", [D, 512], F32, kind="ExternalInput").ap(),
        "Wv": nc.dram_tensor("Wv", [D, 512], F32, kind="ExternalInput").ap(),
        "Wo": nc.dram_tensor("Wo", [D, D], F32, kind="ExternalInput").ap(),
        "bo": nc.dram_tensor("bo", [1, D], F32, kind="ExternalInput").ap(),
        "rope_cos": nc.dram_tensor("rope_cos", [SEQ, 16], F32, kind="ExternalInput").ap(),
        "rope_sin": nc.dram_tensor("rope_sin", [SEQ, 16], F32, kind="ExternalInput").ap(),
    }
    capd = {}
    cnp = build_constants()
    for k in ("ident", "onescol", "onesrow", "sel2"):
        capd[k] = nc.dram_tensor("cin_" + k, list(cnp[k].shape), BF16, kind="ExternalInput").ap()
    stack = ExitStack()
    p = Prog(nc)
    C = Consts(p, stack, capd)
    emit_block0(p, stack, C, x_in, x_out, wd)
    p.barrier()
    p.emit(stack)
    stack.close()
    return nc, cnp


def run_block0(inputs, x_full):
    nc, cnp = build_block0_nc()
    wh = prep_block0_host(inputs)
    in_maps = []
    for c in range(NCORES):
        m = {"x_in": np.ascontiguousarray(x_full[c * BPC:(c + 1) * BPC])}
        m.update(wh)
        m["rope_cos"] = cnp["rope_cos"]
        m["rope_sin"] = cnp["rope_sin"]
        for k in ("ident", "onescol", "onesrow", "sel2"):
            m["cin_" + k] = cnp[k]
        in_maps.append(m)
    res = run_bass_kernel_spmd(nc, in_maps, core_ids=list(range(NCORES)))
    return np.concatenate([r["x_out"] for r in res.results], axis=0)







def emit_block1(p, stack, C, x_in, x_out, W, dram):
    nc = p.nc
    sb = lambda n, s, d=F32: mk_sb(stack, nc, n, s, d)
    ps = lambda n, s, d=F32: mk_ps(stack, nc, n, s, d)
    ROWS = BPC * SEQ
    NIT = 24
    NRT = ROWS // 128
    xr = x_in.rearrange("b s d -> (b s) d")
    xro = x_out.rearrange("b s d -> (b s) d")

    P0 = ps("b1_P0", [128, 1024])
    P1 = ps("b1_P1", [128, 1024])
    ps_tr = ps("b1_pstr", [128, 512], BF16)
    ps_sc = ps("b1_pssc", [128, 8])

    xt = sb("b1_xt", [128, D])
    sq_scr = sb("b1_sq", [128, D], BF16)
    mr = sb("b1_mr", [128, 4])
    lnbf = sb("b1_lnbf", [128, D], BF16)
    lnT = sb("b1_lnT", [128, 8 * ROWS], BF16)
    amx = sb("b1_amx", [128, 8])
    arow = sb("b1_arow", [1, 128])
    asc = sb("b1_asc", [1, 16])
    ascb = sb("b1_ascb", [1, 16], BF16)
    scc = sb("b1_scc", [128, 12])
    wtile = sb("b1_wtile", [128, 512])
    dqs = sb("b1_dqs", [128, 32])
    dqbf = sb("b1_dqbf", [128, 1024], BF16)
    lhsb = [sb(f"b1_lhs{i}", [128, 8 * 128], BF16) for i in range(2)]
    evacP = [sb("b1_evac", [128, 4096]), sb("b1_evacB", [128, 4096])]
    evac2 = sb("b1_evac2", [128, 4096])
    evac2P = [evac2, evac2]
    evac = evacP[0]
    scrF = sb("b1_scrF", [128, 4096])
    scrG = sb("b1_scrG", [128, 4096])
    scrI = Buf(stack.enter_context(nc.sbuf_tensor("b1_scrI", [128, 4096], I32))[:], "b1_scrI")
    obf = sb("b1_obf", [128, 4096], BF16)
    _osl0 = sb("b1_osl0", [128, 24 * 256], BF16)
    osl = [_osl0, _osl0]
    onat = sb("b1_onat", [128, D])
    dqd = lnT

    dq_loc = dram["dq_loc"]; ag_ug = dram["ag_ug"]
    dq_dn_loc = dram["dq_dn_loc"]; ag_dn = dram["ag_dn"]
    raw_up = dram["raw_up"]; raw_gate = dram["raw_gate"]
    t_sp = dram["t_sp"]; u_sp = dram["u_sp"]; p_sp = dram["p_sp"]; o_sp = dram["o_sp"]
    ccs = [dram[f"cc{i}_in"] for i in range(4)]
    cco = [dram[f"cc{i}_out"] for i in range(4)]

    def amax_finalize(col_i, dst_col):
        p.dma("scalar", arow.ap, amx.ap[:, col_i:col_i + 1], reads=(amx,), writes=(arow,))
        p.op("vector", lambda v, _d=dst_col: v.tensor_reduce(asc.ap[0:1, _d:_d + 1], arow.ap,
                                                             AX.X, ALU.max),
             reads=(arow,), writes=(asc,))

    def allreduce_max(rnd, ncells):
        p.dma("gpsimd", ccs[rnd].ap, asc.ap[0:1, 0:16], reads=(asc,), writes=(ccs[rnd],))
        p.op("gpsimd", lambda g, _r=rnd: g.collective_compute(
            "AllReduce", ALU.max, replica_groups=[list(range(NCORES))],
            ins=[ccs[_r].ap], outs=[cco[_r].ap]),
            reads=(ccs[rnd],), writes=(cco[rnd],), cc=True)
        p.dma("sync", asc.ap[0:1, 0:16], cco[rnd].ap, reads=(cco[rnd],), writes=(asc,))

    def make_scales(src_col, rs_col, s_col, extra=1.0):
        p.op("vector", lambda v: v.tensor_scalar(asc.ap[0:1, 14:15], asc.ap[0:1, src_col:src_col + 1],
                                                 1.0 / 32767.0, 1e-12, ALU.mult, ALU.max),
             reads=(asc,), writes=(asc,))
        p.op("vector", lambda v: v.reciprocal(asc.ap[0:1, 15:16], asc.ap[0:1, 14:15]),
             reads=(asc,), writes=(asc,))
        p.op("vector", lambda v: v.tensor_copy(ascb.ap[0:1, 0:1], asc.ap[0:1, 15:16]),
             reads=(asc,), writes=(ascb,))
        p.mm_group([lambda tn: tn.matmul(ps_sc.ap[:, 0:1], C.onesrow.ap, ascb.ap[0:1, 0:1],
                                         start=True, stop=True)],
                   reads=(ascb, C.onesrow), writes=(ps_sc,))
        p.op("vector", lambda v, _rc=rs_col: v.tensor_copy(scc.ap[:, _rc:_rc + 1], ps_sc.ap[:, 0:1]),
             reads=(ps_sc,), writes=(scc,))
        p.op("vector", lambda v, _rc=rs_col, _sc=s_col: v.reciprocal(scc.ap[:, _sc:_sc + 1],
                                                                     scc.ap[:, _rc:_rc + 1]),
             reads=(scc,), writes=(scc,))
        if extra != 1.0:
            p.op("vector", lambda v, _sc=s_col, _x=extra: v.tensor_scalar(
                scc.ap[:, _sc:_sc + 1], scc.ap[:, _sc:_sc + 1], _x, None, ALU.mult),
                reads=(scc,), writes=(scc,))


    def _deq_tile(ncols):
        nb = ncols // 32
        wv = wtile.ap[:, 0:ncols].rearrange("p (b k) -> p b k", k=32)
        p.op("vector", lambda v: v.tensor_reduce(dqs.ap[:, 0:nb], wv, AX.X, ALU.max,
                                                 apply_absolute_value=True),
             reads=(wtile,), writes=(dqs,))
        p.op("vector", lambda v: v.tensor_scalar(dqs.ap[:, 0:nb], dqs.ap[:, 0:nb],
                                                 1.0 / 7.0, 1e-12, ALU.mult, ALU.max),
             reads=(dqs,), writes=(dqs,))
        p.op("vector", lambda v: v.reciprocal(dqs.ap[:, 16:16 + nb], dqs.ap[:, 0:nb]),
             reads=(dqs,), writes=(dqs,))
        rsb_ = dqs.ap[:, 16:16 + nb].unsqueeze(2).to_broadcast([128, nb, 32])
        sb_ = dqs.ap[:, 0:nb].unsqueeze(2).to_broadcast([128, nb, 32])
        iv = scrI.ap[:, 0:ncols]
        p.op("vector", lambda v: v.tensor_tensor(iv.rearrange("p (b k) -> p b k", k=32), wv, rsb_,
                                                 ALU.mult),
             reads=(wtile, dqs), writes=(scrI,))
        p.op("vector", lambda v: v.tensor_tensor(dqbf.ap[:, 0:ncols].rearrange("p (b k) -> p b k", k=32),
                                                 iv.rearrange("p (b k) -> p b k", k=32), sb_, ALU.mult),
             reads=(scrI, dqs), writes=(dqbf,))

    for wname, locrow in (("w_up_sl", 0), ("w_gate_sl", 1)):
        for it in range(NIT):
            src = W[wname][it * 128:(it + 1) * 128, :]
            p.dma("sync", wtile.ap[:, 0:128], src, writes=(wtile,))
            _deq_tile(128)
            p.op("tensor", lambda tn: tn.transpose(ps_tr.ap[:, 0:128], dqbf.ap[:, 0:128], C.ident.ap),
                 reads=(dqbf, C.ident), writes=(ps_tr,))
            p.op("vector", lambda v: v.tensor_copy(dqbf.ap[:, 512:640], ps_tr.ap[:, 0:128]),
                 reads=(ps_tr,), writes=(dqbf,))
            p.dma("gpsimd", dq_loc.ap[locrow * 128:(locrow + 1) * 128, it * 128:(it + 1) * 128],
                  dqbf.ap[:, 512:640], reads=(dqbf,), writes=(dq_loc,))
    for dt in range(8):
        src = W["w_down_sl"][dt * 128:(dt + 1) * 128, :]
        p.dma("sync", wtile.ap[:, 0:384], src, writes=(wtile,))
        _deq_tile(384)
        for u in range(3):
            p.op("tensor", lambda tn, _u=u: tn.transpose(ps_tr.ap[:, _u * 128:(_u + 1) * 128],
                                                         dqbf.ap[:, _u * 128:(_u + 1) * 128], C.ident.ap),
                 reads=(dqbf, C.ident), writes=(ps_tr,))
        for u in range(3):
            p.op("vector", lambda v, _u=u: v.tensor_copy(dqbf.ap[:, 512 + _u * 128:512 + (_u + 1) * 128],
                                                         ps_tr.ap[:, _u * 128:(_u + 1) * 128]),
                 reads=(ps_tr,), writes=(dqbf,))
        for u in range(3):
            p.dma("gpsimd", dq_dn_loc.ap[u * 128:(u + 1) * 128, dt * 128:(dt + 1) * 128],
                  dqbf.ap[:, 512 + u * 128:512 + (u + 1) * 128], reads=(dqbf,), writes=(dq_dn_loc,))
    p.op("gpsimd", lambda g: g.collective_compute(
        "AllGather", ALU.bypass, replica_groups=[list(range(NCORES))],
        ins=[dq_loc.ap], outs=[ag_ug.ap]),
        reads=(dq_loc,), writes=(ag_ug,), cc=True)
    p.op("gpsimd", lambda g: g.collective_compute(
        "AllGather", ALU.bypass, replica_groups=[list(range(NCORES))],
        ins=[dq_dn_loc.ap], outs=[ag_dn.ap]),
        reads=(dq_dn_loc,), writes=(ag_dn,), cc=True)

    p.op("vector", lambda v: v.memset(amx.ap, 0.0), writes=(amx,))
    p.op("vector", lambda v: v.memset(asc.ap, 0.0), writes=(asc,))
    for r in range(NRT):
        p.dma("sync", xt.ap, xr[r * 128:(r + 1) * 128, :], writes=(xt,))
        ln_tile(p, C, xt, xt.ap, mr, sq_scr)
        p.op("vector", lambda v: v.tensor_scalar(lnbf.ap, xt.ap, mr.ap[:, 1:2], mr.ap[:, 0:1],
                                                 ALU.subtract, ALU.mult),
             reads=(xt, mr), writes=(lnbf,))
        p.op("vector", lambda v: v.tensor_reduce(amx.ap[:, 1:2], lnbf.ap, AX.X, ALU.max,
                                                 apply_absolute_value=True),
             reads=(lnbf,), writes=(amx,))
        p.op("vector", lambda v: v.tensor_tensor(amx.ap[:, 0:1], amx.ap[:, 0:1], amx.ap[:, 1:2],
                                                 ALU.max),
             reads=(amx,), writes=(amx,))
        transpose4(p, C, lnbf, lambda dt_: lnbf.ap[:, dt_ * 128:(dt_ + 1) * 128], ps_tr,
                   lnT, lambda dt_, _r=r: lnT.ap[:, dt_ * ROWS + _r * 128:dt_ * ROWS + (_r + 1) * 128], 8)
    amax_finalize(0, 0)
    allreduce_max(0, 1)
    make_scales(0, 0, 1)

    for dt in range(8):
        seg = lnT.ap[:, dt * ROWS:(dt + 1) * ROWS]
        p.op("vector", lambda v, _s=seg: v.tensor_scalar(scrI.ap, _s, scc.ap[:, 0:1], None, ALU.mult),
             reads=(lnT, scc), writes=(scrI,))
        p.op("vector", lambda v, _s=seg: v.tensor_scalar(_s, scrI.ap, scc.ap[:, 1:2], None, ALU.mult),
             reads=(scrI, scc), writes=(lnT,))

    def _phaseB_it(wi, raw, acol, it, lb, evac):
        p.dma_multi("sync",
                    [(lb.ap[:, dt * 128:(dt + 1) * 128],
                      ag_ug.ap[256 * dt + wi * 128:256 * dt + wi * 128 + 128,
                               it * 128:(it + 1) * 128]) for dt in range(8)],
                    lb, reads=(ag_ug,))
        for rc in range(8):
            slot = rc % 4
            pb = (P0, P0, P1, P1)[slot]
            half = (0, 512, 0, 512)[slot]
            dst = pb.ap[:, half:half + 512]
            p.mm_group([(lambda tn, _l=lb.ap[:, dt2 * 128:(dt2 + 1) * 128],
                         _r=lnT.ap[:, dt2 * ROWS + rc * 512:dt2 * ROWS + (rc + 1) * 512],
                         _d=dst, _s=(dt2 == 0), _e=(dt2 == 7):
                         tn.matmul(_d, _l, _r, start=_s, stop=_e)) for dt2 in range(8)],
                       reads=(lb, lnT), writes=(pb,))
            p.op("vector", lambda v, _d=dst, _rc=rc, _e=evac: v.tensor_copy(
                _e.ap[:, _rc * 512:(_rc + 1) * 512], _d),
                reads=(pb,), writes=(evac,))
        p.op("vector", lambda v, _e=evac: v.tensor_reduce(amx.ap[:, 3:4], _e.ap, AX.X, ALU.max,
                                                          apply_absolute_value=True),
             reads=(evac,), writes=(amx,))
        p.op("vector", lambda v, _a=acol: v.tensor_tensor(amx.ap[:, _a:_a + 1],
                                                          amx.ap[:, _a:_a + 1],
                                                          amx.ap[:, 3:4], ALU.max),
             reads=(amx,), writes=(amx,))
        p.dma("gpsimd", raw.ap[it * 128:(it + 1) * 128, :], evac.ap, reads=(evac,), writes=(raw,))

    for wi, (raw, acol) in enumerate(((raw_up, 1), (raw_gate, 2))):
        for it in range(NIT):
            _phaseB_it(wi, raw, acol, it, lhsb[it % 2], evacP[it % 2])
    amax_finalize(1, 0)
    amax_finalize(2, 1)
    allreduce_max(1, 2)
    make_scales(0, 2, 3)
    make_scales(1, 4, 5)

    def _phaseC_it(it, evac, evac2):
        p.dma("sync", evac.ap, raw_gate.ap[it * 128:(it + 1) * 128, :], writes=(evac,))
        p.dma("sync", evac2.ap, raw_up.ap[it * 128:(it + 1) * 128, :], writes=(evac2,))
        p.op("vector", lambda v, _e=evac: v.tensor_scalar(scrI.ap, _e.ap, scc.ap[:, 4:5], None,
                                                          ALU.mult),
             reads=(evac, scc), writes=(scrI,))
        p.op("vector", lambda v: v.tensor_scalar(scrF.ap, scrI.ap, scc.ap[:, 5:6], None, ALU.mult),
             reads=(scrI, scc), writes=(scrF,))
        p.op("scalar", lambda s: s.activation(scrG.ap, scrF.ap, AF.Sigmoid),
             reads=(scrF,), writes=(scrG,))
        p.op("vector", lambda v: v.tensor_scalar(scrI.ap, scrG.ap, 65536.0, None, ALU.mult),
             reads=(scrG,), writes=(scrI,))
        p.op("vector", lambda v: v.tensor_scalar(scrG.ap, scrI.ap, 1.0, 65535.0, ALU.mult, ALU.min),
             reads=(scrI,), writes=(scrG,))
        p.op("vector", lambda v: v.tensor_tensor(scrF.ap, scrF.ap, scrG.ap, ALU.mult),
             reads=(scrG,), writes=(scrF,))
        p.op("vector", lambda v: v.tensor_reduce(amx.ap[:, 3:4], scrF.ap, AX.X, ALU.max,
                                                 apply_absolute_value=True),
             reads=(scrF,), writes=(amx,))
        p.op("vector", lambda v: v.tensor_tensor(amx.ap[:, 4:5], amx.ap[:, 4:5], amx.ap[:, 3:4],
                                                 ALU.max),
             reads=(amx,), writes=(amx,))
        p.dma("gpsimd", t_sp.ap[it * 128:(it + 1) * 128, :], scrF.ap, reads=(scrF,), writes=(t_sp,))
        p.op("vector", lambda v, _e=evac2: v.tensor_scalar(scrI.ap, _e.ap, scc.ap[:, 2:3], None,
                                                           ALU.mult),
             reads=(evac2, scc), writes=(scrI,))
        p.op("vector", lambda v, _e=evac2: v.tensor_scalar(_e.ap, scrI.ap, 1.0, None, ALU.mult),
             reads=(scrI,), writes=(evac2,))
        p.dma("gpsimd", u_sp.ap[it * 128:(it + 1) * 128, :], evac2.ap, reads=(evac2,), writes=(u_sp,))

    for it in range(NIT):
        _phaseC_it(it, evacP[it % 2], evac2P[it % 2])

    amax_finalize(4, 0)
    allreduce_max(2, 1)
    make_scales(0, 6, 7, extra=2.0 ** -16)

    def _phaseD_it(it, evac, evac2):
        p.dma("sync", evac.ap, t_sp.ap[it * 128:(it + 1) * 128, :], writes=(evac,))
        p.dma("sync", evac2.ap, u_sp.ap[it * 128:(it + 1) * 128, :], writes=(evac2,))
        p.op("vector", lambda v, _e=evac: v.tensor_scalar(scrI.ap, _e.ap, scc.ap[:, 6:7], None,
                                                          ALU.mult),
             reads=(evac, scc), writes=(scrI,))
        p.op("vector", lambda v, _e=evac2: v.scalar_tensor_tensor(scrF.ap, scrI.ap, 1.0, _e.ap,
                                                                  ALU.mult, ALU.mult),
             reads=(scrI, evac2), writes=(scrF,))
        p.op("vector", lambda v: v.tensor_reduce(amx.ap[:, 3:4], scrF.ap, AX.X, ALU.max,
                                                 apply_absolute_value=True),
             reads=(scrF,), writes=(amx,))
        p.op("vector", lambda v: v.tensor_tensor(amx.ap[:, 5:6], amx.ap[:, 5:6], amx.ap[:, 3:4],
                                                 ALU.max),
             reads=(amx,), writes=(amx,))
        p.dma("gpsimd", p_sp.ap[it * 128:(it + 1) * 128, :], scrF.ap, reads=(scrF,), writes=(p_sp,))

    for it in range(NIT):
        _phaseD_it(it, evacP[it % 2], evac2P[it % 2])

    amax_finalize(5, 0)
    allreduce_max(3, 1)
    p.op("vector", lambda v: v.tensor_tensor(asc.ap[0:1, 2:3], scc.ap[0:1, 7:8], scc.ap[0:1, 3:4],
                                             ALU.mult),
         reads=(scc,), writes=(asc,))
    p.op("vector", lambda v: v.tensor_tensor(asc.ap[0:1, 0:1], asc.ap[0:1, 0:1], asc.ap[0:1, 2:3],
                                             ALU.mult),
         reads=(asc,), writes=(asc,))
    make_scales(0, 8, 9)
    p.op("vector", lambda v: v.tensor_scalar(scc.ap[:, 10:11], scc.ap[:, 7:8], 1.0, None, ALU.mult),
         reads=(scc,), writes=(scc,))
    p.op("vector", lambda v: v.tensor_tensor(scc.ap[:, 10:11], scc.ap[:, 10:11], scc.ap[:, 3:4],
                                             ALU.mult),
         reads=(scc,), writes=(scc,))
    p.op("vector", lambda v: v.tensor_tensor(scc.ap[:, 8:9], scc.ap[:, 8:9], scc.ap[:, 10:11],
                                             ALU.mult),
         reads=(scc,), writes=(scc,))

    def _phaseE1_it(it, evac):
        p.dma("sync", evac.ap, p_sp.ap[it * 128:(it + 1) * 128, :], writes=(evac,))
        p.op("vector", lambda v, _e=evac: v.tensor_scalar(scrI.ap, _e.ap, scc.ap[:, 8:9], None,
                                                          ALU.mult),
             reads=(evac, scc), writes=(scrI,))
        p.op("vector", lambda v: v.tensor_copy(obf.ap, scrI.ap), reads=(scrI,), writes=(obf,))
        p.dma("gpsimd", o_sp.ap[it * 128:(it + 1) * 128, :], obf.ap, reads=(obf,), writes=(o_sp,))

    for it in range(NIT):
        _phaseE1_it(it, evacP[it % 2])

    p.dma_multi("sync",
                [(dqd.ap[:, it * 1024:(it + 1) * 1024], ag_dn.ap[it * 128:(it + 1) * 128, :])
                 for it in range(NIT)],
                dqd, reads=(ag_dn,))
    for rc in range(16):
        ob = osl[rc % 2]
        p.dma_multi("sync",
                    [(ob.ap[:, it * 256:(it + 1) * 256],
                      o_sp.ap[it * 128:(it + 1) * 128, rc * 256:(rc + 1) * 256])
                     for it in range(NIT)],
                    ob, reads=(o_sp,))
        for rt in range(2):
            for ch in range(2):
                slot = rt * 2 + ch
                pb = (P0, P0, P1, P1)[slot]
                half = (0, 512, 0, 512)[slot]
                dst = pb.ap[:, half:half + 512]
                p.mm_group([(lambda tn, _l=ob.ap[:, it2 * 256 + rt * 128:it2 * 256 + (rt + 1) * 128],
                             _r=dqd.ap[:, it2 * 1024 + ch * 512:it2 * 1024 + (ch + 1) * 512],
                             _d=dst, _s=(it2 == 0), _e=(it2 == 23):
                             tn.matmul(_d, _l, _r, start=_s, stop=_e)) for it2 in range(NIT)],
                           reads=(ob, dqd), writes=(pb,))
        for rt in range(2):
            r = rc * 2 + rt
            for ch in range(2):
                slot = rt * 2 + ch
                pb = (P0, P0, P1, P1)[slot]
                half = (0, 512, 0, 512)[slot]
                p.op("vector", lambda v, _pb=pb, _h=half, _c=ch: v.tensor_scalar(
                    onat.ap[:, _c * 512:(_c + 1) * 512], _pb.ap[:, _h:_h + 512],
                    scc.ap[:, 9:10], None, ALU.mult),
                    reads=(pb, scc), writes=(onat,))
            ln_tile(p, C, onat, onat.ap, mr, sq_scr)
            p.dma("sync", xt.ap, xr[r * 128:(r + 1) * 128, :], writes=(xt,))
            p.op("vector", lambda v: v.tensor_scalar(onat.ap, onat.ap, mr.ap[:, 1:2], mr.ap[:, 0:1],
                                                     ALU.subtract, ALU.mult),
                 reads=(mr,), writes=(onat,))
            p.op("vector", lambda v: v.tensor_tensor(onat.ap, onat.ap, xt.ap, ALU.add),
                 reads=(xt,), writes=(onat,))
            p.dma("gpsimd", xro[r * 128:(r + 1) * 128, :], onat.ap, reads=(onat,))


def make_block1_drams(nc):
    mk = lambda n, s, d=F32: Buf(nc.dram_tensor(n, s, d).ap(), n)
    dram = {
        "dq_loc": mk("b1_dq_loc", [256, 3072], BF16),
        "ag_ug": mk("b1_ag_ug", [2048, 3072], BF16),
        "dq_dn_loc": mk("b1_dq_dn_loc", [384, 1024], BF16),
        "ag_dn": mk("b1_ag_dn", [3072, 1024], BF16),
        "raw_up": mk("b1_raw_up", [3072, 4096]),
        "raw_gate": mk("b1_raw_gate", [3072, 4096]),
        "t_sp": mk("b1_t_sp", [3072, 4096]),
        "u_sp": mk("b1_u_sp", [3072, 4096]),
        "p_sp": mk("b1_p_sp", [3072, 4096]),
        "o_sp": mk("b1_o_sp", [3072, 4096], BF16),
    }
    for i in range(4):
        dram[f"cc{i}_in"] = mk(f"b1_cc{i}_in", [1, 16])
        dram[f"cc{i}_out"] = mk(f"b1_cc{i}_out", [1, 16])
    return dram


def build_block1_nc():
    from contextlib import ExitStack
    nc = bass.Bass("TRN2", target_bir_lowering=False)
    x_in = nc.dram_tensor("x_in", [BPC, SEQ, D], F32, kind="ExternalInput").ap()
    x_out = nc.dram_tensor("x_out", [BPC, SEQ, D], F32, kind="ExternalOutput").ap()
    wd = {
        "w_up_sl": nc.dram_tensor("w_up_sl", [3072, 128], F32, kind="ExternalInput").ap(),
        "w_gate_sl": nc.dram_tensor("w_gate_sl", [3072, 128], F32, kind="ExternalInput").ap(),
        "w_down_sl": nc.dram_tensor("w_down_sl", [1024, 384], F32, kind="ExternalInput").ap(),
    }
    capd = {}
    cnp = build_constants()
    for k in ("ident", "onescol", "onesrow", "sel2"):
        capd[k] = nc.dram_tensor("cin_" + k, list(cnp[k].shape), BF16, kind="ExternalInput").ap()
    stack = ExitStack()
    p = Prog(nc)
    C = Consts(p, stack, capd)
    dram = make_block1_drams(nc)
    emit_block1(p, stack, C, x_in, x_out, wd, dram)
    p.barrier()
    p.emit(stack)
    stack.close()
    return nc, cnp


def block1_inmaps(inputs, x_full, cnp):
    wu = np.ascontiguousarray(inputs["L0_ffn_w_up"])
    wg = np.ascontiguousarray(inputs["L0_ffn_w_gate"])
    wdn = np.ascontiguousarray(inputs["L0_ffn_w_down"])
    in_maps = []
    for c in range(NCORES):
        m = {"x_in": np.ascontiguousarray(x_full[c * BPC:(c + 1) * BPC]),
             "w_up_sl": np.ascontiguousarray(wu[:, c * 128:(c + 1) * 128]),
             "w_gate_sl": np.ascontiguousarray(wg[:, c * 128:(c + 1) * 128]),
             "w_down_sl": np.ascontiguousarray(wdn[:, c * 384:(c + 1) * 384])}
        for k in ("ident", "onescol", "onesrow", "sel2"):
            m["cin_" + k] = cnp[k]
        in_maps.append(m)
    return in_maps


def run_block1(inputs, x_full):
    nc, cnp = build_block1_nc()
    in_maps = block1_inmaps(inputs, x_full, cnp)
    res = run_bass_kernel_spmd(nc, in_maps, core_ids=list(range(NCORES)))
    return np.concatenate([r["x_out"].reshape(BPC, SEQ, D) for r in res.results], axis=0)


class _Rec:
    def __init__(self):
        self.ops = []

    def op(self, e, fn, reads=(), writes=(), dma=False, cc=False):
        self.ops.append(("op", e, fn, reads, writes))

    def dma(self, e, out, in_, reads=(), writes=()):
        self.ops.append(("dma", e, out, in_, reads, writes))

    def mm_group(self, calls, reads=(), writes=()):
        self.ops.append(("mm", calls, reads, writes))


def emit_block2(p, stack, C, x_in, x_out, W, dram):
    nc = p.nc
    sb = lambda n, s, d=F32: mk_sb(stack, nc, n, s, d)
    ps = lambda n, s, d=F32: mk_ps(stack, nc, n, s, d)
    T, BL = 16, 256
    xr = x_in.rearrange("b s d -> (b s) d")
    xro = x_out.rearrange("b s d -> (b s) d")

    P0 = ps("b2_P0", [128, 1024])
    P1 = ps("b2_P1", [128, 1024])
    UP = ps("b2_UP", [128, 1024])
    ps_tr = ps("b2_pstr", [128, 512], BF16)

    xt = sb("b2_xt", [128, D])
    sq_scr = sb("b2_sq", [128, D], BF16)
    mr = sb("b2_mr", [128, 4])
    lnbf = sb("b2_lnbf", [128, D], BF16)
    xTs = sb("b2_xTs", [128, 8 * 256], BF16)
    wkT = sb("b2_wkT", [128, 8 * D], BF16)
    wvT = sb("b2_wvT", [128, 8 * D], BF16)
    wqT = sb("b2_wqT", [128, 8 * D], BF16)
    wstage = sb("b2_wstage", [128, 512])
    S = [sb(f"b2_S{i}", [128, 8192], BF16) for i in range(2)]
    scr = sb("b2_scr", [128, 8192], BF16)
    kt = [sb(f"b2_kt{i}", [128, 1024]) for i in range(2)]
    vt = [sb(f"b2_vt{i}", [128, 1024]) for i in range(2)]
    qt = [sb(f"b2_qt{i}", [128, 1024], BF16) for i in range(2)]
    khat = sb("b2_khat", [128, 1024], BF16)
    kss = sb("b2_kss", [128, 128])
    ret = sb("b2_ret", [128, 1024])
    sqv = sb("b2_sqv", [128, 1024])
    sgv = sb("b2_sgv", [128, 1024])
    onat = sb("b2_onat", [128, D])
    scr2 = sb("b2_scr2", [128, 8192], BF16)
    khat2 = sb("b2_khat2", [128, 1024], BF16)
    kss2 = sb("b2_kss2", [128, 128])
    ret2 = sb("b2_ret2", [128, 1024])
    sqv2 = sb("b2_sqv2", [128, 1024])
    sgv2 = sb("b2_sgv2", [128, 1024])
    mr2 = sb("b2_mr2", [128, 4])
    scrL = [scr, scr2]; khatL = [khat, khat2]; kssL = [kss, kss2]
    retL = [ret, ret2]; sqvL = [sqv, sqv2]; sgvL = [sgv, sgv2]; mrL = [mr, mr2]

    a1_in = dram["a1_in"]; a1_out = dram["a1_out"]
    a2_in = dram["a2_in"]; a2_out = dram["a2_out"]

    def load_weight(dst, dname):
        for t8 in range(8):
            for c0 in (0, 512):
                src = W[dname][t8 * 128:(t8 + 1) * 128, c0:c0 + 512]
                p.dma("sync", wstage.ap[:, 0:512], src, writes=(wstage,))
                p.op("vector", lambda v, _t=t8, _c=c0: v.tensor_copy(
                    dst.ap[:, _t * D + _c:_t * D + _c + 512], wstage.ap[:, 0:512]),
                    reads=(wstage,), writes=(dst,))
    load_weight(wkT, "wkT")
    load_weight(wvT, "wvT")
    load_weight(wqT, "wqT")

    for b in range(BPC):
        for r in range(NT):
            row = b * SEQ + r * 128
            p.dma("sync", xt.ap, xr[row:row + 128, :], writes=(xt,))
            ln_tile(p, C, xt, xt.ap, mr, sq_scr)
            p.op("vector", lambda v: v.tensor_scalar(lnbf.ap, xt.ap, mr.ap[:, 1:2], mr.ap[:, 0:1],
                                                     ALU.subtract, ALU.mult),
                 reads=(xt, mr), writes=(lnbf,))
            dstrow = 512 * (r // 2) + 256 * b + 128 * (r % 2)
            p.dma("gpsimd", a1_in.ap[dstrow:dstrow + 128, :], lnbf.ap,
                  reads=(lnbf,), writes=(a1_in,))
    with allow_ici_all_to_all():
        p.op("gpsimd", lambda g: g.collective_compute(
            "AllToAll", ALU.bypass, replica_groups=[list(range(NCORES))],
            ins=[a1_in.ap], outs=[a1_out.ap]),
            reads=(a1_in,), writes=(a1_out,), cc=True)

    p.op("vector", lambda v: v.memset(S[0].ap, 0.0), writes=(S[0],))
    p.op("gpsimd", lambda g: g.memset(S[1].ap, 0.0), writes=(S[1],))

    def v4(buf):
        return buf.ap.rearrange("p (n i j) -> p n i j", i=8, j=8)

    def v3(buf):
        return buf.ap.rearrange("p (n j) -> p n j", j=8)

    for t in range(T):
        for bt in range(2):
            srow = 512 * (t // 2) + 256 * (t % 2) + 128 * bt
            p.dma("sync", lnbf.ap, a1_out.ap[srow:srow + 128, :], reads=(a1_out,), writes=(lnbf,))
            transpose4(p, C, lnbf, lambda dt_: lnbf.ap[:, dt_ * 128:(dt_ + 1) * 128], ps_tr,
                       xTs, lambda dt_: xTs.ap[:, dt_ * 256 + bt * 128:dt_ * 256 + bt * 128 + 128], 8)
        for bt in range(2):
            for dst, wsrc in ((kt[bt], wkT), (vt[bt], wvT), (qt[bt], wqT)):
                for ch in range(2):
                    pb = P0 if ch == 0 else P1
                    dd = pb.ap[:, 0:512]
                    p.mm_group([(lambda tn, _l=xTs.ap[:, d8 * 256 + bt * 128:d8 * 256 + bt * 128 + 128],
                                 _r=wsrc.ap[:, d8 * D + ch * 512:d8 * D + (ch + 1) * 512],
                                 _d=dd, _s=(d8 == 0), _e=(d8 == 7):
                                 tn.matmul(_d, _l, _r, start=_s, stop=_e)) for d8 in range(8)],
                               reads=(xTs, wsrc), writes=(pb,))
                    p.op("vector", lambda v, _d=dd, _c=ch, _o=dst: v.tensor_copy(
                        _o.ap[:, _c * 512:(_c + 1) * 512], _d),
                        reads=(pb,), writes=(dst,))
        recs = []
        for bt in range(2):
            q = _Rec()
            Sb = S[bt]
            _scr = scrL[bt]; _khat = khatL[bt]; _kss = kssL[bt]
            _ret = retL[bt]; _sqv = sqvL[bt]; _sgv = sgvL[bt]; _mr = mrL[bt]
            q.op("scalar", lambda s, _k=kt[bt], _o=_scr: s.activation(
                _o.ap[:, 0:1024], _k.ap, AF.Square),
                reads=(kt[bt],), writes=(_scr,))
            q.op("vector", lambda v, _i=_scr, _o=_kss: v.tensor_reduce(
                _o.ap, _i.ap[:, 0:1024].rearrange("p (n j) -> p n j", j=8), AX.X, ALU.add),
                reads=(_scr,), writes=(_kss,))
            q.op("scalar", lambda s, _o=_kss: s.activation(_o.ap, _o.ap, AF.Ln, bias=C.col(4)),
                 reads=(_kss, C.cols), writes=(_kss,))
            q.op("scalar", lambda s, _o=_kss: s.activation(_o.ap, _o.ap, AF.Exp, scale=C.col(6)),
                 reads=(_kss, C.cols), writes=(_kss,))
            q.op("scalar", lambda s, _o=_kss: s.activation(_o.ap, _o.ap, AF.Ln, bias=C.col(3)),
                 reads=(_kss, C.cols), writes=(_kss,))
            q.op("scalar", lambda s, _o=_kss: s.activation(_o.ap, _o.ap, AF.Exp, scale=C.col(2)),
                 reads=(_kss, C.cols), writes=(_kss,))
            q.op("vector", lambda v, _k=kt[bt], _o=_khat, _s=_kss: v.tensor_tensor(
                v3(_o), v3(_k), _s.ap.unsqueeze(2).to_broadcast([128, 128, 8]), ALU.mult),
                reads=(kt[bt], _kss), writes=(_khat,))
            khb = _khat.ap.rearrange("p (n j) -> p n j", j=8).unsqueeze(2).to_broadcast([128, 128, 8, 8])
            q.op("vector", lambda v, _S=Sb, _k=khb, _o=_scr: v.tensor_tensor(
                v4(_o), v4(_S), _k, ALU.mult),
                reads=(Sb, _khat), writes=(_scr,))
            q.op("vector", lambda v, _i=_scr, _o=_ret: v.tensor_reduce(
                _o.ap.rearrange("p (n i) -> p n i", i=8), v4(_i), AX.X, ALU.add),
                reads=(_scr,), writes=(_ret,))
            q.op("vector", lambda v, _v=vt[bt], _o=_ret: v.tensor_tensor(
                _o.ap, _v.ap, _o.ap, ALU.subtract),
                reads=(vt[bt],), writes=(_ret,))
            NS = 85
            q.op("vector", lambda v,
                 _o=_scr.ap[:, 0:NS * 64].rearrange("p (n i j) -> p n i j", i=8, j=8),
                 _d=_ret.ap[:, 0:NS * 8].rearrange("p (n i) -> p n i", i=8).unsqueeze(3).to_broadcast([128, NS, 8, 8]),
                 _k=_khat.ap[:, 0:NS * 8].rearrange("p (n j) -> p n j", j=8).unsqueeze(2).to_broadcast([128, NS, 8, 8]):
                 v.tensor_tensor(_o, _d, _k, ALU.mult),
                 reads=(_ret, _khat), writes=(_scr,))
            q.op("gpsimd", lambda g,
                 _o=_scr.ap[:, NS * 64:8192].rearrange("p (n i j) -> p n i j", i=8, j=8),
                 _d=_ret.ap[:, NS * 8:1024].rearrange("p (n i) -> p n i", i=8).unsqueeze(3).to_broadcast([128, 128 - NS, 8, 8]),
                 _k=_khat.ap[:, NS * 8:1024].rearrange("p (n j) -> p n j", j=8).unsqueeze(2).to_broadcast([128, 128 - NS, 8, 8]):
                 g.tensor_tensor(_o, _d, _k, ALU.mult),
                 reads=(_ret, _khat), writes=(_scr,))
            SA = 4288
            q.op("vector", lambda v, _S=Sb.ap, _P=_scr.ap: v.tensor_tensor(
                _S[:, 0:SA], _S[:, 0:SA], _P[:, 0:SA], ALU.add),
                reads=(_scr,), writes=(Sb,))
            q.op("gpsimd", lambda g, _S=Sb.ap, _P=_scr.ap: g.tensor_tensor(
                _S[:, SA:8192], _S[:, SA:8192], _P[:, SA:8192], ALU.add),
                reads=(_scr,), writes=(Sb,))
            q.op("scalar", lambda s, _S=Sb.ap: s.activation(_S, _S, AF.Tanh),
                 reads=(), writes=(Sb,))
            qbb = qt[bt].ap.rearrange("p (n j) -> p n j", j=8).unsqueeze(2).to_broadcast([128, 128, 8, 8])
            q.op("vector", lambda v, _S=Sb, _q=qbb, _o=_scr: v.tensor_tensor(
                v4(_o), v4(_S), _q, ALU.mult),
                reads=(Sb, qt[bt]), writes=(_scr,))
            q.op("vector", lambda v, _i=_scr, _o=_sqv: v.tensor_reduce(
                _o.ap.rearrange("p (n i) -> p n i", i=8), v4(_i), AX.X, ALU.add),
                reads=(_scr,), writes=(_sqv,))
            q.op("scalar", lambda s, _i=_sqv, _o=_sgv: s.activation(_o.ap, _i.ap, AF.Exp, scale=C.col(2)),
                 reads=(_sqv, C.cols), writes=(_sgv,))
            q.op("scalar", lambda s, _o=_sgv: s.activation(_o.ap, _o.ap, AF.Ln, bias=C.col(7)),
                 reads=(_sgv, C.cols), writes=(_sgv,))
            q.op("scalar", lambda s, _o=_sgv: s.activation(_o.ap, _o.ap, AF.Exp, scale=C.col(2)),
                 reads=(_sgv, C.cols), writes=(_sgv,))
            q.op("vector", lambda v, _a=_sgv, _b=_sqv: v.tensor_tensor(
                _a.ap, _a.ap, _b.ap, ALU.mult),
                reads=(_sqv,), writes=(_sgv,))
            q.op("vector", lambda v, _a=_sgv, _b=_sqv: v.tensor_tensor(
                _b.ap, _a.ap, _b.ap, ALU.mult),
                reads=(_sgv,), writes=(_sqv,))
            ln_tile(q, C, _sqv, _sqv.ap, _mr, _scr)
            q.op("vector", lambda v, _o=_sqv, _m=_mr: v.tensor_scalar(
                _o.ap, _o.ap, _m.ap[:, 1:2], _m.ap[:, 0:1], ALU.subtract, ALU.mult),
                reads=(_mr,), writes=(_sqv,))
            drow = 512 * (t // 2) + 256 * (t % 2) + 128 * bt
            q.dma("gpsimd", a2_in.ap[drow:drow + 128, :], _sqv.ap, reads=(_sqv,), writes=(a2_in,))
            recs.append(q.ops)
        for i in range(max(len(r_) for r_ in recs)):
            for r_ in recs:
                if i < len(r_):
                    it = r_[i]
                    if it[0] == "op":
                        p.op(it[1], it[2], reads=it[3], writes=it[4])
                    elif it[0] == "mm":
                        p.mm_group(it[1], reads=it[2], writes=it[3])
                    else:
                        p.dma(it[1], it[2], it[3], reads=it[4], writes=it[5])
    with allow_ici_all_to_all():
        p.op("gpsimd", lambda g: g.collective_compute(
            "AllToAll", ALU.bypass, replica_groups=[list(range(NCORES))],
            ins=[a2_in.ap], outs=[a2_out.ap]),
            reads=(a2_in,), writes=(a2_out,), cc=True)

    for b in range(BPC):
        for r in range(NT):
            row = b * SEQ + r * 128
            srow = 512 * (r // 2) + 256 * b + 128 * (r % 2)
            p.dma("sync", xt.ap, xr[row:row + 128, :], writes=(xt,))
            p.dma("sync", onat.ap, a2_out.ap[srow:srow + 128, :], reads=(a2_out,), writes=(onat,))
            p.op("vector", lambda v: v.tensor_tensor(onat.ap, onat.ap, xt.ap, ALU.add),
                 reads=(xt,), writes=(onat,))
            p.dma("gpsimd", xro[row:row + 128, :], onat.ap, reads=(onat,))


def make_block2_drams(nc):
    mk = lambda n, s, d=F32: Buf(nc.dram_tensor(n, s, d).ap(), n)
    return {
        "a1_in": mk("b2_a1_in", [8 * 512, 1024], BF16),
        "a1_out": mk("b2_a1_out", [8 * 512, 1024], BF16),
        "a2_in": mk("b2_a2_in", [8 * 512, 1024]),
        "a2_out": mk("b2_a2_out", [8 * 512, 1024]),
    }


def build_block2_nc():
    from contextlib import ExitStack
    nc = bass.Bass("TRN2", target_bir_lowering=False)
    x_in = nc.dram_tensor("x_in", [BPC, SEQ, D], F32, kind="ExternalInput").ap()
    x_out = nc.dram_tensor("x_out", [BPC, SEQ, D], F32, kind="ExternalOutput").ap()
    wd = {
        "wkT": nc.dram_tensor("wkT", [D, D], F32, kind="ExternalInput").ap(),
        "wvT": nc.dram_tensor("wvT", [D, D], F32, kind="ExternalInput").ap(),
        "wqT": nc.dram_tensor("wqT", [D, D], F32, kind="ExternalInput").ap(),
    }
    capd = {}
    cnp = build_constants()
    for k in ("ident", "onescol", "onesrow", "sel2"):
        capd[k] = nc.dram_tensor("cin_" + k, list(cnp[k].shape), BF16, kind="ExternalInput").ap()
    stack = ExitStack()
    p = Prog(nc)
    C = Consts(p, stack, capd)
    dram = make_block2_drams(nc)
    emit_block2(p, stack, C, x_in, x_out, wd, dram)
    p.barrier()
    p.emit(stack)
    stack.close()
    return nc, cnp


def block2_inmaps(inputs, x_full, cnp):
    wk = np.ascontiguousarray(inputs["L1_attn_W_k"].T)
    wv = np.ascontiguousarray(inputs["L1_attn_W_v"].T)
    wq = np.ascontiguousarray(inputs["L1_attn_W_q"].T)
    in_maps = []
    for c in range(NCORES):
        m = {"x_in": np.ascontiguousarray(x_full[c * BPC:(c + 1) * BPC]),
             "wkT": wk, "wvT": wv, "wqT": wq}
        for k in ("ident", "onescol", "onesrow", "sel2"):
            m["cin_" + k] = cnp[k]
        in_maps.append(m)
    return in_maps


def run_block2(inputs, x_full):
    nc, cnp = build_block2_nc()
    in_maps = block2_inmaps(inputs, x_full, cnp)
    res = run_bass_kernel_spmd(nc, in_maps, core_ids=list(range(NCORES)))
    return np.concatenate([r["x_out"].reshape(BPC, SEQ, D) for r in res.results], axis=0)


B3_HEADS, B3_DH, B3_WIN, B3_INNER, B3_HID = 8, 64, 512, 512, 2730
B3_LAYERS = 2
B3_STOP_C2 = False
B3_SKIP_W1PREP = False
B3_SKIP_ATTN = False


def emit_block3(p, stack, C, x_in, x_out, W, dram):
    nc = p.nc
    sb = lambda n, s, d=F32: mk_sb(stack, nc, n, s, d)
    ps = lambda n, s, d=F32: mk_ps(stack, nc, n, s, d)
    xr = x_in.rearrange("b s d -> (b s) d")
    xro = x_out.rearrange("b s d -> (b s) d")
    TCOL = 2049

    PA = ps("b3_PA", [128, 1024])
    ps_tr = ps("b3_pstr", [128, 512], BF16)
    ps_l = ps("b3_psl", [128, 1024])
    ps_av = ps("b3_psav", [128, 256])
    ps_s = ps("b3_pss", [33, 256])
    ps_r = ps_l

    AR = Buf(stack.enter_context(nc.sbuf_tensor("b3_AR", [128, 65544], BF16))[:], "b3_AR")
    def carve(off, n):
        return Buf(AR.ap[:, off:off + n], f"b3_ar_{off}"), off + n
    off = 0
    yT, off = carve(off, 8 * TCOL)
    qT, off = carve(off, 4 * SEQ)
    kT, off = carve(off, 4 * SEQ)
    vnat, off = carve(off, 16 * 512)
    attnT, off = carve(off, 4 * SEQ)
    wqkv, off = carve(off, 8 * 1536)
    wo_b, off = carve(off, 4 * 1024)
    assert off <= 65544, off
    off2 = 0
    h2T, off2 = carve(off2, 8 * TCOL)
    h3seg, off2 = carve(off2, 22 * 1024)
    w2_b, off2 = carve(off2, 22 * 1024)
    assert off2 <= 65544, off2

    xt = sb("b3_xt", [128, D])
    sq_scr = sb("b3_sq", [128, D], BF16)
    mr = sb("b3_mr", [128, 4])
    lnbf = sb("b3_lnbf", [128, D], BF16)
    qnP = [sb("b3_qn0", [128, 512]), sb("b3_qn1", [128, 512])]
    scrA = sb("b3_scrA", [128, 1024])
    scrB = sb("b3_scrB", [128, 1024])
    _qbf0 = sb("b3_qbf0", [128, 512], BF16)
    qbfP = [_qbf0, _qbf0]
    nrm = sb("b3_nrm", [128, 8])
    expb2_ = [sb("b3_expb0", [128, 512], BF16), sb("b3_expb1", [128, 512], BF16)]
    acc_s3 = sb("b3_accs", [128, 512], BF16)
    lnrec = sb("b3_lnrec", [33, 256])
    recipb = sb("b3_recipb", [33, 256], BF16)
    rsb = sb("b3_rsb", [128, 256])
    onat = sb("b3_onat", [128, D])
    yt2 = onat
    tabs = [sb(f"b3_tab{i}", [128, 16 * 64], BF16) for i in range(4)]
    masks = sb("b3_masks", [128, 4 * 256], BF16)
    qsb = sb("b3_qsb", [128, 128], BF16)
    statb = sb("b3_statb", [128, 64])
    stsh = sb("b3_stsh", [128, 64])
    mcols = sb("b3_mcols", [128, 16])
    rcols = sb("b3_rcols", [128, 16])
    msh = sb("b3_msh", [128, 32])
    shm = sb("b3_shm", [128, 512])

    lgc = sb("b3_lgc", [128, 8])
    lbb = sb("b3_lbb", [128, 8], BF16)
    wstage = sb("b3_wstage", [128, 512])
    wstage2 = sb("b3_wstage2", [128, 512])
    rawb = sb("b3_rawb", [128, 512], BF16)
    rawb2 = sb("b3_rawb2", [128, 512], BF16)
    lw1 = [sb(f"b3_lw1{i}", [128, 8 * 128], BF16) for i in range(4)]
    c2row = sb("b3_c2row", [1, 5504], BF16)
    b1aL = [sb(f"b3_b1a{i}", [128, 22]) for i in range(2)]
    b1gL = [sb(f"b3_b1g{i}", [128, 22]) for i in range(2)]
    b2b = sb("b3_b2b", [1, 1024], BF16)
    csh = sb("b3_csh", [128, 2])
    cell = sb("b3_cell", [1, 8])
    cellb = sb("b3_cellb", [1, 4], BF16)

    y_a = dram["y_a"]; y_b = dram["y_b"]; y_mid = dram["y_mid"]
    w1bfL = [dram["w1bf0"], dram["w1bf1"]]

    for i in range(4):
        p.dma("sync", tabs[i].ap.rearrange("p (t f) -> p t f", f=64),
              W["xpos"][i].rearrange("(t p) f -> p t f", p=128), writes=(tabs[i],))
    p.dma("sync", masks.ap, W["masks"], writes=(masks,))

    NLAYER = B3_LAYERS

    def shifted_lhs(tile, dt, c0, n):
        base = dt * TCOL + (1 + c0 if dt < 4 else c0)
        return tile.ap[:, base:base + n]

    p.op("vector", lambda v: v.memset(shm.ap, 0.0), writes=(shm,))
    p.dma("gpsimd", shm.ap[0:127, 1:128], C.ident.ap[0:127, 0:127], reads=(C.ident,), writes=(shm,))
    p.dma("gpsimd", shm.ap[1:128, 128:255], C.ident.ap[1:128, 1:128], reads=(C.ident,), writes=(shm,))
    p.dma("gpsimd", shm.ap[127:128, 256:257], C.ident.ap[0:1, 0:1], reads=(C.ident,), writes=(shm,))
    p.dma("gpsimd", shm.ap[0:1, 511:512], C.ident.ap[0:1, 0:1], reads=(C.ident,), writes=(shm,))
    for b in range(BPC):
        for r in range(NT):
            row = b * SEQ + r * 128
            p.dma("sync", xt.ap, xr[row:row + 128, :], writes=(xt,))
            ln_tile(p, C, xt, xt.ap, mr, sq_scr)
            p.op("vector", lambda v: v.tensor_scalar(yt2.ap, xt.ap, mr.ap[:, 1:2], mr.ap[:, 0:1],
                                                     ALU.subtract, ALU.mult),
                 reads=(xt, mr), writes=(yt2,))
            p.dma("gpsimd", y_a.ap[r * 128:(r + 1) * 128, :], yt2.ap, reads=(yt2,), writes=(y_a,))
        for l in range(NLAYER):
            yin = y_a if l == 0 else y_b
            ynext = y_b if l == 0 else y_a
            for dt in range(8):
                for cc_ in range(3):
                    src = W["wqkv"][l][dt * 128:(dt + 1) * 128, cc_ * 512:(cc_ + 1) * 512]
                    p.dma("sync", wstage.ap, src, writes=(wstage,))
                    p.op("vector", lambda v, _dt=dt, _c=cc_: v.tensor_copy(
                        wqkv.ap[:, _dt * 1536 + _c * 512:_dt * 1536 + (_c + 1) * 512], wstage.ap),
                        reads=(wstage,), writes=(wqkv,))
            for dt in range(4):
                for cc_ in range(2):
                    src = W["wo"][l][dt * 128:(dt + 1) * 128, cc_ * 512:(cc_ + 1) * 512]
                    p.dma("sync", wstage.ap, src, writes=(wstage,))
                    p.op("vector", lambda v, _dt=dt, _c=cc_: v.tensor_copy(
                        wo_b.ap[:, _dt * 1024 + _c * 512:_dt * 1024 + (_c + 1) * 512], wstage.ap),
                        reads=(wstage,), writes=(wo_b,))
            p.dma("sync", cell.ap[0:1, 0:8], W["qscale"][l][0:1, 0:8], writes=(cell,))
            p.dma("sync", scrA.ap[0:1, 0:64], W["qscale"][l], writes=(scrA,))
            p.dma("sync", scrA.ap[0:1, 64:128], W["kscale"][l], writes=(scrA,))
            p.op("vector", lambda v: v.tensor_scalar(scrA.ap[0:1, 0:64], scrA.ap[0:1, 0:64],
                                                     8.0, None, ALU.mult),
                 reads=(), writes=(scrA,))
            p.op("vector", lambda v: v.tensor_tensor(scrA.ap[0:1, 128:192], scrA.ap[0:1, 0:64],
                                                     scrA.ap[0:1, 64:128], ALU.mult),
                 reads=(), writes=(scrA,))
            p.op("vector", lambda v: v.tensor_reduce(cell.ap[0:1, 0:1], scrA.ap[0:1, 128:192],
                                                     AX.X, ALU.max, apply_absolute_value=True),
                 reads=(scrA,), writes=(cell,))
            p.op("vector", lambda v: v.tensor_scalar(cell.ap[0:1, 1:2], cell.ap[0:1, 0:1],
                                                     -1.0, None, ALU.mult),
                 reads=(cell,), writes=(cell,))
            p.op("vector", lambda v: v.tensor_copy(cellb.ap[0:1, 0:1], cell.ap[0:1, 1:2]),
                 reads=(cell,), writes=(cellb,))
            p.mm_group([lambda tn: tn.matmul(ps_r.ap[:, 0:1], C.onesrow.ap, cellb.ap[0:1, 0:1],
                                             start=True, stop=True)],
                       reads=(cellb, C.onesrow), writes=(ps_r,))
            p.op("vector", lambda v: v.tensor_copy(csh.ap[:, 0:1], ps_r.ap[:, 0:1]),
                 reads=(ps_r,), writes=(csh,))
            p.op("vector", lambda v: v.tensor_copy(cellb.ap[0:1, 0:4], cell.ap[0:1, 0:4]),
                 reads=(cell,), writes=(cellb,))
            p.op("vector", lambda v: v.tensor_copy(sq_scr.ap[0:1, 0:128], scrA.ap[0:1, 0:128]),
                 reads=(scrA,), writes=(sq_scr,))
            p.mm_group([lambda tn: tn.matmul(ps_r.ap[:, 0:128], C.onesrow.ap, sq_scr.ap[0:1, 0:128],
                                             start=True, stop=True)],
                       reads=(sq_scr, C.onesrow), writes=(ps_r,))
            p.op("vector", lambda v: v.tensor_copy(qsb.ap, ps_r.ap[:, 0:128]),
                 reads=(ps_r,), writes=(qsb,))
            for dt in range(8 if (not B3_SKIP_W1PREP and b == 0) else 0):
                p.dma("sync", lgc.ap[:, dt:dt + 1],
                      W["lng"][l][dt * 128:(dt + 1) * 128].unsqueeze(1), writes=(lgc,))
                p.dma("sync", scrB.ap[0:128, dt:dt + 1],
                      W["lnb"][l][dt * 128:(dt + 1) * 128].unsqueeze(1), writes=(scrB,))
            if b == 0:
                p.op("vector", lambda v: v.tensor_copy(lbb.ap, scrB.ap[:, 0:8]),
                     reads=(scrB,), writes=(lbb,))
            NCW = ([512] * 10 + [340]) if (not B3_SKIP_W1PREP and b == 0) else []
            for ci, cw in enumerate(NCW):
                c0 = ci * 512
                for dt in range(8):
                    ws_ = wstage if dt % 2 == 0 else wstage2
                    rb_ = rawb if dt % 2 == 0 else rawb2
                    src = W["w1"][l][dt * 128:(dt + 1) * 128, c0:c0 + cw]
                    p.dma("sync", ws_.ap[:, 0:cw], src, writes=(ws_,))
                    p.op("vector", lambda v, _cw=cw, _w=ws_, _b=rb_: v.tensor_copy(
                        _b.ap[:, 0:_cw], _w.ap[:, 0:_cw]),
                         reads=(ws_,), writes=(rb_,))
                    p.mm_group(
                        [lambda tn, _dt=dt, _cw=cw, _b=rb_: tn.matmul(
                            ps_s.ap[0:1, 0:min(_cw, 256)], lbb.ap[:, _dt:_dt + 1],
                            _b.ap[:, 0:min(_cw, 256)], start=(_dt == 0), stop=(_dt == 7)),
                         lambda tn, _dt=dt, _cw=cw, _b=rb_: tn.matmul(
                            ps_s.ap[32:33, 0:max(_cw - 256, 1)], lbb.ap[:, _dt:_dt + 1],
                            _b.ap[:, 256:256 + max(_cw - 256, 1)], start=(_dt == 0),
                            stop=(_dt == 7), skip_group_check=True, tile_position=(0, 32))],
                        reads=(rb_, lbb), writes=(ps_s,))
                    p.op("vector", lambda v, _dt=dt, _cw=cw, _w=ws_, _b=rb_: v.tensor_scalar(
                        _b.ap[:, 0:_cw], _w.ap[:, 0:_cw], lgc.ap[:, _dt:_dt + 1],
                        None, ALU.mult),
                        reads=(ws_, lgc), writes=(rb_,))
                    p.dma("gpsimd", w1bfL[l].ap[dt * 128:(dt + 1) * 128, c0:c0 + cw],
                          rb_.ap[:, 0:cw], reads=(rb_,), writes=(w1bfL[l],))
                p.op("vector", lambda v, _c0=c0, _cw=cw: v.tensor_copy(
                    c2row.ap[0:1, _c0:_c0 + min(_cw, 256)], ps_s.ap[0:1, 0:min(_cw, 256)]),
                    reads=(ps_s,), writes=(c2row,))
                p.op("vector", lambda v, _c0=c0, _cw=cw: v.tensor_copy(
                    c2row.ap[0:1, _c0 + 256:_c0 + 256 + max(_cw - 256, 1)],
                    ps_s.ap[32:33, 0:max(_cw - 256, 1)]),
                    reads=(ps_s,), writes=(c2row,))
            for c0 in (range(0, 5460, 1024) if (not B3_SKIP_W1PREP and b == 0) else ()):
                cw = min(1024, 5460 - c0)
                p.dma("sync", scrB.ap[0:1, 0:cw], W["b1"][l][0:1, c0:c0 + cw], writes=(scrB,))
                p.op("vector", lambda v, _c0=c0, _cw=cw: v.tensor_tensor(
                    c2row.ap[0:1, _c0:_c0 + _cw], c2row.ap[0:1, _c0:_c0 + _cw],
                    scrB.ap[0:1, 0:_cw], ALU.add),
                    reads=(scrB,), writes=(c2row,))
            for t22 in (range(22) if (not B3_SKIP_W1PREP and b == 0) else ()):
                cnt = 128 if t22 < 21 else 42
                p.dma("gpsimd", b1aL[l].ap[0:cnt, t22:t22 + 1],
                      c2row.ap[0:1, t22 * 128:t22 * 128 + cnt], reads=(c2row,), writes=(b1aL[l],))
                p.dma("gpsimd", b1gL[l].ap[0:cnt, t22:t22 + 1],
                      c2row.ap[0:1, 2730 + t22 * 128:2730 + t22 * 128 + cnt],
                      reads=(c2row,), writes=(b1gL[l],))

            for dt in range(8):
                p.op("vector", lambda v, _dt=dt: v.memset(yT.ap[:, _dt * TCOL:_dt * TCOL + 1], 0.0),
                     writes=(yT,))
            for r in range(NT):
                p.dma("sync", lnbf.ap, yin.ap[r * 128:(r + 1) * 128, :], reads=(yin,), writes=(lnbf,))
                transpose4(p, C, lnbf, lambda dt_: lnbf.ap[:, dt_ * 128:(dt_ + 1) * 128], ps_tr,
                           yT, lambda dt_, _r=r: yT.ap[:, dt_ * TCOL + 1 + _r * 128:
                                                       dt_ * TCOL + 1 + (_r + 1) * 128], 8)
            for r in range(NT):
                qn, qbf = qnP[r % 2], qbfP[r % 2]
                c0 = r * 128
                for cc_, dstb in ((0, None), (1, None), (2, None)):
                    dstp = PA.ap[:, cc_ * 256:cc_ * 256 + 512] if False else PA.ap[:, 0:512]
                for idx, (lo, evac_to) in enumerate(((0, qn), (512, qn), (1024, None))):
                    pass
                p.mm_group([(lambda tn, _l=shifted_lhs(yT, d8, c0, 128),
                             _r=wqkv.ap[:, d8 * 1536 + 0:d8 * 1536 + 512],
                             _s=(d8 == 0), _e=(d8 == 7):
                             tn.matmul(PA.ap[:, 0:512], _l, _r, start=_s, stop=_e))
                            for d8 in range(8)],
                           reads=(yT, wqkv), writes=(PA,))
                p.op("vector", lambda v, _q=qn: v.tensor_copy(_q.ap, PA.ap[:, 0:512]),
                     reads=(PA,), writes=(qn,))
                rope_b3(p, C, qn, qbf, tabs[0], tabs[1], qsb, 0, r, nrm, sq_scr)
                transpose4(p, C, qbf, lambda u: qbf.ap[:, u * 128:(u + 1) * 128], ps_tr,
                           qT, lambda u, _r=r: qT.ap[:, u * SEQ + _r * 128:u * SEQ + (_r + 1) * 128], 4)
                p.mm_group([(lambda tn, _l=shifted_lhs(yT, d8, c0, 128),
                             _r=wqkv.ap[:, d8 * 1536 + 512:d8 * 1536 + 1024],
                             _s=(d8 == 0), _e=(d8 == 7):
                             tn.matmul(PA.ap[:, 512:1024], _l, _r, start=_s, stop=_e))
                            for d8 in range(8)],
                           reads=(yT, wqkv), writes=(PA,))
                p.op("vector", lambda v, _q=qn: v.tensor_copy(_q.ap, PA.ap[:, 512:1024]),
                     reads=(PA,), writes=(qn,))
                rope_b3(p, C, qn, qbf, tabs[2], tabs[3], qsb, 1, r, nrm, sq_scr)
                transpose4(p, C, qbf, lambda u: qbf.ap[:, u * 128:(u + 1) * 128], ps_tr,
                           kT, lambda u, _r=r: kT.ap[:, u * SEQ + _r * 128:u * SEQ + (_r + 1) * 128], 4)
                p.mm_group([(lambda tn, _l=shifted_lhs(yT, d8, c0, 128),
                             _r=wqkv.ap[:, d8 * 1536 + 1024:d8 * 1536 + 1536],
                             _s=(d8 == 0), _e=(d8 == 7):
                             tn.matmul(PA.ap[:, 0:512], _l, _r, start=_s, stop=_e))
                            for d8 in range(8)],
                           reads=(yT, wqkv), writes=(PA,))
                p.op("vector", lambda v, _r=r: v.tensor_copy(vnat.ap[:, _r * 512:(_r + 1) * 512],
                                                             PA.ap[:, 0:512]),
                     reads=(PA,), writes=(vnat,))
            p.op("vector", lambda v: v.memset(ps_s.ap, 1.0), writes=(ps_s,))
            for u in range(4 if not B3_SKIP_ATTN else 0):
                for ch in range(8):
                    kts = [kt_ for kt_ in range(2 * ch - 4, 2 * ch + 2) if kt_ >= 0]
                    first = True
                    for kt_ in kts:
                        expb = expb2_[kt_ % 2]
                        rel = kt_ - (2 * ch - 4)
                        la = kT.ap[0:64, u * SEQ + kt_ * 128:u * SEQ + (kt_ + 1) * 128]
                        lb_ = kT.ap[64:128, u * SEQ + kt_ * 128:u * SEQ + (kt_ + 1) * 128]
                        ra = qT.ap[0:64, u * SEQ + ch * 256:u * SEQ + (ch + 1) * 256]
                        rb = qT.ap[64:128, u * SEQ + ch * 256:u * SEQ + (ch + 1) * 256]
                        p.mm_group(
                            [lambda tn, _la=la, _ra=ra: tn.matmul(
                                ps_l.ap[:, 0:256], _la, _ra, start=True, stop=True,
                                tile_position=(0, 0)),
                             lambda tn, _lb=lb_, _rb=rb: tn.matmul(
                                ps_l.ap[:, 512:768], _lb, _rb, start=True, stop=True,
                                tile_position=(64, 0))],
                            reads=(kT, qT), writes=(ps_l,))
                        p.op("scalar", lambda s, _x=expb: s.activation(_x.ap[:, 0:256],
                                                              ps_l.ap[:, 0:256],
                                                              AF.Exp, bias=csh.ap[:, 0:1]),
                             reads=(ps_l, csh), writes=(expb,))
                        p.op("scalar", lambda s, _x=expb: s.activation(_x.ap[:, 256:512],
                                                              ps_l.ap[:, 512:768],
                                                              AF.Exp, bias=csh.ap[:, 0:1]),
                             reads=(ps_l, csh), writes=(expb,))
                        mi = {0: 0, 1: 1, 4: 2, 5: 3}.get(rel)
                        if mi is not None:
                            p.op("vector", lambda v, _m=mi, _x=expb: v.tensor_tensor(
                                _x.ap.rearrange("p (h n) -> p h n", h=2),
                                _x.ap.rearrange("p (h n) -> p h n", h=2),
                                masks.ap[:, _m * 256:(_m + 1) * 256].unsqueeze(1)
                                .to_broadcast([128, 2, 256]), ALU.mult),
                                reads=(masks,), writes=(expb,))
                        st_, sp_ = first, (kt_ == kts[-1])
                        va = vnat.ap[:, kt_ * 512 + (2 * u) * 64:kt_ * 512 + (2 * u + 1) * 64]
                        vb = vnat.ap[:, kt_ * 512 + (2 * u + 1) * 64:kt_ * 512 + (2 * u + 2) * 64]
                        p.mm_group(
                            [lambda tn, _va=va, _s=st_, _e=sp_, _x=expb: tn.matmul(
                                ps_av.ap[0:64, :], _va, _x.ap[:, 0:256], start=_s, stop=_e,
                                tile_position=(0, 0)),
                             lambda tn, _vb=vb, _s=st_, _e=sp_, _x=expb: tn.matmul(
                                ps_av.ap[64:128, :], _vb, _x.ap[:, 256:512], start=_s, stop=_e,
                                skip_group_check=True, tile_position=(0, 64))],
                            reads=(vnat, expb), writes=(ps_av,))
                        if first:
                            p.op("vector", lambda v, _x=expb: v.tensor_copy(acc_s3.ap, _x.ap),
                                 reads=(expb,), writes=(acc_s3,))
                        else:
                            p.op("vector", lambda v, _x=expb: v.tensor_tensor(acc_s3.ap, acc_s3.ap,
                                                                              _x.ap, ALU.add),
                                 reads=(expb,), writes=(acc_s3,))
                        first = False
                    p.mm_group(
                        [lambda tn: tn.matmul(ps_s.ap[0:1, :], C.onescol.ap, acc_s3.ap[:, 0:256],
                                              start=True, stop=True, tile_position=(0, 0)),
                         lambda tn: tn.matmul(ps_s.ap[32:33, :], C.onescol.ap, acc_s3.ap[:, 256:512],
                                              start=True, stop=True, skip_group_check=True,
                                              tile_position=(0, 32))],
                        reads=(acc_s3, C.onescol), writes=(ps_s,))
                    p.op("scalar", lambda s: s.activation(lnrec.ap, ps_s.ap, AF.Ln),
                         reads=(ps_s,), writes=(lnrec,))
                    p.op("scalar", lambda s: s.activation(recipb.ap, lnrec.ap, AF.Exp,
                                                          scale=C.col(2, 33)),
                         reads=(lnrec, C.cols), writes=(recipb,))
                    p.op("tensor", lambda tn: tn.matmul(ps_l.ap[:, 0:256], C.onesrow.ap,
                                                        recipb.ap[0:1, :], start=True, stop=True),
                         reads=(recipb, C.onesrow), writes=(ps_l,))
                    p.op("tensor", lambda tn: tn.matmul(ps_l.ap[:, 512:768], C.onesrow33.ap[32:33, :],
                                                        recipb.ap[32:33, :], start=True, stop=True,
                                                        skip_group_check=True, tile_position=(32, 0)),
                         reads=(recipb, C.onesrow33), writes=(ps_l,))
                    p.op("vector", lambda v: v.tensor_copy(rsb.ap[0:64, :], ps_l.ap[0:64, 0:256]),
                         reads=(ps_l,), writes=(rsb,))
                    p.op("vector", lambda v: v.tensor_copy(rsb.ap[64:128, :], ps_l.ap[64:128, 512:768]),
                         reads=(ps_l,), writes=(rsb,))
                    dst = attnT.ap[:, u * SEQ + ch * 256:u * SEQ + (ch + 1) * 256]
                    p.op("vector", lambda v, _d=dst: v.tensor_tensor(_d, ps_av.ap, rsb.ap, ALU.mult),
                         reads=(ps_av, rsb), writes=(attnT,))
            for r in range(NT):
                for cc_ in range(2):
                    dstp = PA.ap[:, cc_ * 512:(cc_ + 1) * 512]
                    p.mm_group([(lambda tn, _l=attnT.ap[:, tau * SEQ + r * 128:tau * SEQ + (r + 1) * 128],
                                 _r=wo_b.ap[:, tau * 1024 + cc_ * 512:tau * 1024 + (cc_ + 1) * 512],
                                 _d=dstp, _s=(tau == 0), _e=(tau == 3):
                                 tn.matmul(_d, _l, _r, start=_s, stop=_e)) for tau in range(4)],
                               reads=(attnT, wo_b), writes=(PA,))
                p.dma("gpsimd", xt.ap, yin.ap[r * 128:(r + 1) * 128, :], reads=(yin,), writes=(xt,))
                for cc_ in range(2):
                    p.op("vector", lambda v, _c=cc_: v.tensor_tensor(
                        onat.ap[:, _c * 512:(_c + 1) * 512], PA.ap[:, _c * 512:(_c + 1) * 512],
                        xt.ap[:, _c * 512:(_c + 1) * 512], ALU.add),
                        reads=(PA, xt), writes=(onat,))
                p.dma("gpsimd", y_mid.ap[r * 128:(r + 1) * 128, :], onat.ap,
                      reads=(onat,), writes=(y_mid,))
                p.op("vector", lambda v, _r=r: v.tensor_reduce(statb.ap[:, 4 * _r:4 * _r + 1],
                                                               onat.ap[:, 0:512], AX.X, ALU.add),
                     reads=(onat,), writes=(statb,))
                p.op("vector", lambda v, _r=r: v.tensor_reduce(statb.ap[:, 4 * _r + 1:4 * _r + 2],
                                                               onat.ap[:, 512:1024], AX.X, ALU.add),
                     reads=(onat,), writes=(statb,))
                p.op("scalar", lambda s, _r=r: s.activation(sq_scr.ap[:, 0:512], onat.ap[:, 0:512],
                                                            AF.Square,
                                                            accum_out=statb.ap[:, 4 * _r + 2:4 * _r + 3]),
                     reads=(onat,), writes=(sq_scr, statb))
                p.op("scalar", lambda s, _r=r: s.activation(sq_scr.ap[:, 512:1024], onat.ap[:, 512:1024],
                                                            AF.Square,
                                                            accum_out=statb.ap[:, 4 * _r + 3:4 * _r + 4]),
                     reads=(onat,), writes=(sq_scr, statb))
            if B3_STOP_C2:
                continue
            p.mm_group([
                lambda tn: tn.matmul(ps_l.ap[:, 0:64], shm.ap[:, 0:128], statb.ap,
                                     start=True, stop=True),
                lambda tn: tn.matmul(ps_l.ap[:, 100:160], shm.ap[:, 256:384], statb.ap[:, 0:60],
                                     start=True, stop=True),
            ], reads=(statb, shm), writes=(ps_l,))
            p.op("vector", lambda v: v.tensor_copy(stsh.ap, ps_l.ap[:, 0:64]),
                 reads=(ps_l,), writes=(stsh,))
            p.op("vector", lambda v: v.tensor_tensor(stsh.ap[:, 4:64], stsh.ap[:, 4:64],
                                                     ps_l.ap[:, 100:160], ALU.add),
                 reads=(ps_l,), writes=(stsh,))
            inv = 1.0 / D
            for r in range(NT):
                p.op("vector", lambda v, _r=r: v.tensor_tensor(
                    mr.ap[:, 2:3], statb.ap[:, 4 * _r:4 * _r + 1], stsh.ap[:, 4 * _r + 1:4 * _r + 2],
                    ALU.add), reads=(statb, stsh), writes=(mr,))
                p.op("vector", lambda v, _r=r: v.tensor_scalar(mcols.ap[:, _r:_r + 1],
                                                               mr.ap[:, 2:3], inv, None, ALU.mult),
                     reads=(mr,), writes=(mcols,))
                p.op("vector", lambda v, _r=r: v.tensor_tensor(
                    mr.ap[:, 3:4], statb.ap[:, 4 * _r + 2:4 * _r + 3],
                    stsh.ap[:, 4 * _r + 3:4 * _r + 4], ALU.add),
                    reads=(statb, stsh), writes=(mr,))
                p.op("vector", lambda v: v.tensor_scalar(mr.ap[:, 3:4], mr.ap[:, 3:4], inv,
                                                         None, ALU.mult),
                     reads=(mr,), writes=(mr,))
                p.op("vector", lambda v, _r=r: v.tensor_tensor(mr.ap[:, 2:3], mcols.ap[:, _r:_r + 1],
                                                               mcols.ap[:, _r:_r + 1], ALU.mult),
                     reads=(mcols,), writes=(mr,))
                p.op("vector", lambda v: v.tensor_tensor(mr.ap[:, 3:4], mr.ap[:, 3:4], mr.ap[:, 2:3],
                                                         ALU.subtract),
                     reads=(mr,), writes=(mr,))
                p.op("scalar", lambda s: s.activation(mr.ap[:, 3:4], mr.ap[:, 3:4], AF.Ln,
                                                      bias=C.col(0)),
                     reads=(mr, C.cols), writes=(mr,))
                p.op("scalar", lambda s, _r=r: s.activation(rcols.ap[:, _r:_r + 1], mr.ap[:, 3:4],
                                                            AF.Exp, scale=C.col(1)),
                     reads=(mr, C.cols), writes=(rcols,))
            p.mm_group([
                lambda tn: tn.matmul(ps_l.ap[:, 160:176], shm.ap[:, 128:256], mcols.ap,
                                     start=True, stop=True),
                lambda tn: tn.matmul(ps_l.ap[:, 176:191], shm.ap[:, 384:512], mcols.ap[:, 1:16],
                                     start=True, stop=True),
                lambda tn: tn.matmul(ps_l.ap[:, 192:208], shm.ap[:, 128:256], rcols.ap,
                                     start=True, stop=True),
                lambda tn: tn.matmul(ps_l.ap[:, 208:223], shm.ap[:, 384:512], rcols.ap[:, 1:16],
                                     start=True, stop=True),
            ], reads=(mcols, rcols, shm), writes=(ps_l,))
            mshv = msh.ap.rearrange("p (r c) -> p r c", c=2)
            p.op("vector", lambda v: v.tensor_copy(mshv[:, :, 0:1],
                                                   ps_l.ap[:, 160:176].unsqueeze(2)),
                 reads=(ps_l,), writes=(msh,))
            p.op("vector", lambda v: v.tensor_copy(mshv[:, :, 1:2],
                                                   ps_l.ap[:, 192:208].unsqueeze(2)),
                 reads=(ps_l,), writes=(msh,))
            p.op("vector", lambda v: v.tensor_tensor(
                mshv[:, 0:15, 0:1], mshv[:, 0:15, 0:1],
                ps_l.ap[:, 176:191].unsqueeze(2), ALU.add),
                reads=(ps_l,), writes=(msh,))
            p.op("vector", lambda v: v.tensor_tensor(
                mshv[:, 0:15, 1:2], mshv[:, 0:15, 1:2],
                ps_l.ap[:, 208:223].unsqueeze(2), ALU.add),
                reads=(ps_l,), writes=(msh,))

            p.barrier()
            for t22 in range(22):
                cnt = 128 if t22 < 21 else 42
                for cc_ in range(2):
                    src = W["w2"][l][t22 * 128:t22 * 128 + cnt, cc_ * 512:(cc_ + 1) * 512]
                    p.dma("sync", wstage.ap[0:cnt, :], src, writes=(wstage,))
                    p.op("vector", lambda v, _t=t22, _c=cc_, _n=cnt: v.tensor_copy(
                        w2_b.ap[0:_n, _t * 1024 + _c * 512:_t * 1024 + (_c + 1) * 512],
                        wstage.ap[0:_n, :]),
                        reads=(wstage,), writes=(w2_b,))
            p.dma("sync", scrB.ap[0:1, 0:1024], W["b2"][l], writes=(scrB,))
            p.op("vector", lambda v: v.tensor_copy(b2b.ap, scrB.ap[0:1, 0:1024]),
                 reads=(scrB,), writes=(b2b,))
            for dt in range(8):
                p.op("vector", lambda v, _dt=dt: v.memset(h2T.ap[:, _dt * TCOL:_dt * TCOL + 1], 0.0),
                     writes=(h2T,))
            p.op("vector", lambda v: v.tensor_tensor(cell.ap[0:1, 2:3], mcols.ap[0:1, 0:1],
                                                     rcols.ap[0:1, 0:1], ALU.mult),
                 reads=(mcols, rcols), writes=(cell,))
            p.op("vector", lambda v: v.tensor_scalar(cell.ap[0:1, 2:3], cell.ap[0:1, 2:3],
                                                     -1.0, None, ALU.mult),
                 reads=(), writes=(cell,))
            p.op("vector", lambda v: v.tensor_copy(cellb.ap[0:1, 1:2], cell.ap[0:1, 2:3]),
                 reads=(cell,), writes=(cellb,))
            p.mm_group([lambda tn: tn.matmul(ps_r.ap[:, 0:1], C.onesrow.ap, cellb.ap[0:1, 1:2],
                                             start=True, stop=True)],
                       reads=(cellb, C.onesrow), writes=(ps_r,))
            for dt in range(4, 8):
                p.op("vector", lambda v, _dt=dt: v.tensor_copy(
                    h2T.ap[:, _dt * TCOL:_dt * TCOL + 1], ps_r.ap[:, 0:1]),
                    reads=(ps_r,), writes=(h2T,))
            for r in range(NT):
                p.dma("gpsimd", xt.ap, y_mid.ap[r * 128:(r + 1) * 128, :], reads=(y_mid,), writes=(xt,))
                p.op("vector", lambda v, _r=r: v.tensor_scalar(
                    lnbf.ap[:, 0:512], xt.ap[:, 0:512], mcols.ap[:, _r:_r + 1],
                    rcols.ap[:, _r:_r + 1], ALU.subtract, ALU.mult),
                    reads=(xt, mcols, rcols), writes=(lnbf,))
                p.op("vector", lambda v, _r=r: v.tensor_scalar(
                    lnbf.ap[:, 512:1024], xt.ap[:, 512:1024], msh.ap[:, 2 * _r:2 * _r + 1],
                    msh.ap[:, 2 * _r + 1:2 * _r + 2], ALU.subtract, ALU.mult),
                    reads=(xt, msh), writes=(lnbf,))
                transpose4(p, C, lnbf, lambda dt_: lnbf.ap[:, dt_ * 128:(dt_ + 1) * 128], ps_tr,
                           h2T, lambda dt_, _r=r: h2T.ap[:, dt_ * TCOL + 1 + _r * 128:
                                                         dt_ * TCOL + 1 + (_r + 1) * 128], 8)
            for ch in range(2):
                n0 = ch * 1024
                for t22 in range(22):
                    cnt = 128 if t22 < 21 else 42
                    la_ = lw1[(t22 % 2) * 2]
                    lg_ = lw1[(t22 % 2) * 2 + 1]
                    p.dma_multi("sync",
                                [(la_.ap[:, d8 * 128:d8 * 128 + cnt],
                                  w1bfL[l].ap[d8 * 128:(d8 + 1) * 128, t22 * 128:t22 * 128 + cnt])
                                 for d8 in range(8)], la_, reads=(w1bfL[l],))
                    p.dma_multi("sync",
                                [(lg_.ap[:, d8 * 128:d8 * 128 + cnt],
                                  w1bfL[l].ap[d8 * 128:(d8 + 1) * 128,
                                          2730 + t22 * 128:2730 + t22 * 128 + cnt])
                                 for d8 in range(8)], lg_, reads=(w1bfL[l],))
                    for half in range(2):
                        p.mm_group([(lambda tn, _l=la_.ap[:, d8 * 128:d8 * 128 + cnt],
                                     _r=shifted_lhs(h2T, d8, n0 + half * 512, 512),
                                     _d=PA.ap[0:cnt, 0:512] if half == 0 else PA.ap[0:cnt, 512:1024],
                                     _s=(d8 == 0), _e=(d8 == 7):
                                     tn.matmul(_d, _l, _r, start=_s, stop=_e))
                                    for d8 in range(8)],
                                   reads=(la_, h2T), writes=(PA,))
                        p.op("vector", lambda v, _t=t22, _n=cnt, _h=half, _b1=b1aL[l]:
                            v.tensor_scalar(
                            scrA.ap[0:_n, _h * 512:(_h + 1) * 512],
                            PA.ap[0:_n, _h * 512:(_h + 1) * 512],
                            _b1.ap[0:_n, _t:_t + 1], None, ALU.add),
                            reads=(PA, b1aL[l]), writes=(scrA,))
                    for half in range(2):
                        p.mm_group([(lambda tn, _l=lg_.ap[:, d8 * 128:d8 * 128 + cnt],
                                     _r=shifted_lhs(h2T, d8, n0 + half * 512, 512),
                                     _d=ps_l.ap[0:cnt, 0:512] if half == 0 else ps_l.ap[0:cnt, 512:1024],
                                     _s=(d8 == 0), _e=(d8 == 7):
                                     tn.matmul(_d, _l, _r, start=_s, stop=_e))
                                    for d8 in range(8)],
                                   reads=(lg_, h2T), writes=(ps_l,))
                        p.op("vector", lambda v, _t=t22, _n=cnt, _h=half, _b1=b1gL[l]:
                            v.tensor_scalar(
                            scrB.ap[0:_n, _h * 512:(_h + 1) * 512],
                            ps_l.ap[0:_n, _h * 512:(_h + 1) * 512],
                            _b1.ap[0:_n, _t:_t + 1], None, ALU.add),
                            reads=(ps_l, b1gL[l]), writes=(scrB,))
                    p.op("scalar", lambda s, _n=cnt: s.activation(scrB.ap[0:_n, :], scrB.ap[0:_n, :],
                                                                  AF.Gelu),
                         reads=(), writes=(scrB,))
                    p.op("vector", lambda v, _t=t22, _n=cnt: v.tensor_tensor(
                        h3seg.ap[0:_n, _t * 1024:(_t + 1) * 1024], scrA.ap[0:_n, :],
                        scrB.ap[0:_n, :], ALU.mult),
                        reads=(scrA, scrB), writes=(h3seg,))
                for nt in range(8):
                    r = ch * 8 + nt
                    wps = PA if nt % 2 == 0 else ps_l
                    p.dma("gpsimd", xt.ap, y_mid.ap[r * 128:(r + 1) * 128, :],
                          reads=(y_mid,), writes=(xt,))
                    for och in range(2):
                        dstp = wps.ap[:, och * 512:(och + 1) * 512]
                        calls = [(lambda tn, _l=h3seg.ap[0:(128 if t2 < 21 else 42),
                                                         t2 * 1024 + nt * 128:t2 * 1024 + (nt + 1) * 128],
                                  _r=w2_b.ap[0:(128 if t2 < 21 else 42),
                                             t2 * 1024 + och * 512:t2 * 1024 + (och + 1) * 512],
                                  _d=dstp, _s=(t2 == 0):
                                  tn.matmul(_d, _l, _r, start=_s, stop=False)) for t2 in range(22)]
                        calls.append(lambda tn, _d=dstp, _o=och: tn.matmul(
                            _d, C.onesrow.ap, b2b.ap[0:1, _o * 512:(_o + 1) * 512],
                            start=False, stop=True))
                        p.mm_group(calls, reads=(h3seg, w2_b, b2b, C.onesrow), writes=(wps,))
                    for och in range(2):
                        p.op("vector", lambda v, _o=och, _w=wps: v.tensor_tensor(
                            onat.ap[:, _o * 512:(_o + 1) * 512], _w.ap[:, _o * 512:(_o + 1) * 512],
                            xt.ap[:, _o * 512:(_o + 1) * 512], ALU.add),
                            reads=(wps, xt), writes=(onat,))
                    p.dma("gpsimd", ynext.ap[r * 128:(r + 1) * 128, :], onat.ap,
                          reads=(onat,), writes=(ynext,))
            p.barrier()
        for r in range(NT):
            row = b * SEQ + r * 128
            p.dma("gpsimd", yt2.ap, y_a.ap[r * 128:(r + 1) * 128, :], reads=(y_a,), writes=(yt2,))
            ln_tile(p, C, yt2, yt2.ap, mr, sq_scr)
            p.dma("sync", xt.ap, xr[row:row + 128, :], writes=(xt,))
            p.op("vector", lambda v: v.tensor_scalar(yt2.ap, yt2.ap, mr.ap[:, 1:2], mr.ap[:, 0:1],
                                                     ALU.subtract, ALU.mult),
                 reads=(mr,), writes=(yt2,))
            p.op("vector", lambda v: v.tensor_tensor(yt2.ap, yt2.ap, xt.ap, ALU.add),
                 reads=(xt,), writes=(yt2,))
            p.dma("gpsimd", xro[row:row + 128, :], yt2.ap, reads=(yt2,))


def rope_b3(p, C, qn, qbf, tabc, tabs_, qsb, which, r, nrm, sq_scr):
    hv = qn.ap.rearrange("p (h d) -> p h d", d=64)
    p.op("scalar", lambda s: s.activation(sq_scr.ap[:, 0:512], qn.ap, AF.Square),
         reads=(qn,), writes=(sq_scr,))
    p.op("vector", lambda v: v.tensor_reduce(nrm.ap, sq_scr.ap[:, 0:512]
                                             .rearrange("p (h d) -> p h d", d=64), AX.X, ALU.add),
         reads=(sq_scr,), writes=(nrm,))
    p.op("vector", lambda v: v.tensor_scalar(nrm.ap, nrm.ap, 1.0, 1e-24, ALU.mult, ALU.max),
         reads=(), writes=(nrm,))
    p.op("scalar", lambda s: s.activation(nrm.ap, nrm.ap, AF.Ln),
         reads=(), writes=(nrm,))
    p.op("scalar", lambda s: s.activation(nrm.ap, nrm.ap, AF.Exp, scale=C.col(1)),
         reads=(C.cols,), writes=(nrm,))
    p.op("vector", lambda v: v.tensor_tensor(hv, hv, nrm.ap.unsqueeze(2).to_broadcast([128, 8, 64]),
                                             ALU.mult),
         reads=(nrm,), writes=(qn,))
    sc = qsb.ap[:, which * 64:(which + 1) * 64].unsqueeze(1).to_broadcast([128, 8, 64])
    p.op("vector", lambda v, _s=sc: v.tensor_tensor(hv, hv, _s, ALU.mult),
         reads=(qsb,), writes=(qn,))
    cs = tabc.ap[:, r * 64:(r + 1) * 64]
    sn = tabs_.ap[:, r * 64:(r + 1) * 64]
    bv = qbf.ap.rearrange("p (h d) -> p h d", d=64)
    x1, x2 = hv[:, :, 0:32], hv[:, :, 32:64]
    c1 = cs[:, 0:32].unsqueeze(1).to_broadcast([128, 8, 32])
    c2 = cs[:, 32:64].unsqueeze(1).to_broadcast([128, 8, 32])
    s1 = sn[:, 0:32].unsqueeze(1).to_broadcast([128, 8, 32])
    s2 = sn[:, 32:64].unsqueeze(1).to_broadcast([128, 8, 32])
    b1_, b2_ = bv[:, :, 0:32], bv[:, :, 32:64]
    p.op("vector", lambda v, _a=b1_, _x=x2, _s=s1: v.tensor_tensor(_a, _x, _s, ALU.mult),
         reads=(qn, tabs_), writes=(qbf,))
    p.op("vector", lambda v, _a=b2_, _x=x1, _s=s2: v.tensor_tensor(_a, _x, _s, ALU.mult),
         reads=(qn, tabs_), writes=(qbf,))
    p.op("vector", lambda v, _x=x1, _c=c1: v.tensor_tensor(_x, _x, _c, ALU.mult),
         reads=(tabc,), writes=(qn,))
    p.op("vector", lambda v, _x=x2, _c=c2: v.tensor_tensor(_x, _x, _c, ALU.mult),
         reads=(tabc,), writes=(qn,))
    p.op("vector", lambda v, _a=b1_, _x=x1: v.tensor_tensor(_a, _x, _a, ALU.subtract),
         reads=(qn,), writes=(qbf,))
    p.op("vector", lambda v, _a=b2_, _x=x2: v.tensor_tensor(_a, _x, _a, ALU.add),
         reads=(qn,), writes=(qbf,))


def build_b3_host():
    DH, WIN = 64, 512
    half = np.arange(0, DH, 2, dtype=np.float64)
    inv_freq = 1.0 / 10000.0 ** (half / DH)
    t = np.arange(SEQ, dtype=np.float64)
    freqs = t[:, None] * inv_freq[None, :]
    base = (half + 0.4 * DH) / (1.4 * DH)
    power = (t - SEQ // 2) / (WIN // 2)
    scale = base[None, :] ** power[:, None]
    cosf, sinf = np.cos(freqs), np.sin(freqs)
    qc = np.concatenate([cosf * scale, cosf * scale], axis=1).astype(BF16NP)
    qs = np.concatenate([sinf * scale, sinf * scale], axis=1).astype(BF16NP)
    kc = np.concatenate([cosf / scale, cosf / scale], axis=1).astype(BF16NP)
    ks = np.concatenate([sinf / scale, sinf / scale], axis=1).astype(BF16NP)
    kk, qq = np.meshgrid(np.arange(128), np.arange(128), indexing="ij")
    tri = (kk <= qq).astype(BF16NP)
    anti = (kk >= qq).astype(BF16NP)
    ones = np.ones((128, 128), BF16NP)
    zeros = np.zeros((128, 128), BF16NP)
    m0 = np.concatenate([anti, zeros], axis=1)
    m1 = np.concatenate([ones, anti], axis=1)
    m4 = np.concatenate([tri, ones], axis=1)
    m5 = np.concatenate([zeros, tri], axis=1)
    masks = np.concatenate([m0, m1, m4, m5], axis=1)
    return (qc, qs, kc, ks), masks


def make_block3_drams(nc):
    mk = lambda n, s, d=F32: Buf(nc.dram_tensor(n, s, d).ap(), n)
    return {
        "y_a": mk("b3_y_a", [SEQ, D], BF16),
        "y_b": mk("b3_y_b", [SEQ, D], BF16),
        "y_mid": mk("b3_y_mid", [SEQ, D], BF16),
        "w1bf0": mk("b3_w1bf0", [1024, 5460], BF16),
        "w1bf1": mk("b3_w1bf1", [1024, 5460], BF16),
    }


def build_block3_nc():
    from contextlib import ExitStack
    nc = bass.Bass("TRN2", target_bir_lowering=False)
    x_in = nc.dram_tensor("x_in", [BPC, SEQ, D], F32, kind="ExternalInput").ap()
    x_out = nc.dram_tensor("x_out", [BPC, SEQ, D], F32, kind="ExternalOutput").ap()
    wd = {
        "wqkv": [nc.dram_tensor(f"wqkv{l}", [D, 1536], F32, kind="ExternalInput").ap() for l in range(2)],
        "wo": [nc.dram_tensor(f"wo{l}", [512, D], F32, kind="ExternalInput").ap() for l in range(2)],
        "w1": [nc.dram_tensor(f"w1_{l}", [D, 5460], F32, kind="ExternalInput").ap() for l in range(2)],
        "w2": [nc.dram_tensor(f"w2_{l}", [B3_HID, D], F32, kind="ExternalInput").ap() for l in range(2)],
        "b1": [nc.dram_tensor(f"b1_{l}", [1, 5460], F32, kind="ExternalInput").ap() for l in range(2)],
        "b2": [nc.dram_tensor(f"b2_{l}", [1, D], F32, kind="ExternalInput").ap() for l in range(2)],
        "lng": [nc.dram_tensor(f"lng{l}", [D], F32, kind="ExternalInput").ap() for l in range(2)],
        "lnb": [nc.dram_tensor(f"lnb{l}", [D], F32, kind="ExternalInput").ap() for l in range(2)],
        "qscale": [nc.dram_tensor(f"qsc{l}", [1, 64], F32, kind="ExternalInput").ap() for l in range(2)],
        "kscale": [nc.dram_tensor(f"ksc{l}", [1, 64], F32, kind="ExternalInput").ap() for l in range(2)],
        "xpos": [nc.dram_tensor(f"xpos{i}", [SEQ, 64], BF16, kind="ExternalInput").ap() for i in range(4)],
        "masks": nc.dram_tensor("b3masks", [128, 1024], BF16, kind="ExternalInput").ap(),
    }
    capd = {}
    cnp = build_constants()
    for k in ("ident", "onescol", "onesrow", "sel2"):
        capd[k] = nc.dram_tensor("cin_" + k, list(cnp[k].shape), BF16, kind="ExternalInput").ap()
    stack = ExitStack()
    p = Prog(nc)
    C = Consts(p, stack, capd)
    dram = make_block3_drams(nc)
    emit_block3(p, stack, C, x_in, x_out, wd, dram)
    p.barrier()
    p.emit(stack)
    stack.close()
    return nc, cnp


def block3_inmaps(inputs, x_full, cnp):
    xpos, masks = build_b3_host()
    base = {}
    for l in range(2):
        base[f"wqkv{l}"] = np.ascontiguousarray(inputs["L1_ffn_wqkv"][l])
        base[f"wo{l}"] = np.ascontiguousarray(inputs["L1_ffn_wo"][l])
        base[f"w1_{l}"] = np.ascontiguousarray(inputs["L1_ffn_w1"][l])
        base[f"w2_{l}"] = np.ascontiguousarray(inputs["L1_ffn_w2"][l])
        base[f"b1_{l}"] = np.ascontiguousarray(inputs["L1_ffn_b1"][l]).reshape(1, 5460)
        base[f"b2_{l}"] = np.ascontiguousarray(inputs["L1_ffn_b2"][l]).reshape(1, D)
        base[f"lng{l}"] = np.ascontiguousarray(inputs["L1_ffn_ln_g"][l])
        base[f"lnb{l}"] = np.ascontiguousarray(inputs["L1_ffn_ln_b"][l])
        base[f"qsc{l}"] = np.ascontiguousarray(inputs["L1_ffn_q_scale"][l]).reshape(1, 64)
        base[f"ksc{l}"] = np.ascontiguousarray(inputs["L1_ffn_k_scale"][l]).reshape(1, 64)
    for i in range(4):
        base[f"xpos{i}"] = xpos[i]
    base["b3masks"] = masks
    for k in ("ident", "onescol", "onesrow", "sel2"):
        base["cin_" + k] = cnp[k]
    in_maps = []
    for c in range(NCORES):
        m = dict(base)
        m["x_in"] = np.ascontiguousarray(x_full[c * BPC:(c + 1) * BPC])
        in_maps.append(m)
    return in_maps


def run_block3(inputs, x_full):
    nc, cnp = build_block3_nc()
    in_maps = block3_inmaps(inputs, x_full, cnp)
    res = run_bass_kernel_spmd(nc, in_maps, core_ids=list(range(NCORES)))
    return np.concatenate([r["x_out"].reshape(BPC, SEQ, D) for r in res.results], axis=0)


def build_fused_nc():
    from contextlib import ExitStack
    nc = bass.Bass("TRN2", target_bir_lowering=False)
    x0 = nc.dram_tensor("x_in", [BPC, SEQ, D], F32, kind="ExternalInput").ap()
    x_out = nc.dram_tensor("x_out", [BPC, SEQ, D], F32, kind="ExternalOutput").ap()
    x1 = nc.dram_tensor("fx1", [BPC, SEQ, D], F32).ap()
    x2 = nc.dram_tensor("fx2", [BPC, SEQ, D], F32).ap()
    x3 = nc.dram_tensor("fx3", [BPC, SEQ, D], F32).ap()
    w0 = {
        "Wq": nc.dram_tensor("Wq", [D, D], F32, kind="ExternalInput").ap(),
        "Wk": nc.dram_tensor("Wk", [D, 512], F32, kind="ExternalInput").ap(),
        "Wv": nc.dram_tensor("Wv", [D, 512], F32, kind="ExternalInput").ap(),
        "Wo": nc.dram_tensor("Wo", [D, D], F32, kind="ExternalInput").ap(),
        "bo": nc.dram_tensor("bo", [1, D], F32, kind="ExternalInput").ap(),
        "rope_cos": nc.dram_tensor("rope_cos", [SEQ, 16], F32, kind="ExternalInput").ap(),
        "rope_sin": nc.dram_tensor("rope_sin", [SEQ, 16], F32, kind="ExternalInput").ap(),
    }
    w1 = {
        "w_up_sl": nc.dram_tensor("w_up_sl", [3072, 128], F32, kind="ExternalInput").ap(),
        "w_gate_sl": nc.dram_tensor("w_gate_sl", [3072, 128], F32, kind="ExternalInput").ap(),
        "w_down_sl": nc.dram_tensor("w_down_sl", [1024, 384], F32, kind="ExternalInput").ap(),
    }
    w2 = {
        "wkT": nc.dram_tensor("wkT", [D, D], F32, kind="ExternalInput").ap(),
        "wvT": nc.dram_tensor("wvT", [D, D], F32, kind="ExternalInput").ap(),
        "wqT": nc.dram_tensor("wqT", [D, D], F32, kind="ExternalInput").ap(),
    }
    w3 = {
        "wqkv": [nc.dram_tensor(f"wqkv{l}", [D, 1536], F32, kind="ExternalInput").ap() for l in range(2)],
        "wo": [nc.dram_tensor(f"wo{l}", [512, D], F32, kind="ExternalInput").ap() for l in range(2)],
        "w1": [nc.dram_tensor(f"w1_{l}", [D, 5460], F32, kind="ExternalInput").ap() for l in range(2)],
        "w2": [nc.dram_tensor(f"w2_{l}", [B3_HID, D], F32, kind="ExternalInput").ap() for l in range(2)],
        "b1": [nc.dram_tensor(f"b1_{l}", [1, 5460], F32, kind="ExternalInput").ap() for l in range(2)],
        "b2": [nc.dram_tensor(f"b2_{l}", [1, D], F32, kind="ExternalInput").ap() for l in range(2)],
        "lng": [nc.dram_tensor(f"lng{l}", [D], F32, kind="ExternalInput").ap() for l in range(2)],
        "lnb": [nc.dram_tensor(f"lnb{l}", [D], F32, kind="ExternalInput").ap() for l in range(2)],
        "qscale": [nc.dram_tensor(f"qsc{l}", [1, 64], F32, kind="ExternalInput").ap() for l in range(2)],
        "kscale": [nc.dram_tensor(f"ksc{l}", [1, 64], F32, kind="ExternalInput").ap() for l in range(2)],
        "xpos": [nc.dram_tensor(f"xpos{i}", [SEQ, 64], BF16, kind="ExternalInput").ap() for i in range(4)],
        "masks": nc.dram_tensor("b3masks", [128, 1024], BF16, kind="ExternalInput").ap(),
    }
    cnp = build_constants()
    capd = {k: nc.dram_tensor("cin_" + k, list(cnp[k].shape), BF16, kind="ExternalInput").ap()
            for k in ("ident", "onescol", "onesrow", "sel2")}
    stack = ExitStack()
    p = Prog(nc)
    C = Consts(p, stack, capd)
    d1 = make_block1_drams(nc)
    d2 = make_block2_drams(nc)
    d3 = make_block3_drams(nc)
    sub = ExitStack()
    emit_block0(p, sub, C, x0, x1, w0)
    p.barrier()
    sub.close()
    sub = ExitStack()
    emit_block1(p, sub, C, x1, x2, w1, d1)
    p.barrier()
    sub.close()
    sub = ExitStack()
    emit_block2(p, sub, C, x2, x3, w2, d2)
    p.barrier()
    sub.close()
    sub = ExitStack()
    emit_block3(p, sub, C, x3, x_out, w3, d3)
    p.barrier()
    sub.close()
    p.emit(stack)
    stack.close()
    return nc, cnp


def fused_inmaps(inputs, cnp):
    x = np.ascontiguousarray(inputs["L0_attn_x"], dtype=np.float32)
    m0 = prep_block0_host(inputs)
    wu = np.ascontiguousarray(inputs["L0_ffn_w_up"])
    wg = np.ascontiguousarray(inputs["L0_ffn_w_gate"])
    wdn = np.ascontiguousarray(inputs["L0_ffn_w_down"])
    wk = np.ascontiguousarray(inputs["L1_attn_W_k"].T)
    wv = np.ascontiguousarray(inputs["L1_attn_W_v"].T)
    wq = np.ascontiguousarray(inputs["L1_attn_W_q"].T)
    xpos, masks = build_b3_host()
    base = dict(m0)
    base.update({"wkT": wk, "wvT": wv, "wqT": wq, "b3masks": masks,
                 "rope_cos": cnp["rope_cos"], "rope_sin": cnp["rope_sin"]})
    for i in range(4):
        base[f"xpos{i}"] = xpos[i]
    for l in range(2):
        base[f"wqkv{l}"] = np.ascontiguousarray(inputs["L1_ffn_wqkv"][l])
        base[f"wo{l}"] = np.ascontiguousarray(inputs["L1_ffn_wo"][l])
        base[f"w1_{l}"] = np.ascontiguousarray(inputs["L1_ffn_w1"][l])
        base[f"w2_{l}"] = np.ascontiguousarray(inputs["L1_ffn_w2"][l])
        base[f"b1_{l}"] = np.ascontiguousarray(inputs["L1_ffn_b1"][l]).reshape(1, 5460)
        base[f"b2_{l}"] = np.ascontiguousarray(inputs["L1_ffn_b2"][l]).reshape(1, D)
        base[f"lng{l}"] = np.ascontiguousarray(inputs["L1_ffn_ln_g"][l])
        base[f"lnb{l}"] = np.ascontiguousarray(inputs["L1_ffn_ln_b"][l])
        base[f"qsc{l}"] = np.ascontiguousarray(inputs["L1_ffn_q_scale"][l]).reshape(1, 64)
        base[f"ksc{l}"] = np.ascontiguousarray(inputs["L1_ffn_k_scale"][l]).reshape(1, 64)
    for k in ("ident", "onescol", "onesrow", "sel2"):
        base["cin_" + k] = cnp[k]
    in_maps = []
    for c in range(NCORES):
        m = dict(base)
        m["x_in"] = np.ascontiguousarray(x[c * BPC:(c + 1) * BPC])
        m["w_up_sl"] = np.ascontiguousarray(wu[:, c * 128:(c + 1) * 128])
        m["w_gate_sl"] = np.ascontiguousarray(wg[:, c * 128:(c + 1) * 128])
        m["w_down_sl"] = np.ascontiguousarray(wdn[:, c * 384:(c + 1) * 384])
        in_maps.append(m)
    return in_maps


FUSED = True


def kernel(**inputs):
    if not FUSED:
        x = np.ascontiguousarray(inputs["L0_attn_x"], dtype=np.float32)
        x = run_block0(inputs, x)
        x = run_block1(inputs, x)
        x = run_block2(inputs, x)
        x = run_block3(inputs, x)
        return x
    nc, cnp = build_fused_nc()
    in_maps = fused_inmaps(inputs, cnp)
    res = run_bass_kernel_spmd(nc, in_maps, core_ids=list(range(NCORES)))
    return np.concatenate([r["x_out"].reshape(BPC, SEQ, D) for r in res.results], axis=0)


# version 19
# speedup vs baseline: 1.0245x; 1.0108x over previous
import numpy as np
import ml_dtypes

import concourse.bass as bass
import concourse.mybir as mybir
from concourse.bass_utils import run_bass_kernel_spmd
from concourse.aot_env import allow_ici_all_to_all

F32 = mybir.dt.float32
BF16 = mybir.dt.bfloat16
I32 = mybir.dt.int32
AF = mybir.ActivationFunctionType
ALU = mybir.AluOpType
AX = mybir.AxisListType
BF16NP = ml_dtypes.bfloat16

NCORES = 8
D = 1024
SEQ = 2048
BPC = 2
NT = SEQ // 128

ENGINES = ("sync", "scalar", "vector", "tensor", "gpsimd")


class Tok:
    __slots__ = ("sem", "val")

    def __init__(self, sem, val):
        self.sem = sem
        self.val = val


class Buf:
    __slots__ = ("ap", "wtok", "rtoks", "name")

    def __init__(self, ap, name=""):
        self.ap = ap
        self.wtok = None
        self.rtoks = []
        self.name = name


class Prog:
    def __init__(self, nc):
        self.nc = nc
        self.q = {e: [] for e in ENGINES}
        self.cnt = {}
        self.lastw = {e: {} for e in ENGINES}
        self.semnames = []
        for e in ENGINES:
            for pfx in ("c_", "d_"):
                self.cnt[pfx + e] = 0
                self.semnames.append(pfx + e)
        self.cnt["cc"] = 0
        self.semnames.append("cc")
        NSLOT = 28
        self.nslot = NSLOT
        for i in range(NSLOT):
            self.cnt[f"dq{i}"] = 0
            self.semnames.append(f"dq{i}")
        self.slot_pool = {"sync": list(range(0, 14)), "scalar": list(range(14, 20)),
                          "gpsimd": list(range(20, 26)), "vector": list(range(26, 28)),
                          "tensor": list(range(26, 28))}
        self.slot_rr = {e: 0 for e in ENGINES}
        self.sems = None
        self.ckpts = {e: [] for e in ENGINES}

    def _ckpt_for(self, q, val):
        for total, cval in self.ckpts[q]:
            if total >= val:
                return Tok("c_" + q, cval)
        total = self.cnt["d_" + q]
        assert total >= val
        self.cnt["c_" + q] += 1
        cval = self.cnt["c_" + q]

        def closure(eng, _tot=total, _q=q, _cv=cval):
            eng.wait_ge(self.sems["d_" + _q], _tot)
            eng.nop().then_inc(self.sems["c_" + _q], 1)
            eng.wait_ge(self.sems["c_" + _q], _cv)

        self.q[q].append(closure)
        self.ckpts[q].append((total, cval))
        return Tok("c_" + q, cval)

    def _collect(self, e, sem, tokval, reads, writes):
        deps = []
        for r in reads:
            if isinstance(r, Buf):
                if r.wtok is not None:
                    deps.append(r.wtok)
            elif r is not None:
                deps.append(r)
        for w in writes:
            if w.wtok is not None:
                deps.append(w.wtok)
            deps.extend(w.rtoks)
        deps2 = []
        for t in deps:
            if t.sem.startswith("d_"):
                q = t.sem[2:]
                if e == q:
                    deps2.append(Tok(t.sem, self.cnt["d_" + q]))
                else:
                    deps2.append(self._ckpt_for(q, t.val))
            else:
                deps2.append(t)
        deps = deps2
        lw = self.lastw[e]
        need = {}
        for t in deps:
            if t.sem == sem and t.val >= tokval:
                continue
            if t.val > max(lw.get(t.sem, 0), need.get(t.sem, 0)):
                need[t.sem] = t.val
        for s, v in need.items():
            lw[s] = v
        return need

    def _finish(self, tok, reads, writes):
        for r in reads:
            if isinstance(r, Buf):
                r.rtoks.append(tok)
        for w in writes:
            w.wtok = tok
            w.rtoks = []

    def op(self, e, fn, reads=(), writes=(), dma=False, cc=False):
        writes = [w for w in writes if w is not None]
        sem = "cc" if cc else (("d_" if dma else "c_") + e)
        amt = 16 if dma else 1
        self.cnt[sem] += amt
        tok = Tok(sem, self.cnt[sem])
        need = self._collect(e, sem, tok.val, reads, writes)

        selfwait = (e == "gpsimd" and not dma and not cc)

        def closure(eng, _fn=fn, _need=need, _sem=sem, _amt=amt, _sw=selfwait,
                    _val=tok.val):
            for s, v in _need.items():
                eng.wait_ge(self.sems[s], v)
            _fn(eng).then_inc(self.sems[_sem], _amt)
            if _sw:
                eng.wait_ge(self.sems[_sem], _val)

        self.q[e].append(closure)
        self._finish(tok, reads, writes)
        return tok

    def mm_group(self, calls, reads=(), writes=()):
        writes = [w for w in writes if w is not None]
        sem = "c_tensor"
        self.cnt[sem] += 1
        tok = Tok(sem, self.cnt[sem])
        need = self._collect("tensor", sem, tok.val, reads, writes)

        def closure(eng, _calls=list(calls), _need=need):
            for s, v in _need.items():
                eng.wait_ge(self.sems[s], v)
            inst = None
            for fn in _calls:
                inst = fn(eng)
            inst.then_inc(self.sems["c_tensor"], 1)

        self.q["tensor"].append(closure)
        self._finish(tok, reads, writes)
        return tok

    def dma(self, e, out, in_, reads=(), writes=()):
        return self.dma_group(e, [(out, in_)], reads=reads, writes=writes)

    def dma_group(self, e, pairs, reads=(), writes=()):
        writes = [w for w in writes if w is not None]
        pool = self.slot_pool[e]
        slot = pool[self.slot_rr[e] % len(pool)]
        self.slot_rr[e] += 1
        sem = f"dq{slot}"
        prev_end = self.cnt[sem]
        self.cnt[sem] += 16 * len(pairs)
        tok = Tok(sem, self.cnt[sem])
        need = self._collect(e, sem, tok.val, reads, writes)
        if prev_end > self.lastw[e].get(sem, 0):
            need = dict(need)
            need[sem] = max(need.get(sem, 0), prev_end)
            self.lastw[e][sem] = need[sem]

        def closure(eng, _pairs=list(pairs), _need=need, _sem=sem):
            for sn, v in _need.items():
                eng.wait_ge(self.sems[sn], v)
            for o, i in _pairs:
                eng.dma_start(out=o, in_=i).then_inc(self.sems[_sem], 16)

        self.q[e].append(closure)
        self._finish(tok, reads, writes)
        return tok

    def dma_multi(self, e, pairs, writebuf, reads=()):
        return self.dma_group(e, pairs, reads=reads, writes=(writebuf,))

    def barrier(self):
        for q in ENGINES:
            if self.cnt["d_" + q] > 0:
                self._ckpt_for(q, self.cnt["d_" + q])
        toks = [Tok(s, self.cnt[s]) for s in self.semnames
                if self.cnt[s] > 0 and not s.startswith("d_")]
        for e in ENGINES:
            lw = self.lastw[e]
            need = {}
            for t in toks:
                if t.sem == "c_" + e:
                    continue
                if t.val > lw.get(t.sem, 0):
                    need[t.sem] = t.val
            for s, v in need.items():
                lw[s] = v
            if need:
                def closure(eng, _need=need):
                    for s, v in _need.items():
                        eng.wait_ge(self.sems[s], v)
                self.q[e].append(closure)

    def tok(self, e):
        return Tok("c_" + e, self.cnt["c_" + e])

    def emit(self, stack):
        self.sems = {s: stack.enter_context(self.nc.semaphore(s)) for s in self.semnames}
        stack.enter_context(allow_ici_all_to_all())
        block = stack.enter_context(self.nc.Block())
        for e in ENGINES:
            items = self.q[e]
            if items:
                getattr(block, e)(lambda eng, _items=items: [f(eng) for f in _items])


def mk_sb(stack, nc, name, shape, dtype=F32):
    return Buf(stack.enter_context(nc.sbuf_tensor(name, shape, dtype))[:], name)


def mk_ps(stack, nc, name, shape, dtype=F32):
    return Buf(stack.enter_context(nc.psum_tensor(name, shape, dtype))[:], name)



class Consts:

    def __init__(self, p, stack, capd):
        nc = p.nc
        self.ident = mk_sb(stack, nc, "c_ident", [128, 128], BF16)
        self.onescol = mk_sb(stack, nc, "c_onescol", [128, 1], BF16)
        self.onesrow = mk_sb(stack, nc, "c_onesrow", [1, 128], BF16)
        self.onesrow33 = mk_sb(stack, nc, "c_onesrow33", [33, 128], BF16)
        self.sel2 = mk_sb(stack, nc, "c_sel2", [2, 128], BF16)
        self.cols = mk_sb(stack, nc, "c_cols", [128, 8])
        p.dma("sync", self.ident.ap, capd["ident"], writes=(self.ident,))
        p.dma("sync", self.onescol.ap, capd["onescol"], writes=(self.onescol,))
        p.dma("sync", self.onesrow.ap, capd["onesrow"], writes=(self.onesrow,))
        p.dma("sync", self.onesrow33.ap[32:33, :], capd["onesrow"], writes=(self.onesrow33,))
        p.op("vector", lambda vv: vv.memset(self.onesrow33.ap[0:32, :], 0.0), writes=(self.onesrow33,))
        p.dma("sync", self.sel2.ap, capd["sel2"], writes=(self.sel2,))
        vals = [1e-5, -0.5, -1.0, 1e-6, 1e-24, -100.0, 0.5, 1.0]
        for i, v in enumerate(vals):
            p.op("vector", lambda vv, _i=i, _v=v: vv.memset(self.cols.ap[:, _i:_i + 1], _v),
                 writes=(self.cols,))

    def col(self, i, n=128):
        return self.cols.ap[0:n, i:i + 1]


def ln_tile(p, C, xbuf, xap, mr, sq_scr, ncol=D, eps=1e-5):
    mrap = mr.ap
    p.op("vector", lambda v, _x=xap: v.tensor_reduce(mrap[:, 2:3], _x, AX.X, ALU.add),
         reads=(xbuf,), writes=(mr,))
    p.op("scalar", lambda s, _x=xap: s.activation(sq_scr.ap[:, 0:ncol], _x, AF.Square,
                                                  accum_out=mrap[:, 3:4]),
         reads=(xbuf,), writes=(sq_scr, mr))
    inv = 1.0 / ncol
    p.op("vector", lambda v: v.tensor_scalar(mrap[:, 1:2], mrap[:, 2:3], inv, None, ALU.mult),
         reads=(mr,), writes=(mr,))
    p.op("vector", lambda v: v.tensor_scalar(mrap[:, 3:4], mrap[:, 3:4], inv, None, ALU.mult),
         reads=(mr,), writes=(mr,))
    p.op("vector", lambda v: v.tensor_tensor(mrap[:, 2:3], mrap[:, 1:2], mrap[:, 1:2], ALU.mult),
         reads=(mr,), writes=(mr,))
    p.op("vector", lambda v: v.tensor_tensor(mrap[:, 3:4], mrap[:, 3:4], mrap[:, 2:3], ALU.subtract),
         reads=(mr,), writes=(mr,))
    epscol = C.col(0) if eps == 1e-5 else None
    assert epscol is not None
    p.op("scalar", lambda s: s.activation(mrap[:, 3:4], mrap[:, 3:4], AF.Ln, bias=epscol),
         reads=(mr, C.cols), writes=(mr,))
    p.op("scalar", lambda s: s.activation(mrap[:, 0:1], mrap[:, 3:4], AF.Exp, scale=C.col(1)),
         reads=(mr, C.cols), writes=(mr,))


def _make_strided_mid(d0, stride, cnt):
    return _ap_mid(d0, stride, cnt)


def _ap_mid(d0, stride, cnt):
    import bass_rust
    ap = d0.unsqueeze(1)
    lst = [list(x) for x in ap.ap]
    lst[1] = [stride, cnt]
    new = bass_rust.AP(tensor=ap.tensor, offset=ap.offset, ap=lst)
    return new


def transpose4(p, C, src_buf, src_ap_fn, ps_tr, dst_buf, dst_ap_fn, n=8):
    src_aps = [src_ap_fn(i) for i in range(n)]
    dst_aps = [dst_ap_fn(i) for i in range(n)]
    for g in range((n + 3) // 4):
        cnt = min(4, n - g * 4)
        for u in range(cnt):
            dt_ = g * 4 + u
            p.op("tensor", lambda tn, _s=src_aps[dt_], _u=u: tn.transpose(
                ps_tr.ap[:, _u * 128:(_u + 1) * 128], _s, C.ident.ap),
                reads=(src_buf, C.ident), writes=(ps_tr,))
        offs = [d.offset for d in dst_aps[g * 4:g * 4 + cnt]]
        uniform = (cnt > 1 and len(set(offs[i + 1] - offs[i] for i in range(cnt - 1))) == 1)
        if uniform:
            stride = offs[1] - offs[0]
            dv = _make_strided_mid(dst_aps[g * 4], stride, cnt)
            p.op("vector", lambda v, _d=dv, _c=cnt: v.tensor_copy(
                _d, ps_tr.ap[:, 0:_c * 128].rearrange("p (c f) -> p c f", f=128)),
                reads=(ps_tr,), writes=(dst_buf,))
        else:
            for u in range(cnt):
                dt_ = g * 4 + u
                p.op("vector", lambda v, _d=dst_aps[dt_], _u=u: v.tensor_copy(
                    _d, ps_tr.ap[:, _u * 128:(_u + 1) * 128]),
                    reads=(ps_tr,), writes=(dst_buf,))



def build_constants():
    c = {}
    c["ident"] = np.eye(128, dtype=BF16NP)
    c["onescol"] = np.ones((128, 1), dtype=BF16NP)
    c["onesrow"] = np.ones((1, 128), dtype=BF16NP)
    sel2 = np.zeros((2, 128), dtype=BF16NP)
    sel2[0, 0:64] = 1
    sel2[1, 64:128] = 1
    c["sel2"] = sel2
    ROT = 32
    inv_freq = 1.0 / 10000.0 ** (np.arange(0, ROT, 2, dtype=np.float64) / ROT)
    freqs = np.arange(SEQ, dtype=np.float64)[:, None] * inv_freq[None, :]
    c["rope_cos"] = np.cos(freqs).astype(np.float32)
    c["rope_sin"] = np.sin(freqs).astype(np.float32)
    return c


HEADPERM = []
for _m in range(4):
    HEADPERM += [4 * _m, 4 * _m + 2, 4 * _m + 1, 4 * _m + 3]



B0_PHASES = 4
B0_ATTN_NORM = True


def emit_block0(p, stack, C, x_in, x_out, W):
    nc = p.nc
    sb = lambda n, s, d=F32: mk_sb(stack, nc, n, s, d)
    ps = lambda n, s, d=F32: mk_ps(stack, nc, n, s, d)
    HEADS, KVH = 16, 8

    ropec = sb("b0_ropec", [128, NT * 16])
    ropes = sb("b0_ropes", [128, NT * 16])
    p.dma("sync", ropec.ap.rearrange("p (t f) -> p t f", f=16),
          W["rope_cos"].rearrange("(t p) f -> p t f", p=128), writes=(ropec,))
    p.dma("sync", ropes.ap.rearrange("p (t f) -> p t f", f=16),
          W["rope_sin"].rearrange("(t p) f -> p t f", p=128), writes=(ropes,))

    wq = sb("b0_wq", [128, 8 * D], BF16)
    wk = sb("b0_wk", [128, 8 * 512], BF16)
    wv = sb("b0_wv", [128, 8 * 512], BF16)
    wo = sb("b0_wo", [128, 8 * D], BF16)
    bo = sb("b0_bo", [1, D], BF16)
    wstage = sb("b0_wstage", [128, 512])

    def load_weight(dst, dram, cols, fold):
        for t in range(8):
            for c0 in range(0, cols, 512):
                w = min(512, cols - c0)
                src = dram[t * 128:(t + 1) * 128, c0:c0 + w]
                p.dma("sync", wstage.ap[:, 0:w], src, writes=(wstage,))
                p.op("vector", lambda v, _t=t, _c0=c0, _w=w: v.tensor_scalar(
                    dst.ap[:, _t * cols + _c0:_t * cols + _c0 + _w], wstage.ap[:, 0:_w],
                    fold, None, ALU.mult),
                    reads=(wstage,), writes=(dst,))

    load_weight(wq, W["Wq"], D, 0.125)
    load_weight(wk, W["Wk"], 512, 1.0)
    load_weight(wv, W["Wv"], 512, 1.0)
    load_weight(wo, W["Wo"], D, 1.0)
    for c0 in (0, 512):
        p.dma("sync", wstage.ap[0:1, 0:512], W["bo"][:, c0:c0 + 512], writes=(wstage,))
        p.op("vector", lambda v, _c0=c0: v.tensor_scalar(bo.ap[:, _c0:_c0 + 512],
                                                         wstage.ap[0:1, 0:512], 1.0, None, ALU.mult),
             reads=(wstage,), writes=(bo,))

    xtP = [sb("b0_xt0", [128, D]), sb("b0_xt1", [128, D])]
    sq_scr = sb("b0_sq", [128, D], BF16)
    mrP = [sb("b0_mr0", [128, 4]), sb("b0_mr1", [128, 4])]
    lnbfP = [sb("b0_lnbf0", [128, D], BF16), sb("b0_lnbf1", [128, D], BF16)]
    lnxT = sb("b0_lnxT", [128, 8 * SEQ], BF16)
    qnatP = [sb("b0_qnat0", [128, D]), sb("b0_qnat1", [128, D])]
    knatP = [sb("b0_knat0", [128, 512]), sb("b0_knat1", [128, 512])]
    qbfP = [sb("b0_qbf0", [128, D], BF16), sb("b0_qbf1", [128, D], BF16)]
    kbfP = [sb("b0_kbf0", [128, 512], BF16), sb("b0_kbf1", [128, 512], BF16)]
    qT = sb("b0_qT", [128, 8 * SEQ], BF16)
    kT = sb("b0_kT", [128, 4 * SEQ], BF16)
    vnat = sb("b0_vnat", [128, NT * 512], BF16)
    attnT = lnxT
    expab = [sb(f"b0_expab{i}", [128, 1024], BF16) for i in range(3)]
    acc_s = sb("b0_accs", [128, 1024])
    lnrec = sb("b0_lnrec", [33, 512])
    recipb = sb("b0_recipb", [33, 512], BF16)
    rsb = sb("b0_rsb", [128, 512])
    onatP = [sb("b0_onat0", [128, D]), sb("b0_onat1", [128, D])]

    P0 = ps("b0_P0", [128, 1024])
    P1 = ps("b0_P1", [128, 1024])
    ps_mm = [P0, P0]
    ps_tr = ps("b0_pstr", [128, 512], BF16)
    ps_l = [P0, P1]
    ps_av = ps("b0_psav", [128, 512])
    ps_s = ps("b0_pss", [33, 512])
    ps_r = ps("b0_psr", [128, 512])
    p.op("vector", lambda v: v.memset(ps_s.ap, 1.0), writes=(ps_s,))

    for b in range(BPC):
        xin_b = x_in[b]
        xout_b = x_out[b]
        for t in range(NT):
            xt, mr, lnbf = xtP[t % 2], mrP[t % 2], lnbfP[t % 2]
            p.dma("sync", xt.ap, xin_b[t * 128:(t + 1) * 128, :], writes=(xt,))
            ln_tile(p, C, xt, xt.ap, mr, sq_scr)
            p.op("vector", lambda v, _x=xt, _m=mr, _l=lnbf: v.tensor_scalar(
                _l.ap, _x.ap, _m.ap[:, 1:2], _m.ap[:, 0:1], ALU.subtract, ALU.mult),
                 reads=(xt, mr), writes=(lnbf,))
            transpose4(p, C, lnbf, lambda dt_, _l=lnbf: _l.ap[:, dt_ * 128:(dt_ + 1) * 128], ps_tr,
                       lnxT, lambda dt_, _t=t: lnxT.ap[:, dt_ * SEQ + _t * 128:dt_ * SEQ + (_t + 1) * 128], 8)

        def _qkv_proj(t):
            knat = knatP[t % 2]
            qnat = qnatP[t % 2]
            lh = [lnxT.ap[:, kt * SEQ + t * 128:kt * SEQ + (t + 1) * 128] for kt in range(8)]
            p.mm_group([(lambda tn, _l=lh[kt], _r=wk.ap[:, kt * 512:(kt + 1) * 512],
                         _s=(kt == 0), _e=(kt == 7):
                         tn.matmul(P0.ap[:, 0:512], _l, _r, start=_s, stop=_e)) for kt in range(8)],
                       reads=(lnxT, wk), writes=(P0,))
            p.op("vector", lambda v, _k=knat: v.tensor_copy(_k.ap, P0.ap[:, 0:512]),
                 reads=(P0,), writes=(knat,))
            p.mm_group([(lambda tn, _l=lh[kt], _r=wv.ap[:, kt * 512:(kt + 1) * 512],
                         _s=(kt == 0), _e=(kt == 7):
                         tn.matmul(P0.ap[:, 512:1024], _l, _r, start=_s, stop=_e)) for kt in range(8)],
                       reads=(lnxT, wv), writes=(P0,))
            p.op("vector", lambda v, _t=t: v.tensor_copy(vnat.ap[:, _t * 512:(_t + 1) * 512],
                                                         P0.ap[:, 512:1024]),
                 reads=(P0,), writes=(vnat,))
            for ch in range(2):
                dstp = P1.ap[:, ch * 512:(ch + 1) * 512]
                p.mm_group([(lambda tn, _l=lh[kt], _d=dstp,
                             _r=wq.ap[:, kt * D + ch * 512:kt * D + (ch + 1) * 512],
                             _s=(kt == 0), _e=(kt == 7):
                             tn.matmul(_d, _l, _r, start=_s, stop=_e)) for kt in range(8)],
                           reads=(lnxT, wq), writes=(P1,))
                p.op("vector", lambda v, _ch=ch, _d=dstp, _q=qnat: v.tensor_copy(
                    _q.ap[:, _ch * 512:(_ch + 1) * 512], _d),
                     reads=(P1,), writes=(qnat,))

        def _qkv_post(t):
            knat, kbf = knatP[t % 2], kbfP[t % 2]
            qnat, qbf = qnatP[t % 2], qbfP[t % 2]
            rope_apply(p, C, knat, kbf, ropec, ropes, t, KVH)
            transpose4(p, C, kbf, lambda dt_, _k=kbf: _k.ap[:, dt_ * 128:(dt_ + 1) * 128], ps_tr,
                       kT, lambda dt_, _t=t: kT.ap[:, dt_ * SEQ + _t * 128:dt_ * SEQ + (_t + 1) * 128], 4)
            rope_apply(p, C, qnat, qbf, ropec, ropes, t, HEADS)
            transpose4(p, C, qbf, lambda dt_, _q=qbf: _q.ap[:, dt_ * 128:(dt_ + 1) * 128], ps_tr,
                       qT, lambda dt_, _t=t: qT.ap[:, dt_ * SEQ + _t * 128:dt_ * SEQ + (_t + 1) * 128], 8)

        for t in range(NT if B0_PHASES >= 2 else 0):
            _qkv_proj(t)
            if t > 0:
                _qkv_post(t - 1)
        if B0_PHASES >= 2:
            _qkv_post(NT - 1)

        for m in range(4 if B0_PHASES >= 3 else 0):
            for jj in range(2):
                tq = 2 * m + jj
                for ch in range(4):
                    for kt in range(NT):
                        pl = ps_l[kt % 2]
                        exb = expab[kt % 3]
                        la = kT.ap[0:64, m * SEQ + kt * 128:m * SEQ + (kt + 1) * 128]
                        lb = kT.ap[64:128, m * SEQ + kt * 128:m * SEQ + (kt + 1) * 128]
                        ra = qT.ap[0:64, tq * SEQ + ch * 512:tq * SEQ + (ch + 1) * 512]
                        rb = qT.ap[64:128, tq * SEQ + ch * 512:tq * SEQ + (ch + 1) * 512]
                        p.mm_group(
                            [lambda tn, _la=la, _ra=ra, _pl=pl: tn.matmul(
                                _pl.ap[:, 0:512], _la, _ra, start=True, stop=True,
                                tile_position=(0, 0)),
                             lambda tn, _lb=lb, _rb=rb, _pl=pl: tn.matmul(
                                _pl.ap[:, 512:1024], _lb, _rb, start=True, stop=True,
                                tile_position=(64, 0))],
                            reads=(kT, qT), writes=(pl,))
                        p.op("scalar", lambda s, _pl=pl, _e=exb: s.activation(_e.ap, _pl.ap, AF.Exp),
                             reads=(pl,), writes=(exb,))
                        st_, sp_ = (kt == 0), (kt == NT - 1)
                        va = vnat.ap[:, kt * 512 + (2 * m) * 64:kt * 512 + (2 * m + 1) * 64]
                        vb = vnat.ap[:, kt * 512 + (2 * m + 1) * 64:kt * 512 + (2 * m + 2) * 64]
                        p.mm_group(
                            [lambda tn, _va=va, _s=st_, _e=sp_, _x=exb: tn.matmul(
                                ps_av.ap[0:64, :], _va, _x.ap[:, 0:512], start=_s, stop=_e,
                                tile_position=(0, 0)),
                             lambda tn, _vb=vb, _s=st_, _e=sp_, _x=exb: tn.matmul(
                                ps_av.ap[64:128, :], _vb, _x.ap[:, 512:1024], start=_s, stop=_e,
                                skip_group_check=True, tile_position=(0, 64))],
                            reads=(vnat, exb), writes=(ps_av,))
                        if st_:
                            p.op("vector", lambda v, _x=exb: v.tensor_copy(acc_s.ap, _x.ap),
                                 reads=(exb,), writes=(acc_s,))
                        else:
                            p.op("vector", lambda v, _x=exb: v.tensor_tensor(acc_s.ap, acc_s.ap,
                                                                             _x.ap, ALU.add),
                                 reads=(exb,), writes=(acc_s,))
                    if not B0_ATTN_NORM:
                        continue
                    p.op("vector", lambda v: v.tensor_copy(sq_scr.ap, acc_s.ap),
                         reads=(acc_s,), writes=(sq_scr,))
                    p.mm_group(
                        [lambda tn: tn.matmul(ps_s.ap[0:1, :], C.onescol.ap, sq_scr.ap[:, 0:512],
                                              start=True, stop=True, tile_position=(0, 0)),
                         lambda tn: tn.matmul(ps_s.ap[32:33, :], C.onescol.ap, sq_scr.ap[:, 512:1024],
                                              start=True, stop=True, skip_group_check=True,
                                              tile_position=(0, 32))],
                        reads=(sq_scr, C.onescol), writes=(ps_s,))
                    p.op("scalar", lambda s: s.activation(lnrec.ap, ps_s.ap, AF.Ln),
                         reads=(ps_s,), writes=(lnrec,))
                    p.op("scalar", lambda s: s.activation(recipb.ap, lnrec.ap, AF.Exp,
                                                          scale=C.col(2, 33)),
                         reads=(lnrec, C.cols), writes=(recipb,))
                    p.op("tensor", lambda tn: tn.matmul(P0.ap[:, 0:512], C.onesrow.ap,
                                                        recipb.ap[0:1, :], start=True, stop=True),
                         reads=(recipb, C.onesrow), writes=(P0,))
                    p.op("tensor", lambda tn: tn.matmul(P0.ap[:, 512:1024], C.onesrow33.ap[32:33, :],
                                                        recipb.ap[32:33, :], start=True, stop=True,
                                                        skip_group_check=True, tile_position=(32, 0)),
                         reads=(recipb, C.onesrow33), writes=(P0,))
                    p.op("vector", lambda v: v.tensor_copy(rsb.ap[0:64, :], P0.ap[0:64, 0:512]),
                         reads=(P0,), writes=(rsb,))
                    p.op("vector", lambda v: v.tensor_copy(rsb.ap[64:128, :], P0.ap[64:128, 512:1024]),
                         reads=(P0,), writes=(rsb,))
                    dst = attnT.ap[:, tq * SEQ + ch * 512:tq * SEQ + (ch + 1) * 512]
                    p.op("vector", lambda v, _d=dst: v.tensor_tensor(_d, ps_av.ap, rsb.ap, ALU.mult),
                         reads=(ps_av, rsb), writes=(attnT,))

        if B0_PHASES < 4:
            for t in range(NT):
                p.dma("sync", xt.ap, xin_b[t * 128:(t + 1) * 128, :], writes=(xt,))
                p.dma("gpsimd", x_out[b][t * 128:(t + 1) * 128, :], xt.ap, reads=(xt,))
            continue
        for t in range(NT):
            onat, mr, xt = onatP[t % 2], mrP[t % 2], xtP[t % 2]
            for ch in range(2):
                dstp = P1.ap[:, ch * 512:(ch + 1) * 512]
                calls = []
                for tau in range(8):
                    lhs = attnT.ap[:, tau * SEQ + t * 128:tau * SEQ + (t + 1) * 128]
                    rhs = wo.ap[:, tau * D + ch * 512:tau * D + (ch + 1) * 512]
                    calls.append(lambda tn, _l=lhs, _r=rhs, _d=dstp, _s=(tau == 0):
                                 tn.matmul(_d, _l, _r, start=_s, stop=False))
                calls.append(lambda tn, _d=dstp, _r=bo.ap[:, ch * 512:(ch + 1) * 512]:
                             tn.matmul(_d, C.onesrow.ap, _r, start=False, stop=True))
                p.mm_group(calls, reads=(attnT, wo, bo, C.onesrow), writes=(P1,))
            for ch in range(2):
                p.op("vector", lambda v, _ch=ch, _o=onat: v.tensor_copy(
                    _o.ap[:, _ch * 512:(_ch + 1) * 512], P1.ap[:, _ch * 512:(_ch + 1) * 512]),
                     reads=(P1,), writes=(onat,))
            ln_tile(p, C, onat, onat.ap, mr, sq_scr)
            p.dma("sync", xt.ap, xin_b[t * 128:(t + 1) * 128, :], writes=(xt,))
            p.op("vector", lambda v, _o=onat, _m=mr: v.tensor_scalar(
                _o.ap, _o.ap, _m.ap[:, 1:2], _m.ap[:, 0:1], ALU.subtract, ALU.mult),
                 reads=(mr,), writes=(onat,))
            p.op("vector", lambda v, _o=onat, _x=xt: v.tensor_tensor(_o.ap, _o.ap, _x.ap, ALU.add),
                 reads=(xt,), writes=(onat,))
            p.dma("gpsimd", xout_b[t * 128:(t + 1) * 128, :], onat.ap, reads=(onat,))


def rope_apply(p, C, nat, bfout, ropec, ropes, t, nheads):
    natv = nat.ap.rearrange("p (h d) -> p h d", d=64)
    bfv = bfout.ap[:, 0:nheads * 64].rearrange("p (h d) -> p h d", d=64)
    cosap = ropec.ap[:, t * 16:(t + 1) * 16].unsqueeze(1).to_broadcast([128, nheads, 16])
    sinap = ropes.ap[:, t * 16:(t + 1) * 16].unsqueeze(1).to_broadcast([128, nheads, 16])
    rotv = natv[:, :, 0:32].rearrange("p h (i two) -> p h i two", two=2)
    bfrot = bfv[:, :, 0:32].rearrange("p h (i two) -> p h i two", two=2)
    ev, ov = rotv[:, :, :, 0], rotv[:, :, :, 1]
    be, bo_ = bfrot[:, :, :, 0], bfrot[:, :, :, 1]
    p.op("vector", lambda v: v.tensor_tensor(be, ev, cosap, ALU.mult),
         reads=(nat, ropec), writes=(bfout,))
    p.op("vector", lambda v: v.tensor_tensor(bo_, ov, sinap, ALU.mult),
         reads=(nat, ropes), writes=(bfout,))
    p.op("vector", lambda v: v.tensor_tensor(be, be, bo_, ALU.subtract),
         reads=(bfout,), writes=(bfout,))
    p.op("vector", lambda v: v.tensor_tensor(bo_, ov, cosap, ALU.mult),
         reads=(nat, ropec), writes=(bfout,))
    p.op("vector", lambda v: v.tensor_tensor(ev, ev, sinap, ALU.mult),
         reads=(ropes,), writes=(nat,))
    p.op("vector", lambda v: v.tensor_tensor(bo_, bo_, ev, ALU.add),
         reads=(nat,), writes=(bfout,))
    p.op("vector", lambda v: v.tensor_copy(bfv[:, :, 32:64], natv[:, :, 32:64]),
         reads=(nat,), writes=(bfout,))



def prep_block0_host(inputs):
    Wq = np.ascontiguousarray(inputs["L0_attn_Wq"])
    Wo = np.ascontiguousarray(inputs["L0_attn_Wo"])
    Wq_p = np.empty_like(Wq)
    Wo_p = np.empty_like(Wo)
    for g, h in enumerate(HEADPERM):
        Wq_p[:, g * 64:(g + 1) * 64] = Wq[:, h * 64:(h + 1) * 64]
        Wo_p[g * 64:(g + 1) * 64, :] = Wo[h * 64:(h + 1) * 64, :]
    return {
        "Wq": Wq_p, "Wk": np.ascontiguousarray(inputs["L0_attn_Wk"]),
        "Wv": np.ascontiguousarray(inputs["L0_attn_Wv"]),
        "Wo": Wo_p, "bo": inputs["L0_attn_bo"].reshape(1, D),
    }


def build_block0_nc():
    from contextlib import ExitStack
    nc = bass.Bass("TRN2", target_bir_lowering=False)
    x_in = nc.dram_tensor("x_in", [BPC, SEQ, D], F32, kind="ExternalInput").ap()
    x_out = nc.dram_tensor("x_out", [BPC, SEQ, D], F32, kind="ExternalOutput").ap()
    wd = {
        "Wq": nc.dram_tensor("Wq", [D, D], F32, kind="ExternalInput").ap(),
        "Wk": nc.dram_tensor("Wk", [D, 512], F32, kind="ExternalInput").ap(),
        "Wv": nc.dram_tensor("Wv", [D, 512], F32, kind="ExternalInput").ap(),
        "Wo": nc.dram_tensor("Wo", [D, D], F32, kind="ExternalInput").ap(),
        "bo": nc.dram_tensor("bo", [1, D], F32, kind="ExternalInput").ap(),
        "rope_cos": nc.dram_tensor("rope_cos", [SEQ, 16], F32, kind="ExternalInput").ap(),
        "rope_sin": nc.dram_tensor("rope_sin", [SEQ, 16], F32, kind="ExternalInput").ap(),
    }
    capd = {}
    cnp = build_constants()
    for k in ("ident", "onescol", "onesrow", "sel2"):
        capd[k] = nc.dram_tensor("cin_" + k, list(cnp[k].shape), BF16, kind="ExternalInput").ap()
    stack = ExitStack()
    p = Prog(nc)
    C = Consts(p, stack, capd)
    emit_block0(p, stack, C, x_in, x_out, wd)
    p.barrier()
    p.emit(stack)
    stack.close()
    return nc, cnp


def run_block0(inputs, x_full):
    nc, cnp = build_block0_nc()
    wh = prep_block0_host(inputs)
    in_maps = []
    for c in range(NCORES):
        m = {"x_in": np.ascontiguousarray(x_full[c * BPC:(c + 1) * BPC])}
        m.update(wh)
        m["rope_cos"] = cnp["rope_cos"]
        m["rope_sin"] = cnp["rope_sin"]
        for k in ("ident", "onescol", "onesrow", "sel2"):
            m["cin_" + k] = cnp[k]
        in_maps.append(m)
    res = run_bass_kernel_spmd(nc, in_maps, core_ids=list(range(NCORES)))
    return np.concatenate([r["x_out"] for r in res.results], axis=0)







def emit_block1(p, stack, C, x_in, x_out, W, dram):
    nc = p.nc
    sb = lambda n, s, d=F32: mk_sb(stack, nc, n, s, d)
    ps = lambda n, s, d=F32: mk_ps(stack, nc, n, s, d)
    ROWS = BPC * SEQ
    NIT = 24
    NRT = ROWS // 128
    xr = x_in.rearrange("b s d -> (b s) d")
    xro = x_out.rearrange("b s d -> (b s) d")

    P0 = ps("b1_P0", [128, 1024])
    P1 = ps("b1_P1", [128, 1024])
    ps_tr = ps("b1_pstr", [128, 512], BF16)
    ps_sc = ps("b1_pssc", [128, 8])

    xt = sb("b1_xt", [128, D])
    sq_scr = sb("b1_sq", [128, D], BF16)
    mr = sb("b1_mr", [128, 4])
    lnbf = sb("b1_lnbf", [128, D], BF16)
    lnT = sb("b1_lnT", [128, 8 * ROWS], BF16)
    amx = sb("b1_amx", [128, 8])
    arow = sb("b1_arow", [1, 128])
    asc = sb("b1_asc", [1, 16])
    ascb = sb("b1_ascb", [1, 16], BF16)
    scc = sb("b1_scc", [128, 12])
    wtile = sb("b1_wtile", [128, 512])
    dqs = sb("b1_dqs", [128, 32])
    dqbf = sb("b1_dqbf", [128, 1024], BF16)
    lhsb = [sb(f"b1_lhs{i}", [128, 8 * 128], BF16) for i in range(2)]
    evacP = [sb("b1_evac", [128, 4096]), sb("b1_evacB", [128, 4096])]
    evac2 = sb("b1_evac2", [128, 4096])
    evac2P = [evac2, evac2]
    evac = evacP[0]
    scrF = sb("b1_scrF", [128, 4096])
    scrG = sb("b1_scrG", [128, 4096])
    scrI = Buf(stack.enter_context(nc.sbuf_tensor("b1_scrI", [128, 4096], I32))[:], "b1_scrI")
    obf = sb("b1_obf", [128, 4096], BF16)
    _osl0 = sb("b1_osl0", [128, 24 * 256], BF16)
    osl = [_osl0, _osl0]
    onat = sb("b1_onat", [128, D])
    dqd = lnT

    dq_loc = dram["dq_loc"]; ag_ug = dram["ag_ug"]
    dq_dn_loc = dram["dq_dn_loc"]; ag_dn = dram["ag_dn"]
    raw_up = dram["raw_up"]; raw_gate = dram["raw_gate"]
    t_sp = dram["t_sp"]; u_sp = dram["u_sp"]; p_sp = dram["p_sp"]; o_sp = dram["o_sp"]
    ccs = [dram[f"cc{i}_in"] for i in range(4)]
    cco = [dram[f"cc{i}_out"] for i in range(4)]

    def amax_finalize(col_i, dst_col):
        p.dma("scalar", arow.ap, amx.ap[:, col_i:col_i + 1], reads=(amx,), writes=(arow,))
        p.op("vector", lambda v, _d=dst_col: v.tensor_reduce(asc.ap[0:1, _d:_d + 1], arow.ap,
                                                             AX.X, ALU.max),
             reads=(arow,), writes=(asc,))

    def allreduce_max(rnd, ncells):
        p.dma("gpsimd", ccs[rnd].ap, asc.ap[0:1, 0:16], reads=(asc,), writes=(ccs[rnd],))
        p.op("gpsimd", lambda g, _r=rnd: g.collective_compute(
            "AllReduce", ALU.max, replica_groups=[list(range(NCORES))],
            ins=[ccs[_r].ap], outs=[cco[_r].ap]),
            reads=(ccs[rnd],), writes=(cco[rnd],), cc=True)
        p.dma("sync", asc.ap[0:1, 0:16], cco[rnd].ap, reads=(cco[rnd],), writes=(asc,))

    def make_scales(src_col, rs_col, s_col, extra=1.0):
        p.op("vector", lambda v: v.tensor_scalar(asc.ap[0:1, 14:15], asc.ap[0:1, src_col:src_col + 1],
                                                 1.0 / 32767.0, 1e-12, ALU.mult, ALU.max),
             reads=(asc,), writes=(asc,))
        p.op("vector", lambda v: v.reciprocal(asc.ap[0:1, 15:16], asc.ap[0:1, 14:15]),
             reads=(asc,), writes=(asc,))
        p.op("vector", lambda v: v.tensor_copy(ascb.ap[0:1, 0:1], asc.ap[0:1, 15:16]),
             reads=(asc,), writes=(ascb,))
        p.mm_group([lambda tn: tn.matmul(ps_sc.ap[:, 0:1], C.onesrow.ap, ascb.ap[0:1, 0:1],
                                         start=True, stop=True)],
                   reads=(ascb, C.onesrow), writes=(ps_sc,))
        p.op("vector", lambda v, _rc=rs_col: v.tensor_copy(scc.ap[:, _rc:_rc + 1], ps_sc.ap[:, 0:1]),
             reads=(ps_sc,), writes=(scc,))
        p.op("vector", lambda v, _rc=rs_col, _sc=s_col: v.reciprocal(scc.ap[:, _sc:_sc + 1],
                                                                     scc.ap[:, _rc:_rc + 1]),
             reads=(scc,), writes=(scc,))
        if extra != 1.0:
            p.op("vector", lambda v, _sc=s_col, _x=extra: v.tensor_scalar(
                scc.ap[:, _sc:_sc + 1], scc.ap[:, _sc:_sc + 1], _x, None, ALU.mult),
                reads=(scc,), writes=(scc,))


    def _deq_tile(ncols):
        nb = ncols // 32
        wv = wtile.ap[:, 0:ncols].rearrange("p (b k) -> p b k", k=32)
        p.op("vector", lambda v: v.tensor_reduce(dqs.ap[:, 0:nb], wv, AX.X, ALU.max,
                                                 apply_absolute_value=True),
             reads=(wtile,), writes=(dqs,))
        p.op("vector", lambda v: v.tensor_scalar(dqs.ap[:, 0:nb], dqs.ap[:, 0:nb],
                                                 1.0 / 7.0, 1e-12, ALU.mult, ALU.max),
             reads=(dqs,), writes=(dqs,))
        p.op("vector", lambda v: v.reciprocal(dqs.ap[:, 16:16 + nb], dqs.ap[:, 0:nb]),
             reads=(dqs,), writes=(dqs,))
        rsb_ = dqs.ap[:, 16:16 + nb].unsqueeze(2).to_broadcast([128, nb, 32])
        sb_ = dqs.ap[:, 0:nb].unsqueeze(2).to_broadcast([128, nb, 32])
        iv = scrI.ap[:, 0:ncols]
        p.op("vector", lambda v: v.tensor_tensor(iv.rearrange("p (b k) -> p b k", k=32), wv, rsb_,
                                                 ALU.mult),
             reads=(wtile, dqs), writes=(scrI,))
        p.op("vector", lambda v: v.tensor_tensor(dqbf.ap[:, 0:ncols].rearrange("p (b k) -> p b k", k=32),
                                                 iv.rearrange("p (b k) -> p b k", k=32), sb_, ALU.mult),
             reads=(scrI, dqs), writes=(dqbf,))

    for wname, locrow in (("w_up_sl", 0), ("w_gate_sl", 1)):
        for it in range(NIT):
            src = W[wname][it * 128:(it + 1) * 128, :]
            p.dma("sync", wtile.ap[:, 0:128], src, writes=(wtile,))
            _deq_tile(128)
            p.op("tensor", lambda tn: tn.transpose(ps_tr.ap[:, 0:128], dqbf.ap[:, 0:128], C.ident.ap),
                 reads=(dqbf, C.ident), writes=(ps_tr,))
            p.op("vector", lambda v: v.tensor_copy(dqbf.ap[:, 512:640], ps_tr.ap[:, 0:128]),
                 reads=(ps_tr,), writes=(dqbf,))
            p.dma("gpsimd", dq_loc.ap[locrow * 128:(locrow + 1) * 128, it * 128:(it + 1) * 128],
                  dqbf.ap[:, 512:640], reads=(dqbf,), writes=(dq_loc,))
    for dt in range(8):
        src = W["w_down_sl"][dt * 128:(dt + 1) * 128, :]
        p.dma("sync", wtile.ap[:, 0:384], src, writes=(wtile,))
        _deq_tile(384)
        for u in range(3):
            p.op("tensor", lambda tn, _u=u: tn.transpose(ps_tr.ap[:, _u * 128:(_u + 1) * 128],
                                                         dqbf.ap[:, _u * 128:(_u + 1) * 128], C.ident.ap),
                 reads=(dqbf, C.ident), writes=(ps_tr,))
        for u in range(3):
            p.op("vector", lambda v, _u=u: v.tensor_copy(dqbf.ap[:, 512 + _u * 128:512 + (_u + 1) * 128],
                                                         ps_tr.ap[:, _u * 128:(_u + 1) * 128]),
                 reads=(ps_tr,), writes=(dqbf,))
        for u in range(3):
            p.dma("gpsimd", dq_dn_loc.ap[u * 128:(u + 1) * 128, dt * 128:(dt + 1) * 128],
                  dqbf.ap[:, 512 + u * 128:512 + (u + 1) * 128], reads=(dqbf,), writes=(dq_dn_loc,))
    p.op("gpsimd", lambda g: g.collective_compute(
        "AllGather", ALU.bypass, replica_groups=[list(range(NCORES))],
        ins=[dq_loc.ap], outs=[ag_ug.ap]),
        reads=(dq_loc,), writes=(ag_ug,), cc=True)
    p.op("gpsimd", lambda g: g.collective_compute(
        "AllGather", ALU.bypass, replica_groups=[list(range(NCORES))],
        ins=[dq_dn_loc.ap], outs=[ag_dn.ap]),
        reads=(dq_dn_loc,), writes=(ag_dn,), cc=True)

    p.op("vector", lambda v: v.memset(amx.ap, 0.0), writes=(amx,))
    p.op("vector", lambda v: v.memset(asc.ap, 0.0), writes=(asc,))
    for r in range(NRT):
        p.dma("sync", xt.ap, xr[r * 128:(r + 1) * 128, :], writes=(xt,))
        ln_tile(p, C, xt, xt.ap, mr, sq_scr)
        p.op("vector", lambda v: v.tensor_scalar(lnbf.ap, xt.ap, mr.ap[:, 1:2], mr.ap[:, 0:1],
                                                 ALU.subtract, ALU.mult),
             reads=(xt, mr), writes=(lnbf,))
        p.op("vector", lambda v: v.tensor_reduce(amx.ap[:, 1:2], lnbf.ap, AX.X, ALU.max,
                                                 apply_absolute_value=True),
             reads=(lnbf,), writes=(amx,))
        p.op("vector", lambda v: v.tensor_tensor(amx.ap[:, 0:1], amx.ap[:, 0:1], amx.ap[:, 1:2],
                                                 ALU.max),
             reads=(amx,), writes=(amx,))
        transpose4(p, C, lnbf, lambda dt_: lnbf.ap[:, dt_ * 128:(dt_ + 1) * 128], ps_tr,
                   lnT, lambda dt_, _r=r: lnT.ap[:, dt_ * ROWS + _r * 128:dt_ * ROWS + (_r + 1) * 128], 8)
    amax_finalize(0, 0)
    allreduce_max(0, 1)
    make_scales(0, 0, 1)

    for dt in range(8):
        seg = lnT.ap[:, dt * ROWS:(dt + 1) * ROWS]
        p.op("vector", lambda v, _s=seg: v.tensor_scalar(scrI.ap, _s, scc.ap[:, 0:1], None, ALU.mult),
             reads=(lnT, scc), writes=(scrI,))
        p.op("vector", lambda v, _s=seg: v.tensor_scalar(_s, scrI.ap, scc.ap[:, 1:2], None, ALU.mult),
             reads=(scrI, scc), writes=(lnT,))

    def _phaseB_it(wi, raw, acol, it, lb, evac):
        p.dma_multi("sync",
                    [(lb.ap[:, dt * 128:(dt + 1) * 128],
                      ag_ug.ap[256 * dt + wi * 128:256 * dt + wi * 128 + 128,
                               it * 128:(it + 1) * 128]) for dt in range(8)],
                    lb, reads=(ag_ug,))
        for rc in range(8):
            slot = rc % 4
            pb = (P0, P0, P1, P1)[slot]
            half = (0, 512, 0, 512)[slot]
            dst = pb.ap[:, half:half + 512]
            p.mm_group([(lambda tn, _l=lb.ap[:, dt2 * 128:(dt2 + 1) * 128],
                         _r=lnT.ap[:, dt2 * ROWS + rc * 512:dt2 * ROWS + (rc + 1) * 512],
                         _d=dst, _s=(dt2 == 0), _e=(dt2 == 7):
                         tn.matmul(_d, _l, _r, start=_s, stop=_e)) for dt2 in range(8)],
                       reads=(lb, lnT), writes=(pb,))
            p.op("vector", lambda v, _d=dst, _rc=rc, _e=evac: v.tensor_copy(
                _e.ap[:, _rc * 512:(_rc + 1) * 512], _d),
                reads=(pb,), writes=(evac,))
        p.op("vector", lambda v, _e=evac: v.tensor_reduce(amx.ap[:, 3:4], _e.ap, AX.X, ALU.max,
                                                          apply_absolute_value=True),
             reads=(evac,), writes=(amx,))
        p.op("vector", lambda v, _a=acol: v.tensor_tensor(amx.ap[:, _a:_a + 1],
                                                          amx.ap[:, _a:_a + 1],
                                                          amx.ap[:, 3:4], ALU.max),
             reads=(amx,), writes=(amx,))
        p.dma("gpsimd", raw.ap[it * 128:(it + 1) * 128, :], evac.ap, reads=(evac,), writes=(raw,))

    for wi, (raw, acol) in enumerate(((raw_up, 1), (raw_gate, 2))):
        for it in range(NIT):
            _phaseB_it(wi, raw, acol, it, lhsb[it % 2], evacP[it % 2])
    amax_finalize(1, 0)
    amax_finalize(2, 1)
    allreduce_max(1, 2)
    make_scales(0, 2, 3)
    make_scales(1, 4, 5)

    def _phaseC_it(it, evac, evac2):
        p.dma("sync", evac.ap, raw_gate.ap[it * 128:(it + 1) * 128, :], writes=(evac,))
        p.dma("sync", evac2.ap, raw_up.ap[it * 128:(it + 1) * 128, :], writes=(evac2,))
        p.op("vector", lambda v, _e=evac: v.tensor_scalar(scrI.ap, _e.ap, scc.ap[:, 4:5], None,
                                                          ALU.mult),
             reads=(evac, scc), writes=(scrI,))
        p.op("vector", lambda v: v.tensor_scalar(scrF.ap, scrI.ap, scc.ap[:, 5:6], None, ALU.mult),
             reads=(scrI, scc), writes=(scrF,))
        p.op("scalar", lambda s: s.activation(scrG.ap, scrF.ap, AF.Sigmoid),
             reads=(scrF,), writes=(scrG,))
        p.op("vector", lambda v: v.tensor_scalar(scrI.ap, scrG.ap, 65536.0, None, ALU.mult),
             reads=(scrG,), writes=(scrI,))
        p.op("vector", lambda v: v.tensor_scalar(scrG.ap, scrI.ap, 1.0, 65535.0, ALU.mult, ALU.min),
             reads=(scrI,), writes=(scrG,))
        p.op("vector", lambda v: v.tensor_tensor(scrF.ap, scrF.ap, scrG.ap, ALU.mult),
             reads=(scrG,), writes=(scrF,))
        p.op("vector", lambda v: v.tensor_reduce(amx.ap[:, 3:4], scrF.ap, AX.X, ALU.max,
                                                 apply_absolute_value=True),
             reads=(scrF,), writes=(amx,))
        p.op("vector", lambda v: v.tensor_tensor(amx.ap[:, 4:5], amx.ap[:, 4:5], amx.ap[:, 3:4],
                                                 ALU.max),
             reads=(amx,), writes=(amx,))
        p.dma("gpsimd", t_sp.ap[it * 128:(it + 1) * 128, :], scrF.ap, reads=(scrF,), writes=(t_sp,))
        p.op("vector", lambda v, _e=evac2: v.tensor_scalar(scrI.ap, _e.ap, scc.ap[:, 2:3], None,
                                                           ALU.mult),
             reads=(evac2, scc), writes=(scrI,))
        p.op("vector", lambda v, _e=evac2: v.tensor_scalar(_e.ap, scrI.ap, 1.0, None, ALU.mult),
             reads=(scrI,), writes=(evac2,))
        p.dma("gpsimd", u_sp.ap[it * 128:(it + 1) * 128, :], evac2.ap, reads=(evac2,), writes=(u_sp,))

    for it in range(NIT):
        _phaseC_it(it, evacP[it % 2], evac2P[it % 2])

    amax_finalize(4, 0)
    allreduce_max(2, 1)
    make_scales(0, 6, 7, extra=2.0 ** -16)

    def _phaseD_it(it, evac, evac2):
        p.dma("sync", evac.ap, t_sp.ap[it * 128:(it + 1) * 128, :], writes=(evac,))
        p.dma("sync", evac2.ap, u_sp.ap[it * 128:(it + 1) * 128, :], writes=(evac2,))
        p.op("vector", lambda v, _e=evac: v.tensor_scalar(scrI.ap, _e.ap, scc.ap[:, 6:7], None,
                                                          ALU.mult),
             reads=(evac, scc), writes=(scrI,))
        p.op("vector", lambda v, _e=evac2: v.scalar_tensor_tensor(scrF.ap, scrI.ap, 1.0, _e.ap,
                                                                  ALU.mult, ALU.mult),
             reads=(scrI, evac2), writes=(scrF,))
        p.op("vector", lambda v: v.tensor_reduce(amx.ap[:, 3:4], scrF.ap, AX.X, ALU.max,
                                                 apply_absolute_value=True),
             reads=(scrF,), writes=(amx,))
        p.op("vector", lambda v: v.tensor_tensor(amx.ap[:, 5:6], amx.ap[:, 5:6], amx.ap[:, 3:4],
                                                 ALU.max),
             reads=(amx,), writes=(amx,))
        p.dma("gpsimd", p_sp.ap[it * 128:(it + 1) * 128, :], scrF.ap, reads=(scrF,), writes=(p_sp,))

    for it in range(NIT):
        _phaseD_it(it, evacP[it % 2], evac2P[it % 2])

    amax_finalize(5, 0)
    allreduce_max(3, 1)
    p.op("vector", lambda v: v.tensor_tensor(asc.ap[0:1, 2:3], scc.ap[0:1, 7:8], scc.ap[0:1, 3:4],
                                             ALU.mult),
         reads=(scc,), writes=(asc,))
    p.op("vector", lambda v: v.tensor_tensor(asc.ap[0:1, 0:1], asc.ap[0:1, 0:1], asc.ap[0:1, 2:3],
                                             ALU.mult),
         reads=(asc,), writes=(asc,))
    make_scales(0, 8, 9)
    p.op("vector", lambda v: v.tensor_scalar(scc.ap[:, 10:11], scc.ap[:, 7:8], 1.0, None, ALU.mult),
         reads=(scc,), writes=(scc,))
    p.op("vector", lambda v: v.tensor_tensor(scc.ap[:, 10:11], scc.ap[:, 10:11], scc.ap[:, 3:4],
                                             ALU.mult),
         reads=(scc,), writes=(scc,))
    p.op("vector", lambda v: v.tensor_tensor(scc.ap[:, 8:9], scc.ap[:, 8:9], scc.ap[:, 10:11],
                                             ALU.mult),
         reads=(scc,), writes=(scc,))

    def _phaseE1_it(it, evac):
        p.dma("sync", evac.ap, p_sp.ap[it * 128:(it + 1) * 128, :], writes=(evac,))
        p.op("vector", lambda v, _e=evac: v.tensor_scalar(scrI.ap, _e.ap, scc.ap[:, 8:9], None,
                                                          ALU.mult),
             reads=(evac, scc), writes=(scrI,))
        p.op("vector", lambda v: v.tensor_copy(obf.ap, scrI.ap), reads=(scrI,), writes=(obf,))
        p.dma("gpsimd", o_sp.ap[it * 128:(it + 1) * 128, :], obf.ap, reads=(obf,), writes=(o_sp,))

    for it in range(NIT):
        _phaseE1_it(it, evacP[it % 2])

    p.dma_multi("sync",
                [(dqd.ap[:, it * 1024:(it + 1) * 1024], ag_dn.ap[it * 128:(it + 1) * 128, :])
                 for it in range(NIT)],
                dqd, reads=(ag_dn,))
    for rc in range(16):
        ob = osl[rc % 2]
        p.dma_multi("sync",
                    [(ob.ap[:, it * 256:(it + 1) * 256],
                      o_sp.ap[it * 128:(it + 1) * 128, rc * 256:(rc + 1) * 256])
                     for it in range(NIT)],
                    ob, reads=(o_sp,))
        for rt in range(2):
            for ch in range(2):
                slot = rt * 2 + ch
                pb = (P0, P0, P1, P1)[slot]
                half = (0, 512, 0, 512)[slot]
                dst = pb.ap[:, half:half + 512]
                p.mm_group([(lambda tn, _l=ob.ap[:, it2 * 256 + rt * 128:it2 * 256 + (rt + 1) * 128],
                             _r=dqd.ap[:, it2 * 1024 + ch * 512:it2 * 1024 + (ch + 1) * 512],
                             _d=dst, _s=(it2 == 0), _e=(it2 == 23):
                             tn.matmul(_d, _l, _r, start=_s, stop=_e)) for it2 in range(NIT)],
                           reads=(ob, dqd), writes=(pb,))
        for rt in range(2):
            r = rc * 2 + rt
            for ch in range(2):
                slot = rt * 2 + ch
                pb = (P0, P0, P1, P1)[slot]
                half = (0, 512, 0, 512)[slot]
                p.op("vector", lambda v, _pb=pb, _h=half, _c=ch: v.tensor_scalar(
                    onat.ap[:, _c * 512:(_c + 1) * 512], _pb.ap[:, _h:_h + 512],
                    scc.ap[:, 9:10], None, ALU.mult),
                    reads=(pb, scc), writes=(onat,))
            ln_tile(p, C, onat, onat.ap, mr, sq_scr)
            p.dma("sync", xt.ap, xr[r * 128:(r + 1) * 128, :], writes=(xt,))
            p.op("vector", lambda v: v.tensor_scalar(onat.ap, onat.ap, mr.ap[:, 1:2], mr.ap[:, 0:1],
                                                     ALU.subtract, ALU.mult),
                 reads=(mr,), writes=(onat,))
            p.op("vector", lambda v: v.tensor_tensor(onat.ap, onat.ap, xt.ap, ALU.add),
                 reads=(xt,), writes=(onat,))
            p.dma("gpsimd", xro[r * 128:(r + 1) * 128, :], onat.ap, reads=(onat,))


def make_block1_drams(nc):
    mk = lambda n, s, d=F32: Buf(nc.dram_tensor(n, s, d).ap(), n)
    dram = {
        "dq_loc": mk("b1_dq_loc", [256, 3072], BF16),
        "ag_ug": mk("b1_ag_ug", [2048, 3072], BF16),
        "dq_dn_loc": mk("b1_dq_dn_loc", [384, 1024], BF16),
        "ag_dn": mk("b1_ag_dn", [3072, 1024], BF16),
        "raw_up": mk("b1_raw_up", [3072, 4096]),
        "raw_gate": mk("b1_raw_gate", [3072, 4096]),
        "t_sp": mk("b1_t_sp", [3072, 4096]),
        "u_sp": mk("b1_u_sp", [3072, 4096]),
        "p_sp": mk("b1_p_sp", [3072, 4096]),
        "o_sp": mk("b1_o_sp", [3072, 4096], BF16),
    }
    for i in range(4):
        dram[f"cc{i}_in"] = mk(f"b1_cc{i}_in", [1, 16])
        dram[f"cc{i}_out"] = mk(f"b1_cc{i}_out", [1, 16])
    return dram


def build_block1_nc():
    from contextlib import ExitStack
    nc = bass.Bass("TRN2", target_bir_lowering=False)
    x_in = nc.dram_tensor("x_in", [BPC, SEQ, D], F32, kind="ExternalInput").ap()
    x_out = nc.dram_tensor("x_out", [BPC, SEQ, D], F32, kind="ExternalOutput").ap()
    wd = {
        "w_up_sl": nc.dram_tensor("w_up_sl", [3072, 128], F32, kind="ExternalInput").ap(),
        "w_gate_sl": nc.dram_tensor("w_gate_sl", [3072, 128], F32, kind="ExternalInput").ap(),
        "w_down_sl": nc.dram_tensor("w_down_sl", [1024, 384], F32, kind="ExternalInput").ap(),
    }
    capd = {}
    cnp = build_constants()
    for k in ("ident", "onescol", "onesrow", "sel2"):
        capd[k] = nc.dram_tensor("cin_" + k, list(cnp[k].shape), BF16, kind="ExternalInput").ap()
    stack = ExitStack()
    p = Prog(nc)
    C = Consts(p, stack, capd)
    dram = make_block1_drams(nc)
    emit_block1(p, stack, C, x_in, x_out, wd, dram)
    p.barrier()
    p.emit(stack)
    stack.close()
    return nc, cnp


def block1_inmaps(inputs, x_full, cnp):
    wu = np.ascontiguousarray(inputs["L0_ffn_w_up"])
    wg = np.ascontiguousarray(inputs["L0_ffn_w_gate"])
    wdn = np.ascontiguousarray(inputs["L0_ffn_w_down"])
    in_maps = []
    for c in range(NCORES):
        m = {"x_in": np.ascontiguousarray(x_full[c * BPC:(c + 1) * BPC]),
             "w_up_sl": np.ascontiguousarray(wu[:, c * 128:(c + 1) * 128]),
             "w_gate_sl": np.ascontiguousarray(wg[:, c * 128:(c + 1) * 128]),
             "w_down_sl": np.ascontiguousarray(wdn[:, c * 384:(c + 1) * 384])}
        for k in ("ident", "onescol", "onesrow", "sel2"):
            m["cin_" + k] = cnp[k]
        in_maps.append(m)
    return in_maps


def run_block1(inputs, x_full):
    nc, cnp = build_block1_nc()
    in_maps = block1_inmaps(inputs, x_full, cnp)
    res = run_bass_kernel_spmd(nc, in_maps, core_ids=list(range(NCORES)))
    return np.concatenate([r["x_out"].reshape(BPC, SEQ, D) for r in res.results], axis=0)


class _Rec:
    def __init__(self):
        self.ops = []

    def op(self, e, fn, reads=(), writes=(), dma=False, cc=False):
        self.ops.append(("op", e, fn, reads, writes))

    def dma(self, e, out, in_, reads=(), writes=()):
        self.ops.append(("dma", e, out, in_, reads, writes))

    def mm_group(self, calls, reads=(), writes=()):
        self.ops.append(("mm", calls, reads, writes))


def emit_block2(p, stack, C, x_in, x_out, W, dram):
    nc = p.nc
    sb = lambda n, s, d=F32: mk_sb(stack, nc, n, s, d)
    ps = lambda n, s, d=F32: mk_ps(stack, nc, n, s, d)
    T, BL = 16, 256
    xr = x_in.rearrange("b s d -> (b s) d")
    xro = x_out.rearrange("b s d -> (b s) d")

    P0 = ps("b2_P0", [128, 1024])
    P1 = ps("b2_P1", [128, 1024])
    UP = ps("b2_UP", [128, 1024])
    ps_tr = ps("b2_pstr", [128, 512], BF16)

    xt = sb("b2_xt", [128, D])
    sq_scr = sb("b2_sq", [128, D], BF16)
    mr = sb("b2_mr", [128, 4])
    lnbf = sb("b2_lnbf", [128, D], BF16)
    xTs = sb("b2_xTs", [128, 8 * 256], BF16)
    wkT = sb("b2_wkT", [128, 8 * D], BF16)
    wvT = sb("b2_wvT", [128, 8 * D], BF16)
    wqT = sb("b2_wqT", [128, 8 * D], BF16)
    wstage = sb("b2_wstage", [128, 512])
    S = [sb(f"b2_S{i}", [128, 8192], BF16) for i in range(2)]
    scr = sb("b2_scr", [128, 8192], BF16)
    kt = [sb(f"b2_kt{i}", [128, 1024]) for i in range(2)]
    vt = [sb(f"b2_vt{i}", [128, 1024]) for i in range(2)]
    qt = [sb(f"b2_qt{i}", [128, 1024], BF16) for i in range(2)]
    khat = sb("b2_khat", [128, 1024], BF16)
    kss = sb("b2_kss", [128, 128])
    ret = sb("b2_ret", [128, 1024])
    sqv = sb("b2_sqv", [128, 1024])
    sgv = sb("b2_sgv", [128, 1024])
    onat = sb("b2_onat", [128, D])
    scr2 = sb("b2_scr2", [128, 8192], BF16)
    khat2 = sb("b2_khat2", [128, 1024], BF16)
    kss2 = sb("b2_kss2", [128, 128])
    ret2 = sb("b2_ret2", [128, 1024])
    sqv2 = sb("b2_sqv2", [128, 1024])
    sgv2 = sb("b2_sgv2", [128, 1024])
    mr2 = sb("b2_mr2", [128, 4])
    scrL = [scr, scr2]; khatL = [khat, khat2]; kssL = [kss, kss2]
    retL = [ret, ret2]; sqvL = [sqv, sqv2]; sgvL = [sgv, sgv2]; mrL = [mr, mr2]

    a1_in = dram["a1_in"]; a1_out = dram["a1_out"]
    a2_in = dram["a2_in"]; a2_out = dram["a2_out"]

    def load_weight(dst, dname):
        for t8 in range(8):
            for c0 in (0, 512):
                src = W[dname][t8 * 128:(t8 + 1) * 128, c0:c0 + 512]
                p.dma("sync", wstage.ap[:, 0:512], src, writes=(wstage,))
                p.op("vector", lambda v, _t=t8, _c=c0: v.tensor_copy(
                    dst.ap[:, _t * D + _c:_t * D + _c + 512], wstage.ap[:, 0:512]),
                    reads=(wstage,), writes=(dst,))
    load_weight(wkT, "wkT")
    load_weight(wvT, "wvT")
    load_weight(wqT, "wqT")

    for b in range(BPC):
        for r in range(NT):
            row = b * SEQ + r * 128
            p.dma("sync", xt.ap, xr[row:row + 128, :], writes=(xt,))
            ln_tile(p, C, xt, xt.ap, mr, sq_scr)
            p.op("vector", lambda v: v.tensor_scalar(lnbf.ap, xt.ap, mr.ap[:, 1:2], mr.ap[:, 0:1],
                                                     ALU.subtract, ALU.mult),
                 reads=(xt, mr), writes=(lnbf,))
            dstrow = 512 * (r // 2) + 256 * b + 128 * (r % 2)
            p.dma("gpsimd", a1_in.ap[dstrow:dstrow + 128, :], lnbf.ap,
                  reads=(lnbf,), writes=(a1_in,))
    with allow_ici_all_to_all():
        p.op("gpsimd", lambda g: g.collective_compute(
            "AllToAll", ALU.bypass, replica_groups=[list(range(NCORES))],
            ins=[a1_in.ap], outs=[a1_out.ap]),
            reads=(a1_in,), writes=(a1_out,), cc=True)

    p.op("vector", lambda v: v.memset(S[0].ap, 0.0), writes=(S[0],))
    p.op("gpsimd", lambda g: g.memset(S[1].ap, 0.0), writes=(S[1],))

    def v4(buf):
        return buf.ap.rearrange("p (n i j) -> p n i j", i=8, j=8)

    def v3(buf):
        return buf.ap.rearrange("p (n j) -> p n j", j=8)

    for t in range(T):
        for bt in range(2):
            srow = 512 * (t // 2) + 256 * (t % 2) + 128 * bt
            p.dma("sync", lnbf.ap, a1_out.ap[srow:srow + 128, :], reads=(a1_out,), writes=(lnbf,))
            transpose4(p, C, lnbf, lambda dt_: lnbf.ap[:, dt_ * 128:(dt_ + 1) * 128], ps_tr,
                       xTs, lambda dt_: xTs.ap[:, dt_ * 256 + bt * 128:dt_ * 256 + bt * 128 + 128], 8)
        for bt in range(2):
            for dst, wsrc in ((kt[bt], wkT), (vt[bt], wvT), (qt[bt], wqT)):
                for ch in range(2):
                    pb = P0 if ch == 0 else P1
                    dd = pb.ap[:, 0:512]
                    p.mm_group([(lambda tn, _l=xTs.ap[:, d8 * 256 + bt * 128:d8 * 256 + bt * 128 + 128],
                                 _r=wsrc.ap[:, d8 * D + ch * 512:d8 * D + (ch + 1) * 512],
                                 _d=dd, _s=(d8 == 0), _e=(d8 == 7):
                                 tn.matmul(_d, _l, _r, start=_s, stop=_e)) for d8 in range(8)],
                               reads=(xTs, wsrc), writes=(pb,))
                    p.op("vector", lambda v, _d=dd, _c=ch, _o=dst: v.tensor_copy(
                        _o.ap[:, _c * 512:(_c + 1) * 512], _d),
                        reads=(pb,), writes=(dst,))
        recs = []
        for bt in range(2):
            q = _Rec()
            Sb = S[bt]
            _scr = scrL[bt]; _khat = khatL[bt]; _kss = kssL[bt]
            _ret = retL[bt]; _sqv = sqvL[bt]; _sgv = sgvL[bt]; _mr = mrL[bt]
            q.op("scalar", lambda s, _k=kt[bt], _o=_scr: s.activation(
                _o.ap[:, 0:1024], _k.ap, AF.Square),
                reads=(kt[bt],), writes=(_scr,))
            q.op("vector", lambda v, _i=_scr, _o=_kss: v.tensor_reduce(
                _o.ap, _i.ap[:, 0:1024].rearrange("p (n j) -> p n j", j=8), AX.X, ALU.add),
                reads=(_scr,), writes=(_kss,))
            q.op("scalar", lambda s, _o=_kss: s.activation(_o.ap, _o.ap, AF.Ln, bias=C.col(4)),
                 reads=(_kss, C.cols), writes=(_kss,))
            q.op("scalar", lambda s, _o=_kss: s.activation(_o.ap, _o.ap, AF.Exp, scale=C.col(6)),
                 reads=(_kss, C.cols), writes=(_kss,))
            q.op("scalar", lambda s, _o=_kss: s.activation(_o.ap, _o.ap, AF.Ln, bias=C.col(3)),
                 reads=(_kss, C.cols), writes=(_kss,))
            q.op("scalar", lambda s, _o=_kss: s.activation(_o.ap, _o.ap, AF.Exp, scale=C.col(2)),
                 reads=(_kss, C.cols), writes=(_kss,))
            q.op("vector", lambda v, _k=kt[bt], _o=_khat, _s=_kss: v.tensor_tensor(
                v3(_o), v3(_k), _s.ap.unsqueeze(2).to_broadcast([128, 128, 8]), ALU.mult),
                reads=(kt[bt], _kss), writes=(_khat,))
            khb = _khat.ap.rearrange("p (n j) -> p n j", j=8).unsqueeze(2).to_broadcast([128, 128, 8, 8])
            q.op("vector", lambda v, _S=Sb, _k=khb, _o=_scr: v.tensor_tensor(
                v4(_o), v4(_S), _k, ALU.mult),
                reads=(Sb, _khat), writes=(_scr,))
            q.op("vector", lambda v, _i=_scr, _o=_ret: v.tensor_reduce(
                _o.ap.rearrange("p (n i) -> p n i", i=8), v4(_i), AX.X, ALU.add),
                reads=(_scr,), writes=(_ret,))
            q.op("vector", lambda v, _v=vt[bt], _o=_ret: v.tensor_tensor(
                _o.ap, _v.ap, _o.ap, ALU.subtract),
                reads=(vt[bt],), writes=(_ret,))
            NS = 85
            q.op("vector", lambda v,
                 _o=_scr.ap[:, 0:NS * 64].rearrange("p (n i j) -> p n i j", i=8, j=8),
                 _d=_ret.ap[:, 0:NS * 8].rearrange("p (n i) -> p n i", i=8).unsqueeze(3).to_broadcast([128, NS, 8, 8]),
                 _k=_khat.ap[:, 0:NS * 8].rearrange("p (n j) -> p n j", j=8).unsqueeze(2).to_broadcast([128, NS, 8, 8]):
                 v.tensor_tensor(_o, _d, _k, ALU.mult),
                 reads=(_ret, _khat), writes=(_scr,))
            q.op("gpsimd", lambda g,
                 _o=_scr.ap[:, NS * 64:8192].rearrange("p (n i j) -> p n i j", i=8, j=8),
                 _d=_ret.ap[:, NS * 8:1024].rearrange("p (n i) -> p n i", i=8).unsqueeze(3).to_broadcast([128, 128 - NS, 8, 8]),
                 _k=_khat.ap[:, NS * 8:1024].rearrange("p (n j) -> p n j", j=8).unsqueeze(2).to_broadcast([128, 128 - NS, 8, 8]):
                 g.tensor_tensor(_o, _d, _k, ALU.mult),
                 reads=(_ret, _khat), writes=(_scr,))
            SA = 4288
            q.op("vector", lambda v, _S=Sb.ap, _P=_scr.ap: v.tensor_tensor(
                _S[:, 0:SA], _S[:, 0:SA], _P[:, 0:SA], ALU.add),
                reads=(_scr,), writes=(Sb,))
            q.op("gpsimd", lambda g, _S=Sb.ap, _P=_scr.ap: g.tensor_tensor(
                _S[:, SA:8192], _S[:, SA:8192], _P[:, SA:8192], ALU.add),
                reads=(_scr,), writes=(Sb,))
            q.op("scalar", lambda s, _S=Sb.ap: s.activation(_S, _S, AF.Tanh),
                 reads=(), writes=(Sb,))
            qbb = qt[bt].ap.rearrange("p (n j) -> p n j", j=8).unsqueeze(2).to_broadcast([128, 128, 8, 8])
            q.op("vector", lambda v, _S=Sb, _q=qbb, _o=_scr: v.tensor_tensor(
                v4(_o), v4(_S), _q, ALU.mult),
                reads=(Sb, qt[bt]), writes=(_scr,))
            q.op("vector", lambda v, _i=_scr, _o=_sqv: v.tensor_reduce(
                _o.ap.rearrange("p (n i) -> p n i", i=8), v4(_i), AX.X, ALU.add),
                reads=(_scr,), writes=(_sqv,))
            q.op("scalar", lambda s, _i=_sqv, _o=_sgv: s.activation(_o.ap, _i.ap, AF.Exp, scale=C.col(2)),
                 reads=(_sqv, C.cols), writes=(_sgv,))
            q.op("scalar", lambda s, _o=_sgv: s.activation(_o.ap, _o.ap, AF.Ln, bias=C.col(7)),
                 reads=(_sgv, C.cols), writes=(_sgv,))
            q.op("scalar", lambda s, _o=_sgv: s.activation(_o.ap, _o.ap, AF.Exp, scale=C.col(2)),
                 reads=(_sgv, C.cols), writes=(_sgv,))
            q.op("vector", lambda v, _a=_sgv, _b=_sqv: v.tensor_tensor(
                _a.ap, _a.ap, _b.ap, ALU.mult),
                reads=(_sqv,), writes=(_sgv,))
            q.op("vector", lambda v, _a=_sgv, _b=_sqv: v.tensor_tensor(
                _b.ap, _a.ap, _b.ap, ALU.mult),
                reads=(_sgv,), writes=(_sqv,))
            ln_tile(q, C, _sqv, _sqv.ap, _mr, _scr)
            q.op("vector", lambda v, _o=_sqv, _m=_mr: v.tensor_scalar(
                _o.ap, _o.ap, _m.ap[:, 1:2], _m.ap[:, 0:1], ALU.subtract, ALU.mult),
                reads=(_mr,), writes=(_sqv,))
            drow = 512 * (t // 2) + 256 * (t % 2) + 128 * bt
            q.dma("gpsimd", a2_in.ap[drow:drow + 128, :], _sqv.ap, reads=(_sqv,), writes=(a2_in,))
            recs.append(q.ops)
        for i in range(max(len(r_) for r_ in recs)):
            for r_ in recs:
                if i < len(r_):
                    it = r_[i]
                    if it[0] == "op":
                        p.op(it[1], it[2], reads=it[3], writes=it[4])
                    elif it[0] == "mm":
                        p.mm_group(it[1], reads=it[2], writes=it[3])
                    else:
                        p.dma(it[1], it[2], it[3], reads=it[4], writes=it[5])
    with allow_ici_all_to_all():
        p.op("gpsimd", lambda g: g.collective_compute(
            "AllToAll", ALU.bypass, replica_groups=[list(range(NCORES))],
            ins=[a2_in.ap], outs=[a2_out.ap]),
            reads=(a2_in,), writes=(a2_out,), cc=True)

    for b in range(BPC):
        for r in range(NT):
            row = b * SEQ + r * 128
            srow = 512 * (r // 2) + 256 * b + 128 * (r % 2)
            p.dma("sync", xt.ap, xr[row:row + 128, :], writes=(xt,))
            p.dma("sync", onat.ap, a2_out.ap[srow:srow + 128, :], reads=(a2_out,), writes=(onat,))
            p.op("vector", lambda v: v.tensor_tensor(onat.ap, onat.ap, xt.ap, ALU.add),
                 reads=(xt,), writes=(onat,))
            p.dma("gpsimd", xro[row:row + 128, :], onat.ap, reads=(onat,))


def make_block2_drams(nc):
    mk = lambda n, s, d=F32: Buf(nc.dram_tensor(n, s, d).ap(), n)
    return {
        "a1_in": mk("b2_a1_in", [8 * 512, 1024], BF16),
        "a1_out": mk("b2_a1_out", [8 * 512, 1024], BF16),
        "a2_in": mk("b2_a2_in", [8 * 512, 1024]),
        "a2_out": mk("b2_a2_out", [8 * 512, 1024]),
    }


def build_block2_nc():
    from contextlib import ExitStack
    nc = bass.Bass("TRN2", target_bir_lowering=False)
    x_in = nc.dram_tensor("x_in", [BPC, SEQ, D], F32, kind="ExternalInput").ap()
    x_out = nc.dram_tensor("x_out", [BPC, SEQ, D], F32, kind="ExternalOutput").ap()
    wd = {
        "wkT": nc.dram_tensor("wkT", [D, D], F32, kind="ExternalInput").ap(),
        "wvT": nc.dram_tensor("wvT", [D, D], F32, kind="ExternalInput").ap(),
        "wqT": nc.dram_tensor("wqT", [D, D], F32, kind="ExternalInput").ap(),
    }
    capd = {}
    cnp = build_constants()
    for k in ("ident", "onescol", "onesrow", "sel2"):
        capd[k] = nc.dram_tensor("cin_" + k, list(cnp[k].shape), BF16, kind="ExternalInput").ap()
    stack = ExitStack()
    p = Prog(nc)
    C = Consts(p, stack, capd)
    dram = make_block2_drams(nc)
    emit_block2(p, stack, C, x_in, x_out, wd, dram)
    p.barrier()
    p.emit(stack)
    stack.close()
    return nc, cnp


def block2_inmaps(inputs, x_full, cnp):
    wk = np.ascontiguousarray(inputs["L1_attn_W_k"].T)
    wv = np.ascontiguousarray(inputs["L1_attn_W_v"].T)
    wq = np.ascontiguousarray(inputs["L1_attn_W_q"].T)
    in_maps = []
    for c in range(NCORES):
        m = {"x_in": np.ascontiguousarray(x_full[c * BPC:(c + 1) * BPC]),
             "wkT": wk, "wvT": wv, "wqT": wq}
        for k in ("ident", "onescol", "onesrow", "sel2"):
            m["cin_" + k] = cnp[k]
        in_maps.append(m)
    return in_maps


def run_block2(inputs, x_full):
    nc, cnp = build_block2_nc()
    in_maps = block2_inmaps(inputs, x_full, cnp)
    res = run_bass_kernel_spmd(nc, in_maps, core_ids=list(range(NCORES)))
    return np.concatenate([r["x_out"].reshape(BPC, SEQ, D) for r in res.results], axis=0)


B3_HEADS, B3_DH, B3_WIN, B3_INNER, B3_HID = 8, 64, 512, 512, 2730
B3_LAYERS = 2
B3_STOP_C2 = False
B3_SKIP_W1PREP = False
B3_SKIP_ATTN = False


def emit_block3(p, stack, C, x_in, x_out, W, dram):
    nc = p.nc
    sb = lambda n, s, d=F32: mk_sb(stack, nc, n, s, d)
    ps = lambda n, s, d=F32: mk_ps(stack, nc, n, s, d)
    xr = x_in.rearrange("b s d -> (b s) d")
    xro = x_out.rearrange("b s d -> (b s) d")
    TCOL = 2049

    PA = ps("b3_PA", [128, 1024])
    ps_tr = ps("b3_pstr", [128, 512], BF16)
    ps_l = ps("b3_psl", [128, 1024])
    ps_av = ps("b3_psav", [128, 256])
    ps_s = ps("b3_pss", [33, 256])
    ps_r = ps_l

    AR = Buf(stack.enter_context(nc.sbuf_tensor("b3_AR", [128, 65544], BF16))[:], "b3_AR")
    def carve(off, n):
        return Buf(AR.ap[:, off:off + n], f"b3_ar_{off}"), off + n
    off = 0
    yT, off = carve(off, 8 * TCOL)
    qT, off = carve(off, 4 * SEQ)
    kT, off = carve(off, 4 * SEQ)
    vnat, off = carve(off, 16 * 512)
    attnT, off = carve(off, 4 * SEQ)
    wqkv, off = carve(off, 8 * 1536)
    wo_b, off = carve(off, 4 * 1024)
    assert off <= 65544, off
    off2 = 0
    h2T, off2 = carve(off2, 8 * TCOL)
    h3seg, off2 = carve(off2, 22 * 1024)
    w2_b, off2 = carve(off2, 22 * 1024)
    assert off2 <= 65544, off2

    xt = sb("b3_xt", [128, D])
    sq_scr = sb("b3_sq", [128, D], BF16)
    mr = sb("b3_mr", [128, 4])
    lnbf = sb("b3_lnbf", [128, D], BF16)
    qnP = [sb("b3_qn0", [128, 512]), sb("b3_qn1", [128, 512])]
    scrA = sb("b3_scrA", [128, 1024])
    scrB = sb("b3_scrB", [128, 1024])
    _qbf0 = sb("b3_qbf0", [128, 512], BF16)
    qbfP = [_qbf0, _qbf0]
    nrm = sb("b3_nrm", [128, 8])
    expb2_ = [sb("b3_expb0", [128, 512], BF16), sb("b3_expb1", [128, 512], BF16)]
    acc_s3 = sb("b3_accs", [128, 512], BF16)
    lnrec = sb("b3_lnrec", [33, 256])
    recipb = sb("b3_recipb", [33, 256], BF16)
    rsb = sb("b3_rsb", [128, 256])
    onat = sb("b3_onat", [128, D])
    yt2 = onat
    tabs = [sb(f"b3_tab{i}", [128, 16 * 64], BF16) for i in range(4)]
    masks = sb("b3_masks", [128, 4 * 256], BF16)
    qsb = sb("b3_qsb", [128, 128], BF16)
    statb = sb("b3_statb", [128, 64])
    stsh = sb("b3_stsh", [128, 64])
    mcols = sb("b3_mcols", [128, 16])
    rcols = sb("b3_rcols", [128, 16])
    msh = sb("b3_msh", [128, 32])
    shm = sb("b3_shm", [128, 512])

    lgc = sb("b3_lgc", [128, 8])
    lbb = sb("b3_lbb", [128, 8], BF16)
    wstage = sb("b3_wstage", [128, 512])
    wstage2 = sb("b3_wstage2", [128, 512])
    rawb = sb("b3_rawb", [128, 512], BF16)
    rawb2 = sb("b3_rawb2", [128, 512], BF16)
    lw1 = [sb(f"b3_lw1{i}", [128, 8 * 128], BF16) for i in range(4)]
    c2row = sb("b3_c2row", [1, 5504], BF16)
    b1aL = [sb(f"b3_b1a{i}", [128, 22]) for i in range(2)]
    b1gL = [sb(f"b3_b1g{i}", [128, 22]) for i in range(2)]
    b2b = sb("b3_b2b", [1, 1024], BF16)
    csh = sb("b3_csh", [128, 2])
    cell = sb("b3_cell", [1, 8])
    cellb = sb("b3_cellb", [1, 4], BF16)

    y_a = dram["y_a"]; y_b = dram["y_b"]; y_mid = dram["y_mid"]
    w1bfL = [dram["w1bf0"], dram["w1bf1"]]

    for i in range(4):
        p.dma("sync", tabs[i].ap.rearrange("p (t f) -> p t f", f=64),
              W["xpos"][i].rearrange("(t p) f -> p t f", p=128), writes=(tabs[i],))
    p.dma("sync", masks.ap, W["masks"], writes=(masks,))

    NLAYER = B3_LAYERS

    def shifted_lhs(tile, dt, c0, n):
        base = dt * TCOL + (1 + c0 if dt < 4 else c0)
        return tile.ap[:, base:base + n]

    p.op("vector", lambda v: v.memset(shm.ap, 0.0), writes=(shm,))
    p.dma("gpsimd", shm.ap[0:127, 1:128], C.ident.ap[0:127, 0:127], reads=(C.ident,), writes=(shm,))
    p.dma("gpsimd", shm.ap[1:128, 128:255], C.ident.ap[1:128, 1:128], reads=(C.ident,), writes=(shm,))
    p.dma("gpsimd", shm.ap[127:128, 256:257], C.ident.ap[0:1, 0:1], reads=(C.ident,), writes=(shm,))
    p.dma("gpsimd", shm.ap[0:1, 511:512], C.ident.ap[0:1, 0:1], reads=(C.ident,), writes=(shm,))
    for b in range(BPC):
        for r in range(NT):
            row = b * SEQ + r * 128
            p.dma("sync", xt.ap, xr[row:row + 128, :], writes=(xt,))
            ln_tile(p, C, xt, xt.ap, mr, sq_scr)
            p.op("vector", lambda v: v.tensor_scalar(yt2.ap, xt.ap, mr.ap[:, 1:2], mr.ap[:, 0:1],
                                                     ALU.subtract, ALU.mult),
                 reads=(xt, mr), writes=(yt2,))
            p.dma("gpsimd", y_a.ap[r * 128:(r + 1) * 128, :], yt2.ap, reads=(yt2,), writes=(y_a,))
        for l in range(NLAYER):
            yin = y_a if l == 0 else y_b
            ynext = y_b if l == 0 else y_a
            for dt in range(8):
                for cc_ in range(3):
                    src = W["wqkv"][l][dt * 128:(dt + 1) * 128, cc_ * 512:(cc_ + 1) * 512]
                    p.dma("sync", wstage.ap, src, writes=(wstage,))
                    p.op("vector", lambda v, _dt=dt, _c=cc_: v.tensor_copy(
                        wqkv.ap[:, _dt * 1536 + _c * 512:_dt * 1536 + (_c + 1) * 512], wstage.ap),
                        reads=(wstage,), writes=(wqkv,))
            for dt in range(4):
                for cc_ in range(2):
                    src = W["wo"][l][dt * 128:(dt + 1) * 128, cc_ * 512:(cc_ + 1) * 512]
                    p.dma("sync", wstage.ap, src, writes=(wstage,))
                    p.op("vector", lambda v, _dt=dt, _c=cc_: v.tensor_copy(
                        wo_b.ap[:, _dt * 1024 + _c * 512:_dt * 1024 + (_c + 1) * 512], wstage.ap),
                        reads=(wstage,), writes=(wo_b,))
            p.dma("sync", cell.ap[0:1, 0:8], W["qscale"][l][0:1, 0:8], writes=(cell,))
            p.dma("sync", scrA.ap[0:1, 0:64], W["qscale"][l], writes=(scrA,))
            p.dma("sync", scrA.ap[0:1, 64:128], W["kscale"][l], writes=(scrA,))
            p.op("vector", lambda v: v.tensor_scalar(scrA.ap[0:1, 0:64], scrA.ap[0:1, 0:64],
                                                     8.0, None, ALU.mult),
                 reads=(), writes=(scrA,))
            p.op("vector", lambda v: v.tensor_tensor(scrA.ap[0:1, 128:192], scrA.ap[0:1, 0:64],
                                                     scrA.ap[0:1, 64:128], ALU.mult),
                 reads=(), writes=(scrA,))
            p.op("vector", lambda v: v.tensor_reduce(cell.ap[0:1, 0:1], scrA.ap[0:1, 128:192],
                                                     AX.X, ALU.max, apply_absolute_value=True),
                 reads=(scrA,), writes=(cell,))
            p.op("vector", lambda v: v.tensor_scalar(cell.ap[0:1, 1:2], cell.ap[0:1, 0:1],
                                                     -1.0, None, ALU.mult),
                 reads=(cell,), writes=(cell,))
            p.op("vector", lambda v: v.tensor_copy(cellb.ap[0:1, 0:1], cell.ap[0:1, 1:2]),
                 reads=(cell,), writes=(cellb,))
            p.mm_group([lambda tn: tn.matmul(ps_r.ap[:, 0:1], C.onesrow.ap, cellb.ap[0:1, 0:1],
                                             start=True, stop=True)],
                       reads=(cellb, C.onesrow), writes=(ps_r,))
            p.op("vector", lambda v: v.tensor_copy(csh.ap[:, 0:1], ps_r.ap[:, 0:1]),
                 reads=(ps_r,), writes=(csh,))
            p.op("vector", lambda v: v.tensor_copy(cellb.ap[0:1, 0:4], cell.ap[0:1, 0:4]),
                 reads=(cell,), writes=(cellb,))
            p.op("vector", lambda v: v.tensor_copy(sq_scr.ap[0:1, 0:128], scrA.ap[0:1, 0:128]),
                 reads=(scrA,), writes=(sq_scr,))
            p.mm_group([lambda tn: tn.matmul(ps_r.ap[:, 0:128], C.onesrow.ap, sq_scr.ap[0:1, 0:128],
                                             start=True, stop=True)],
                       reads=(sq_scr, C.onesrow), writes=(ps_r,))
            p.op("vector", lambda v: v.tensor_copy(qsb.ap, ps_r.ap[:, 0:128]),
                 reads=(ps_r,), writes=(qsb,))
            for dt in range(8 if (not B3_SKIP_W1PREP and b == 0) else 0):
                p.dma("sync", lgc.ap[:, dt:dt + 1],
                      W["lng"][l][dt * 128:(dt + 1) * 128].unsqueeze(1), writes=(lgc,))
                p.dma("sync", scrB.ap[0:128, dt:dt + 1],
                      W["lnb"][l][dt * 128:(dt + 1) * 128].unsqueeze(1), writes=(scrB,))
            if b == 0:
                p.op("vector", lambda v: v.tensor_copy(lbb.ap, scrB.ap[:, 0:8]),
                     reads=(scrB,), writes=(lbb,))
            NCW = ([512] * 10 + [340]) if (not B3_SKIP_W1PREP and b == 0) else []
            for ci, cw in enumerate(NCW):
                c0 = ci * 512
                for dt in range(8):
                    ws_ = wstage if dt % 2 == 0 else wstage2
                    rb_ = rawb if dt % 2 == 0 else rawb2
                    src = W["w1"][l][dt * 128:(dt + 1) * 128, c0:c0 + cw]
                    p.dma("sync", ws_.ap[:, 0:cw], src, writes=(ws_,))
                    p.op("vector", lambda v, _cw=cw, _w=ws_, _b=rb_: v.tensor_copy(
                        _b.ap[:, 0:_cw], _w.ap[:, 0:_cw]),
                         reads=(ws_,), writes=(rb_,))
                    p.mm_group(
                        [lambda tn, _dt=dt, _cw=cw, _b=rb_: tn.matmul(
                            ps_s.ap[0:1, 0:min(_cw, 256)], lbb.ap[:, _dt:_dt + 1],
                            _b.ap[:, 0:min(_cw, 256)], start=(_dt == 0), stop=(_dt == 7)),
                         lambda tn, _dt=dt, _cw=cw, _b=rb_: tn.matmul(
                            ps_s.ap[32:33, 0:max(_cw - 256, 1)], lbb.ap[:, _dt:_dt + 1],
                            _b.ap[:, 256:256 + max(_cw - 256, 1)], start=(_dt == 0),
                            stop=(_dt == 7), skip_group_check=True, tile_position=(0, 32))],
                        reads=(rb_, lbb), writes=(ps_s,))
                    p.op("vector", lambda v, _dt=dt, _cw=cw, _w=ws_, _b=rb_: v.tensor_scalar(
                        _b.ap[:, 0:_cw], _w.ap[:, 0:_cw], lgc.ap[:, _dt:_dt + 1],
                        None, ALU.mult),
                        reads=(ws_, lgc), writes=(rb_,))
                    p.dma("gpsimd", w1bfL[l].ap[dt * 128:(dt + 1) * 128, c0:c0 + cw],
                          rb_.ap[:, 0:cw], reads=(rb_,), writes=(w1bfL[l],))
                p.op("vector", lambda v, _c0=c0, _cw=cw: v.tensor_copy(
                    c2row.ap[0:1, _c0:_c0 + min(_cw, 256)], ps_s.ap[0:1, 0:min(_cw, 256)]),
                    reads=(ps_s,), writes=(c2row,))
                p.op("vector", lambda v, _c0=c0, _cw=cw: v.tensor_copy(
                    c2row.ap[0:1, _c0 + 256:_c0 + 256 + max(_cw - 256, 1)],
                    ps_s.ap[32:33, 0:max(_cw - 256, 1)]),
                    reads=(ps_s,), writes=(c2row,))
            for c0 in (range(0, 5460, 1024) if (not B3_SKIP_W1PREP and b == 0) else ()):
                cw = min(1024, 5460 - c0)
                p.dma("sync", scrB.ap[0:1, 0:cw], W["b1"][l][0:1, c0:c0 + cw], writes=(scrB,))
                p.op("vector", lambda v, _c0=c0, _cw=cw: v.tensor_tensor(
                    c2row.ap[0:1, _c0:_c0 + _cw], c2row.ap[0:1, _c0:_c0 + _cw],
                    scrB.ap[0:1, 0:_cw], ALU.add),
                    reads=(scrB,), writes=(c2row,))
            for t22 in (range(22) if (not B3_SKIP_W1PREP and b == 0) else ()):
                cnt = 128 if t22 < 21 else 42
                p.dma("gpsimd", b1aL[l].ap[0:cnt, t22:t22 + 1],
                      c2row.ap[0:1, t22 * 128:t22 * 128 + cnt], reads=(c2row,), writes=(b1aL[l],))
                p.dma("gpsimd", b1gL[l].ap[0:cnt, t22:t22 + 1],
                      c2row.ap[0:1, 2730 + t22 * 128:2730 + t22 * 128 + cnt],
                      reads=(c2row,), writes=(b1gL[l],))

            for dt in range(8):
                p.op("vector", lambda v, _dt=dt: v.memset(yT.ap[:, _dt * TCOL:_dt * TCOL + 1], 0.0),
                     writes=(yT,))
            for r in range(NT):
                p.dma("sync", lnbf.ap, yin.ap[r * 128:(r + 1) * 128, :], reads=(yin,), writes=(lnbf,))
                transpose4(p, C, lnbf, lambda dt_: lnbf.ap[:, dt_ * 128:(dt_ + 1) * 128], ps_tr,
                           yT, lambda dt_, _r=r: yT.ap[:, dt_ * TCOL + 1 + _r * 128:
                                                       dt_ * TCOL + 1 + (_r + 1) * 128], 8)
            for r in range(NT):
                qn, qbf = qnP[r % 2], qbfP[r % 2]
                c0 = r * 128
                for cc_, dstb in ((0, None), (1, None), (2, None)):
                    dstp = PA.ap[:, cc_ * 256:cc_ * 256 + 512] if False else PA.ap[:, 0:512]
                for idx, (lo, evac_to) in enumerate(((0, qn), (512, qn), (1024, None))):
                    pass
                p.mm_group([(lambda tn, _l=shifted_lhs(yT, d8, c0, 128),
                             _r=wqkv.ap[:, d8 * 1536 + 0:d8 * 1536 + 512],
                             _s=(d8 == 0), _e=(d8 == 7):
                             tn.matmul(PA.ap[:, 0:512], _l, _r, start=_s, stop=_e))
                            for d8 in range(8)],
                           reads=(yT, wqkv), writes=(PA,))
                p.op("vector", lambda v, _q=qn: v.tensor_copy(_q.ap, PA.ap[:, 0:512]),
                     reads=(PA,), writes=(qn,))
                rope_b3(p, C, qn, qbf, tabs[0], tabs[1], qsb, 0, r, nrm, sq_scr)
                transpose4(p, C, qbf, lambda u: qbf.ap[:, u * 128:(u + 1) * 128], ps_tr,
                           qT, lambda u, _r=r: qT.ap[:, u * SEQ + _r * 128:u * SEQ + (_r + 1) * 128], 4)
                p.mm_group([(lambda tn, _l=shifted_lhs(yT, d8, c0, 128),
                             _r=wqkv.ap[:, d8 * 1536 + 512:d8 * 1536 + 1024],
                             _s=(d8 == 0), _e=(d8 == 7):
                             tn.matmul(PA.ap[:, 512:1024], _l, _r, start=_s, stop=_e))
                            for d8 in range(8)],
                           reads=(yT, wqkv), writes=(PA,))
                p.op("vector", lambda v, _q=qn: v.tensor_copy(_q.ap, PA.ap[:, 512:1024]),
                     reads=(PA,), writes=(qn,))
                rope_b3(p, C, qn, qbf, tabs[2], tabs[3], qsb, 1, r, nrm, sq_scr)
                transpose4(p, C, qbf, lambda u: qbf.ap[:, u * 128:(u + 1) * 128], ps_tr,
                           kT, lambda u, _r=r: kT.ap[:, u * SEQ + _r * 128:u * SEQ + (_r + 1) * 128], 4)
                p.mm_group([(lambda tn, _l=shifted_lhs(yT, d8, c0, 128),
                             _r=wqkv.ap[:, d8 * 1536 + 1024:d8 * 1536 + 1536],
                             _s=(d8 == 0), _e=(d8 == 7):
                             tn.matmul(PA.ap[:, 0:512], _l, _r, start=_s, stop=_e))
                            for d8 in range(8)],
                           reads=(yT, wqkv), writes=(PA,))
                p.op("vector", lambda v, _r=r: v.tensor_copy(vnat.ap[:, _r * 512:(_r + 1) * 512],
                                                             PA.ap[:, 0:512]),
                     reads=(PA,), writes=(vnat,))
            p.op("vector", lambda v: v.memset(ps_s.ap, 1.0), writes=(ps_s,))
            for u in range(4 if not B3_SKIP_ATTN else 0):
                for ch in range(8):
                    kts = [kt_ for kt_ in range(2 * ch - 4, 2 * ch + 2) if kt_ >= 0]
                    first = True
                    for kt_ in kts:
                        expb = expb2_[kt_ % 2]
                        rel = kt_ - (2 * ch - 4)
                        la = kT.ap[0:64, u * SEQ + kt_ * 128:u * SEQ + (kt_ + 1) * 128]
                        lb_ = kT.ap[64:128, u * SEQ + kt_ * 128:u * SEQ + (kt_ + 1) * 128]
                        ra = qT.ap[0:64, u * SEQ + ch * 256:u * SEQ + (ch + 1) * 256]
                        rb = qT.ap[64:128, u * SEQ + ch * 256:u * SEQ + (ch + 1) * 256]
                        p.mm_group(
                            [lambda tn, _la=la, _ra=ra: tn.matmul(
                                ps_l.ap[:, 0:256], _la, _ra, start=True, stop=True,
                                tile_position=(0, 0)),
                             lambda tn, _lb=lb_, _rb=rb: tn.matmul(
                                ps_l.ap[:, 512:768], _lb, _rb, start=True, stop=True,
                                tile_position=(64, 0))],
                            reads=(kT, qT), writes=(ps_l,))
                        p.op("scalar", lambda s, _x=expb: s.activation(_x.ap[:, 0:256],
                                                              ps_l.ap[:, 0:256],
                                                              AF.Exp, bias=csh.ap[:, 0:1]),
                             reads=(ps_l, csh), writes=(expb,))
                        p.op("scalar", lambda s, _x=expb: s.activation(_x.ap[:, 256:512],
                                                              ps_l.ap[:, 512:768],
                                                              AF.Exp, bias=csh.ap[:, 0:1]),
                             reads=(ps_l, csh), writes=(expb,))
                        mi = {0: 0, 1: 1, 4: 2, 5: 3}.get(rel)
                        if mi is not None:
                            p.op("vector", lambda v, _m=mi, _x=expb: v.tensor_tensor(
                                _x.ap.rearrange("p (h n) -> p h n", h=2),
                                _x.ap.rearrange("p (h n) -> p h n", h=2),
                                masks.ap[:, _m * 256:(_m + 1) * 256].unsqueeze(1)
                                .to_broadcast([128, 2, 256]), ALU.mult),
                                reads=(masks,), writes=(expb,))
                        st_, sp_ = first, (kt_ == kts[-1])
                        va = vnat.ap[:, kt_ * 512 + (2 * u) * 64:kt_ * 512 + (2 * u + 1) * 64]
                        vb = vnat.ap[:, kt_ * 512 + (2 * u + 1) * 64:kt_ * 512 + (2 * u + 2) * 64]
                        p.mm_group(
                            [lambda tn, _va=va, _s=st_, _e=sp_, _x=expb: tn.matmul(
                                ps_av.ap[0:64, :], _va, _x.ap[:, 0:256], start=_s, stop=_e,
                                tile_position=(0, 0)),
                             lambda tn, _vb=vb, _s=st_, _e=sp_, _x=expb: tn.matmul(
                                ps_av.ap[64:128, :], _vb, _x.ap[:, 256:512], start=_s, stop=_e,
                                skip_group_check=True, tile_position=(0, 64))],
                            reads=(vnat, expb), writes=(ps_av,))
                        if first:
                            p.op("vector", lambda v, _x=expb: v.tensor_copy(acc_s3.ap, _x.ap),
                                 reads=(expb,), writes=(acc_s3,))
                        else:
                            p.op("vector", lambda v, _x=expb: v.tensor_tensor(acc_s3.ap, acc_s3.ap,
                                                                              _x.ap, ALU.add),
                                 reads=(expb,), writes=(acc_s3,))
                        first = False
                    p.mm_group(
                        [lambda tn: tn.matmul(ps_s.ap[0:1, :], C.onescol.ap, acc_s3.ap[:, 0:256],
                                              start=True, stop=True, tile_position=(0, 0)),
                         lambda tn: tn.matmul(ps_s.ap[32:33, :], C.onescol.ap, acc_s3.ap[:, 256:512],
                                              start=True, stop=True, skip_group_check=True,
                                              tile_position=(0, 32))],
                        reads=(acc_s3, C.onescol), writes=(ps_s,))
                    p.op("scalar", lambda s: s.activation(lnrec.ap, ps_s.ap, AF.Ln),
                         reads=(ps_s,), writes=(lnrec,))
                    p.op("scalar", lambda s: s.activation(recipb.ap, lnrec.ap, AF.Exp,
                                                          scale=C.col(2, 33)),
                         reads=(lnrec, C.cols), writes=(recipb,))
                    p.op("tensor", lambda tn: tn.matmul(ps_l.ap[:, 0:256], C.onesrow.ap,
                                                        recipb.ap[0:1, :], start=True, stop=True),
                         reads=(recipb, C.onesrow), writes=(ps_l,))
                    p.op("tensor", lambda tn: tn.matmul(ps_l.ap[:, 512:768], C.onesrow33.ap[32:33, :],
                                                        recipb.ap[32:33, :], start=True, stop=True,
                                                        skip_group_check=True, tile_position=(32, 0)),
                         reads=(recipb, C.onesrow33), writes=(ps_l,))
                    p.op("vector", lambda v: v.tensor_copy(rsb.ap[0:64, :], ps_l.ap[0:64, 0:256]),
                         reads=(ps_l,), writes=(rsb,))
                    p.op("vector", lambda v: v.tensor_copy(rsb.ap[64:128, :], ps_l.ap[64:128, 512:768]),
                         reads=(ps_l,), writes=(rsb,))
                    dst = attnT.ap[:, u * SEQ + ch * 256:u * SEQ + (ch + 1) * 256]
                    p.op("vector", lambda v, _d=dst: v.tensor_tensor(_d, ps_av.ap, rsb.ap, ALU.mult),
                         reads=(ps_av, rsb), writes=(attnT,))
            for r in range(NT):
                for cc_ in range(2):
                    dstp = PA.ap[:, cc_ * 512:(cc_ + 1) * 512]
                    p.mm_group([(lambda tn, _l=attnT.ap[:, tau * SEQ + r * 128:tau * SEQ + (r + 1) * 128],
                                 _r=wo_b.ap[:, tau * 1024 + cc_ * 512:tau * 1024 + (cc_ + 1) * 512],
                                 _d=dstp, _s=(tau == 0), _e=(tau == 3):
                                 tn.matmul(_d, _l, _r, start=_s, stop=_e)) for tau in range(4)],
                               reads=(attnT, wo_b), writes=(PA,))
                p.dma("gpsimd", xt.ap, yin.ap[r * 128:(r + 1) * 128, :], reads=(yin,), writes=(xt,))
                for cc_ in range(2):
                    p.op("vector", lambda v, _c=cc_: v.tensor_tensor(
                        onat.ap[:, _c * 512:(_c + 1) * 512], PA.ap[:, _c * 512:(_c + 1) * 512],
                        xt.ap[:, _c * 512:(_c + 1) * 512], ALU.add),
                        reads=(PA, xt), writes=(onat,))
                p.dma("gpsimd", y_mid.ap[r * 128:(r + 1) * 128, :], onat.ap,
                      reads=(onat,), writes=(y_mid,))
                p.op("vector", lambda v, _r=r: v.tensor_reduce(statb.ap[:, 4 * _r:4 * _r + 1],
                                                               onat.ap[:, 0:512], AX.X, ALU.add),
                     reads=(onat,), writes=(statb,))
                p.op("vector", lambda v, _r=r: v.tensor_reduce(statb.ap[:, 4 * _r + 1:4 * _r + 2],
                                                               onat.ap[:, 512:1024], AX.X, ALU.add),
                     reads=(onat,), writes=(statb,))
                p.op("scalar", lambda s, _r=r: s.activation(sq_scr.ap[:, 0:512], onat.ap[:, 0:512],
                                                            AF.Square,
                                                            accum_out=statb.ap[:, 4 * _r + 2:4 * _r + 3]),
                     reads=(onat,), writes=(sq_scr, statb))
                p.op("scalar", lambda s, _r=r: s.activation(sq_scr.ap[:, 512:1024], onat.ap[:, 512:1024],
                                                            AF.Square,
                                                            accum_out=statb.ap[:, 4 * _r + 3:4 * _r + 4]),
                     reads=(onat,), writes=(sq_scr, statb))
            if B3_STOP_C2:
                continue
            p.mm_group([
                lambda tn: tn.matmul(ps_l.ap[:, 0:64], shm.ap[:, 0:128], statb.ap,
                                     start=True, stop=True),
                lambda tn: tn.matmul(ps_l.ap[:, 100:160], shm.ap[:, 256:384], statb.ap[:, 0:60],
                                     start=True, stop=True),
            ], reads=(statb, shm), writes=(ps_l,))
            p.op("vector", lambda v: v.tensor_copy(stsh.ap, ps_l.ap[:, 0:64]),
                 reads=(ps_l,), writes=(stsh,))
            p.op("vector", lambda v: v.tensor_tensor(stsh.ap[:, 4:64], stsh.ap[:, 4:64],
                                                     ps_l.ap[:, 100:160], ALU.add),
                 reads=(ps_l,), writes=(stsh,))
            inv = 1.0 / D
            for r in range(NT):
                p.op("vector", lambda v, _r=r: v.tensor_tensor(
                    mr.ap[:, 2:3], statb.ap[:, 4 * _r:4 * _r + 1], stsh.ap[:, 4 * _r + 1:4 * _r + 2],
                    ALU.add), reads=(statb, stsh), writes=(mr,))
                p.op("vector", lambda v, _r=r: v.tensor_scalar(mcols.ap[:, _r:_r + 1],
                                                               mr.ap[:, 2:3], inv, None, ALU.mult),
                     reads=(mr,), writes=(mcols,))
                p.op("vector", lambda v, _r=r: v.tensor_tensor(
                    mr.ap[:, 3:4], statb.ap[:, 4 * _r + 2:4 * _r + 3],
                    stsh.ap[:, 4 * _r + 3:4 * _r + 4], ALU.add),
                    reads=(statb, stsh), writes=(mr,))
                p.op("vector", lambda v: v.tensor_scalar(mr.ap[:, 3:4], mr.ap[:, 3:4], inv,
                                                         None, ALU.mult),
                     reads=(mr,), writes=(mr,))
                p.op("vector", lambda v, _r=r: v.tensor_tensor(mr.ap[:, 2:3], mcols.ap[:, _r:_r + 1],
                                                               mcols.ap[:, _r:_r + 1], ALU.mult),
                     reads=(mcols,), writes=(mr,))
                p.op("vector", lambda v: v.tensor_tensor(mr.ap[:, 3:4], mr.ap[:, 3:4], mr.ap[:, 2:3],
                                                         ALU.subtract),
                     reads=(mr,), writes=(mr,))
                p.op("scalar", lambda s: s.activation(mr.ap[:, 3:4], mr.ap[:, 3:4], AF.Ln,
                                                      bias=C.col(0)),
                     reads=(mr, C.cols), writes=(mr,))
                p.op("scalar", lambda s, _r=r: s.activation(rcols.ap[:, _r:_r + 1], mr.ap[:, 3:4],
                                                            AF.Exp, scale=C.col(1)),
                     reads=(mr, C.cols), writes=(rcols,))
            p.mm_group([
                lambda tn: tn.matmul(ps_l.ap[:, 160:176], shm.ap[:, 128:256], mcols.ap,
                                     start=True, stop=True),
                lambda tn: tn.matmul(ps_l.ap[:, 176:191], shm.ap[:, 384:512], mcols.ap[:, 1:16],
                                     start=True, stop=True),
                lambda tn: tn.matmul(ps_l.ap[:, 192:208], shm.ap[:, 128:256], rcols.ap,
                                     start=True, stop=True),
                lambda tn: tn.matmul(ps_l.ap[:, 208:223], shm.ap[:, 384:512], rcols.ap[:, 1:16],
                                     start=True, stop=True),
            ], reads=(mcols, rcols, shm), writes=(ps_l,))
            mshv = msh.ap.rearrange("p (r c) -> p r c", c=2)
            p.op("vector", lambda v: v.tensor_copy(mshv[:, :, 0:1],
                                                   ps_l.ap[:, 160:176].unsqueeze(2)),
                 reads=(ps_l,), writes=(msh,))
            p.op("vector", lambda v: v.tensor_copy(mshv[:, :, 1:2],
                                                   ps_l.ap[:, 192:208].unsqueeze(2)),
                 reads=(ps_l,), writes=(msh,))
            p.op("vector", lambda v: v.tensor_tensor(
                mshv[:, 0:15, 0:1], mshv[:, 0:15, 0:1],
                ps_l.ap[:, 176:191].unsqueeze(2), ALU.add),
                reads=(ps_l,), writes=(msh,))
            p.op("vector", lambda v: v.tensor_tensor(
                mshv[:, 0:15, 1:2], mshv[:, 0:15, 1:2],
                ps_l.ap[:, 208:223].unsqueeze(2), ALU.add),
                reads=(ps_l,), writes=(msh,))

            p.barrier()
            for t22 in range(22):
                cnt = 128 if t22 < 21 else 42
                for cc_ in range(2):
                    src = W["w2"][l][t22 * 128:t22 * 128 + cnt, cc_ * 512:(cc_ + 1) * 512]
                    p.dma("sync", wstage.ap[0:cnt, :], src, writes=(wstage,))
                    p.op("vector", lambda v, _t=t22, _c=cc_, _n=cnt: v.tensor_copy(
                        w2_b.ap[0:_n, _t * 1024 + _c * 512:_t * 1024 + (_c + 1) * 512],
                        wstage.ap[0:_n, :]),
                        reads=(wstage,), writes=(w2_b,))
            p.dma("sync", scrB.ap[0:1, 0:1024], W["b2"][l], writes=(scrB,))
            p.op("vector", lambda v: v.tensor_copy(b2b.ap, scrB.ap[0:1, 0:1024]),
                 reads=(scrB,), writes=(b2b,))
            for dt in range(8):
                p.op("vector", lambda v, _dt=dt: v.memset(h2T.ap[:, _dt * TCOL:_dt * TCOL + 1], 0.0),
                     writes=(h2T,))
            p.op("vector", lambda v: v.tensor_tensor(cell.ap[0:1, 2:3], mcols.ap[0:1, 0:1],
                                                     rcols.ap[0:1, 0:1], ALU.mult),
                 reads=(mcols, rcols), writes=(cell,))
            p.op("vector", lambda v: v.tensor_scalar(cell.ap[0:1, 2:3], cell.ap[0:1, 2:3],
                                                     -1.0, None, ALU.mult),
                 reads=(), writes=(cell,))
            p.op("vector", lambda v: v.tensor_copy(cellb.ap[0:1, 1:2], cell.ap[0:1, 2:3]),
                 reads=(cell,), writes=(cellb,))
            p.mm_group([lambda tn: tn.matmul(ps_r.ap[:, 0:1], C.onesrow.ap, cellb.ap[0:1, 1:2],
                                             start=True, stop=True)],
                       reads=(cellb, C.onesrow), writes=(ps_r,))
            for dt in range(4, 8):
                p.op("vector", lambda v, _dt=dt: v.tensor_copy(
                    h2T.ap[:, _dt * TCOL:_dt * TCOL + 1], ps_r.ap[:, 0:1]),
                    reads=(ps_r,), writes=(h2T,))
            for r in range(NT):
                p.dma("gpsimd", xt.ap, y_mid.ap[r * 128:(r + 1) * 128, :], reads=(y_mid,), writes=(xt,))
                p.op("vector", lambda v, _r=r: v.tensor_scalar(
                    lnbf.ap[:, 0:512], xt.ap[:, 0:512], mcols.ap[:, _r:_r + 1],
                    rcols.ap[:, _r:_r + 1], ALU.subtract, ALU.mult),
                    reads=(xt, mcols, rcols), writes=(lnbf,))
                p.op("vector", lambda v, _r=r: v.tensor_scalar(
                    lnbf.ap[:, 512:1024], xt.ap[:, 512:1024], msh.ap[:, 2 * _r:2 * _r + 1],
                    msh.ap[:, 2 * _r + 1:2 * _r + 2], ALU.subtract, ALU.mult),
                    reads=(xt, msh), writes=(lnbf,))
                transpose4(p, C, lnbf, lambda dt_: lnbf.ap[:, dt_ * 128:(dt_ + 1) * 128], ps_tr,
                           h2T, lambda dt_, _r=r: h2T.ap[:, dt_ * TCOL + 1 + _r * 128:
                                                         dt_ * TCOL + 1 + (_r + 1) * 128], 8)
            for ch in range(2):
                n0 = ch * 1024
                for t22 in range(22):
                    cnt = 128 if t22 < 21 else 42
                    la_ = lw1[(t22 % 2) * 2]
                    lg_ = lw1[(t22 % 2) * 2 + 1]
                    p.dma_multi("sync",
                                [(la_.ap[:, d8 * 128:d8 * 128 + cnt],
                                  w1bfL[l].ap[d8 * 128:(d8 + 1) * 128, t22 * 128:t22 * 128 + cnt])
                                 for d8 in range(8)], la_, reads=(w1bfL[l],))
                    p.dma_multi("sync",
                                [(lg_.ap[:, d8 * 128:d8 * 128 + cnt],
                                  w1bfL[l].ap[d8 * 128:(d8 + 1) * 128,
                                          2730 + t22 * 128:2730 + t22 * 128 + cnt])
                                 for d8 in range(8)], lg_, reads=(w1bfL[l],))
                    for half in range(2):
                        p.mm_group([(lambda tn, _l=la_.ap[:, d8 * 128:d8 * 128 + cnt],
                                     _r=shifted_lhs(h2T, d8, n0 + half * 512, 512),
                                     _d=PA.ap[0:cnt, 0:512] if half == 0 else PA.ap[0:cnt, 512:1024],
                                     _s=(d8 == 0), _e=(d8 == 7):
                                     tn.matmul(_d, _l, _r, start=_s, stop=_e))
                                    for d8 in range(8)],
                                   reads=(la_, h2T), writes=(PA,))
                        p.op("vector", lambda v, _t=t22, _n=cnt, _h=half, _b1=b1aL[l]:
                            v.tensor_scalar(
                            scrA.ap[0:_n, _h * 512:(_h + 1) * 512],
                            PA.ap[0:_n, _h * 512:(_h + 1) * 512],
                            _b1.ap[0:_n, _t:_t + 1], None, ALU.add),
                            reads=(PA, b1aL[l]), writes=(scrA,))
                    for half in range(2):
                        p.mm_group([(lambda tn, _l=lg_.ap[:, d8 * 128:d8 * 128 + cnt],
                                     _r=shifted_lhs(h2T, d8, n0 + half * 512, 512),
                                     _d=ps_l.ap[0:cnt, 0:512] if half == 0 else ps_l.ap[0:cnt, 512:1024],
                                     _s=(d8 == 0), _e=(d8 == 7):
                                     tn.matmul(_d, _l, _r, start=_s, stop=_e))
                                    for d8 in range(8)],
                                   reads=(lg_, h2T), writes=(ps_l,))
                        p.op("vector", lambda v, _t=t22, _n=cnt, _h=half, _b1=b1gL[l]:
                            v.tensor_scalar(
                            scrB.ap[0:_n, _h * 512:(_h + 1) * 512],
                            ps_l.ap[0:_n, _h * 512:(_h + 1) * 512],
                            _b1.ap[0:_n, _t:_t + 1], None, ALU.add),
                            reads=(ps_l, b1gL[l]), writes=(scrB,))
                    p.op("scalar", lambda s, _n=cnt: s.activation(scrB.ap[0:_n, :], scrB.ap[0:_n, :],
                                                                  AF.Gelu),
                         reads=(), writes=(scrB,))
                    p.op("vector", lambda v, _t=t22, _n=cnt: v.tensor_tensor(
                        h3seg.ap[0:_n, _t * 1024:(_t + 1) * 1024], scrA.ap[0:_n, :],
                        scrB.ap[0:_n, :], ALU.mult),
                        reads=(scrA, scrB), writes=(h3seg,))
                for nt in range(8):
                    r = ch * 8 + nt
                    wps = PA if nt % 2 == 0 else ps_l
                    p.dma("gpsimd", xt.ap, y_mid.ap[r * 128:(r + 1) * 128, :],
                          reads=(y_mid,), writes=(xt,))
                    for och in range(2):
                        dstp = wps.ap[:, och * 512:(och + 1) * 512]
                        calls = [(lambda tn, _l=h3seg.ap[0:(128 if t2 < 21 else 42),
                                                         t2 * 1024 + nt * 128:t2 * 1024 + (nt + 1) * 128],
                                  _r=w2_b.ap[0:(128 if t2 < 21 else 42),
                                             t2 * 1024 + och * 512:t2 * 1024 + (och + 1) * 512],
                                  _d=dstp, _s=(t2 == 0):
                                  tn.matmul(_d, _l, _r, start=_s, stop=False)) for t2 in range(22)]
                        calls.append(lambda tn, _d=dstp, _o=och: tn.matmul(
                            _d, C.onesrow.ap, b2b.ap[0:1, _o * 512:(_o + 1) * 512],
                            start=False, stop=True))
                        p.mm_group(calls, reads=(h3seg, w2_b, b2b, C.onesrow), writes=(wps,))
                    for och in range(2):
                        p.op("vector", lambda v, _o=och, _w=wps: v.tensor_tensor(
                            onat.ap[:, _o * 512:(_o + 1) * 512], _w.ap[:, _o * 512:(_o + 1) * 512],
                            xt.ap[:, _o * 512:(_o + 1) * 512], ALU.add),
                            reads=(wps, xt), writes=(onat,))
                    p.dma("gpsimd", ynext.ap[r * 128:(r + 1) * 128, :], onat.ap,
                          reads=(onat,), writes=(ynext,))
            p.barrier()
        for r in range(NT):
            row = b * SEQ + r * 128
            p.dma("gpsimd", yt2.ap, y_a.ap[r * 128:(r + 1) * 128, :], reads=(y_a,), writes=(yt2,))
            ln_tile(p, C, yt2, yt2.ap, mr, sq_scr)
            p.dma("sync", xt.ap, xr[row:row + 128, :], writes=(xt,))
            p.op("vector", lambda v: v.tensor_scalar(yt2.ap, yt2.ap, mr.ap[:, 1:2], mr.ap[:, 0:1],
                                                     ALU.subtract, ALU.mult),
                 reads=(mr,), writes=(yt2,))
            p.op("vector", lambda v: v.tensor_tensor(yt2.ap, yt2.ap, xt.ap, ALU.add),
                 reads=(xt,), writes=(yt2,))
            p.dma("gpsimd", xro[row:row + 128, :], yt2.ap, reads=(yt2,))


def rope_b3(p, C, qn, qbf, tabc, tabs_, qsb, which, r, nrm, sq_scr):
    hv = qn.ap.rearrange("p (h d) -> p h d", d=64)
    p.op("scalar", lambda s: s.activation(sq_scr.ap[:, 0:512], qn.ap, AF.Square),
         reads=(qn,), writes=(sq_scr,))
    p.op("vector", lambda v: v.tensor_reduce(nrm.ap, sq_scr.ap[:, 0:512]
                                             .rearrange("p (h d) -> p h d", d=64), AX.X, ALU.add),
         reads=(sq_scr,), writes=(nrm,))
    p.op("vector", lambda v: v.tensor_scalar(nrm.ap, nrm.ap, 1.0, 1e-24, ALU.mult, ALU.max),
         reads=(), writes=(nrm,))
    p.op("scalar", lambda s: s.activation(nrm.ap, nrm.ap, AF.Ln),
         reads=(), writes=(nrm,))
    p.op("scalar", lambda s: s.activation(nrm.ap, nrm.ap, AF.Exp, scale=C.col(1)),
         reads=(C.cols,), writes=(nrm,))
    p.op("vector", lambda v: v.tensor_tensor(hv, hv, nrm.ap.unsqueeze(2).to_broadcast([128, 8, 64]),
                                             ALU.mult),
         reads=(nrm,), writes=(qn,))
    sc = qsb.ap[:, which * 64:(which + 1) * 64].unsqueeze(1).to_broadcast([128, 8, 64])
    p.op("vector", lambda v, _s=sc: v.tensor_tensor(hv, hv, _s, ALU.mult),
         reads=(qsb,), writes=(qn,))
    cs = tabc.ap[:, r * 64:(r + 1) * 64]
    sn = tabs_.ap[:, r * 64:(r + 1) * 64]
    bv = qbf.ap.rearrange("p (h d) -> p h d", d=64)
    x1, x2 = hv[:, :, 0:32], hv[:, :, 32:64]
    c1 = cs[:, 0:32].unsqueeze(1).to_broadcast([128, 8, 32])
    c2 = cs[:, 32:64].unsqueeze(1).to_broadcast([128, 8, 32])
    s1 = sn[:, 0:32].unsqueeze(1).to_broadcast([128, 8, 32])
    s2 = sn[:, 32:64].unsqueeze(1).to_broadcast([128, 8, 32])
    b1_, b2_ = bv[:, :, 0:32], bv[:, :, 32:64]
    p.op("vector", lambda v, _a=b1_, _x=x2, _s=s1: v.tensor_tensor(_a, _x, _s, ALU.mult),
         reads=(qn, tabs_), writes=(qbf,))
    p.op("vector", lambda v, _a=b2_, _x=x1, _s=s2: v.tensor_tensor(_a, _x, _s, ALU.mult),
         reads=(qn, tabs_), writes=(qbf,))
    p.op("vector", lambda v, _x=x1, _c=c1: v.tensor_tensor(_x, _x, _c, ALU.mult),
         reads=(tabc,), writes=(qn,))
    p.op("vector", lambda v, _x=x2, _c=c2: v.tensor_tensor(_x, _x, _c, ALU.mult),
         reads=(tabc,), writes=(qn,))
    p.op("vector", lambda v, _a=b1_, _x=x1: v.tensor_tensor(_a, _x, _a, ALU.subtract),
         reads=(qn,), writes=(qbf,))
    p.op("vector", lambda v, _a=b2_, _x=x2: v.tensor_tensor(_a, _x, _a, ALU.add),
         reads=(qn,), writes=(qbf,))


def build_b3_host():
    DH, WIN = 64, 512
    half = np.arange(0, DH, 2, dtype=np.float64)
    inv_freq = 1.0 / 10000.0 ** (half / DH)
    t = np.arange(SEQ, dtype=np.float64)
    freqs = t[:, None] * inv_freq[None, :]
    base = (half + 0.4 * DH) / (1.4 * DH)
    power = (t - SEQ // 2) / (WIN // 2)
    scale = base[None, :] ** power[:, None]
    cosf, sinf = np.cos(freqs), np.sin(freqs)
    qc = np.concatenate([cosf * scale, cosf * scale], axis=1).astype(BF16NP)
    qs = np.concatenate([sinf * scale, sinf * scale], axis=1).astype(BF16NP)
    kc = np.concatenate([cosf / scale, cosf / scale], axis=1).astype(BF16NP)
    ks = np.concatenate([sinf / scale, sinf / scale], axis=1).astype(BF16NP)
    kk, qq = np.meshgrid(np.arange(128), np.arange(128), indexing="ij")
    tri = (kk <= qq).astype(BF16NP)
    anti = (kk >= qq).astype(BF16NP)
    ones = np.ones((128, 128), BF16NP)
    zeros = np.zeros((128, 128), BF16NP)
    m0 = np.concatenate([anti, zeros], axis=1)
    m1 = np.concatenate([ones, anti], axis=1)
    m4 = np.concatenate([tri, ones], axis=1)
    m5 = np.concatenate([zeros, tri], axis=1)
    masks = np.concatenate([m0, m1, m4, m5], axis=1)
    return (qc, qs, kc, ks), masks


def make_block3_drams(nc):
    mk = lambda n, s, d=F32: Buf(nc.dram_tensor(n, s, d).ap(), n)
    return {
        "y_a": mk("b3_y_a", [SEQ, D], BF16),
        "y_b": mk("b3_y_b", [SEQ, D], BF16),
        "y_mid": mk("b3_y_mid", [SEQ, D], BF16),
        "w1bf0": mk("b3_w1bf0", [1024, 5460], BF16),
        "w1bf1": mk("b3_w1bf1", [1024, 5460], BF16),
    }


def build_block3_nc():
    from contextlib import ExitStack
    nc = bass.Bass("TRN2", target_bir_lowering=False)
    x_in = nc.dram_tensor("x_in", [BPC, SEQ, D], F32, kind="ExternalInput").ap()
    x_out = nc.dram_tensor("x_out", [BPC, SEQ, D], F32, kind="ExternalOutput").ap()
    wd = {
        "wqkv": [nc.dram_tensor(f"wqkv{l}", [D, 1536], F32, kind="ExternalInput").ap() for l in range(2)],
        "wo": [nc.dram_tensor(f"wo{l}", [512, D], F32, kind="ExternalInput").ap() for l in range(2)],
        "w1": [nc.dram_tensor(f"w1_{l}", [D, 5460], F32, kind="ExternalInput").ap() for l in range(2)],
        "w2": [nc.dram_tensor(f"w2_{l}", [B3_HID, D], F32, kind="ExternalInput").ap() for l in range(2)],
        "b1": [nc.dram_tensor(f"b1_{l}", [1, 5460], F32, kind="ExternalInput").ap() for l in range(2)],
        "b2": [nc.dram_tensor(f"b2_{l}", [1, D], F32, kind="ExternalInput").ap() for l in range(2)],
        "lng": [nc.dram_tensor(f"lng{l}", [D], F32, kind="ExternalInput").ap() for l in range(2)],
        "lnb": [nc.dram_tensor(f"lnb{l}", [D], F32, kind="ExternalInput").ap() for l in range(2)],
        "qscale": [nc.dram_tensor(f"qsc{l}", [1, 64], F32, kind="ExternalInput").ap() for l in range(2)],
        "kscale": [nc.dram_tensor(f"ksc{l}", [1, 64], F32, kind="ExternalInput").ap() for l in range(2)],
        "xpos": [nc.dram_tensor(f"xpos{i}", [SEQ, 64], BF16, kind="ExternalInput").ap() for i in range(4)],
        "masks": nc.dram_tensor("b3masks", [128, 1024], BF16, kind="ExternalInput").ap(),
    }
    capd = {}
    cnp = build_constants()
    for k in ("ident", "onescol", "onesrow", "sel2"):
        capd[k] = nc.dram_tensor("cin_" + k, list(cnp[k].shape), BF16, kind="ExternalInput").ap()
    stack = ExitStack()
    p = Prog(nc)
    C = Consts(p, stack, capd)
    dram = make_block3_drams(nc)
    emit_block3(p, stack, C, x_in, x_out, wd, dram)
    p.barrier()
    p.emit(stack)
    stack.close()
    return nc, cnp


def block3_inmaps(inputs, x_full, cnp):
    xpos, masks = build_b3_host()
    base = {}
    for l in range(2):
        base[f"wqkv{l}"] = np.ascontiguousarray(inputs["L1_ffn_wqkv"][l])
        base[f"wo{l}"] = np.ascontiguousarray(inputs["L1_ffn_wo"][l])
        base[f"w1_{l}"] = np.ascontiguousarray(inputs["L1_ffn_w1"][l])
        base[f"w2_{l}"] = np.ascontiguousarray(inputs["L1_ffn_w2"][l])
        base[f"b1_{l}"] = np.ascontiguousarray(inputs["L1_ffn_b1"][l]).reshape(1, 5460)
        base[f"b2_{l}"] = np.ascontiguousarray(inputs["L1_ffn_b2"][l]).reshape(1, D)
        base[f"lng{l}"] = np.ascontiguousarray(inputs["L1_ffn_ln_g"][l])
        base[f"lnb{l}"] = np.ascontiguousarray(inputs["L1_ffn_ln_b"][l])
        base[f"qsc{l}"] = np.ascontiguousarray(inputs["L1_ffn_q_scale"][l]).reshape(1, 64)
        base[f"ksc{l}"] = np.ascontiguousarray(inputs["L1_ffn_k_scale"][l]).reshape(1, 64)
    for i in range(4):
        base[f"xpos{i}"] = xpos[i]
    base["b3masks"] = masks
    for k in ("ident", "onescol", "onesrow", "sel2"):
        base["cin_" + k] = cnp[k]
    in_maps = []
    for c in range(NCORES):
        m = dict(base)
        m["x_in"] = np.ascontiguousarray(x_full[c * BPC:(c + 1) * BPC])
        in_maps.append(m)
    return in_maps


def run_block3(inputs, x_full):
    nc, cnp = build_block3_nc()
    in_maps = block3_inmaps(inputs, x_full, cnp)
    res = run_bass_kernel_spmd(nc, in_maps, core_ids=list(range(NCORES)))
    return np.concatenate([r["x_out"].reshape(BPC, SEQ, D) for r in res.results], axis=0)


def build_fused_nc():
    from contextlib import ExitStack
    nc = bass.Bass("TRN2", target_bir_lowering=False)
    x0 = nc.dram_tensor("x_in", [BPC, SEQ, D], F32, kind="ExternalInput").ap()
    x_out = nc.dram_tensor("x_out", [BPC, SEQ, D], F32, kind="ExternalOutput").ap()
    x1 = nc.dram_tensor("fx1", [BPC, SEQ, D], F32).ap()
    x2 = nc.dram_tensor("fx2", [BPC, SEQ, D], F32).ap()
    x3 = nc.dram_tensor("fx3", [BPC, SEQ, D], F32).ap()
    w0 = {
        "Wq": nc.dram_tensor("Wq", [D, D], F32, kind="ExternalInput").ap(),
        "Wk": nc.dram_tensor("Wk", [D, 512], F32, kind="ExternalInput").ap(),
        "Wv": nc.dram_tensor("Wv", [D, 512], F32, kind="ExternalInput").ap(),
        "Wo": nc.dram_tensor("Wo", [D, D], F32, kind="ExternalInput").ap(),
        "bo": nc.dram_tensor("bo", [1, D], F32, kind="ExternalInput").ap(),
        "rope_cos": nc.dram_tensor("rope_cos", [SEQ, 16], F32, kind="ExternalInput").ap(),
        "rope_sin": nc.dram_tensor("rope_sin", [SEQ, 16], F32, kind="ExternalInput").ap(),
    }
    w1 = {
        "w_up_sl": nc.dram_tensor("w_up_sl", [3072, 128], F32, kind="ExternalInput").ap(),
        "w_gate_sl": nc.dram_tensor("w_gate_sl", [3072, 128], F32, kind="ExternalInput").ap(),
        "w_down_sl": nc.dram_tensor("w_down_sl", [1024, 384], F32, kind="ExternalInput").ap(),
    }
    w2 = {
        "wkT": nc.dram_tensor("wkT", [D, D], F32, kind="ExternalInput").ap(),
        "wvT": nc.dram_tensor("wvT", [D, D], F32, kind="ExternalInput").ap(),
        "wqT": nc.dram_tensor("wqT", [D, D], F32, kind="ExternalInput").ap(),
    }
    w3 = {
        "wqkv": [nc.dram_tensor(f"wqkv{l}", [D, 1536], F32, kind="ExternalInput").ap() for l in range(2)],
        "wo": [nc.dram_tensor(f"wo{l}", [512, D], F32, kind="ExternalInput").ap() for l in range(2)],
        "w1": [nc.dram_tensor(f"w1_{l}", [D, 5460], F32, kind="ExternalInput").ap() for l in range(2)],
        "w2": [nc.dram_tensor(f"w2_{l}", [B3_HID, D], F32, kind="ExternalInput").ap() for l in range(2)],
        "b1": [nc.dram_tensor(f"b1_{l}", [1, 5460], F32, kind="ExternalInput").ap() for l in range(2)],
        "b2": [nc.dram_tensor(f"b2_{l}", [1, D], F32, kind="ExternalInput").ap() for l in range(2)],
        "lng": [nc.dram_tensor(f"lng{l}", [D], F32, kind="ExternalInput").ap() for l in range(2)],
        "lnb": [nc.dram_tensor(f"lnb{l}", [D], F32, kind="ExternalInput").ap() for l in range(2)],
        "qscale": [nc.dram_tensor(f"qsc{l}", [1, 64], F32, kind="ExternalInput").ap() for l in range(2)],
        "kscale": [nc.dram_tensor(f"ksc{l}", [1, 64], F32, kind="ExternalInput").ap() for l in range(2)],
        "xpos": [nc.dram_tensor(f"xpos{i}", [SEQ, 64], BF16, kind="ExternalInput").ap() for i in range(4)],
        "masks": nc.dram_tensor("b3masks", [128, 1024], BF16, kind="ExternalInput").ap(),
    }
    cnp = build_constants()
    capd = {k: nc.dram_tensor("cin_" + k, list(cnp[k].shape), BF16, kind="ExternalInput").ap()
            for k in ("ident", "onescol", "onesrow", "sel2")}
    stack = ExitStack()
    p = Prog(nc)
    C = Consts(p, stack, capd)
    d1 = make_block1_drams(nc)
    d2 = make_block2_drams(nc)
    d3 = make_block3_drams(nc)
    sub = ExitStack()
    emit_block0(p, sub, C, x0, x1, w0)
    p.barrier()
    sub.close()
    sub = ExitStack()
    emit_block1(p, sub, C, x1, x2, w1, d1)
    p.barrier()
    sub.close()
    sub = ExitStack()
    emit_block2(p, sub, C, x2, x3, w2, d2)
    p.barrier()
    sub.close()
    sub = ExitStack()
    emit_block3(p, sub, C, x3, x_out, w3, d3)
    p.barrier()
    sub.close()
    p.emit(stack)
    stack.close()
    return nc, cnp


def fused_inmaps(inputs, cnp):
    x = np.ascontiguousarray(inputs["L0_attn_x"], dtype=np.float32)
    m0 = prep_block0_host(inputs)
    wu = np.ascontiguousarray(inputs["L0_ffn_w_up"])
    wg = np.ascontiguousarray(inputs["L0_ffn_w_gate"])
    wdn = np.ascontiguousarray(inputs["L0_ffn_w_down"])
    wk = np.ascontiguousarray(inputs["L1_attn_W_k"].T)
    wv = np.ascontiguousarray(inputs["L1_attn_W_v"].T)
    wq = np.ascontiguousarray(inputs["L1_attn_W_q"].T)
    xpos, masks = build_b3_host()
    base = dict(m0)
    base.update({"wkT": wk, "wvT": wv, "wqT": wq, "b3masks": masks,
                 "rope_cos": cnp["rope_cos"], "rope_sin": cnp["rope_sin"]})
    for i in range(4):
        base[f"xpos{i}"] = xpos[i]
    for l in range(2):
        base[f"wqkv{l}"] = np.ascontiguousarray(inputs["L1_ffn_wqkv"][l])
        base[f"wo{l}"] = np.ascontiguousarray(inputs["L1_ffn_wo"][l])
        base[f"w1_{l}"] = np.ascontiguousarray(inputs["L1_ffn_w1"][l])
        base[f"w2_{l}"] = np.ascontiguousarray(inputs["L1_ffn_w2"][l])
        base[f"b1_{l}"] = np.ascontiguousarray(inputs["L1_ffn_b1"][l]).reshape(1, 5460)
        base[f"b2_{l}"] = np.ascontiguousarray(inputs["L1_ffn_b2"][l]).reshape(1, D)
        base[f"lng{l}"] = np.ascontiguousarray(inputs["L1_ffn_ln_g"][l])
        base[f"lnb{l}"] = np.ascontiguousarray(inputs["L1_ffn_ln_b"][l])
        base[f"qsc{l}"] = np.ascontiguousarray(inputs["L1_ffn_q_scale"][l]).reshape(1, 64)
        base[f"ksc{l}"] = np.ascontiguousarray(inputs["L1_ffn_k_scale"][l]).reshape(1, 64)
    for k in ("ident", "onescol", "onesrow", "sel2"):
        base["cin_" + k] = cnp[k]
    in_maps = []
    for c in range(NCORES):
        m = dict(base)
        m["x_in"] = np.ascontiguousarray(x[c * BPC:(c + 1) * BPC])
        m["w_up_sl"] = np.ascontiguousarray(wu[:, c * 128:(c + 1) * 128])
        m["w_gate_sl"] = np.ascontiguousarray(wg[:, c * 128:(c + 1) * 128])
        m["w_down_sl"] = np.ascontiguousarray(wdn[:, c * 384:(c + 1) * 384])
        in_maps.append(m)
    return in_maps


FUSED = True


def kernel(**inputs):
    if not FUSED:
        x = np.ascontiguousarray(inputs["L0_attn_x"], dtype=np.float32)
        x = run_block0(inputs, x)
        x = run_block1(inputs, x)
        x = run_block2(inputs, x)
        x = run_block3(inputs, x)
        return x
    nc, cnp = build_fused_nc()
    in_maps = fused_inmaps(inputs, cnp)
    res = run_bass_kernel_spmd(nc, in_maps, core_ids=list(range(NCORES)))
    return np.concatenate([r["x_out"].reshape(BPC, SEQ, D) for r in res.results], axis=0)
